# version 10
# baseline (speedup 1.0000x reference)
import numpy as np
from contextlib import ExitStack

import concourse.bass as bass
import concourse.mybir as mybir
from concourse.bass_utils import run_bass_kernel_spmd

F32 = mybir.dt.float32
BF16 = mybir.dt.bfloat16
AF = mybir.ActivationFunctionType
ALU = mybir.AluOpType
AX = mybir.AxisListType

D, L, NB, NST, E, INTER = 1024, 2048, 2, 16, 16, 512
NCORE = 8
J = 32
NEG = -1e30
EPS = 1e-5
NDT, NCH = 8, 4
STAGE = 3


class B:
    def __init__(self, debug=False):
        self.debug = debug
        nc = self.nc = bass.Bass("TRN2", target_bir_lowering=False, debug=False)
        self.ops = {k: [] for k in ("sy", "g", "t", "v", "s")}
        self.h = {}
        self.c = {}
        for s in ("qS", "qG", "tE", "vE", "sE", "pE", "F"):
            self.h[s] = nc.alloc_semaphore(s)
            self.c[s] = 0

    def on(self, eng):
        def deco(fn):
            self.ops[eng].append(fn)
        return deco

    def bump(self, s, k=1):
        self.c[s] += k
        return self.c[s]

    def fence(self):
        tgt = self.c["F"] + 5
        hF = self.h["F"]
        qs_now = self.c["qS"]
        qg_now = self.c["qG"]
        for k in ("sy", "g", "t", "v", "s"):
            def go(e, _t=tgt, _k=k, _qs=qs_now, _qg=qg_now):
                if _k in ("v", "s", "g"):
                    e.drain()
                if _k == "sy":
                    e.wait_ge(self.h["qS"], _qs)
                if _k == "g":
                    e.wait_ge(self.h["qG"], _qg)
                e.nop().then_inc(hF, 1)
                e.wait_ge(hF, _t)
            self.ops[k].append(go)
        self.c["F"] = tgt

    def flush(self, block):
        ops = self.ops
        self.ops = {k: [] for k in ops}
        block.sync(lambda e, o=tuple(ops["sy"]): [f(e) for f in o])
        block.gpsimd(lambda e, o=tuple(ops["g"]): [f(e) for f in o])
        block.tensor(lambda e, o=tuple(ops["t"]): [f(e) for f in o])
        block.vector(lambda e, o=tuple(ops["v"]): [f(e) for f in o])
        block.scalar(lambda e, o=tuple(ops["s"]): [f(e) for f in o])

    def declare(self):
        nc = self.nc
        self.x_in = nc.dram_tensor("x", [NB, L, D], F32, kind="ExternalInput").ap()
        self.w = {}
        for name, shape in [
            ("A", [D, NST, NST]), ("Bp", [D, NST]), ("Cp", [D, NST]), ("Dp", [D]),
            ("Wg", [D, D]), ("bg", [D]), ("gamma", [D]), ("beta", [D]),
            ("gate_w", [E, D]), ("gate_b", [E]),
            ("We1", [E, D, INTER]), ("be1", [E, INTER]), ("We2", [E, INTER, D]),
            ("be2", [E, D]), ("We3", [E, D, INTER]), ("be3", [E, INTER]),
            ("ws1", [D, D]), ("bs1", [D]), ("ws2", [D, D]), ("bs2", [D]),
            ("ws3", [D, D]), ("bs3", [D]),
        ]:
            self.w[name] = nc.dram_tensor(name, shape, F32, kind="ExternalInput").ap()
        self.ident_in = nc.dram_tensor("c_ident", [128, 128], F32, kind="ExternalInput").ap()
        self.sel_in = nc.dram_tensor("c_sel", [E, E * 128], F32, kind="ExternalInput").ap()
        self.ones_in = nc.dram_tensor("c_ones", [1, 128], F32, kind="ExternalInput").ap()
        self.out = nc.dram_tensor("out", [NB, L, D], F32, kind="ExternalOutput").ap()
        self.dbg = {}
        if self.debug:
            for nm, shape in [("x1", [NB, L, D]), ("comb", [NB, E, L]),
                              ("ln3", [NB, L, D]), ("xg", [NB, NDT, 128, L]),
                              ("y", [NB, L, D]), ("ln1", [NB, L, D]),
                              ("gate", [NB, NDT, 128, L]), ("wgT", [D, D]),
                              ("diag", [NDT, 128, J * 128]), ("vcol", [128, 5 * NDT]),
                              ("hbuf", [128, NDT * J]), ("gammab", [128, D]),
                              ("rt", [20, 128, 64]), ("yacc", [NDT, 128, L]), ("ffn", [128, 4096])]:
                self.dbg[nm] = nc.dram_tensor("dbg_" + nm, shape, F32, kind="ExternalOutput").ap()
        dr = lambda n, s, dt=BF16: nc.dram_tensor(n, s, dt, kind="Internal").ap()
        self.wb = {
            "We1": dr("wb_we1", [E, D, INTER]), "We2": dr("wb_we2", [E, INTER, D]),
            "We3": dr("wb_we3", [E, D, INTER]), "ws1": dr("wb_ws1", [D, D]),
            "ws2": dr("wb_ws2", [D, D]), "ws3": dr("wb_ws3", [D, D]),
            "Wg": dr("wb_wg", [D, D]),
        }
        self.wgT_dram = dr("wgT_dram", [D, D])
        self.gb_dram = dr("gb_dram", [2, 128, D], F32)
        self.zz_dram = dr("zz_dram", [NDT, 128, L])
        self.diag_dram = dr("diag_dram", [NDT, 128, J * 128])
        self.xg_f = dr("xg_f", [NDT, 128, L], F32)
        self.xg_b = dr("xg_b", [NDT, 128, J + L])
        self.x1_scr = dr("x1_scr", [L, D], F32)
        self.yacc = dr("yacc", [NDT, 128, L], F32)

    def persistent(self, SB, PSm):
        self.ident = SB("ident", [128, 128])
        self.sel_sb = SB("sel_sb", [E, E * 128])
        self.ones_sb = SB("ones_sb", [1, 128])
        self.gate_wT = SB("gate_wT", [128, NDT * E])
        self.gb_bcast = SB("gb_bcast", [128, E])
        self.vcol = SB("vcol", [128, 5 * NDT])
        self.be1c = SB("be1c", [128, 4 * E])
        self.be3c = SB("be3c", [128, 4 * E])
        self.epscol = SB("epscol", [128, 1])
        self.combine_T = SB("combine_T", [E, L])
        self.hbuf = SB("hbuf", [128, NDT * J])
        self.zpad = SB("zpad", [128, J], BF16)
        self.s1c = SB("s1c", [128, 4]); self.s2c = SB("s2c", [128, 4])
        self.muc = SB("muc", [128, 4]); self.varc = SB("varc", [128, 4])
        self.rstdc = SB("rstdc", [128, 4]); self.tnc = SB("tnc", [128, 4])
        self.RT_SIZES = [
            ("m", 4), ("rec", 4), ("sum", 4), ("sd", 64), ("ex", 64), ("orig", 64),
            ("sc2", 64), ("m1", 16), ("eqa", 64), ("sgm", 64), ("m2", 16),
            ("gsc", 16), ("gm1", 4), ("eqg", 16), ("gsc2", 16), ("gm2", 4),
            ("keep", 16), ("smt1", 64), ("smt2", 64), ("sm", 64), ("t1", 4),
            ("eq1", 64), ("sm2", 64), ("t2", 4), ("eq2", 64), ("comb", 64)]
        self.PA = PSm("psA", [128, 2048])
        self.PB = PSm("psB", [128, 2048])
        self.P = [self.PA[:, i * 512:(i + 1) * 512] for i in range(4)] + \
                 [self.PB[:, i * 512:(i + 1) * 512] for i in range(4)]

    def col(self, which, dt):
        k = {"bg": 0, "Dp": 1, "bs1": 2, "bs2": 3, "bs3": 4}[which]
        return self.vcol[:, k * NDT + dt: k * NDT + dt + 1]

    def emit_rstd(self, n=4):
        h, c = self.h, self.c
        vE_stats = self.bump("vE")

        @self.on("v")
        def _(v):
            v.drain()
            v.nop().then_inc(h["vE"], 1)
        sE_rstd = self.bump("sE")

        @self.on("s")
        def _(s, _w=vE_stats):
            s.wait_ge(h["vE"], _w)
            s.activation(self.rstdc[:, 0:n], self.varc[:, 0:n], AF.Ln,
                         bias=self.epscol[:, 0:1], scale=1.0)
            s.drain()
            s.activation(self.rstdc[:, 0:n], self.rstdc[:, 0:n], AF.Exp, scale=-0.5)
            s.drain()
            s.nop().then_inc(h["sE"], 1)

        @self.on("v")
        def _(v, _w=sE_rstd):
            v.wait_ge(h["sE"], _w)
            rc, tn, vc = self.rstdc[:, 0:n], self.tnc[:, 0:n], self.varc[:, 0:n]
            v.tensor_tensor(tn, rc, rc, ALU.mult); v.drain()
            v.tensor_tensor(tn, tn, vc, ALU.mult); v.drain()
            v.tensor_scalar(tn, tn, -0.5, 1.5, ALU.mult, ALU.add); v.drain()
            v.tensor_tensor(rc, rc, tn, ALU.mult); v.drain()

    def emit_stats(self, tiles):
        @self.on("v")
        def _(v):
            for tt, tile in enumerate(tiles):
                v.tensor_reduce(self.s1c[:, tt:tt + 1], tile, AX.X, ALU.add)
                v.scalar_tensor_tensor(self.sq[:], tile, 1.0, tile, ALU.mult, ALU.mult,
                                       accum_out=self.s2c[:, tt:tt + 1])
            v.drain()
            v.tensor_scalar(self.muc[:], self.s1c[:], 1.0 / D, None, ALU.mult)
            v.tensor_scalar(self.varc[:], self.s2c[:], 1.0 / D, None, ALU.mult)
            v.drain()
            v.tensor_tensor(self.tnc[:], self.muc[:], self.muc[:], ALU.mult)
            v.drain()
            v.tensor_tensor(self.varc[:], self.varc[:], self.tnc[:], ALU.subtract)
            v.drain()

    def startup(self):
        nc, h, w = self.nc, self.h, self.w
        st = ExitStack()
        S = lambda n, s, dt=F32: st.enter_context(nc.sbuf_tensor(n, s, dt))
        vrow = S("vrow", [8, 5 * 128])
        grow = S("grow", [1, D])
        brow = S("brow", [1, D])
        gbrow = S("gbrow", [1, E])
        gwt_st = S("gwt_st", [16, D])
        be_st = S("be_st", [16, 2 * INTER])
        wg_st = S("wg_st", [128, 2 * D])
        wgT_st = S("wgT_st", [128, 2 * D], BF16)
        gbst = S("gbst", [128, 2 * D])
        P = self.P

        with st, nc.Block() as blk:
            @self.on("sy")
            def _(sy):
                for dst, src in [
                    (self.ident[:], self.ident_in), (self.ones_sb[:], self.ones_in),
                    (self.sel_sb[:], self.sel_in),
                    (grow[0:1, :], w["gamma"].rearrange("(a d) -> a d", a=1)),
                    (brow[0:1, :], w["beta"].rearrange("(a d) -> a d", a=1)),
                    (gbrow[:], w["gate_b"].rearrange("(a e) -> a e", a=1)),
                    (gwt_st[:], w["gate_w"]),
                    (vrow[0:8, 0:128], w["bg"].rearrange("(dt p) -> dt p", p=128)),
                    (vrow[0:8, 128:256], w["Dp"].rearrange("(dt p) -> dt p", p=128)),
                    (vrow[0:8, 256:384], w["bs1"].rearrange("(dt p) -> dt p", p=128)),
                    (vrow[0:8, 384:512], w["bs2"].rearrange("(dt p) -> dt p", p=128)),
                    (vrow[0:8, 512:640], w["bs3"].rearrange("(dt p) -> dt p", p=128)),
                    (be_st[:, 0:INTER], w["be1"]),
                    (be_st[:, INTER:], w["be3"]),
                ]:
                    sy.dma_start(out=dst, in_=src).then_inc(h["qS"], 16)
            qS_loads = self.bump("qS", 16 * 14)


            @self.on("v")
            def _(v):
                v.memset(self.epscol[:], EPS)
                v.memset(self.zpad[:], 0.0)
                v.drain()

            tE_sm = self.bump("tE")

            @self.on("t")
            def _(t, _w=qS_loads):
                t.wait_ge(h["qS"], _w)
                for k in range(5):
                    t.transpose(P[0][0:128, k * 8:(k + 1) * 8],
                                vrow[0:8, k * 128:(k + 1) * 128], self.ident[0:8, 0:8])
                t.matmul(P[1][:, 0:512], self.ones_sb[:, 0:128], grow[0:1, 0:512], start=True, stop=True)
                t.matmul(P[2][:, 0:512], self.ones_sb[:, 0:128], grow[0:1, 512:1024], start=True, stop=True)
                t.matmul(P[3][:, 0:512], self.ones_sb[:, 0:128], brow[0:1, 0:512], start=True, stop=True)
                t.matmul(P[4][:, 0:512], self.ones_sb[:, 0:128], brow[0:1, 512:1024], start=True, stop=True)
                t.matmul(P[5][:, 0:E], self.ones_sb[:, 0:128], gbrow[0:1, :], start=True, stop=True)
                for dt in range(NDT):
                    t.transpose(P[6][0:128, dt * 16:(dt + 1) * 16],
                                gwt_st[0:16, dt * 128:(dt + 1) * 128], self.ident[0:16, 0:16])
                mmi = None
                for ib in range(4):
                    t.transpose(P[7][0:128, ib * 16:(ib + 1) * 16],
                                be_st[0:16, ib * 128:(ib + 1) * 128], self.ident[0:16, 0:16])
                    mmi = t.transpose(P[7][0:128, 64 + ib * 16: 64 + (ib + 1) * 16],
                                      be_st[0:16, INTER + ib * 128: INTER + (ib + 1) * 128],
                                      self.ident[0:16, 0:16])
                mmi.then_inc(h["tE"], 1)

            sE_sm = self.bump("sE")
            self._sE_sm = sE_sm

            @self.on("s")
            def _(s, _w=tE_sm):
                s.wait_ge(h["tE"], _w)
                s.copy(self.vcol[:, 0:40], P[0][:, 0:40])
                s.copy(gbst[:, 0:512], P[1][:, 0:512])
                s.copy(gbst[:, 512:1024], P[2][:, 0:512])
                s.copy(gbst[:, 1024:1536], P[3][:, 0:512])
                s.copy(gbst[:, 1536:2048], P[4][:, 0:512])
                s.copy(self.gb_bcast[:], P[5][:, 0:E])
                s.copy(self.gate_wT[:], P[6][:, 0:NDT * E])
                s.copy(self.be1c[:], P[7][:, 0:64])
                s.copy(self.be3c[:], P[7][:, 64:128])
                s.drain()
                s.nop().then_inc(h["sE"], 1)

            qG_gb = self.bump("qG", 16)

            @self.on("g")
            def _(g, _w=sE_sm):
                g.wait_ge(h["sE"], _w)
                g.dma_start(out=self.gb_dram.rearrange("a q d2 -> q a d2"),
                            in_=gbst[:].rearrange("q (a d2) -> q a d2", d2=D)).then_inc(h["qG"], 16)

            qS_wg0 = self.c["qS"] + 16
            self.bump("qS", 16 * NDT)
            tE_wg0 = self.c["tE"] + 1
            self.bump("tE", NDT)
            sE_wg0 = self.c["sE"] + 1
            self.bump("sE", NDT)
            qG_wgT0 = self.c["qG"] + 16
            self.bump("qG", 16 * NDT)

            @self.on("sy")
            def _(sy):
                for odt in range(NDT):
                    if odt >= 2:
                        sy.wait_ge(h["tE"], tE_wg0 + odt - 2)
                    sy.dma_start(out=wg_st[:, (odt % 2) * D:(odt % 2) * D + D],
                                 in_=w["Wg"].rearrange("(odt q) d2 -> odt q d2", q=128)[odt]
                                 ).then_inc(h["qS"], 16)

            @self.on("t")
            def _(t, _s=sE_sm):
                t.wait_ge(h["sE"], _s)
                for odt in range(NDT):
                    t.wait_ge(h["qS"], qS_wg0 + odt * 16)
                    if odt >= 1:
                        t.wait_ge(h["sE"], sE_wg0 + odt - 1)
                    b0, b1 = 2 * (odt % 2), 2 * (odt % 2) + 1
                    mmi = None
                    for kdt in range(NDT):
                        pb = P[b0] if kdt < 4 else P[b1]
                        mmi = t.transpose(pb[0:128, (kdt % 4) * 128:(kdt % 4 + 1) * 128],
                                          wg_st[:, (odt % 2) * D + kdt * 128:(odt % 2) * D + (kdt + 1) * 128],
                                          self.ident[:])
                    mmi.then_inc(h["tE"], 1)

            @self.on("s")
            def _(s):
                for odt in range(NDT):
                    s.wait_ge(h["tE"], tE_wg0 + odt)
                    if odt >= 2:
                        s.wait_ge(h["qG"], qG_wgT0 + 16 * (odt - 2))
                    b0, b1 = 2 * (odt % 2), 2 * (odt % 2) + 1
                    s.copy(wgT_st[:, (odt % 2) * D:(odt % 2) * D + 512], P[b0][:, 0:512])
                    s.copy(wgT_st[:, (odt % 2) * D + 512:(odt % 2) * D + D], P[b1][:, 0:512])
                    s.drain()
                    s.nop().then_inc(h["sE"], 1)

            @self.on("g")
            def _(g):
                for odt in range(NDT):
                    g.wait_ge(h["sE"], sE_wg0 + odt)
                    g.dma_start(out=self.wgT_dram.rearrange("(kdt p) e2 -> p kdt e2", p=128)[:, :, odt * 128:(odt + 1) * 128],
                                in_=wgT_st[:, (odt % 2) * D:(odt % 2) * D + D]
                                .rearrange("p (kdt q) -> p kdt q", q=128)).then_inc(h["qG"], 16)

            if self.debug and "wgT" in self.dbg:
                self.bump("qG", 80)

                @self.on("g")
                def _(g, _q=self.c["qG"] - 80):
                    g.wait_ge(h["qG"], _q)
                    g.dma_start(out=self.dbg["wgT"], in_=self.wgT_dram).then_inc(h["qG"], 16)
                    g.dma_start(out=self.dbg["diag"], in_=self.diag_dram).then_inc(h["qG"], 16)
                    g.dma_start(out=self.dbg["vcol"], in_=self.vcol[:]).then_inc(h["qG"], 16)
                    g.dma_start(out=self.dbg["hbuf"], in_=self.hbuf[:]).then_inc(h["qG"], 16)
                    g.dma_start(out=self.dbg["gammab"], in_=self.gb_dram[0]).then_inc(h["qG"], 16)

            self.fence()
            self.flush(blk)

    def emit_taps(self, A_f, Bp_f, Cp_f, u1, dg_st, qs_ld, se_sm):
        h = self.h
        vE_tap = self.bump("vE")

        @self.on("v")
        def _(v, _w=qs_ld, _ws=se_sm):
            v.wait_ge(h["qS"], _w)
            v.wait_ge(h["sE"], _ws)
            A3 = A_f[:].rearrange("p (dt n m) -> p (dt n) m", n=NST, m=NST)
            hb3 = lambda j: (self.hbuf[:].rearrange("p (dt jj) -> p dt jj", jj=J)
                             [:, :, j:j + 1].rearrange("p dt one -> p (dt one)"))
            v.tensor_copy(u1[:, 0:128], Bp_f[:])
            v.drain()
            for j in range(J):
                v.tensor_tensor(u1[:, 128:256], u1[:, 0:128], Cp_f[:], ALU.mult)
                v.drain()
                v.tensor_reduce(hb3(j), u1[:, 128:256].rearrange("p (dt n) -> p dt n", n=NST),
                                AX.X, ALU.add)
                v.drain()
                if j < J - 1:
                    v.tensor_tensor(u1[:, 256:2304].rearrange("p (dtn m) -> p dtn m", m=NST),
                                    A3, u1[:, 0:128].broadcast_to([128, 128, 16]), ALU.mult)
                    v.drain()
                    for half in range(2):
                        v.tensor_reduce(u1[:, half * 64: half * 64 + 64]
                                        .rearrange("p (dt m) -> p dt m", m=NST),
                                        u1[:, 256 + half * 1024: 256 + half * 1024 + 1024]
                                        .rearrange("p (dt n m) -> p dt m n", n=NST, m=NST),
                                        AX.X, ALU.add)
                    v.drain()
            v.tensor_tensor(hb3(0), hb3(0), self.vcol[:, NDT:2 * NDT], ALU.add)
            v.drain()
            v.nop().then_inc(h["vE"], 1)

        vE_dg0 = self.c["vE"] + 1
        self.bump("vE", NDT)
        qG_dg0 = self.c["qG"] + 16
        self.bump("qG", 16 * NDT + 16 * NDT)

        @self.on("v")
        def _(v, _w=vE_tap):
            v.wait_ge(h["vE"], _w)
            for dt in range(NDT):
                sl = (dt % 2) * J * 128
                if dt >= 2:
                    v.wait_ge(h["qG"], qG_dg0 + 16 * (dt - 2))
                for j in range(J):
                    v.tensor_scalar(dg_st[:, sl + j * 128: sl + (j + 1) * 128],
                                    self.ident[:],
                                    self.hbuf[:, dt * J + j: dt * J + j + 1],
                                    None, ALU.mult)
                v.drain()
                v.nop().then_inc(h["vE"], 1)

        @self.on("g")
        def _(g):
            for dt in range(NDT):
                g.wait_ge(h["vE"], vE_dg0 + dt)
                sl = (dt % 2) * J * 128
                g.dma_start(out=self.diag_dram[dt],
                            in_=dg_st[:, sl: sl + J * 128]).then_inc(h["qG"], 16)
            for dt in range(NDT):
                g.dma_start(out=self.xg_b[dt, :, 0:J], in_=self.zpad[:]).then_inc(h["qG"], 16)

    def phaseA1(self, p, T):
        h, P = self.h, self.P
        xsl, ln1 = T["xsl"], T["ln1"]
        l1f, l1b = T["l1f"], T["l1b"]
        wgT, gt, xgf = T["wgT"], T["gt"], T["xgf"]

        qs_wgT = self.bump("qS", 16)

        @self.on("sy")
        def _(sy):
            sy.dma_start(out=wgT[:].rearrange("pp (kdt e2) -> pp kdt e2", e2=D),
                         in_=self.wgT_dram.rearrange("(kdt pp) e2 -> pp kdt e2", pp=128)
                         ).then_inc(h["qS"], 16)

        ve_ln1 = [None] * NCH
        qsx = [None] * NCH
        for c in range(NCH):
            qsx[c] = self.bump("qS", 16)

            @self.on("sy")
            def _(sy, _c=c):
                if _c >= 2:
                    sy.wait_ge(h["vE"], ve_ln1[_c - 2])
                sy.dma_start(out=xsl[:, (_c % 2) * 4096:(_c % 2) * 4096 + 4096]
                             .rearrange("pp (tt d) -> pp tt d", d=D),
                             in_=self.x_in[p].rearrange("(cc tt q) d -> cc q tt d", q=128, tt=4)[_c]
                             ).then_inc(h["qS"], 16)

            xtile = lambda tt, _c=c: xsl[:, (_c % 2) * 4096 + tt * 1024:(_c % 2) * 4096 + (tt + 1) * 1024]

            @self.on("v")
            def _(v, _c=c, _q=qsx[c]):
                v.wait_ge(h["qS"], _q)
            self.emit_stats([xtile(tt) for tt in range(4)])
            self.emit_rstd()
            ve_ln1[c] = self.bump("vE")

            @self.on("v")
            def _(v, _xt=xtile):
                for tt in range(4):
                    v.tensor_scalar(ln1[:, tt * 1024:(tt + 1) * 1024], _xt(tt),
                                    self.muc[:, tt:tt + 1], self.rstdc[:, tt:tt + 1],
                                    ALU.subtract, ALU.mult)
                v.drain()
                v.nop().then_inc(h["vE"], 1)

            if self.debug and "ln1" in self.dbg:
                self.bump("qS", 16)

                @self.on("sy")
                def _(sy, _c=c, _v=ve_ln1[c] + 1, _p=p):
                    sy.wait_ge(h["vE"], _v)
                    sy.dma_start(out=self.dbg["ln1"][_p].rearrange("(cc tt q) d -> cc q tt d", q=128, tt=4)[_c],
                                 in_=ln1[:].rearrange("q (tt d) -> q tt d", d=D)).then_inc(h["qS"], 16)

            te_tp = [self.bump("tE") for _ in range(NDT)]
            se_tp = [self.bump("sE") for _ in range(NDT)]

            @self.on("t")
            def _(t, _c=c, _v=ve_ln1[c], _te=tuple(te_tp), _se=tuple(se_tp)):
                t.wait_ge(h["vE"], _v)
                for dt in range(NDT):
                    if dt >= 2:
                        t.wait_ge(h["sE"], _se[dt - 2])
                    mmi = None
                    for tt in range(4):
                        mmi = t.transpose(P[dt % 2][0:128, tt * 128:(tt + 1) * 128],
                                          ln1[:, tt * 1024 + dt * 128: tt * 1024 + (dt + 1) * 128],
                                          self.ident[:])
                    mmi.then_inc(h["tE"], 1)

            @self.on("s")
            def _(s, _te=tuple(te_tp)):
                for dt in range(NDT):
                    s.wait_ge(h["tE"], _te[dt])
                    s.copy(l1b[:, dt * 512:(dt + 1) * 512], P[dt % 2][:, 0:512])
                    s.copy(l1f[:, dt * 512:(dt + 1) * 512], P[dt % 2][:, 0:512])
                    s.drain()
                    s.nop().then_inc(h["sE"], 1)

            te_g = [self.bump("tE") for _ in range(NDT)]
            se_g = [self.bump("sE") for _ in range(NDT)]
            ve_xg = [self.bump("vE") for _ in range(NDT)]
            qs_xgf = [self.bump("qS", 16) for _ in range(NDT)]
            qg_xgb = [self.bump("qG", 16) for _ in range(NDT)]

            @self.on("t")
            def _(t, _c=c, _sall=se_tp[NDT - 1], _se=tuple(se_g), _q=qs_wgT):
                t.wait_ge(h["sE"], _sall)
                t.wait_ge(h["qS"], _q)
                for odt in range(NDT):
                    if odt >= 2:
                        t.wait_ge(h["sE"], _se[odt - 2])
                    mmi = None
                    for kdt in range(NDT):
                        mmi = t.matmul(P[2 + odt % 2][:, 0:512],
                                       wgT[:, kdt * D + odt * 128: kdt * D + (odt + 1) * 128],
                                       l1b[:, kdt * 512:(kdt + 1) * 512],
                                       start=(kdt == 0), stop=(kdt == NDT - 1))
                    mmi.then_inc(h["tE"], 1)

            @self.on("s")
            def _(s, _te=tuple(te_g), _ve=tuple(ve_xg)):
                for odt in range(NDT):
                    s.wait_ge(h["tE"], _te[odt])
                    if odt >= 2:
                        s.wait_ge(h["vE"], _ve[odt - 2])
                    s.activation(gt[:, (odt % 2) * 512:(odt % 2) * 512 + 512],
                                 P[2 + odt % 2][:, 0:512], AF.Sigmoid,
                                 bias=self.col("bg", odt), scale=1.0)
                    s.drain()
                    s.nop().then_inc(h["sE"], 1)

            @self.on("v")
            def _(v, _se=tuple(se_g), _qsx=tuple(qs_xgf), _qgx=tuple(qg_xgb)):
                for odt in range(NDT):
                    v.wait_ge(h["sE"], _se[odt])
                    if odt >= 2:
                        v.wait_ge(h["qS"], _qsx[odt - 2])
                        v.wait_ge(h["qG"], _qgx[odt - 2])
                    v.tensor_tensor(xgf[:, (odt % 2) * 512:(odt % 2) * 512 + 512],
                                    l1f[:, odt * 512:(odt + 1) * 512],
                                    gt[:, (odt % 2) * 512:(odt % 2) * 512 + 512], ALU.mult)
                    v.drain()
                    v.nop().then_inc(h["vE"], 1)

            @self.on("sy")
            def _(sy, _c=c, _ve=tuple(ve_xg)):
                for odt in range(NDT):
                    sy.wait_ge(h["vE"], _ve[odt])
                    sy.dma_start(out=self.xg_f[odt, :, _c * 512:(_c + 1) * 512],
                                 in_=xgf[:, (odt % 2) * 512:(odt % 2) * 512 + 512]
                                 ).then_inc(h["qS"], 16)

            @self.on("g")
            def _(g, _c=c, _ve=tuple(ve_xg)):
                for odt in range(NDT):
                    g.wait_ge(h["vE"], _ve[odt])
                    g.dma_start(out=self.xg_b[odt, :, J + _c * 512: J + (_c + 1) * 512],
                                in_=xgf[:, (odt % 2) * 512:(odt % 2) * 512 + 512]
                                ).then_inc(h["qG"], 16)
            if self.debug and "gate" in self.dbg:
                self.bump("qG", 16 * NDT)

                @self.on("g")
                def _(g, _c=c, _se=tuple(se_g), _p=p):
                    for odt in range(NDT):
                        g.wait_ge(h["sE"], _se[odt])
                        g.dma_start(out=self.dbg["gate"][_p, odt, :, _c * 512:(_c + 1) * 512],
                                    in_=gt[:, (odt % 2) * 512:(odt % 2) * 512 + 512]
                                    ).then_inc(h["qG"], 16)

        if p == 0:
            A_f, Bp_f, Cp_f, u1, dg_st = T["tap"]
            qs_abc = self.bump("qS", 48)

            @self.on("sy")
            def _(sy):
                sy.dma_start(out=A_f[:].rearrange("p (dt nm) -> p dt nm", nm=256),
                             in_=self.w["A"].rearrange("(dt p) n m -> p dt (n m)", p=128)
                             ).then_inc(h["qS"], 16)
                sy.dma_start(out=Bp_f[:].rearrange("p (dt n) -> p dt n", n=NST),
                             in_=self.w["Bp"].rearrange("(dt p) n -> p dt n", p=128)
                             ).then_inc(h["qS"], 16)
                sy.dma_start(out=Cp_f[:].rearrange("p (dt n) -> p dt n", n=NST),
                             in_=self.w["Cp"].rearrange("(dt p) n -> p dt n", p=128)
                             ).then_inc(h["qS"], 16)

            self.emit_taps(A_f, Bp_f, Cp_f, u1, dg_st, qs_abc, self._sE_sm)
            self.qG_casts = self.bump("qG", 16 * 6)

            @self.on("g")
            def _(g):
                for dst, src in [(self.wb["We1"], self.w["We1"]), (self.wb["We3"], self.w["We3"]),
                                 (self.wb["We2"], self.w["We2"]),
                                 (self.wb["ws1"], self.w["ws1"]), (self.wb["ws3"], self.w["ws3"]),
                                 (self.wb["ws2"], self.w["ws2"])]:
                    g.dma_start(out=dst, in_=src).then_inc(h["qG"], 16)

    def phaseA2(self, p, T):
        h, P = self.h, self.P
        dg, xb, xf = T["dg"], T["xb"], T["xf"]
        yF, ytp, xre, ln3 = T["yF"], T["ytp"], T["xre"], T["ln3"]
        l3f = T["l3f"]
        W = J + 1024

        qs_gb = self.bump("qS", 16)

        @self.on("sy")
        def _(sy):
            sy.dma_start(out=self.gamma_b[:], in_=self.gb_dram[0]).then_inc(h["qS"], 16)

        qs_gb2 = self.bump("qS", 16)

        @self.on("sy")
        def _(sy):
            sy.dma_start(out=self.beta_b[:], in_=self.gb_dram[1]).then_inc(h["qS"], 16)
        self._qs_gb = qs_gb2

        conv_hist = []
        ytr_hist = []
        join_hist = []

        seyt_hist = []
        self._a2_prev_ve_x1 = None
        for pr in range(2):
            c0 = 2 * pr
            se_yt = []
            for dt in range(NDT):
                gi = pr * NDT + dt
                qs_dg = self.bump("qS", 48)

                @self.on("sy")
                def _(sy, _dt=dt, _c0=c0, _gi=gi,
                      _g=(conv_hist[gi - 2] if gi >= 2 else None),
                      _gj=(join_hist[gi - 1] if gi >= 1 else None)):
                    if _g is not None:
                        sy.wait_ge(h["tE"], _g)
                    if _gj is not None:
                        sy.wait_ge(h["vE"], _gj)
                    sl = _dt % 2
                    sy.dma_start(out=dg[:, sl * J * 128:(sl + 1) * J * 128],
                                 in_=self.diag_dram[_dt]).then_inc(h["qS"], 16)
                    sy.dma_start(out=xb[:, sl * W: sl * W + W],
                                 in_=self.xg_b[_dt, :, _c0 * 512: _c0 * 512 + W]
                                 ).then_inc(h["qS"], 16)
                    sy.dma_start(out=xf[:],
                                 in_=self.xg_f[_dt, :, _c0 * 512: _c0 * 512 + 1024]
                                 ).then_inc(h["qS"], 16)

                te_conv = self.bump("tE")
                conv_hist.append(te_conv)
                ve_join = self.bump("vE")
                join_hist.append(ve_join)
                te_yt = self.bump("tE")
                ytr_hist.append(te_yt)
                se_yt.append(self.bump("sE"))

                @self.on("t")
                def _(t, _dt=dt, _q=qs_dg, _gi=gi,
                      _vjprev=(join_hist[gi - 1] if gi >= 1 else None),
                      _vj=ve_join,
                      _seprev=(se_yt[dt - 1] if dt >= 1 else None),
                      _se_prev_pair=(self._seyt_last if pr == 1 and dt == 0 else None)):
                    sl = _dt % 2
                    t.wait_ge(h["qS"], _q)
                    if _vjprev is not None:
                        t.wait_ge(h["vE"], _vjprev)
                    mmi = None
                    for j in range(J):
                        for cc in range(2):
                            mmi = t.matmul(P[cc][:, 0:512],
                                           dg[:, sl * J * 128 + j * 128: sl * J * 128 + (j + 1) * 128],
                                           xb[:, sl * W + J + cc * 512 - j: sl * W + J + cc * 512 - j + 512],
                                           start=(j == 0), stop=(j == J - 1),
                                           skip_group_check=True)
                    mmi.then_inc(h["tE"], 1)
                    t.wait_ge(h["vE"], _vj)
                    if _seprev is not None:
                        t.wait_ge(h["sE"], _seprev)
                    if _se_prev_pair is not None:
                        t.wait_ge(h["sE"], _se_prev_pair)
                    mmi = None
                    for wd in range(8):
                        pb = P[2] if wd < 4 else P[3]
                        mmi = t.transpose(pb[0:128, (wd % 4) * 128:(wd % 4 + 1) * 128],
                                          yF[:, sl * 1024 + wd * 128: sl * 1024 + (wd + 1) * 128],
                                          self.ident[:])
                    mmi.then_inc(h["tE"], 1)

                @self.on("v")
                def _(v, _dt=dt, _tc=te_conv, _gi=gi,
                      _typrev=(ytr_hist[gi - 2] if gi >= 2 else None)):
                    v.wait_ge(h["tE"], _tc)
                    if _typrev is not None:
                        v.wait_ge(h["tE"], _typrev)
                    sl = _dt % 2
                    v.tensor_tensor(yF[:, sl * 1024: sl * 1024 + 1024],
                                    self.PA[:, 0:1024], xf[:], ALU.add)
                    v.drain()
                    v.nop().then_inc(h["vE"], 1)

                @self.on("s")
                def _(s, _dt=dt, _ty=te_yt, _pr=pr):
                    s.wait_ge(h["tE"], _ty)
                    s.copy(ytp[:, (_pr % 2) * 8192:(_pr % 2) * 8192 + 8192].rearrange("q (wd d1) -> q wd d1", d1=1024)[:, 0:4, _dt * 128:(_dt + 1) * 128],
                           P[2][:, 0:512].rearrange("q (wd d1) -> q wd d1", d1=128))
                    s.copy(ytp[:, (_pr % 2) * 8192:(_pr % 2) * 8192 + 8192].rearrange("q (wd d1) -> q wd d1", d1=1024)[:, 4:8, _dt * 128:(_dt + 1) * 128],
                           P[3][:, 0:512].rearrange("q (wd d1) -> q wd d1", d1=128))
                    s.drain()
                    s.nop().then_inc(h["sE"], 1)

            seyt_hist.append(se_yt[NDT - 1])
            self._seyt_last = se_yt[NDT - 1]

            if self.debug and "y" in self.dbg:
                self.bump("qS", 16)

                @self.on("sy")
                def _(sy, _pr=pr, _s=se_yt[NDT - 1], _p=p):
                    sy.wait_ge(h["sE"], _s)
                    sy.dma_start(out=self.dbg["y"][_p].rearrange("(q2 tt q) d -> q2 q tt d", q=128, tt=8)[_pr],
                                 in_=ytp[:, (_pr % 2) * 8192:(_pr % 2) * 8192 + 8192].rearrange("q (tt d) -> q tt d", d=D)).then_inc(h["qS"], 16)
            if self.debug and "xg" in self.dbg and pr == 0:
                self.bump("qG", 16 * NDT)

                @self.on("g")
                def _(g, _p=p, _q=self.c["qS"]):
                    g.wait_ge(h["qS"], _q)
                    for dt in range(NDT):
                        g.dma_start(out=self.dbg["xg"][_p, dt], in_=self.xg_f[dt]).then_inc(h["qG"], 16)

        for pr in range(2):
            c0 = 2 * pr
            for cc in range(2):
                c = c0 + cc
                ytile = lambda tt, _cc=cc, _pr=pr: ytp[:, (_pr % 2) * 8192 + (_cc * 4 + tt) * 1024:(_pr % 2) * 8192 + (_cc * 4 + tt + 1) * 1024]

                qs_xre = self.bump("qS", 16)

                @self.on("sy")
                def _(sy, _c=c, _cc=cc, _vx=getattr(self, "_a2_prev_ve_x1", None)):
                    if _vx is not None:
                        sy.wait_ge(h["vE"], _vx)
                    sy.dma_start(out=xre[:].rearrange("q (tt d) -> q tt d", d=D),
                                 in_=self.x_in[p].rearrange("(ch tt q) d -> ch q tt d", q=128, tt=4)[_c]
                                 ).then_inc(h["qS"], 16)

                @self.on("v")
                def _(v, _s=seyt_hist[pr], _q=self._qs_gb):
                    v.wait_ge(h["sE"], _s)
                    v.wait_ge(h["qS"], _q)
                self.emit_stats([ytile(tt) for tt in range(4)])
                self.emit_rstd()

                @self.on("v")
                def _(v, _yt=ytile):
                    for tt in range(4):
                        tile = _yt(tt)
                        v.tensor_scalar(self.sq[:], tile, self.muc[:, tt:tt + 1],
                                        self.rstdc[:, tt:tt + 1], ALU.subtract, ALU.mult)
                        v.drain()
                        v.tensor_tensor(self.sq[:], self.sq[:], self.gamma_b[:], ALU.mult)
                        v.drain()
                        v.scalar_tensor_tensor(tile, self.sq[:], 1.0, self.beta_b[:],
                                               ALU.mult, ALU.add,
                                               accum_out=self.s1c[:, tt:tt + 1])
                        v.drain()
                        v.scalar_tensor_tensor(self.sq[:], tile, 1.0, tile,
                                               ALU.mult, ALU.mult,
                                               accum_out=self.s2c[:, tt:tt + 1])
                        v.drain()
                    v.tensor_scalar(self.muc[:], self.s1c[:], 1.0 / D, None, ALU.mult)
                    v.tensor_scalar(self.varc[:], self.s2c[:], 1.0 / D, None, ALU.mult)
                    v.drain()
                    v.tensor_tensor(self.tnc[:], self.muc[:], self.muc[:], ALU.mult)
                    v.drain()
                    v.tensor_tensor(self.varc[:], self.varc[:], self.tnc[:], ALU.subtract)
                    v.drain()
                self.emit_rstd()
                ve_x1 = self.bump("vE")

                @self.on("v")
                def _(v, _cc=cc, _q=qs_xre, _yt=ytile):
                    v.wait_ge(h["qS"], _q)
                    for tt in range(4):
                        tile = _yt(tt)
                        v.tensor_scalar(self.sq[:], tile, self.muc[:, tt:tt + 1],
                                        self.rstdc[:, tt:tt + 1], ALU.subtract, ALU.mult)
                        v.drain()
                        v.tensor_tensor(tile, self.sq[:],
                                        xre[:, tt * 1024:(tt + 1) * 1024],
                                        ALU.add)
                        v.drain()
                        v.tensor_reduce(self.s1c[:, tt:tt + 1], tile, AX.X, ALU.add)
                        v.scalar_tensor_tensor(self.sq[:], tile, 1.0, tile, ALU.mult, ALU.mult,
                                               accum_out=self.s2c[:, tt:tt + 1])
                    v.drain()
                    v.tensor_scalar(self.muc[:], self.s1c[:], 1.0 / D, None, ALU.mult)
                    v.tensor_scalar(self.varc[:], self.s2c[:], 1.0 / D, None, ALU.mult)
                    v.drain()
                    v.tensor_tensor(self.tnc[:], self.muc[:], self.muc[:], ALU.mult)
                    v.drain()
                    v.tensor_tensor(self.varc[:], self.varc[:], self.tnc[:], ALU.subtract)
                    v.drain()
                    v.nop().then_inc(h["vE"], 1)
                self.emit_rstd()
                self._a2_prev_ve_x1 = ve_x1
                ve_ln3 = self.bump("vE")

                @self.on("v")
                def _(v, _yt=ytile):
                    for tt in range(4):
                        v.tensor_scalar(ln3[:, tt * 1024:(tt + 1) * 1024], _yt(tt),
                                        self.muc[:, tt:tt + 1], self.rstdc[:, tt:tt + 1],
                                        ALU.subtract, ALU.mult)
                    v.drain()
                    v.nop().then_inc(h["vE"], 1)

                self.bump("qS", 16)
                if STAGE == 1:
                    self.bump("qS", 16)
                if self.debug and "x1" in self.dbg:
                    self.bump("qS", 16)

                @self.on("sy")
                def _(sy, _c=c, _cc=cc, _v=ve_ln3, _pr2=pr % 2):
                    sy.wait_ge(h["vE"], _v)
                    sy.dma_start(out=self.x1_scr.rearrange("(ch tt q) d -> ch q tt d", q=128, tt=4)[_c],
                                 in_=ytp[:, _pr2 * 8192 + _cc * 4096: _pr2 * 8192 + (_cc + 1) * 4096]
                                 .rearrange("q (tt d) -> q tt d", d=D)).then_inc(h["qS"], 16)
                    if STAGE == 1:
                        sy.dma_start(out=self.out[p].rearrange("(ch tt q) d -> ch q tt d", q=128, tt=4)[_c],
                                     in_=ytp[:, _pr2 * 8192 + _cc * 4096: _pr2 * 8192 + (_cc + 1) * 4096]
                                     .rearrange("q (tt d) -> q tt d", d=D)).then_inc(h["qS"], 16)
                    if self.debug and "x1" in self.dbg:
                        sy.dma_start(out=self.dbg["x1"][p].rearrange("(ch tt q) d -> ch q tt d", q=128, tt=4)[_c],
                                     in_=ytp[:, _pr2 * 8192 + _cc * 4096: _pr2 * 8192 + (_cc + 1) * 4096]
                                     .rearrange("q (tt d) -> q tt d", d=D)).then_inc(h["qS"], 16)

                if self.debug and "ln3" in self.dbg:
                    self.bump("qS", 16)

                    @self.on("sy")
                    def _(sy, _c=c, _v=ve_ln3):
                        sy.wait_ge(h["vE"], _v)
                        sy.dma_start(out=self.dbg["ln3"][p].rearrange("(ch tt q) d -> ch q tt d", q=128, tt=4)[_c],
                                     in_=ln3[:].rearrange("q (tt d) -> q tt d", d=D)).then_inc(h["qS"], 16)

                se_l3 = []
                te_sc = []
                for dt in range(NDT):
                    te_l3 = self.bump("tE")
                    se_l3.append(self.bump("sE"))
                    te_sc.append(self.bump("tE"))

                    @self.on("t")
                    def _(t, _dt=dt, _v=ve_ln3, _se=se_l3[dt],
                          _seprev=(se_l3[dt - 2] if dt >= 2 else None)):
                        if _dt == 0:
                            t.wait_ge(h["vE"], _v)
                        if _seprev is not None:
                            t.wait_ge(h["sE"], _seprev)
                        mmi = None
                        for tt in range(4):
                            mmi = t.transpose(P[2 + _dt % 2][0:128, tt * 128:(tt + 1) * 128],
                                              ln3[:, tt * 1024 + _dt * 128: tt * 1024 + (_dt + 1) * 128],
                                              self.ident[:])
                        mmi.then_inc(h["tE"], 1)
                        t.wait_ge(h["sE"], _se)
                        t.matmul(P[6][0:16, 0:512], self.gate_wT[:, _dt * 16:(_dt + 1) * 16],
                                 l3f[:, (_dt % 2) * 512:(_dt % 2) * 512 + 512],
                                 start=(_dt == 0), stop=(_dt == NDT - 1),
                                 skip_group_check=True).then_inc(h["tE"], 1)

                    @self.on("s")
                    def _(s, _c=c, _dt=dt, _tl=te_l3,
                          _tscprev=(te_sc[dt - 2] if dt >= 2 else None)):
                        s.wait_ge(h["tE"], _tl)
                        if _tscprev is not None:
                            s.wait_ge(h["tE"], _tscprev)
                        s.copy(self.l3x_bf[:, _dt * L + _c * 512: _dt * L + (_c + 1) * 512],
                               P[2 + _dt % 2][:, 0:512])
                        s.copy(l3f[:, (_dt % 2) * 512:(_dt % 2) * 512 + 512],
                               P[2 + _dt % 2][:, 0:512])
                        s.drain()
                        s.nop().then_inc(h["sE"], 1)

                self.emit_routing(c, te_sc[NDT - 1], p)

            self.fence()

    def emit_routing(self, c, te_scores, p=0):
        h, P, rt = self.h, self.P, self.rt
        ve_sc = self.bump("vE")

        @self.on("v")
        def _(v, _t=te_scores):
            v.wait_ge(h["tE"], _t)
            v.tensor_copy(self.sc_sb[0:16, 0:512], P[6][0:16, 0:512])
            v.drain()
            v.nop().then_inc(h["vE"], 1)

        te_t1 = self.bump("tE")

        @self.on("t")
        def _(t, _v=ve_sc):
            t.wait_ge(h["vE"], _v)
            mmi = None
            for tt in range(4):
                mmi = t.transpose(P[7][0:128, tt * 16:(tt + 1) * 16],
                                  self.sc_sb[0:16, tt * 128:(tt + 1) * 128],
                                  self.ident[0:16, 0:16])
            mmi.then_inc(h["tE"], 1)

        ve_rt = self.bump("vE")

        @self.on("v")
        def _(v, _t=te_t1):
            v.wait_ge(h["tE"], _t)
            sc = rt["sd"]
            v.tensor_copy(sc[:], P[7][:, 0:64])
            v.drain()
            v3 = lambda a: a[:].rearrange("q (tt e) -> q tt e", e=16)
            g3 = lambda a: a[:].rearrange("q (tg j) -> q tg j", j=4)
            bc = lambda a, n=16: a[:].broadcast_to([128, a.shape[1] if hasattr(a, 'shape') else 4, n])
            v.tensor_reduce(rt["m"][:], v3(sc), AX.X, ALU.max)
            v.drain()
            v.tensor_tensor(v3(rt["ex"]), v3(sc), rt["m"][:].broadcast_to([128, 4, 16]), ALU.subtract)
            v.drain()
            v.nop().then_inc(h["vE"], 1)
        sE_exp = self.bump("sE")

        @self.on("s")
        def _(s, _v=ve_rt):
            s.wait_ge(h["vE"], _v)
            s.activation(rt["ex"][:], rt["ex"][:], AF.Exp)
            s.drain()
            s.nop().then_inc(h["sE"], 1)

        ve_comb = self.bump("vE")

        @self.on("v")
        def _(v, _s=sE_exp):
            v3 = lambda a: a[:].rearrange("q (tt e) -> q tt e", e=16)
            g3 = lambda a: a[:].rearrange("q (tg j) -> q tg j", j=4)
            v.wait_ge(h["sE"], _s)
            v.tensor_reduce(rt["sum"][:], v3(rt["ex"]), AX.X, ALU.add)
            v.drain()
            v.reciprocal(rt["rec"][:], rt["sum"][:])
            v.drain()
            v.tensor_tensor(v3(rt["orig"]), v3(rt["ex"]),
                            rt["rec"][:].broadcast_to([128, 4, 16]), ALU.mult)
            v.drain()
            gbv = self.gb_bcast[:, 0:16].broadcast_to([128, 16, 4]).rearrange("q e tt -> q tt e")
            v.tensor_tensor(v3(rt["sc2"]), v3(rt["orig"]), gbv, ALU.add)
            v.drain()
            v.tensor_reduce(rt["m1"][:], g3(rt["sc2"]), AX.X, ALU.max)
            v.drain()
            v.tensor_tensor(g3(rt["eqa"]), g3(rt["sc2"]),
                            rt["m1"][:].broadcast_to([128, 16, 4]), ALU.is_equal)
            v.drain()
            v.scalar_tensor_tensor(rt["sgm"][:], rt["eqa"][:], NEG, rt["sc2"][:],
                                   ALU.mult, ALU.add)
            v.drain()
            v.tensor_reduce(rt["m2"][:], g3(rt["sgm"]), AX.X, ALU.max)
            v.drain()
            v.tensor_tensor(rt["gsc"][:], rt["m1"][:], rt["m2"][:], ALU.add)
            v.drain()
            q3 = lambda a: a[:].rearrange("q (tt gg) -> q tt gg", gg=4)
            v.tensor_reduce(rt["gm1"][:], q3(rt["gsc"]), AX.X, ALU.max)
            v.drain()
            v.tensor_tensor(q3(rt["eqg"]), q3(rt["gsc"]),
                            rt["gm1"][:].broadcast_to([128, 4, 4]), ALU.is_equal)
            v.drain()
            v.scalar_tensor_tensor(rt["gsc2"][:], rt["eqg"][:], NEG, rt["gsc"][:],
                                   ALU.mult, ALU.add)
            v.drain()
            v.tensor_reduce(rt["gm2"][:], q3(rt["gsc2"]), AX.X, ALU.max)
            v.drain()
            v.tensor_tensor(q3(rt["keep"]), q3(rt["gsc"]),
                            rt["gm2"][:].broadcast_to([128, 4, 4]), ALU.is_ge)
            v.drain()
            v.tensor_tensor(g3(rt["smt1"]), g3(rt["sc2"]),
                            rt["keep"][:].broadcast_to([128, 16, 4]), ALU.mult)
            v.drain()
            v.tensor_scalar(g3(rt["smt2"]), rt["keep"][:].broadcast_to([128, 16, 4]),
                            1.0, -NEG, ALU.subtract, ALU.mult)
            v.drain()
            v.tensor_tensor(rt["sm"][:], rt["smt1"][:], rt["smt2"][:], ALU.add)
            v.drain()
            v.tensor_reduce(rt["t1"][:], v3(rt["sm"]), AX.X, ALU.max)
            v.drain()
            v.tensor_tensor(v3(rt["eq1"]), v3(rt["sm"]),
                            rt["t1"][:].broadcast_to([128, 4, 16]), ALU.is_equal)
            v.drain()
            v.scalar_tensor_tensor(rt["sm2"][:], rt["eq1"][:], NEG, rt["sm"][:],
                                   ALU.mult, ALU.add)
            v.drain()
            v.tensor_reduce(rt["t2"][:], v3(rt["sm2"]), AX.X, ALU.max)
            v.drain()
            v.tensor_tensor(v3(rt["eq2"]), v3(rt["sm2"]),
                            rt["t2"][:].broadcast_to([128, 4, 16]), ALU.is_equal)
            v.drain()
            v.tensor_tensor(rt["comb"][:], rt["eq1"][:], rt["eq2"][:], ALU.add)
            v.drain()
            v.tensor_tensor(rt["comb"][:], rt["comb"][:], rt["orig"][:], ALU.mult)
            v.drain()
            v.nop().then_inc(h["vE"], 1)

        if self.debug and "rt" in self.dbg and p == 0 and c == 0:
            keys = ["sd", "ex", "orig", "sc2", "m1", "sgm", "m2", "gsc", "gm1",
                    "gm2", "keep", "smt1", "smt2", "sm", "t1", "eq1",
                    "sm2", "t2", "eq2", "comb"]
            self.bump("qS", 16 * 20)

            @self.on("sy")
            def _(sy, _v=ve_comb, _keys=tuple(keys)):
                sy.wait_ge(h["vE"], _v)
                for i, kk in enumerate(_keys):
                    ap = rt[kk][:]
                    w = ap.shape[1] if hasattr(ap, "shape") else 64
                    sy.dma_start(out=self.dbg["rt"][i, :, 0:rt[kk].shape[1]],
                                 in_=ap).then_inc(h["qS"], 16)

        te_t2 = self.bump("tE")

        @self.on("t")
        def _(t, _v=ve_comb):
            t.wait_ge(h["vE"], _v)
            mmi = None
            for tt in range(4):
                mmi = t.transpose(P[7][0:16, tt * 128:(tt + 1) * 128],
                                  rt["comb"][:, tt * 16:(tt + 1) * 16], self.ident[:])
            mmi.then_inc(h["tE"], 1)

        ve_ct = self.bump("vE")

        @self.on("v")
        def _(v, _t=te_t2, _c=c):
            v.wait_ge(h["tE"], _t)
            v.tensor_copy(self.combine_T[0:16, _c * 512:(_c + 1) * 512], P[7][0:16, 0:512])
            v.drain()
            v.nop().then_inc(h["vE"], 1)

    def phaseB(self, p, T):
        h, P = self.h, self.P
        we, hbf, h1s, h3s, cb = T["we"], T["hbf"], T["h1s"], T["h3s"], T["cb"]
        WSLOT = 12288
        w1t = lambda sl, kdt, ib: we[:, sl * WSLOT + kdt * 512 + ib * 128: sl * WSLOT + kdt * 512 + (ib + 1) * 128]
        w3t = lambda sl, kdt, ib: we[:, sl * WSLOT + 4096 + kdt * 512 + ib * 128: sl * WSLOT + 4096 + kdt * 512 + (ib + 1) * 128]
        w2t = lambda sl, ib, odt: we[:, sl * WSLOT + 8192 + ib * 1024 + odt * 128: sl * WSLOT + 8192 + ib * 1024 + (odt + 1) * 128]

        qs_we = [None] * E
        prev_yo_te = {}
        prev_ve_yo_last = None

        for e in range(E):
            sl = e % 2
            qs_we[e] = self.bump("qS", 48)

            @self.on("sy")
            def _(sy, _e=e, _sl=sl):
                if _e >= 2:
                    sy.wait_ge(h["tE"], prev_yo_te[_e - 2])
                sy.dma_start(out=we[:, _sl * WSLOT: _sl * WSLOT + 4096]
                             .rearrange("q (kdt i) -> q kdt i", i=512),
                             in_=self.wb["We1"][_e].rearrange("(kdt q) i -> q kdt i", q=128)
                             ).then_inc(h["qS"], 16)
                sy.dma_start(out=we[:, _sl * WSLOT + 4096: _sl * WSLOT + 8192]
                             .rearrange("q (kdt i) -> q kdt i", i=512),
                             in_=self.wb["We3"][_e].rearrange("(kdt q) i -> q kdt i", q=128)
                             ).then_inc(h["qS"], 16)
                sy.dma_start(out=we[:, _sl * WSLOT + 8192: _sl * WSLOT + 12288]
                             .rearrange("q (ib d1) -> q ib d1", d1=1024),
                             in_=self.wb["We2"][_e].rearrange("(ib q) d1 -> q ib d1", q=128)
                             ).then_inc(h["qS"], 16)

            te_cb = [self.bump("tE") for _ in range(4)]
            se_cb = [self.bump("sE") for _ in range(4)]

            @self.on("t")
            def _(t, _e=e, _te=tuple(te_cb), _se=tuple(se_cb), _vy=prev_ve_yo_last):
                if _vy is not None:
                    t.wait_ge(h["vE"], _vy)
                for ch in range(4):
                    if ch >= 2:
                        t.wait_ge(h["sE"], _se[ch - 2])
                    t.matmul(P[6 + ch % 2][:, 0:512], self.sel_sb[:, _e * 128:(_e + 1) * 128],
                             self.combine_T[0:16, ch * 512:(ch + 1) * 512],
                             start=True, stop=True).then_inc(h["tE"], 1)

            @self.on("s")
            def _(s, _te=tuple(te_cb)):
                for ch in range(4):
                    s.wait_ge(h["tE"], _te[ch])
                    s.copy(cb[:, ch * 512:(ch + 1) * 512], P[6 + ch % 2][:, 0:512])
                    s.drain()
                    s.nop().then_inc(h["sE"], 1)

            if e == 0:
                prev_se_h1 = None
                prev_ve_h3s = None
                prev_ve_hbf = None
            ve_hbf = [None] * 4
            for ib in range(4):
                te_h1 = self.bump("tE")
                se_h1 = self.bump("sE")
                te_h3 = self.bump("tE")
                ve_h3s = self.bump("vE")
                ve_hbf[ib] = self.bump("vE")

                @self.on("t")
                def _(t, _e=e, _sl=sl, _ib=ib, _q=qs_we[e], _secb=se_cb[3],
                      _sh1=prev_se_h1, _vh3=prev_ve_h3s,
                      _vyo=(prev_ve_yo_last if (e >= 1 and ib == 0) else None)):
                    t.wait_ge(h["qS"], _q)
                    if _ib == 0:
                        t.wait_ge(h["sE"], _secb)
                    if _vyo is not None:
                        t.wait_ge(h["vE"], _vyo)
                    if _sh1 is not None:
                        t.wait_ge(h["sE"], _sh1)
                    mmi = None
                    for kdt in range(NDT):
                        for ch in range(4):
                            mmi = t.matmul(P[ch][:, 0:512], w1t(_sl, kdt, _ib),
                                           self.l3x_bf[:, kdt * L + ch * 512: kdt * L + (ch + 1) * 512],
                                           start=(kdt == 0), stop=(kdt == NDT - 1),
                                           skip_group_check=True)
                    mmi.then_inc(h["tE"], 1)
                    if _vh3 is not None:
                        t.wait_ge(h["vE"], _vh3)
                    mmi = None
                    for kdt in range(NDT):
                        for ch in range(4):
                            mmi = t.matmul(P[4 + ch][:, 0:512], w3t(_sl, kdt, _ib),
                                           self.l3x_bf[:, kdt * L + ch * 512: kdt * L + (ch + 1) * 512],
                                           start=(kdt == 0), stop=(kdt == NDT - 1),
                                           skip_group_check=True)
                    mmi.then_inc(h["tE"], 1)

                @self.on("s")
                def _(s, _e=e, _ib=ib, _t=te_h1, _vb=prev_ve_hbf):
                    s.wait_ge(h["tE"], _t)
                    if _vb is not None:
                        s.wait_ge(h["vE"], _vb)
                    s.activation(h1s[:], self.PA[:], AF.Silu,
                                 bias=self.be1c[:, _ib * 16 + _e: _ib * 16 + _e + 1],
                                 scale=1.0)
                    s.drain()
                    s.nop().then_inc(h["sE"], 1)

                @self.on("v")
                def _(v, _e=e, _ib=ib, _t=te_h3, _s1=se_h1, _secb=se_cb[3]):
                    v.wait_ge(h["tE"], _t)
                    if _ib == 0:
                        v.wait_ge(h["sE"], _secb)
                    v.scalar_tensor_tensor(h3s[:], self.PB[:],
                                           self.be3c[:, _ib * 16 + _e: _ib * 16 + _e + 1],
                                           cb[:], ALU.add, ALU.mult)
                    v.drain()
                    v.nop().then_inc(h["vE"], 1)
                    v.wait_ge(h["sE"], _s1)
                    v.tensor_tensor(hbf[:, _ib * L: (_ib + 1) * L], h3s[:], h1s[:], ALU.mult)
                    v.drain()
                    v.nop().then_inc(h["vE"], 1)
                prev_se_h1 = se_h1
                prev_ve_h3s = ve_h3s
                prev_ve_hbf = ve_hbf[ib]

            te_yo = [None] * NDT
            ve_yo = [None] * NDT
            for odt in range(NDT):
                te_yo[odt] = self.bump("tE")
                ve_yo[odt] = self.bump("vE")

            @self.on("t")
            def _(t, _e=e, _sl=sl, _ph=ve_hbf[3], _vy=tuple(ve_yo)):
                t.wait_ge(h["vE"], _ph)
                for odt in range(NDT):
                    if odt >= 2:
                        t.wait_ge(h["vE"], _vy[odt - 2])
                    pb = (odt % 2) * 4
                    mmi = None
                    for ib in range(4):
                        for ch in range(4):
                            mmi = t.matmul(P[pb + ch][:, 0:512], w2t(_sl, ib, odt),
                                           hbf[:, ib * L + ch * 512: ib * L + (ch + 1) * 512],
                                           start=(ib == 0), stop=(ib == 3),
                                           skip_group_check=True)
                    mmi.then_inc(h["tE"], 1)
            prev_yo_te[e] = te_yo[NDT - 1]

            @self.on("v")
            def _(v, _e=e, _ty=tuple(te_yo)):
                for odt in range(NDT):
                    v.wait_ge(h["tE"], _ty[odt])
                    pw = self.PA if odt % 2 == 0 else self.PB
                    dst = self.y_acc[:, odt * L: (odt + 1) * L]
                    if _e == 0:
                        v.tensor_copy(dst, pw[:])
                    else:
                        v.tensor_tensor(dst, dst, pw[:], ALU.add)
                    v.drain()
                    v.nop().then_inc(h["vE"], 1)
            prev_ve_yo_last = ve_yo[NDT - 1]
        self.fence()

    def phaseZ(self, p, T):
        h, P = self.h, self.P
        zw = T["zw"]
        z1s, z3s, zzw = T["z1s"], T["z3s"], T["zzw"]
        zw1 = lambda kdt, ob: zw[:, kdt * 1024 + ob * 128: kdt * 1024 + (ob + 1) * 128]
        zw3 = lambda kdt, ob: zw[:, 8192 + kdt * 1024 + ob * 128: 8192 + kdt * 1024 + (ob + 1) * 128]

        qs_zw = self.bump("qS", 32)

        @self.on("sy")
        def _(sy):
            for k, srcw in [(0, self.wb["ws1"]), (8192, self.wb["ws3"])]:
                sy.dma_start(out=zw[:, k: k + 8192].rearrange("q (kdt o) -> q kdt o", o=1024),
                             in_=srcw.rearrange("(kdt q) o -> q kdt o", q=128)
                             ).then_inc(h["qS"], 16)

        pe_zz = [None] * NDT
        te_z1_hist = []
        qg_zz = [None] * NDT
        for ob in range(NDT):
            te_z1 = self.bump("tE")
            te_z1_hist.append(te_z1)
            se_z1 = self.bump("sE")
            te_z3 = self.bump("tE")
            se_z3 = self.bump("sE")
            pe_zz[ob] = self.bump("pE")
            qg_zz[ob] = self.bump("qG", 16)

            @self.on("t")
            def _(t, _ob=ob, _q=qs_zw, _prev=pe_zz[ob - 1] if ob >= 1 else None):
                t.wait_ge(h["qS"], _q)
                if _prev is not None:
                    t.wait_ge(h["pE"], _prev)
                mmi = None
                for kdt in range(NDT):
                    for ch in range(4):
                        mmi = t.matmul(P[ch][:, 0:512], zw1(kdt, _ob),
                                       self.l3x_bf[:, kdt * L + ch * 512: kdt * L + (ch + 1) * 512],
                                       start=(kdt == 0), stop=(kdt == NDT - 1),
                                       skip_group_check=True)
                mmi.then_inc(h["tE"], 1)

            @self.on("s")
            def _(s, _ob=ob, _t=te_z1, _pz=(pe_zz[ob - 1] if ob >= 1 else None)):
                s.wait_ge(h["tE"], _t)
                if _pz is not None:
                    s.wait_ge(h["pE"], _pz)
                s.activation(z1s[:], self.PA[:], AF.Silu, bias=self.col("bs1", _ob), scale=1.0)
                s.drain()
                s.nop().then_inc(h["sE"], 1)

            @self.on("t")
            def _(t, _ob=ob):
                mmi = None
                for kdt in range(NDT):
                    for ch in range(4):
                        mmi = t.matmul(P[4 + ch][:, 0:512], zw3(kdt, _ob),
                                       self.l3x_bf[:, kdt * L + ch * 512: kdt * L + (ch + 1) * 512],
                                       start=(kdt == 0), stop=(kdt == NDT - 1),
                                       skip_group_check=True)
                mmi.then_inc(h["tE"], 1)

            @self.on("s")
            def _(s, _ob=ob, _t=te_z3, _pz=(pe_zz[ob - 1] if ob >= 1 else None)):
                s.wait_ge(h["tE"], _t)
                if _pz is not None:
                    s.wait_ge(h["pE"], _pz)
                s.activation(z3s[:], self.PB[:], AF.Identity, bias=self.col("bs3", _ob), scale=1.0)
                s.drain()
                s.nop().then_inc(h["sE"], 1)


            @self.on("g")
            def _(g, _ob=ob, _s3=se_z3, _qz=(qg_zz[ob - 2] if ob >= 2 else None)):
                g.wait_ge(h["sE"], _s3)
                if _qz is not None:
                    g.wait_ge(h["qG"], _qz)
                sl = (_ob % 2) * L
                g.tensor_tensor(zzw[:, sl: sl + L], z3s[:, 0:L], z1s[:, 0:L], ALU.mult)
                g.drain()
                g.nop().then_inc(h["pE"], 1)
                g.dma_start(out=self.zz_dram[_ob], in_=zzw[:, sl: sl + L]).then_inc(h["qG"], 16)
        self._z_state = (te_z1_hist, qg_zz)

    def phaseZ2(self, p, T):
        h, P = self.h, self.P
        zw, zzst = T["zw"], T["zzst"]
        y2f, y2t, x1r = T["y2f"], T["y2t"], T["x1r"]
        zw2 = lambda ob, odt: zw[:, ob * 1024 + odt * 128: ob * 1024 + (odt + 1) * 128]
        te_z1_hist, qg_zz = self._z_state

        qs_zw2 = self.bump("qS", 32)

        @self.on("sy")
        def _(sy, _t=te_z1_hist[NDT - 1]):
            sy.dma_start(out=self.be2_sb[:], in_=self.w["be2"]).then_inc(h["qS"], 16)
            sy.wait_ge(h["tE"], _t)
            sy.dma_start(out=zw[:, 0:8192].rearrange("q (kdt o) -> q kdt o", o=1024),
                         in_=self.wb["ws2"].rearrange("(kdt q) o -> q kdt o", q=128)
                         ).then_inc(h["qS"], 16)

        prev_ve_fin = None
        prev_qs_out = None
        te_z2_hist = []
        for ch in range(4):
            qs_zzst = self.bump("qS", 16)
            qs_x1 = self.bump("qS", 16)

            @self.on("sy")
            def _(sy, _ch=ch, _vf=prev_ve_fin, _qz=qg_zz[NDT - 1],
                  _tz=(te_z2_hist[ch - 1] if ch >= 1 else None)):
                sy.wait_ge(h["qG"], _qz)
                if _tz is not None:
                    sy.wait_ge(h["tE"], _tz)
                sy.dma_start(out=zzst[:, 0:4096]
                             .rearrange("q (ob o) -> q ob o", o=512),
                             in_=self.zz_dram.rearrange("ob q t -> q ob t")[:, :, _ch * 512:(_ch + 1) * 512]
                             ).then_inc(h["qS"], 16)
                if _vf is not None:
                    sy.wait_ge(h["vE"], _vf)
                sy.dma_start(out=x1r[:].rearrange("q (tt d) -> q tt d", d=D),
                             in_=self.x1_scr.rearrange("(cc tt q) d -> cc q tt d", q=128, tt=4)[_ch]
                             ).then_inc(h["qS"], 16)

            se_y2 = []
            ve_z2 = []
            for odt in range(NDT):
                te_z2 = self.bump("tE")
                ve_z2.append(self.bump("vE"))
                te_y2 = self.bump("tE")
                se_y2.append(self.bump("sE"))

                @self.on("t")
                def _(t, _ch=ch, _odt=odt, _vz=ve_z2[odt], _qw=qs_zw2, _qz=qs_zzst,
                      _vzprev=(ve_z2[odt - 2] if odt >= 2 else None),
                      _seprev=(se_y2[odt - 1] if odt >= 1 else None)):
                    if _odt == 0:
                        t.wait_ge(h["qS"], max(_qw, _qz))
                    if _vzprev is not None:
                        t.wait_ge(h["vE"], _vzprev)
                    mmi = None
                    for ob in range(NDT):
                        mmi = t.matmul(P[_odt % 2][:, 0:512], zw2(ob, _odt),
                                       zzst[:, ob * 512:(ob + 1) * 512],
                                       start=(ob == 0), stop=False, skip_group_check=True)
                    mmi = t.matmul(P[_odt % 2][:, 0:512],
                                   self.be2_sb[0:16, _odt * 128:(_odt + 1) * 128],
                                   self.combine_T[0:16, _ch * 512:(_ch + 1) * 512],
                                   start=False, stop=True, skip_group_check=True)
                    mmi.then_inc(h["tE"], 1)
                    t.wait_ge(h["vE"], _vz)
                    if _seprev is not None:
                        t.wait_ge(h["sE"], _seprev)
                    mmi = None
                    for tt in range(4):
                        mmi = t.transpose(P[2 + _odt % 2][0:128, tt * 128:(tt + 1) * 128],
                                          y2f[:, (_odt % 2) * 512 + tt * 128:(_odt % 2) * 512 + (tt + 1) * 128],
                                          self.ident[:])
                    mmi.then_inc(h["tE"], 1)

                @self.on("v")
                def _(v, _ch=ch, _odt=odt, _tz=te_z2):
                    v.wait_ge(h["tE"], _tz)
                    v.scalar_tensor_tensor(y2f[:, (_odt % 2) * 512:(_odt % 2) * 512 + 512],
                                           P[_odt % 2][:, 0:512], self.col("bs2", _odt),
                                           self.y_acc[:, _odt * L + _ch * 512: _odt * L + (_ch + 1) * 512],
                                           ALU.add, ALU.add)
                    v.drain()
                    v.nop().then_inc(h["vE"], 1)

                @self.on("s")
                def _(s, _odt=odt, _ty=te_y2, _qo=(prev_qs_out if odt == 0 else None)):
                    s.wait_ge(h["tE"], _ty)
                    if _qo is not None:
                        s.wait_ge(h["qS"], _qo)
                    s.copy(y2t[:].rearrange("q (tt d) -> q tt d", d=D)[:, :, _odt * 128:(_odt + 1) * 128],
                           P[2 + _odt % 2][:, 0:512].rearrange("q (tt d1) -> q tt d1", d1=128))
                    s.drain()
                    s.nop().then_inc(h["sE"], 1)
            te_z2_hist.append(self.c["tE"])

            @self.on("v")
            def _(v, _s=se_y2[NDT - 1]):
                v.wait_ge(h["sE"], _s)
            y2tile = lambda tt: y2t[:, tt * 1024:(tt + 1) * 1024]
            self.emit_stats([y2tile(tt) for tt in range(4)])
            self.emit_rstd()
            ve_fin = self.bump("vE")

            @self.on("v")
            def _(v, _q=qs_x1, _yt=y2tile):
                v.wait_ge(h["qS"], _q)
                for tt in range(4):
                    v.tensor_scalar(self.sq[:], _yt(tt), self.muc[:, tt:tt + 1],
                                    self.rstdc[:, tt:tt + 1], ALU.subtract, ALU.mult)
                    v.drain()
                    v.tensor_tensor(_yt(tt), self.sq[:],
                                    x1r[:, tt * 1024:(tt + 1) * 1024], ALU.add)
                    v.drain()
                v.nop().then_inc(h["vE"], 1)

            qs_out = self.bump("qS", 16)

            @self.on("sy")
            def _(sy, _ch=ch, _v=ve_fin, _p=p):
                sy.wait_ge(h["vE"], _v)
                sy.dma_start(out=self.out[_p].rearrange("(cc tt q) d -> cc q tt d", q=128, tt=4)[_ch],
                             in_=y2t[:].rearrange("q (tt d) -> q tt d", d=D)).then_inc(h["qS"], 16)
            prev_ve_fin = ve_fin
            prev_qs_out = qs_out

        if self.debug and "comb" in self.dbg:
            self.bump("qS", 16)

            @self.on("sy")
            def _(sy, _p=p):
                sy.dma_start(out=self.dbg["comb"][_p], in_=self.combine_T[:]).then_inc(h["qS"], 16)

        self.fence()

    def build(self):
        nc = self.nc
        self.declare()
        top = ExitStack()
        SB = lambda n, s, dt=F32: top.enter_context(nc.sbuf_tensor(n, s, dt))
        PSm = lambda n, s, dt=F32: top.enter_context(nc.psum_tensor(n, s, dt))
        with top:
            self.persistent(SB, PSm)
            self.sc_sb = SB("sc_sb", [16, 512])
            self.startup()
            for p in range(NB):
                sfx = f"_p{p}"
                ps = ExitStack()
                S2 = lambda n, s, dt=F32: ps.enter_context(nc.sbuf_tensor(n + sfx, s, dt))
                with ps:
                    self.l3x_bf = S2("l3x_bf", [128, NDT * L], BF16)
                    a1 = ExitStack()
                    S3 = lambda n, s, dt=F32: a1.enter_context(nc.sbuf_tensor(n + sfx, s, dt))
                    with a1, nc.Block() as blk:
                        TA = {
                            "xsl": S3("xsl", [128, 2 * 4096]),
                            "ln1": S3("ln1", [128, 4096]),
                            "l1f": S3("l1f", [128, NDT * 512]),
                            "l1b": S3("l1b", [128, NDT * 512], BF16),
                            "wgT": S3("wgT", [128, NDT * D], BF16),
                            "gt": S3("gt", [128, 2 * 512]),
                            "xgf": S3("xgf", [128, 2 * 512]),
                        }
                        self.sq = S3("sq1", [128, 1024])
                        if p == 0:
                            TA["tap"] = (S3("A_f", [128, NDT * 256]),
                                         S3("Bp_f", [128, NDT * NST]),
                                         S3("Cp_f", [128, NDT * NST]),
                                         S3("u1", [128, 2304]),
                                         S3("dg_st", [128, 2 * J * 128], BF16))
                        self.phaseA1(p, TA)
                        self.fence()
                        self.flush(blk)
                    a2 = ExitStack()
                    S4 = lambda n, s, dt=F32: a2.enter_context(nc.sbuf_tensor(n + sfx, s, dt))
                    with a2, nc.Block() as blk:
                        TB = {
                            "dg": S4("dg", [128, 2 * J * 128], BF16),
                            "xb": S4("xb", [128, 2 * (J + 1024)], BF16),
                            "xf": S4("xf", [128, 1024]),
                            "yF": S4("yF", [128, 2 * 1024]),
                            "ytp": S4("ytp", [128, 2 * 8 * 1024]),
                            "xre": S4("xre", [128, 4096]),
                            "ln3": S4("ln3", [128, 4096]),
                            "l3f": S4("l3f", [128, 2 * 512]),
                        }
                        self.sq = S4("sq2", [128, 1024])
                        self.rt = {nm: S4("rt_" + nm, [128, sz]) for nm, sz in self.RT_SIZES}
                        self.gamma_b = S4("gamma_b", [128, D])
                        self.beta_b = S4("beta_b", [128, D])
                        self.phaseA2(p, TB)
                        self.flush(blk)
                    if STAGE >= 3:
                        ya = ExitStack()
                        self.y_acc = ya.enter_context(nc.sbuf_tensor("y_acc" + sfx, [128, NDT * L], F32))
                        ps.enter_context(ya.pop_all() if False else ya)
                        b1 = ExitStack()
                        S5 = lambda n, s, dt=F32: b1.enter_context(nc.sbuf_tensor(n + sfx, s, dt))
                        with b1, nc.Block() as blk:
                            TC = {
                                "we": S5("we", [128, 2 * 12288], BF16),
                                "hbf": S5("hbf", [128, 4 * L], BF16),
                                "h1s": S5("h1s", [128, L]),
                                "h3s": S5("h3s", [128, L]),
                                "cb": S5("cb", [128, L]),
                            }
                            self.phaseB(p, TC)
                            self.flush(blk)
                        z0 = ExitStack()
                        S60 = lambda n, s, dt=F32: z0.enter_context(nc.sbuf_tensor(n + sfx, s, dt))
                        with z0:
                            zw = S60("zw", [128, 2 * 8192], BF16)
                            za = ExitStack()
                            S6 = lambda n, s, dt=F32: za.enter_context(nc.sbuf_tensor(n + sfx, s, dt))
                            with za, nc.Block() as blk:
                                TD = {
                                    "zw": zw,
                                    "z1s": S6("z1s", [128, L]),
                                    "z3s": S6("z3s", [128, L]),
                                    "zzw": S6("zzw", [128, 2 * L], BF16),
                                }
                                self.phaseZ(p, TD)
                                self.fence()
                                self.flush(blk)
                            zb = ExitStack()
                            S6 = lambda n, s, dt=F32: zb.enter_context(nc.sbuf_tensor(n + sfx, s, dt))
                            with zb, nc.Block() as blk:
                                TE2 = {
                                    "zw": zw,
                                    "zzst": S6("zzst", [128, 4096], BF16),
                                    "y2f": S6("y2f", [128, 2 * 512]),
                                    "y2t": S6("y2t", [128, 4096]),
                                    "x1r": S6("x1r", [128, 4096]),
                                }
                                self.sq = S6("sq3", [128, 1024])
                                self.be2_sb = S6("be2_sb", [E, D])
                                self.phaseZ2(p, TE2)
                                self.flush(blk)
            with nc.Block() as blk:
                @self.on("sy")
                def _(sy):
                    sy.wait_ge(self.h["qS"], self.c["qS"])
                    sy.wait_ge(self.h["qG"], self.c["qG"])
                self.flush(blk)
        return nc


def make_consts():
    sel = np.zeros((E, E * 128), np.float32)
    for e in range(E):
        sel[e, e * 128:(e + 1) * 128] = 1.0
    return {
        "c_ident": np.eye(128, dtype=np.float32),
        "c_sel": sel,
        "c_ones": np.ones((1, 128), np.float32),
    }


_CACHE = {}


def get_nc(debug=False):
    key = (STAGE, debug)
    if key not in _CACHE:
        b = B(debug=debug)
        nc = b.build()
        _CACHE[key] = (nc, b.dbg)
    return _CACHE[key]


def kernel(**inputs):
    nc, _ = get_nc(False)
    consts = make_consts()
    x = np.ascontiguousarray(inputs["L0_attn_x"])
    ren = {
        "A": "L0_attn_A", "Bp": "L0_attn_Bp", "Cp": "L0_attn_Cp", "Dp": "L0_attn_Dp",
        "Wg": "L0_attn_Wg", "bg": "L0_attn_bg", "gamma": "L0_attn_gamma",
        "beta": "L0_attn_beta", "gate_w": "L0_ffn_gate_w", "gate_b": "L0_ffn_gate_b",
        "We1": "L0_ffn_We1", "be1": "L0_ffn_be1", "We2": "L0_ffn_We2",
        "be2": "L0_ffn_be2", "We3": "L0_ffn_We3", "be3": "L0_ffn_be3",
        "ws1": "L0_ffn_ws1", "bs1": "L0_ffn_bs1", "ws2": "L0_ffn_ws2",
        "bs2": "L0_ffn_bs2", "ws3": "L0_ffn_ws3", "bs3": "L0_ffn_bs3",
    }
    base = {k: np.ascontiguousarray(inputs[v]) for k, v in ren.items()}
    base.update(consts)
    in_maps = []
    for c in range(NCORE):
        m = dict(base)
        m["x"] = np.ascontiguousarray(x[c * NB:(c + 1) * NB])
        in_maps.append(m)
    res = run_bass_kernel_spmd(nc, in_maps, core_ids=list(range(NCORE)))
    return np.concatenate([r["out"] for r in res.results], axis=0)


if __name__ == "__main__":
    b = B(debug=False)
    b.build()
    print("build ok, instructions:", b.nc.n_instructions())


# version 11
# speedup vs baseline: 1.0647x; 1.0647x over previous
import numpy as np
from contextlib import ExitStack

import concourse.bass as bass
import concourse.mybir as mybir
from concourse.bass_utils import run_bass_kernel_spmd

F32 = mybir.dt.float32
BF16 = mybir.dt.bfloat16
AF = mybir.ActivationFunctionType
ALU = mybir.AluOpType
AX = mybir.AxisListType

D, L, NB, NST, E, INTER = 1024, 2048, 2, 16, 16, 512
NCORE = 8
J = 32
NEG = -1e30
EPS = 1e-5
NDT, NCH = 8, 4
STAGE = 3


class B:
    def __init__(self, debug=False):
        self.debug = debug
        nc = self.nc = bass.Bass("TRN2", target_bir_lowering=False, debug=False)
        self.ops = {k: [] for k in ("sy", "g", "t", "v", "s")}
        self.h = {}
        self.c = {}
        for s in ("qS", "qG", "tE", "vE", "sE", "pE", "F"):
            self.h[s] = nc.alloc_semaphore(s)
            self.c[s] = 0

    def on(self, eng):
        def deco(fn):
            self.ops[eng].append(fn)
        return deco

    def bump(self, s, k=1):
        self.c[s] += k
        return self.c[s]

    def fence(self):
        tgt = self.c["F"] + 5
        hF = self.h["F"]
        qs_now = self.c["qS"]
        qg_now = self.c["qG"]
        for k in ("sy", "g", "t", "v", "s"):
            def go(e, _t=tgt, _k=k, _qs=qs_now, _qg=qg_now):
                if _k in ("v", "s", "g"):
                    e.drain()
                if _k == "sy":
                    e.wait_ge(self.h["qS"], _qs)
                if _k == "g":
                    e.wait_ge(self.h["qG"], _qg)
                e.nop().then_inc(hF, 1)
                e.wait_ge(hF, _t)
            self.ops[k].append(go)
        self.c["F"] = tgt

    def flush(self, block):
        ops = self.ops
        self.ops = {k: [] for k in ops}
        block.sync(lambda e, o=tuple(ops["sy"]): [f(e) for f in o])
        block.gpsimd(lambda e, o=tuple(ops["g"]): [f(e) for f in o])
        block.tensor(lambda e, o=tuple(ops["t"]): [f(e) for f in o])
        block.vector(lambda e, o=tuple(ops["v"]): [f(e) for f in o])
        block.scalar(lambda e, o=tuple(ops["s"]): [f(e) for f in o])

    def declare(self):
        nc = self.nc
        self.x_in = nc.dram_tensor("x", [NB, L, D], F32, kind="ExternalInput").ap()
        self.w = {}
        for name, shape in [
            ("A", [D, NST, NST]), ("Bp", [D, NST]), ("Cp", [D, NST]), ("Dp", [D]),
            ("Wg", [D, D]), ("bg", [D]), ("gamma", [D]), ("beta", [D]),
            ("gate_w", [E, D]), ("gate_b", [E]),
            ("We1", [E, D, INTER]), ("be1", [E, INTER]), ("We2", [E, INTER, D]),
            ("be2", [E, D]), ("We3", [E, D, INTER]), ("be3", [E, INTER]),
            ("ws1", [D, D]), ("bs1", [D]), ("ws2", [D, D]), ("bs2", [D]),
            ("ws3", [D, D]), ("bs3", [D]),
        ]:
            self.w[name] = nc.dram_tensor(name, shape, F32, kind="ExternalInput").ap()
        self.ident_in = nc.dram_tensor("c_ident", [128, 128], F32, kind="ExternalInput").ap()
        self.sel_in = nc.dram_tensor("c_sel", [E, E * 128], F32, kind="ExternalInput").ap()
        self.ones_in = nc.dram_tensor("c_ones", [1, 128], F32, kind="ExternalInput").ap()
        self.out = nc.dram_tensor("out", [NB, L, D], F32, kind="ExternalOutput").ap()
        self.dbg = {}
        if self.debug:
            for nm, shape in [("x1", [NB, L, D]), ("comb", [NB, E, L]),
                              ("ln3", [NB, L, D]), ("xg", [NB, NDT, 128, L]),
                              ("y", [NB, L, D]), ("ln1", [NB, L, D]),
                              ("gate", [NB, NDT, 128, L]), ("wgT", [D, D]),
                              ("diag", [NDT, 128, J * 128]), ("vcol", [128, 5 * NDT]),
                              ("hbuf", [128, NDT * J]), ("gammab", [128, D]),
                              ("rt", [20, 128, 64]), ("yacc", [NDT, 128, L]), ("ffn", [128, 4096])]:
                self.dbg[nm] = nc.dram_tensor("dbg_" + nm, shape, F32, kind="ExternalOutput").ap()
        dr = lambda n, s, dt=BF16: nc.dram_tensor(n, s, dt, kind="Internal").ap()
        self.wb = {
            "We1": dr("wb_we1", [E, D, INTER]), "We2": dr("wb_we2", [E, INTER, D]),
            "We3": dr("wb_we3", [E, D, INTER]), "ws1": dr("wb_ws1", [D, D]),
            "ws2": dr("wb_ws2", [D, D]), "ws3": dr("wb_ws3", [D, D]),
            "Wg": dr("wb_wg", [D, D]),
        }
        self.wgT_dram = dr("wgT_dram", [D, D])
        self.gb_dram = dr("gb_dram", [2, 128, D], F32)
        self.zz_dram = dr("zz_dram", [NDT, 128, L])
        self.diag_dram = dr("diag_dram", [NDT, 128, J * 128])
        self.xg_f = dr("xg_f", [NDT, 128, L], F32)
        self.xg_b = dr("xg_b", [NDT, 128, J + L])
        self.x1_scr = dr("x1_scr", [L, D], F32)
        self.yacc = dr("yacc", [NDT, 128, L], F32)

    def persistent(self, SB, PSm):
        self.ident = SB("ident", [128, 128])
        self.sel_sb = SB("sel_sb", [E, E * 128])
        self.ones_sb = SB("ones_sb", [1, 128])
        self.gate_wT = SB("gate_wT", [128, NDT * E])
        self.gb_bcast = SB("gb_bcast", [128, E])
        self.vcol = SB("vcol", [128, 5 * NDT])
        self.be1c = SB("be1c", [128, 4 * E])
        self.be3c = SB("be3c", [128, 4 * E])
        self.epscol = SB("epscol", [128, 1])
        self.combine_T = SB("combine_T", [E, L])
        self.hbuf = SB("hbuf", [128, NDT * J])
        self.zpad = SB("zpad", [128, J], BF16)
        self.s1c = SB("s1c", [128, 4]); self.s2c = SB("s2c", [128, 4])
        self.muc = SB("muc", [128, 4]); self.varc = SB("varc", [128, 4])
        self.rstdc = SB("rstdc", [128, 4]); self.tnc = SB("tnc", [128, 4])
        self.RT_SIZES = [
            ("m", 4), ("rec", 4), ("sum", 4), ("sd", 64), ("ex", 64), ("orig", 64),
            ("sc2", 64), ("m1", 16), ("eqa", 64), ("sgm", 64), ("m2", 16),
            ("gsc", 16), ("gm1", 4), ("eqg", 16), ("gsc2", 16), ("gm2", 4),
            ("keep", 16), ("smt1", 64), ("smt2", 64), ("sm", 64), ("t1", 4),
            ("eq1", 64), ("sm2", 64), ("t2", 4), ("eq2", 64), ("comb", 64)]
        self.PA = PSm("psA", [128, 2048])
        self.PB = PSm("psB", [128, 2048])
        self.P = [self.PA[:, i * 512:(i + 1) * 512] for i in range(4)] + \
                 [self.PB[:, i * 512:(i + 1) * 512] for i in range(4)]

    def col(self, which, dt):
        k = {"bg": 0, "Dp": 1, "bs1": 2, "bs2": 3, "bs3": 4}[which]
        return self.vcol[:, k * NDT + dt: k * NDT + dt + 1]

    def emit_rstd(self, n=4):
        h, c = self.h, self.c
        vE_stats = self.bump("vE")

        @self.on("v")
        def _(v):
            v.drain()
            v.nop().then_inc(h["vE"], 1)
        sE_rstd = self.bump("sE")

        @self.on("s")
        def _(s, _w=vE_stats):
            s.wait_ge(h["vE"], _w)
            s.activation(self.rstdc[:, 0:n], self.varc[:, 0:n], AF.Ln,
                         bias=self.epscol[:, 0:1], scale=1.0)
            s.drain()
            s.activation(self.rstdc[:, 0:n], self.rstdc[:, 0:n], AF.Exp, scale=-0.5)
            s.drain()
            s.nop().then_inc(h["sE"], 1)

        @self.on("v")
        def _(v, _w=sE_rstd):
            v.wait_ge(h["sE"], _w)
            rc, tn, vc = self.rstdc[:, 0:n], self.tnc[:, 0:n], self.varc[:, 0:n]
            v.tensor_tensor(tn, rc, rc, ALU.mult); v.drain()
            v.tensor_tensor(tn, tn, vc, ALU.mult); v.drain()
            v.tensor_scalar(tn, tn, -0.5, 1.5, ALU.mult, ALU.add); v.drain()
            v.tensor_tensor(rc, rc, tn, ALU.mult); v.drain()

    def emit_stats(self, tiles):
        @self.on("v")
        def _(v):
            for tt, tile in enumerate(tiles):
                v.tensor_reduce(self.s1c[:, tt:tt + 1], tile, AX.X, ALU.add)
                v.scalar_tensor_tensor(self.sq[:], tile, 1.0, tile, ALU.mult, ALU.mult,
                                       accum_out=self.s2c[:, tt:tt + 1])
            v.drain()
            v.tensor_scalar(self.muc[:], self.s1c[:], 1.0 / D, None, ALU.mult)
            v.tensor_scalar(self.varc[:], self.s2c[:], 1.0 / D, None, ALU.mult)
            v.drain()
            v.tensor_tensor(self.tnc[:], self.muc[:], self.muc[:], ALU.mult)
            v.drain()
            v.tensor_tensor(self.varc[:], self.varc[:], self.tnc[:], ALU.subtract)
            v.drain()

    def startup(self):
        nc, h, w = self.nc, self.h, self.w
        st = ExitStack()
        S = lambda n, s, dt=F32: st.enter_context(nc.sbuf_tensor(n, s, dt))
        vrow = S("vrow", [8, 5 * 128])
        grow = S("grow", [1, D])
        brow = S("brow", [1, D])
        gbrow = S("gbrow", [1, E])
        gwt_st = S("gwt_st", [16, D])
        be_st = S("be_st", [16, 2 * INTER])
        wg_st = S("wg_st", [128, 2 * D])
        wgT_st = S("wgT_st", [128, 2 * D], BF16)
        gbst = S("gbst", [128, 2 * D])
        P = self.P

        with st, nc.Block() as blk:
            @self.on("sy")
            def _(sy):
                for dst, src in [
                    (self.ident[:], self.ident_in), (self.ones_sb[:], self.ones_in),
                    (self.sel_sb[:], self.sel_in),
                    (grow[0:1, :], w["gamma"].rearrange("(a d) -> a d", a=1)),
                    (brow[0:1, :], w["beta"].rearrange("(a d) -> a d", a=1)),
                    (gbrow[:], w["gate_b"].rearrange("(a e) -> a e", a=1)),
                    (gwt_st[:], w["gate_w"]),
                    (vrow[0:8, 0:128], w["bg"].rearrange("(dt p) -> dt p", p=128)),
                    (vrow[0:8, 128:256], w["Dp"].rearrange("(dt p) -> dt p", p=128)),
                    (vrow[0:8, 256:384], w["bs1"].rearrange("(dt p) -> dt p", p=128)),
                    (vrow[0:8, 384:512], w["bs2"].rearrange("(dt p) -> dt p", p=128)),
                    (vrow[0:8, 512:640], w["bs3"].rearrange("(dt p) -> dt p", p=128)),
                    (be_st[:, 0:INTER], w["be1"]),
                    (be_st[:, INTER:], w["be3"]),
                ]:
                    sy.dma_start(out=dst, in_=src).then_inc(h["qS"], 16)
            qS_loads = self.bump("qS", 16 * 14)


            @self.on("v")
            def _(v):
                v.memset(self.epscol[:], EPS)
                v.memset(self.zpad[:], 0.0)
                v.drain()

            tE_sm = self.bump("tE")

            @self.on("t")
            def _(t, _w=qS_loads):
                t.wait_ge(h["qS"], _w)
                for k in range(5):
                    t.transpose(P[0][0:128, k * 8:(k + 1) * 8],
                                vrow[0:8, k * 128:(k + 1) * 128], self.ident[0:8, 0:8])
                t.matmul(P[1][:, 0:512], self.ones_sb[:, 0:128], grow[0:1, 0:512], start=True, stop=True)
                t.matmul(P[2][:, 0:512], self.ones_sb[:, 0:128], grow[0:1, 512:1024], start=True, stop=True)
                t.matmul(P[3][:, 0:512], self.ones_sb[:, 0:128], brow[0:1, 0:512], start=True, stop=True)
                t.matmul(P[4][:, 0:512], self.ones_sb[:, 0:128], brow[0:1, 512:1024], start=True, stop=True)
                t.matmul(P[5][:, 0:E], self.ones_sb[:, 0:128], gbrow[0:1, :], start=True, stop=True)
                for dt in range(NDT):
                    t.transpose(P[6][0:128, dt * 16:(dt + 1) * 16],
                                gwt_st[0:16, dt * 128:(dt + 1) * 128], self.ident[0:16, 0:16])
                mmi = None
                for ib in range(4):
                    t.transpose(P[7][0:128, ib * 16:(ib + 1) * 16],
                                be_st[0:16, ib * 128:(ib + 1) * 128], self.ident[0:16, 0:16])
                    mmi = t.transpose(P[7][0:128, 64 + ib * 16: 64 + (ib + 1) * 16],
                                      be_st[0:16, INTER + ib * 128: INTER + (ib + 1) * 128],
                                      self.ident[0:16, 0:16])
                mmi.then_inc(h["tE"], 1)

            sE_sm = self.bump("sE")
            self._sE_sm = sE_sm

            @self.on("s")
            def _(s, _w=tE_sm):
                s.wait_ge(h["tE"], _w)
                s.copy(self.vcol[:, 0:40], P[0][:, 0:40])
                s.copy(gbst[:, 0:512], P[1][:, 0:512])
                s.copy(gbst[:, 512:1024], P[2][:, 0:512])
                s.copy(gbst[:, 1024:1536], P[3][:, 0:512])
                s.copy(gbst[:, 1536:2048], P[4][:, 0:512])
                s.copy(self.gb_bcast[:], P[5][:, 0:E])
                s.copy(self.gate_wT[:], P[6][:, 0:NDT * E])
                s.copy(self.be1c[:], P[7][:, 0:64])
                s.copy(self.be3c[:], P[7][:, 64:128])
                s.drain()
                s.nop().then_inc(h["sE"], 1)

            qG_gb = self.bump("qG", 16)

            @self.on("g")
            def _(g, _w=sE_sm):
                g.wait_ge(h["sE"], _w)
                g.dma_start(out=self.gb_dram.rearrange("a q d2 -> q a d2"),
                            in_=gbst[:].rearrange("q (a d2) -> q a d2", d2=D)).then_inc(h["qG"], 16)

            qS_wg0 = self.c["qS"] + 16
            self.bump("qS", 16 * NDT)
            tE_wg0 = self.c["tE"] + 1
            self.bump("tE", NDT)
            sE_wg0 = self.c["sE"] + 1
            self.bump("sE", NDT)
            qG_wgT0 = self.c["qG"] + 16
            self.bump("qG", 16 * NDT)

            @self.on("sy")
            def _(sy):
                for odt in range(NDT):
                    if odt >= 2:
                        sy.wait_ge(h["tE"], tE_wg0 + odt - 2)
                    sy.dma_start(out=wg_st[:, (odt % 2) * D:(odt % 2) * D + D],
                                 in_=w["Wg"].rearrange("(odt q) d2 -> odt q d2", q=128)[odt]
                                 ).then_inc(h["qS"], 16)

            @self.on("t")
            def _(t, _s=sE_sm):
                t.wait_ge(h["sE"], _s)
                for odt in range(NDT):
                    t.wait_ge(h["qS"], qS_wg0 + odt * 16)
                    if odt >= 1:
                        t.wait_ge(h["sE"], sE_wg0 + odt - 1)
                    b0, b1 = 2 * (odt % 2), 2 * (odt % 2) + 1
                    mmi = None
                    for kdt in range(NDT):
                        pb = P[b0] if kdt < 4 else P[b1]
                        mmi = t.transpose(pb[0:128, (kdt % 4) * 128:(kdt % 4 + 1) * 128],
                                          wg_st[:, (odt % 2) * D + kdt * 128:(odt % 2) * D + (kdt + 1) * 128],
                                          self.ident[:])
                    mmi.then_inc(h["tE"], 1)

            @self.on("s")
            def _(s):
                for odt in range(NDT):
                    s.wait_ge(h["tE"], tE_wg0 + odt)
                    if odt >= 2:
                        s.wait_ge(h["qG"], qG_wgT0 + 16 * (odt - 2))
                    b0, b1 = 2 * (odt % 2), 2 * (odt % 2) + 1
                    s.copy(wgT_st[:, (odt % 2) * D:(odt % 2) * D + 512], P[b0][:, 0:512])
                    s.copy(wgT_st[:, (odt % 2) * D + 512:(odt % 2) * D + D], P[b1][:, 0:512])
                    s.drain()
                    s.nop().then_inc(h["sE"], 1)

            @self.on("g")
            def _(g):
                for odt in range(NDT):
                    g.wait_ge(h["sE"], sE_wg0 + odt)
                    g.dma_start(out=self.wgT_dram.rearrange("(kdt p) e2 -> p kdt e2", p=128)[:, :, odt * 128:(odt + 1) * 128],
                                in_=wgT_st[:, (odt % 2) * D:(odt % 2) * D + D]
                                .rearrange("p (kdt q) -> p kdt q", q=128)).then_inc(h["qG"], 16)

            if self.debug and "wgT" in self.dbg:
                self.bump("qG", 80)

                @self.on("g")
                def _(g, _q=self.c["qG"] - 80):
                    g.wait_ge(h["qG"], _q)
                    g.dma_start(out=self.dbg["wgT"], in_=self.wgT_dram).then_inc(h["qG"], 16)
                    g.dma_start(out=self.dbg["diag"], in_=self.diag_dram).then_inc(h["qG"], 16)
                    g.dma_start(out=self.dbg["vcol"], in_=self.vcol[:]).then_inc(h["qG"], 16)
                    g.dma_start(out=self.dbg["hbuf"], in_=self.hbuf[:]).then_inc(h["qG"], 16)
                    g.dma_start(out=self.dbg["gammab"], in_=self.gb_dram[0]).then_inc(h["qG"], 16)

            self.fence()
            self.flush(blk)

    def emit_taps(self, A_f, Bp_f, Cp_f, u1, dg_st, qs_ld, se_sm):
        h = self.h
        vE_tap = self.bump("vE")

        @self.on("v")
        def _(v, _w=qs_ld, _ws=se_sm):
            v.wait_ge(h["qS"], _w)
            v.wait_ge(h["sE"], _ws)
            A3 = A_f[:].rearrange("p (dt n m) -> p (dt n) m", n=NST, m=NST)
            hb3 = lambda j: (self.hbuf[:].rearrange("p (dt jj) -> p dt jj", jj=J)
                             [:, :, j:j + 1].rearrange("p dt one -> p (dt one)"))
            v.tensor_copy(u1[:, 0:128], Bp_f[:])
            v.drain()
            for j in range(J):
                v.tensor_tensor(u1[:, 128:256], u1[:, 0:128], Cp_f[:], ALU.mult)
                v.drain()
                v.tensor_reduce(hb3(j), u1[:, 128:256].rearrange("p (dt n) -> p dt n", n=NST),
                                AX.X, ALU.add)
                v.drain()
                if j < J - 1:
                    v.tensor_tensor(u1[:, 256:2304].rearrange("p (dtn m) -> p dtn m", m=NST),
                                    A3, u1[:, 0:128].broadcast_to([128, 128, 16]), ALU.mult)
                    v.drain()
                    for half in range(2):
                        v.tensor_reduce(u1[:, half * 64: half * 64 + 64]
                                        .rearrange("p (dt m) -> p dt m", m=NST),
                                        u1[:, 256 + half * 1024: 256 + half * 1024 + 1024]
                                        .rearrange("p (dt n m) -> p dt m n", n=NST, m=NST),
                                        AX.X, ALU.add)
                    v.drain()
            v.tensor_tensor(hb3(0), hb3(0), self.vcol[:, NDT:2 * NDT], ALU.add)
            v.drain()
            v.nop().then_inc(h["vE"], 1)

        vE_dg0 = self.c["vE"] + 1
        self.bump("vE", NDT)
        qG_dg0 = self.c["qG"] + 16
        self.bump("qG", 16 * NDT + 16 * NDT)

        @self.on("v")
        def _(v, _w=vE_tap):
            v.wait_ge(h["vE"], _w)
            for dt in range(NDT):
                sl = (dt % 2) * J * 128
                if dt >= 2:
                    v.wait_ge(h["qG"], qG_dg0 + 16 * (dt - 2))
                for j in range(J):
                    v.tensor_scalar(dg_st[:, sl + j * 128: sl + (j + 1) * 128],
                                    self.ident[:],
                                    self.hbuf[:, dt * J + j: dt * J + j + 1],
                                    None, ALU.mult)
                v.drain()
                v.nop().then_inc(h["vE"], 1)

        @self.on("g")
        def _(g):
            for dt in range(NDT):
                g.wait_ge(h["vE"], vE_dg0 + dt)
                sl = (dt % 2) * J * 128
                g.dma_start(out=self.diag_dram[dt],
                            in_=dg_st[:, sl: sl + J * 128]).then_inc(h["qG"], 16)
            for dt in range(NDT):
                g.dma_start(out=self.xg_b[dt, :, 0:J], in_=self.zpad[:]).then_inc(h["qG"], 16)

    def phaseA1(self, p, T):
        h, P = self.h, self.P
        xsl, ln1 = T["xsl"], T["ln1"]
        l1f, l1b = T["l1f"], T["l1b"]
        wgT, gt, xgf = T["wgT"], T["gt"], T["xgf"]

        qs_wgT = self.bump("qS", 16)

        @self.on("sy")
        def _(sy):
            sy.dma_start(out=wgT[:].rearrange("pp (kdt e2) -> pp kdt e2", e2=D),
                         in_=self.wgT_dram.rearrange("(kdt pp) e2 -> pp kdt e2", pp=128)
                         ).then_inc(h["qS"], 16)

        ve_ln1 = [None] * NCH
        qsx = [None] * NCH
        for c in range(NCH):
            qsx[c] = self.bump("qS", 16)

            @self.on("sy")
            def _(sy, _c=c):
                if _c >= 2:
                    sy.wait_ge(h["vE"], ve_ln1[_c - 2])
                sy.dma_start(out=xsl[:, (_c % 2) * 4096:(_c % 2) * 4096 + 4096]
                             .rearrange("pp (tt d) -> pp tt d", d=D),
                             in_=self.x_in[p].rearrange("(cc tt q) d -> cc q tt d", q=128, tt=4)[_c]
                             ).then_inc(h["qS"], 16)

            xtile = lambda tt, _c=c: xsl[:, (_c % 2) * 4096 + tt * 1024:(_c % 2) * 4096 + (tt + 1) * 1024]

            @self.on("v")
            def _(v, _c=c, _q=qsx[c]):
                v.wait_ge(h["qS"], _q)
            self.emit_stats([xtile(tt) for tt in range(4)])
            self.emit_rstd()
            ve_ln1[c] = self.bump("vE")

            @self.on("v")
            def _(v, _xt=xtile):
                for tt in range(4):
                    v.tensor_scalar(ln1[:, tt * 1024:(tt + 1) * 1024], _xt(tt),
                                    self.muc[:, tt:tt + 1], self.rstdc[:, tt:tt + 1],
                                    ALU.subtract, ALU.mult)
                v.drain()
                v.nop().then_inc(h["vE"], 1)

            if self.debug and "ln1" in self.dbg:
                self.bump("qS", 16)

                @self.on("sy")
                def _(sy, _c=c, _v=ve_ln1[c] + 1, _p=p):
                    sy.wait_ge(h["vE"], _v)
                    sy.dma_start(out=self.dbg["ln1"][_p].rearrange("(cc tt q) d -> cc q tt d", q=128, tt=4)[_c],
                                 in_=ln1[:].rearrange("q (tt d) -> q tt d", d=D)).then_inc(h["qS"], 16)

            te_tp = [self.bump("tE") for _ in range(NDT)]
            se_tp = [self.bump("sE") for _ in range(NDT)]

            @self.on("t")
            def _(t, _c=c, _v=ve_ln1[c], _te=tuple(te_tp), _se=tuple(se_tp)):
                t.wait_ge(h["vE"], _v)
                for dt in range(NDT):
                    if dt >= 2:
                        t.wait_ge(h["sE"], _se[dt - 2])
                    mmi = None
                    for tt in range(4):
                        mmi = t.transpose(P[dt % 2][0:128, tt * 128:(tt + 1) * 128],
                                          ln1[:, tt * 1024 + dt * 128: tt * 1024 + (dt + 1) * 128],
                                          self.ident[:])
                    mmi.then_inc(h["tE"], 1)

            @self.on("s")
            def _(s, _te=tuple(te_tp)):
                for dt in range(NDT):
                    s.wait_ge(h["tE"], _te[dt])
                    s.copy(l1b[:, dt * 512:(dt + 1) * 512], P[dt % 2][:, 0:512])
                    s.copy(l1f[:, dt * 512:(dt + 1) * 512], P[dt % 2][:, 0:512])
                    s.drain()
                    s.nop().then_inc(h["sE"], 1)

            te_g = [self.bump("tE") for _ in range(NDT)]
            se_g = [self.bump("sE") for _ in range(NDT)]
            ve_xg = [self.bump("vE") for _ in range(NDT)]
            qs_xgf = [self.bump("qS", 16) for _ in range(NDT)]
            qg_xgb = [self.bump("qG", 16) for _ in range(NDT)]

            @self.on("t")
            def _(t, _c=c, _sall=se_tp[NDT - 1], _se=tuple(se_g), _q=qs_wgT):
                t.wait_ge(h["sE"], _sall)
                t.wait_ge(h["qS"], _q)
                for odt in range(NDT):
                    if odt >= 2:
                        t.wait_ge(h["sE"], _se[odt - 2])
                    mmi = None
                    for kdt in range(NDT):
                        mmi = t.matmul(P[2 + odt % 2][:, 0:512],
                                       wgT[:, kdt * D + odt * 128: kdt * D + (odt + 1) * 128],
                                       l1b[:, kdt * 512:(kdt + 1) * 512],
                                       start=(kdt == 0), stop=(kdt == NDT - 1))
                    mmi.then_inc(h["tE"], 1)

            @self.on("s")
            def _(s, _te=tuple(te_g), _ve=tuple(ve_xg)):
                for odt in range(NDT):
                    s.wait_ge(h["tE"], _te[odt])
                    if odt >= 2:
                        s.wait_ge(h["vE"], _ve[odt - 2])
                    s.activation(gt[:, (odt % 2) * 512:(odt % 2) * 512 + 512],
                                 P[2 + odt % 2][:, 0:512], AF.Sigmoid,
                                 bias=self.col("bg", odt), scale=1.0)
                    s.drain()
                    s.nop().then_inc(h["sE"], 1)

            @self.on("v")
            def _(v, _se=tuple(se_g), _qsx=tuple(qs_xgf), _qgx=tuple(qg_xgb)):
                for odt in range(NDT):
                    v.wait_ge(h["sE"], _se[odt])
                    if odt >= 2:
                        v.wait_ge(h["qS"], _qsx[odt - 2])
                        v.wait_ge(h["qG"], _qgx[odt - 2])
                    v.tensor_tensor(xgf[:, (odt % 2) * 512:(odt % 2) * 512 + 512],
                                    l1f[:, odt * 512:(odt + 1) * 512],
                                    gt[:, (odt % 2) * 512:(odt % 2) * 512 + 512], ALU.mult)
                    v.drain()
                    v.nop().then_inc(h["vE"], 1)

            @self.on("sy")
            def _(sy, _c=c, _ve=tuple(ve_xg)):
                for odt in range(NDT):
                    sy.wait_ge(h["vE"], _ve[odt])
                    sy.dma_start(out=self.xg_f[odt, :, _c * 512:(_c + 1) * 512],
                                 in_=xgf[:, (odt % 2) * 512:(odt % 2) * 512 + 512]
                                 ).then_inc(h["qS"], 16)

            @self.on("g")
            def _(g, _c=c, _ve=tuple(ve_xg)):
                for odt in range(NDT):
                    g.wait_ge(h["vE"], _ve[odt])
                    g.dma_start(out=self.xg_b[odt, :, J + _c * 512: J + (_c + 1) * 512],
                                in_=xgf[:, (odt % 2) * 512:(odt % 2) * 512 + 512]
                                ).then_inc(h["qG"], 16)
            if self.debug and "gate" in self.dbg:
                self.bump("qG", 16 * NDT)

                @self.on("g")
                def _(g, _c=c, _se=tuple(se_g), _p=p):
                    for odt in range(NDT):
                        g.wait_ge(h["sE"], _se[odt])
                        g.dma_start(out=self.dbg["gate"][_p, odt, :, _c * 512:(_c + 1) * 512],
                                    in_=gt[:, (odt % 2) * 512:(odt % 2) * 512 + 512]
                                    ).then_inc(h["qG"], 16)

        if p == 0:
            A_f, Bp_f, Cp_f, u1, dg_st = T["tap"]
            qs_abc = self.bump("qS", 48)

            @self.on("sy")
            def _(sy):
                sy.dma_start(out=A_f[:].rearrange("p (dt nm) -> p dt nm", nm=256),
                             in_=self.w["A"].rearrange("(dt p) n m -> p dt (n m)", p=128)
                             ).then_inc(h["qS"], 16)
                sy.dma_start(out=Bp_f[:].rearrange("p (dt n) -> p dt n", n=NST),
                             in_=self.w["Bp"].rearrange("(dt p) n -> p dt n", p=128)
                             ).then_inc(h["qS"], 16)
                sy.dma_start(out=Cp_f[:].rearrange("p (dt n) -> p dt n", n=NST),
                             in_=self.w["Cp"].rearrange("(dt p) n -> p dt n", p=128)
                             ).then_inc(h["qS"], 16)

            self.emit_taps(A_f, Bp_f, Cp_f, u1, dg_st, qs_abc, self._sE_sm)
            self.qG_casts = self.bump("qG", 16 * 6)

            @self.on("g")
            def _(g):
                for dst, src in [(self.wb["We1"], self.w["We1"]), (self.wb["We3"], self.w["We3"]),
                                 (self.wb["We2"], self.w["We2"]),
                                 (self.wb["ws1"], self.w["ws1"]), (self.wb["ws3"], self.w["ws3"]),
                                 (self.wb["ws2"], self.w["ws2"])]:
                    g.dma_start(out=dst, in_=src).then_inc(h["qG"], 16)

    def phaseA2(self, p, T):
        h, P = self.h, self.P
        dg, xb, xf = T["dg"], T["xb"], T["xf"]
        yF, ytp, xre, ln3 = T["yF"], T["ytp"], T["xre"], T["ln3"]
        l3f = T["l3f"]
        W = J + 1024

        qs_gb = self.bump("qS", 16)

        @self.on("sy")
        def _(sy):
            sy.dma_start(out=self.gamma_b[:], in_=self.gb_dram[0]).then_inc(h["qS"], 16)

        qs_gb2 = self.bump("qS", 16)

        @self.on("sy")
        def _(sy):
            sy.dma_start(out=self.beta_b[:], in_=self.gb_dram[1]).then_inc(h["qS"], 16)
        self._qs_gb = qs_gb2

        conv_hist = []
        ytr_hist = []
        join_hist = []

        seyt_hist = []
        self._a2_prev_ve_x1 = None
        for pr in range(2):
            c0 = 2 * pr
            se_yt = []
            for dt in range(NDT):
                gi = pr * NDT + dt
                qs_dg = self.bump("qS", 48)

                @self.on("sy")
                def _(sy, _dt=dt, _c0=c0, _gi=gi,
                      _g=(conv_hist[gi - 2] if gi >= 2 else None),
                      _gj=(join_hist[gi - 1] if gi >= 1 else None)):
                    if _g is not None:
                        sy.wait_ge(h["tE"], _g)
                    if _gj is not None:
                        sy.wait_ge(h["vE"], _gj)
                    sl = _dt % 2
                    sy.dma_start(out=dg[:, sl * J * 128:(sl + 1) * J * 128],
                                 in_=self.diag_dram[_dt]).then_inc(h["qS"], 16)
                    sy.dma_start(out=xb[:, sl * W: sl * W + W],
                                 in_=self.xg_b[_dt, :, _c0 * 512: _c0 * 512 + W]
                                 ).then_inc(h["qS"], 16)
                    sy.dma_start(out=xf[:],
                                 in_=self.xg_f[_dt, :, _c0 * 512: _c0 * 512 + 1024]
                                 ).then_inc(h["qS"], 16)

                te_conv = self.bump("tE")
                conv_hist.append(te_conv)
                ve_join = self.bump("vE")
                join_hist.append(ve_join)
                te_yt = self.bump("tE")
                ytr_hist.append(te_yt)
                se_yt.append(self.bump("sE"))

                @self.on("t")
                def _(t, _dt=dt, _q=qs_dg, _gi=gi,
                      _vjprev=(join_hist[gi - 1] if gi >= 1 else None),
                      _vj=ve_join,
                      _seprev=(se_yt[dt - 1] if dt >= 1 else None),
                      _se_prev_pair=(self._seyt_last if pr == 1 and dt == 0 else None)):
                    sl = _dt % 2
                    t.wait_ge(h["qS"], _q)
                    if _vjprev is not None:
                        t.wait_ge(h["vE"], _vjprev)
                    mmi = None
                    for j in range(J):
                        for cc in range(2):
                            mmi = t.matmul(P[cc][:, 0:512],
                                           dg[:, sl * J * 128 + j * 128: sl * J * 128 + (j + 1) * 128],
                                           xb[:, sl * W + J + cc * 512 - j: sl * W + J + cc * 512 - j + 512],
                                           start=(j == 0), stop=(j == J - 1),
                                           skip_group_check=True)
                    mmi.then_inc(h["tE"], 1)
                    t.wait_ge(h["vE"], _vj)
                    if _seprev is not None:
                        t.wait_ge(h["sE"], _seprev)
                    if _se_prev_pair is not None:
                        t.wait_ge(h["sE"], _se_prev_pair)
                    mmi = None
                    for wd in range(8):
                        pb = P[2] if wd < 4 else P[3]
                        mmi = t.transpose(pb[0:128, (wd % 4) * 128:(wd % 4 + 1) * 128],
                                          yF[:, sl * 1024 + wd * 128: sl * 1024 + (wd + 1) * 128],
                                          self.ident[:])
                    mmi.then_inc(h["tE"], 1)

                @self.on("v")
                def _(v, _dt=dt, _tc=te_conv, _gi=gi,
                      _typrev=(ytr_hist[gi - 2] if gi >= 2 else None)):
                    v.wait_ge(h["tE"], _tc)
                    if _typrev is not None:
                        v.wait_ge(h["tE"], _typrev)
                    sl = _dt % 2
                    v.tensor_tensor(yF[:, sl * 1024: sl * 1024 + 1024],
                                    self.PA[:, 0:1024], xf[:], ALU.add)
                    v.drain()
                    v.nop().then_inc(h["vE"], 1)

                @self.on("s")
                def _(s, _dt=dt, _ty=te_yt, _pr=pr):
                    s.wait_ge(h["tE"], _ty)
                    s.copy(ytp[:, (_pr % 2) * 8192:(_pr % 2) * 8192 + 8192].rearrange("q (wd d1) -> q wd d1", d1=1024)[:, 0:4, _dt * 128:(_dt + 1) * 128],
                           P[2][:, 0:512].rearrange("q (wd d1) -> q wd d1", d1=128))
                    s.copy(ytp[:, (_pr % 2) * 8192:(_pr % 2) * 8192 + 8192].rearrange("q (wd d1) -> q wd d1", d1=1024)[:, 4:8, _dt * 128:(_dt + 1) * 128],
                           P[3][:, 0:512].rearrange("q (wd d1) -> q wd d1", d1=128))
                    s.drain()
                    s.nop().then_inc(h["sE"], 1)

            seyt_hist.append(se_yt[NDT - 1])
            self._seyt_last = se_yt[NDT - 1]

            if self.debug and "y" in self.dbg:
                self.bump("qS", 16)

                @self.on("sy")
                def _(sy, _pr=pr, _s=se_yt[NDT - 1], _p=p):
                    sy.wait_ge(h["sE"], _s)
                    sy.dma_start(out=self.dbg["y"][_p].rearrange("(q2 tt q) d -> q2 q tt d", q=128, tt=8)[_pr],
                                 in_=ytp[:, (_pr % 2) * 8192:(_pr % 2) * 8192 + 8192].rearrange("q (tt d) -> q tt d", d=D)).then_inc(h["qS"], 16)
            if self.debug and "xg" in self.dbg and pr == 0:
                self.bump("qG", 16 * NDT)

                @self.on("g")
                def _(g, _p=p, _q=self.c["qS"]):
                    g.wait_ge(h["qS"], _q)
                    for dt in range(NDT):
                        g.dma_start(out=self.dbg["xg"][_p, dt], in_=self.xg_f[dt]).then_inc(h["qG"], 16)

        for pr in range(2):
            c0 = 2 * pr
            for cc in range(2):
                c = c0 + cc
                ytile = lambda tt, _cc=cc, _pr=pr: ytp[:, (_pr % 2) * 8192 + (_cc * 4 + tt) * 1024:(_pr % 2) * 8192 + (_cc * 4 + tt + 1) * 1024]

                qs_xre = self.bump("qS", 16)

                @self.on("sy")
                def _(sy, _c=c, _cc=cc, _vx=getattr(self, "_a2_prev_ve_x1", None)):
                    if _vx is not None:
                        sy.wait_ge(h["vE"], _vx)
                    sy.dma_start(out=xre[:].rearrange("q (tt d) -> q tt d", d=D),
                                 in_=self.x_in[p].rearrange("(ch tt q) d -> ch q tt d", q=128, tt=4)[_c]
                                 ).then_inc(h["qS"], 16)

                @self.on("v")
                def _(v, _s=seyt_hist[pr], _q=self._qs_gb):
                    v.wait_ge(h["sE"], _s)
                    v.wait_ge(h["qS"], _q)
                self.emit_stats([ytile(tt) for tt in range(4)])
                self.emit_rstd()

                @self.on("v")
                def _(v, _yt=ytile):
                    for tt in range(4):
                        tile = _yt(tt)
                        v.tensor_scalar(self.sq[:], tile, self.muc[:, tt:tt + 1],
                                        self.rstdc[:, tt:tt + 1], ALU.subtract, ALU.mult)
                        v.drain()
                        v.tensor_tensor(self.sq[:], self.sq[:], self.gamma_b[:], ALU.mult)
                        v.drain()
                        v.scalar_tensor_tensor(tile, self.sq[:], 1.0, self.beta_b[:],
                                               ALU.mult, ALU.add,
                                               accum_out=self.s1c[:, tt:tt + 1])
                        v.drain()
                        v.scalar_tensor_tensor(self.sq[:], tile, 1.0, tile,
                                               ALU.mult, ALU.mult,
                                               accum_out=self.s2c[:, tt:tt + 1])
                        v.drain()
                    v.tensor_scalar(self.muc[:], self.s1c[:], 1.0 / D, None, ALU.mult)
                    v.tensor_scalar(self.varc[:], self.s2c[:], 1.0 / D, None, ALU.mult)
                    v.drain()
                    v.tensor_tensor(self.tnc[:], self.muc[:], self.muc[:], ALU.mult)
                    v.drain()
                    v.tensor_tensor(self.varc[:], self.varc[:], self.tnc[:], ALU.subtract)
                    v.drain()
                self.emit_rstd()
                ve_x1 = self.bump("vE")

                @self.on("v")
                def _(v, _cc=cc, _q=qs_xre, _yt=ytile):
                    v.wait_ge(h["qS"], _q)
                    for tt in range(4):
                        tile = _yt(tt)
                        v.tensor_scalar(self.sq[:], tile, self.muc[:, tt:tt + 1],
                                        self.rstdc[:, tt:tt + 1], ALU.subtract, ALU.mult)
                        v.drain()
                        v.scalar_tensor_tensor(tile, self.sq[:], 1.0,
                                               xre[:, tt * 1024:(tt + 1) * 1024],
                                               ALU.mult, ALU.add,
                                               accum_out=self.s1c[:, tt:tt + 1])
                        v.drain()
                        v.scalar_tensor_tensor(self.sq[:], tile, 1.0, tile, ALU.mult, ALU.mult,
                                               accum_out=self.s2c[:, tt:tt + 1])
                    v.drain()
                    v.tensor_scalar(self.muc[:], self.s1c[:], 1.0 / D, None, ALU.mult)
                    v.tensor_scalar(self.varc[:], self.s2c[:], 1.0 / D, None, ALU.mult)
                    v.drain()
                    v.tensor_tensor(self.tnc[:], self.muc[:], self.muc[:], ALU.mult)
                    v.drain()
                    v.tensor_tensor(self.varc[:], self.varc[:], self.tnc[:], ALU.subtract)
                    v.drain()
                    v.nop().then_inc(h["vE"], 1)
                self.emit_rstd()
                self._a2_prev_ve_x1 = ve_x1
                ve_ln3 = self.bump("vE")

                @self.on("v")
                def _(v, _yt=ytile):
                    for tt in range(4):
                        v.tensor_scalar(ln3[:, tt * 1024:(tt + 1) * 1024], _yt(tt),
                                        self.muc[:, tt:tt + 1], self.rstdc[:, tt:tt + 1],
                                        ALU.subtract, ALU.mult)
                    v.drain()
                    v.nop().then_inc(h["vE"], 1)

                self.bump("qS", 16)
                if STAGE == 1:
                    self.bump("qS", 16)
                if self.debug and "x1" in self.dbg:
                    self.bump("qS", 16)

                @self.on("sy")
                def _(sy, _c=c, _cc=cc, _v=ve_ln3, _pr2=pr % 2):
                    sy.wait_ge(h["vE"], _v)
                    sy.dma_start(out=self.x1_scr.rearrange("(ch tt q) d -> ch q tt d", q=128, tt=4)[_c],
                                 in_=ytp[:, _pr2 * 8192 + _cc * 4096: _pr2 * 8192 + (_cc + 1) * 4096]
                                 .rearrange("q (tt d) -> q tt d", d=D)).then_inc(h["qS"], 16)
                    if STAGE == 1:
                        sy.dma_start(out=self.out[p].rearrange("(ch tt q) d -> ch q tt d", q=128, tt=4)[_c],
                                     in_=ytp[:, _pr2 * 8192 + _cc * 4096: _pr2 * 8192 + (_cc + 1) * 4096]
                                     .rearrange("q (tt d) -> q tt d", d=D)).then_inc(h["qS"], 16)
                    if self.debug and "x1" in self.dbg:
                        sy.dma_start(out=self.dbg["x1"][p].rearrange("(ch tt q) d -> ch q tt d", q=128, tt=4)[_c],
                                     in_=ytp[:, _pr2 * 8192 + _cc * 4096: _pr2 * 8192 + (_cc + 1) * 4096]
                                     .rearrange("q (tt d) -> q tt d", d=D)).then_inc(h["qS"], 16)

                if self.debug and "ln3" in self.dbg:
                    self.bump("qS", 16)

                    @self.on("sy")
                    def _(sy, _c=c, _v=ve_ln3):
                        sy.wait_ge(h["vE"], _v)
                        sy.dma_start(out=self.dbg["ln3"][p].rearrange("(ch tt q) d -> ch q tt d", q=128, tt=4)[_c],
                                     in_=ln3[:].rearrange("q (tt d) -> q tt d", d=D)).then_inc(h["qS"], 16)

                se_l3 = []
                te_sc = []
                for dt in range(NDT):
                    te_l3 = self.bump("tE")
                    se_l3.append(self.bump("sE"))
                    te_sc.append(self.bump("tE"))

                    @self.on("t")
                    def _(t, _dt=dt, _v=ve_ln3, _se=se_l3[dt],
                          _seprev=(se_l3[dt - 2] if dt >= 2 else None)):
                        if _dt == 0:
                            t.wait_ge(h["vE"], _v)
                        if _seprev is not None:
                            t.wait_ge(h["sE"], _seprev)
                        mmi = None
                        for tt in range(4):
                            mmi = t.transpose(P[2 + _dt % 2][0:128, tt * 128:(tt + 1) * 128],
                                              ln3[:, tt * 1024 + _dt * 128: tt * 1024 + (_dt + 1) * 128],
                                              self.ident[:])
                        mmi.then_inc(h["tE"], 1)
                        t.wait_ge(h["sE"], _se)
                        t.matmul(P[6][0:16, 0:512], self.gate_wT[:, _dt * 16:(_dt + 1) * 16],
                                 l3f[:, (_dt % 2) * 512:(_dt % 2) * 512 + 512],
                                 start=(_dt == 0), stop=(_dt == NDT - 1),
                                 skip_group_check=True).then_inc(h["tE"], 1)

                    @self.on("s")
                    def _(s, _c=c, _dt=dt, _tl=te_l3,
                          _tscprev=(te_sc[dt - 2] if dt >= 2 else None)):
                        s.wait_ge(h["tE"], _tl)
                        if _tscprev is not None:
                            s.wait_ge(h["tE"], _tscprev)
                        s.copy(self.l3x_bf[:, _dt * L + _c * 512: _dt * L + (_c + 1) * 512],
                               P[2 + _dt % 2][:, 0:512])
                        s.copy(l3f[:, (_dt % 2) * 512:(_dt % 2) * 512 + 512],
                               P[2 + _dt % 2][:, 0:512])
                        s.drain()
                        s.nop().then_inc(h["sE"], 1)

                self.emit_routing(c, te_sc[NDT - 1], p)

            self.fence()

    def emit_routing(self, c, te_scores, p=0):
        h, P, rt = self.h, self.P, self.rt
        ve_sc = self.bump("vE")

        @self.on("v")
        def _(v, _t=te_scores):
            v.wait_ge(h["tE"], _t)
            v.tensor_copy(self.sc_sb[0:16, 0:512], P[6][0:16, 0:512])
            v.drain()
            v.nop().then_inc(h["vE"], 1)

        te_t1 = self.bump("tE")

        @self.on("t")
        def _(t, _v=ve_sc):
            t.wait_ge(h["vE"], _v)
            mmi = None
            for tt in range(4):
                mmi = t.transpose(P[7][0:128, tt * 16:(tt + 1) * 16],
                                  self.sc_sb[0:16, tt * 128:(tt + 1) * 128],
                                  self.ident[0:16, 0:16])
            mmi.then_inc(h["tE"], 1)

        ve_rt = self.bump("vE")

        @self.on("v")
        def _(v, _t=te_t1):
            v.wait_ge(h["tE"], _t)
            sc = rt["sd"]
            v.tensor_copy(sc[:], P[7][:, 0:64])
            v.drain()
            v3 = lambda a: a[:].rearrange("q (tt e) -> q tt e", e=16)
            g3 = lambda a: a[:].rearrange("q (tg j) -> q tg j", j=4)
            bc = lambda a, n=16: a[:].broadcast_to([128, a.shape[1] if hasattr(a, 'shape') else 4, n])
            v.tensor_reduce(rt["m"][:], v3(sc), AX.X, ALU.max)
            v.drain()
            v.tensor_tensor(v3(rt["ex"]), v3(sc), rt["m"][:].broadcast_to([128, 4, 16]), ALU.subtract)
            v.drain()
            v.nop().then_inc(h["vE"], 1)
        sE_exp = self.bump("sE")

        @self.on("s")
        def _(s, _v=ve_rt):
            s.wait_ge(h["vE"], _v)
            s.activation(rt["ex"][:], rt["ex"][:], AF.Exp)
            s.drain()
            s.nop().then_inc(h["sE"], 1)

        ve_comb = self.bump("vE")

        @self.on("v")
        def _(v, _s=sE_exp):
            v3 = lambda a: a[:].rearrange("q (tt e) -> q tt e", e=16)
            g3 = lambda a: a[:].rearrange("q (tg j) -> q tg j", j=4)
            v.wait_ge(h["sE"], _s)
            v.tensor_reduce(rt["sum"][:], v3(rt["ex"]), AX.X, ALU.add)
            v.drain()
            v.reciprocal(rt["rec"][:], rt["sum"][:])
            v.drain()
            v.tensor_tensor(v3(rt["orig"]), v3(rt["ex"]),
                            rt["rec"][:].broadcast_to([128, 4, 16]), ALU.mult)
            v.drain()
            gbv = self.gb_bcast[:, 0:16].broadcast_to([128, 16, 4]).rearrange("q e tt -> q tt e")
            v.tensor_tensor(v3(rt["sc2"]), v3(rt["orig"]), gbv, ALU.add)
            v.drain()
            v.tensor_reduce(rt["m1"][:], g3(rt["sc2"]), AX.X, ALU.max)
            v.drain()
            v.tensor_tensor(g3(rt["eqa"]), g3(rt["sc2"]),
                            rt["m1"][:].broadcast_to([128, 16, 4]), ALU.is_equal)
            v.drain()
            v.scalar_tensor_tensor(rt["sgm"][:], rt["eqa"][:], NEG, rt["sc2"][:],
                                   ALU.mult, ALU.add)
            v.drain()
            v.tensor_reduce(rt["m2"][:], g3(rt["sgm"]), AX.X, ALU.max)
            v.drain()
            v.tensor_tensor(rt["gsc"][:], rt["m1"][:], rt["m2"][:], ALU.add)
            v.drain()
            q3 = lambda a: a[:].rearrange("q (tt gg) -> q tt gg", gg=4)
            v.tensor_reduce(rt["gm1"][:], q3(rt["gsc"]), AX.X, ALU.max)
            v.drain()
            v.tensor_tensor(q3(rt["eqg"]), q3(rt["gsc"]),
                            rt["gm1"][:].broadcast_to([128, 4, 4]), ALU.is_equal)
            v.drain()
            v.scalar_tensor_tensor(rt["gsc2"][:], rt["eqg"][:], NEG, rt["gsc"][:],
                                   ALU.mult, ALU.add)
            v.drain()
            v.tensor_reduce(rt["gm2"][:], q3(rt["gsc2"]), AX.X, ALU.max)
            v.drain()
            v.tensor_tensor(q3(rt["keep"]), q3(rt["gsc"]),
                            rt["gm2"][:].broadcast_to([128, 4, 4]), ALU.is_ge)
            v.drain()
            v.tensor_tensor(g3(rt["smt1"]), g3(rt["sc2"]),
                            rt["keep"][:].broadcast_to([128, 16, 4]), ALU.mult)
            v.drain()
            v.tensor_scalar(g3(rt["smt2"]), rt["keep"][:].broadcast_to([128, 16, 4]),
                            1.0, -NEG, ALU.subtract, ALU.mult)
            v.drain()
            v.tensor_tensor(rt["sm"][:], rt["smt1"][:], rt["smt2"][:], ALU.add)
            v.drain()
            v.tensor_reduce(rt["t1"][:], v3(rt["sm"]), AX.X, ALU.max)
            v.drain()
            v.tensor_tensor(v3(rt["eq1"]), v3(rt["sm"]),
                            rt["t1"][:].broadcast_to([128, 4, 16]), ALU.is_equal)
            v.drain()
            v.scalar_tensor_tensor(rt["sm2"][:], rt["eq1"][:], NEG, rt["sm"][:],
                                   ALU.mult, ALU.add)
            v.drain()
            v.tensor_reduce(rt["t2"][:], v3(rt["sm2"]), AX.X, ALU.max)
            v.drain()
            v.tensor_tensor(v3(rt["eq2"]), v3(rt["sm2"]),
                            rt["t2"][:].broadcast_to([128, 4, 16]), ALU.is_equal)
            v.drain()
            v.tensor_tensor(rt["comb"][:], rt["eq1"][:], rt["eq2"][:], ALU.add)
            v.drain()
            v.tensor_tensor(rt["comb"][:], rt["comb"][:], rt["orig"][:], ALU.mult)
            v.drain()
            v.nop().then_inc(h["vE"], 1)

        if self.debug and "rt" in self.dbg and p == 0 and c == 0:
            keys = ["sd", "ex", "orig", "sc2", "m1", "sgm", "m2", "gsc", "gm1",
                    "gm2", "keep", "smt1", "smt2", "sm", "t1", "eq1",
                    "sm2", "t2", "eq2", "comb"]
            self.bump("qS", 16 * 20)

            @self.on("sy")
            def _(sy, _v=ve_comb, _keys=tuple(keys)):
                sy.wait_ge(h["vE"], _v)
                for i, kk in enumerate(_keys):
                    ap = rt[kk][:]
                    w = ap.shape[1] if hasattr(ap, "shape") else 64
                    sy.dma_start(out=self.dbg["rt"][i, :, 0:rt[kk].shape[1]],
                                 in_=ap).then_inc(h["qS"], 16)

        te_t2 = self.bump("tE")

        @self.on("t")
        def _(t, _v=ve_comb):
            t.wait_ge(h["vE"], _v)
            mmi = None
            for tt in range(4):
                mmi = t.transpose(P[7][0:16, tt * 128:(tt + 1) * 128],
                                  rt["comb"][:, tt * 16:(tt + 1) * 16], self.ident[:])
            mmi.then_inc(h["tE"], 1)

        ve_ct = self.bump("vE")

        @self.on("v")
        def _(v, _t=te_t2, _c=c):
            v.wait_ge(h["tE"], _t)
            v.tensor_copy(self.combine_T[0:16, _c * 512:(_c + 1) * 512], P[7][0:16, 0:512])
            v.drain()
            v.nop().then_inc(h["vE"], 1)

    def phaseB(self, p, T):
        h, P = self.h, self.P
        we, hbf, h1s, h3s, cb = T["we"], T["hbf"], T["h1s"], T["h3s"], T["cb"]
        WSLOT = 12288
        w1t = lambda sl, kdt, ib: we[:, sl * WSLOT + kdt * 512 + ib * 128: sl * WSLOT + kdt * 512 + (ib + 1) * 128]
        w3t = lambda sl, kdt, ib: we[:, sl * WSLOT + 4096 + kdt * 512 + ib * 128: sl * WSLOT + 4096 + kdt * 512 + (ib + 1) * 128]
        w2t = lambda sl, ib, odt: we[:, sl * WSLOT + 8192 + ib * 1024 + odt * 128: sl * WSLOT + 8192 + ib * 1024 + (odt + 1) * 128]

        qs_we = [None] * E
        prev_yo_te = {}
        prev_ve_yo_last = None

        for e in range(E):
            sl = e % 2
            qs_we[e] = self.bump("qS", 48)

            @self.on("sy")
            def _(sy, _e=e, _sl=sl):
                if _e >= 2:
                    sy.wait_ge(h["tE"], prev_yo_te[_e - 2])
                sy.dma_start(out=we[:, _sl * WSLOT: _sl * WSLOT + 4096]
                             .rearrange("q (kdt i) -> q kdt i", i=512),
                             in_=self.wb["We1"][_e].rearrange("(kdt q) i -> q kdt i", q=128)
                             ).then_inc(h["qS"], 16)
                sy.dma_start(out=we[:, _sl * WSLOT + 4096: _sl * WSLOT + 8192]
                             .rearrange("q (kdt i) -> q kdt i", i=512),
                             in_=self.wb["We3"][_e].rearrange("(kdt q) i -> q kdt i", q=128)
                             ).then_inc(h["qS"], 16)
                sy.dma_start(out=we[:, _sl * WSLOT + 8192: _sl * WSLOT + 12288]
                             .rearrange("q (ib d1) -> q ib d1", d1=1024),
                             in_=self.wb["We2"][_e].rearrange("(ib q) d1 -> q ib d1", q=128)
                             ).then_inc(h["qS"], 16)

            te_cb = [self.bump("tE") for _ in range(4)]
            se_cb = [self.bump("sE") for _ in range(4)]

            @self.on("t")
            def _(t, _e=e, _te=tuple(te_cb), _se=tuple(se_cb), _vy=prev_ve_yo_last):
                if _vy is not None:
                    t.wait_ge(h["vE"], _vy)
                for ch in range(4):
                    if ch >= 2:
                        t.wait_ge(h["sE"], _se[ch - 2])
                    t.matmul(P[6 + ch % 2][:, 0:512], self.sel_sb[:, _e * 128:(_e + 1) * 128],
                             self.combine_T[0:16, ch * 512:(ch + 1) * 512],
                             start=True, stop=True).then_inc(h["tE"], 1)

            @self.on("s")
            def _(s, _te=tuple(te_cb)):
                for ch in range(4):
                    s.wait_ge(h["tE"], _te[ch])
                    s.copy(cb[:, ch * 512:(ch + 1) * 512], P[6 + ch % 2][:, 0:512])
                    s.drain()
                    s.nop().then_inc(h["sE"], 1)

            if e == 0:
                prev_se_h1 = None
                prev_ve_h3s = None
                prev_ve_hbf = None
            ve_hbf = [None] * 4
            for ib in range(4):
                te_h1 = self.bump("tE")
                se_h1 = self.bump("sE")
                te_h3 = self.bump("tE")
                ve_h3s = self.bump("vE")
                ve_hbf[ib] = self.bump("vE")

                @self.on("t")
                def _(t, _e=e, _sl=sl, _ib=ib, _q=qs_we[e], _secb=se_cb[3],
                      _sh1=prev_se_h1, _vh3=prev_ve_h3s,
                      _vyo=(prev_ve_yo_last if (e >= 1 and ib == 0) else None)):
                    t.wait_ge(h["qS"], _q)
                    if _ib == 0:
                        t.wait_ge(h["sE"], _secb)
                    if _vyo is not None:
                        t.wait_ge(h["vE"], _vyo)
                    if _sh1 is not None:
                        t.wait_ge(h["sE"], _sh1)
                    mmi = None
                    for kdt in range(NDT):
                        for ch in range(4):
                            mmi = t.matmul(P[ch][:, 0:512], w1t(_sl, kdt, _ib),
                                           self.l3x_bf[:, kdt * L + ch * 512: kdt * L + (ch + 1) * 512],
                                           start=(kdt == 0), stop=(kdt == NDT - 1),
                                           skip_group_check=True)
                    mmi.then_inc(h["tE"], 1)
                    if _vh3 is not None:
                        t.wait_ge(h["vE"], _vh3)
                    mmi = None
                    for kdt in range(NDT):
                        for ch in range(4):
                            mmi = t.matmul(P[4 + ch][:, 0:512], w3t(_sl, kdt, _ib),
                                           self.l3x_bf[:, kdt * L + ch * 512: kdt * L + (ch + 1) * 512],
                                           start=(kdt == 0), stop=(kdt == NDT - 1),
                                           skip_group_check=True)
                    mmi.then_inc(h["tE"], 1)

                @self.on("s")
                def _(s, _e=e, _ib=ib, _t=te_h1, _vb=prev_ve_hbf):
                    s.wait_ge(h["tE"], _t)
                    if _vb is not None:
                        s.wait_ge(h["vE"], _vb)
                    s.activation(h1s[:], self.PA[:], AF.Silu,
                                 bias=self.be1c[:, _ib * 16 + _e: _ib * 16 + _e + 1],
                                 scale=1.0)
                    s.drain()
                    s.nop().then_inc(h["sE"], 1)

                @self.on("v")
                def _(v, _e=e, _ib=ib, _t=te_h3, _s1=se_h1, _secb=se_cb[3]):
                    v.wait_ge(h["tE"], _t)
                    if _ib == 0:
                        v.wait_ge(h["sE"], _secb)
                    v.scalar_tensor_tensor(h3s[:], self.PB[:],
                                           self.be3c[:, _ib * 16 + _e: _ib * 16 + _e + 1],
                                           cb[:], ALU.add, ALU.mult)
                    v.drain()
                    v.nop().then_inc(h["vE"], 1)
                    v.wait_ge(h["sE"], _s1)
                    v.tensor_tensor(hbf[:, _ib * L: (_ib + 1) * L], h3s[:], h1s[:], ALU.mult)
                    v.drain()
                    v.nop().then_inc(h["vE"], 1)
                prev_se_h1 = se_h1
                prev_ve_h3s = ve_h3s
                prev_ve_hbf = ve_hbf[ib]

            te_yo = [None] * NDT
            ve_yo = [None] * NDT
            for odt in range(NDT):
                te_yo[odt] = self.bump("tE")
                ve_yo[odt] = self.bump("vE")

            @self.on("t")
            def _(t, _e=e, _sl=sl, _ph=ve_hbf[3], _vy=tuple(ve_yo)):
                t.wait_ge(h["vE"], _ph)
                for odt in range(NDT):
                    if odt >= 2:
                        t.wait_ge(h["vE"], _vy[odt - 2])
                    pb = (odt % 2) * 4
                    mmi = None
                    for ib in range(4):
                        for ch in range(4):
                            mmi = t.matmul(P[pb + ch][:, 0:512], w2t(_sl, ib, odt),
                                           hbf[:, ib * L + ch * 512: ib * L + (ch + 1) * 512],
                                           start=(ib == 0), stop=(ib == 3),
                                           skip_group_check=True)
                    mmi.then_inc(h["tE"], 1)
            prev_yo_te[e] = te_yo[NDT - 1]

            @self.on("v")
            def _(v, _e=e, _ty=tuple(te_yo)):
                for odt in range(NDT):
                    v.wait_ge(h["tE"], _ty[odt])
                    pw = self.PA if odt % 2 == 0 else self.PB
                    dst = self.y_acc[:, odt * L: (odt + 1) * L]
                    if _e == 0:
                        v.tensor_copy(dst, pw[:])
                    else:
                        v.tensor_tensor(dst, dst, pw[:], ALU.add)
                    v.drain()
                    v.nop().then_inc(h["vE"], 1)
            prev_ve_yo_last = ve_yo[NDT - 1]
        self.fence()

    def phaseZ(self, p, T):
        h, P = self.h, self.P
        zw = T["zw"]
        z1s, z3s, zzw = T["z1s"], T["z3s"], T["zzw"]
        zw1 = lambda kdt, ob: zw[:, kdt * 1024 + ob * 128: kdt * 1024 + (ob + 1) * 128]
        zw3 = lambda kdt, ob: zw[:, 8192 + kdt * 1024 + ob * 128: 8192 + kdt * 1024 + (ob + 1) * 128]

        qs_zw = self.bump("qS", 32)

        @self.on("sy")
        def _(sy):
            for k, srcw in [(0, self.wb["ws1"]), (8192, self.wb["ws3"])]:
                sy.dma_start(out=zw[:, k: k + 8192].rearrange("q (kdt o) -> q kdt o", o=1024),
                             in_=srcw.rearrange("(kdt q) o -> q kdt o", q=128)
                             ).then_inc(h["qS"], 16)

        pe_zz = [None] * NDT
        te_z1_hist = []
        qg_zz = [None] * NDT
        for ob in range(NDT):
            te_z1 = self.bump("tE")
            te_z1_hist.append(te_z1)
            se_z1 = self.bump("sE")
            te_z3 = self.bump("tE")
            se_z3 = self.bump("sE")
            pe_zz[ob] = self.bump("pE")
            qg_zz[ob] = self.bump("qG", 16)

            @self.on("t")
            def _(t, _ob=ob, _q=qs_zw, _prev=pe_zz[ob - 1] if ob >= 1 else None):
                t.wait_ge(h["qS"], _q)
                if _prev is not None:
                    t.wait_ge(h["pE"], _prev)
                mmi = None
                for kdt in range(NDT):
                    for ch in range(4):
                        mmi = t.matmul(P[ch][:, 0:512], zw1(kdt, _ob),
                                       self.l3x_bf[:, kdt * L + ch * 512: kdt * L + (ch + 1) * 512],
                                       start=(kdt == 0), stop=(kdt == NDT - 1),
                                       skip_group_check=True)
                mmi.then_inc(h["tE"], 1)

            @self.on("s")
            def _(s, _ob=ob, _t=te_z1, _pz=(pe_zz[ob - 1] if ob >= 1 else None)):
                s.wait_ge(h["tE"], _t)
                if _pz is not None:
                    s.wait_ge(h["pE"], _pz)
                s.activation(z1s[:], self.PA[:], AF.Silu, bias=self.col("bs1", _ob), scale=1.0)
                s.drain()
                s.nop().then_inc(h["sE"], 1)

            @self.on("t")
            def _(t, _ob=ob):
                mmi = None
                for kdt in range(NDT):
                    for ch in range(4):
                        mmi = t.matmul(P[4 + ch][:, 0:512], zw3(kdt, _ob),
                                       self.l3x_bf[:, kdt * L + ch * 512: kdt * L + (ch + 1) * 512],
                                       start=(kdt == 0), stop=(kdt == NDT - 1),
                                       skip_group_check=True)
                mmi.then_inc(h["tE"], 1)

            @self.on("s")
            def _(s, _ob=ob, _t=te_z3, _pz=(pe_zz[ob - 1] if ob >= 1 else None)):
                s.wait_ge(h["tE"], _t)
                if _pz is not None:
                    s.wait_ge(h["pE"], _pz)
                s.activation(z3s[:], self.PB[:], AF.Identity, bias=self.col("bs3", _ob), scale=1.0)
                s.drain()
                s.nop().then_inc(h["sE"], 1)


            @self.on("g")
            def _(g, _ob=ob, _s3=se_z3, _qz=(qg_zz[ob - 2] if ob >= 2 else None)):
                g.wait_ge(h["sE"], _s3)
                if _qz is not None:
                    g.wait_ge(h["qG"], _qz)
                sl = (_ob % 2) * L
                g.tensor_tensor(zzw[:, sl: sl + L], z3s[:, 0:L], z1s[:, 0:L], ALU.mult)
                g.drain()
                g.nop().then_inc(h["pE"], 1)
                g.dma_start(out=self.zz_dram[_ob], in_=zzw[:, sl: sl + L]).then_inc(h["qG"], 16)
        self._z_state = (te_z1_hist, qg_zz)

    def phaseZ2(self, p, T):
        h, P = self.h, self.P
        zw, zzst = T["zw"], T["zzst"]
        y2f, y2t, x1r = T["y2f"], T["y2t"], T["x1r"]
        zw2 = lambda ob, odt: zw[:, ob * 1024 + odt * 128: ob * 1024 + (odt + 1) * 128]
        te_z1_hist, qg_zz = self._z_state

        qs_zw2 = self.bump("qS", 32)

        @self.on("sy")
        def _(sy, _t=te_z1_hist[NDT - 1]):
            sy.dma_start(out=self.be2_sb[:], in_=self.w["be2"]).then_inc(h["qS"], 16)
            sy.wait_ge(h["tE"], _t)
            sy.dma_start(out=zw[:, 0:8192].rearrange("q (kdt o) -> q kdt o", o=1024),
                         in_=self.wb["ws2"].rearrange("(kdt q) o -> q kdt o", q=128)
                         ).then_inc(h["qS"], 16)

        prev_ve_fin = None
        prev_qs_out = None
        te_z2_hist = []
        for ch in range(4):
            qs_zzst = self.bump("qS", 16)
            qs_x1 = self.bump("qS", 16)

            @self.on("sy")
            def _(sy, _ch=ch, _vf=prev_ve_fin, _qz=qg_zz[NDT - 1],
                  _tz=(te_z2_hist[ch - 1] if ch >= 1 else None)):
                sy.wait_ge(h["qG"], _qz)
                if _tz is not None:
                    sy.wait_ge(h["tE"], _tz)
                sy.dma_start(out=zzst[:, 0:4096]
                             .rearrange("q (ob o) -> q ob o", o=512),
                             in_=self.zz_dram.rearrange("ob q t -> q ob t")[:, :, _ch * 512:(_ch + 1) * 512]
                             ).then_inc(h["qS"], 16)
                if _vf is not None:
                    sy.wait_ge(h["vE"], _vf)
                sy.dma_start(out=x1r[:].rearrange("q (tt d) -> q tt d", d=D),
                             in_=self.x1_scr.rearrange("(cc tt q) d -> cc q tt d", q=128, tt=4)[_ch]
                             ).then_inc(h["qS"], 16)

            se_y2 = []
            ve_z2 = []
            for odt in range(NDT):
                te_z2 = self.bump("tE")
                ve_z2.append(self.bump("vE"))
                te_y2 = self.bump("tE")
                se_y2.append(self.bump("sE"))

                @self.on("t")
                def _(t, _ch=ch, _odt=odt, _vz=ve_z2[odt], _qw=qs_zw2, _qz=qs_zzst,
                      _vzprev=(ve_z2[odt - 2] if odt >= 2 else None),
                      _seprev=(se_y2[odt - 1] if odt >= 1 else None)):
                    if _odt == 0:
                        t.wait_ge(h["qS"], max(_qw, _qz))
                    if _vzprev is not None:
                        t.wait_ge(h["vE"], _vzprev)
                    mmi = None
                    for ob in range(NDT):
                        mmi = t.matmul(P[_odt % 2][:, 0:512], zw2(ob, _odt),
                                       zzst[:, ob * 512:(ob + 1) * 512],
                                       start=(ob == 0), stop=False, skip_group_check=True)
                    mmi = t.matmul(P[_odt % 2][:, 0:512],
                                   self.be2_sb[0:16, _odt * 128:(_odt + 1) * 128],
                                   self.combine_T[0:16, _ch * 512:(_ch + 1) * 512],
                                   start=False, stop=True, skip_group_check=True)
                    mmi.then_inc(h["tE"], 1)
                    t.wait_ge(h["vE"], _vz)
                    if _seprev is not None:
                        t.wait_ge(h["sE"], _seprev)
                    mmi = None
                    for tt in range(4):
                        mmi = t.transpose(P[2 + _odt % 2][0:128, tt * 128:(tt + 1) * 128],
                                          y2f[:, (_odt % 2) * 512 + tt * 128:(_odt % 2) * 512 + (tt + 1) * 128],
                                          self.ident[:])
                    mmi.then_inc(h["tE"], 1)

                @self.on("v")
                def _(v, _ch=ch, _odt=odt, _tz=te_z2):
                    v.wait_ge(h["tE"], _tz)
                    v.scalar_tensor_tensor(y2f[:, (_odt % 2) * 512:(_odt % 2) * 512 + 512],
                                           P[_odt % 2][:, 0:512], self.col("bs2", _odt),
                                           self.y_acc[:, _odt * L + _ch * 512: _odt * L + (_ch + 1) * 512],
                                           ALU.add, ALU.add)
                    v.drain()
                    v.nop().then_inc(h["vE"], 1)

                @self.on("s")
                def _(s, _odt=odt, _ty=te_y2, _qo=(prev_qs_out if odt == 0 else None)):
                    s.wait_ge(h["tE"], _ty)
                    if _qo is not None:
                        s.wait_ge(h["qS"], _qo)
                    s.copy(y2t[:].rearrange("q (tt d) -> q tt d", d=D)[:, :, _odt * 128:(_odt + 1) * 128],
                           P[2 + _odt % 2][:, 0:512].rearrange("q (tt d1) -> q tt d1", d1=128))
                    s.drain()
                    s.nop().then_inc(h["sE"], 1)
            te_z2_hist.append(self.c["tE"])

            @self.on("v")
            def _(v, _s=se_y2[NDT - 1]):
                v.wait_ge(h["sE"], _s)
            y2tile = lambda tt: y2t[:, tt * 1024:(tt + 1) * 1024]
            self.emit_stats([y2tile(tt) for tt in range(4)])
            self.emit_rstd()
            ve_fin = self.bump("vE")

            @self.on("v")
            def _(v, _q=qs_x1, _yt=y2tile):
                v.wait_ge(h["qS"], _q)
                for tt in range(4):
                    v.tensor_scalar(self.sq[:], _yt(tt), self.muc[:, tt:tt + 1],
                                    self.rstdc[:, tt:tt + 1], ALU.subtract, ALU.mult)
                    v.drain()
                    v.tensor_tensor(_yt(tt), self.sq[:],
                                    x1r[:, tt * 1024:(tt + 1) * 1024], ALU.add)
                    v.drain()
                v.nop().then_inc(h["vE"], 1)

            qs_out = self.bump("qS", 16)

            @self.on("sy")
            def _(sy, _ch=ch, _v=ve_fin, _p=p):
                sy.wait_ge(h["vE"], _v)
                sy.dma_start(out=self.out[_p].rearrange("(cc tt q) d -> cc q tt d", q=128, tt=4)[_ch],
                             in_=y2t[:].rearrange("q (tt d) -> q tt d", d=D)).then_inc(h["qS"], 16)
            prev_ve_fin = ve_fin
            prev_qs_out = qs_out

        if self.debug and "comb" in self.dbg:
            self.bump("qS", 16)

            @self.on("sy")
            def _(sy, _p=p):
                sy.dma_start(out=self.dbg["comb"][_p], in_=self.combine_T[:]).then_inc(h["qS"], 16)

        self.fence()

    def build(self):
        nc = self.nc
        self.declare()
        top = ExitStack()
        SB = lambda n, s, dt=F32: top.enter_context(nc.sbuf_tensor(n, s, dt))
        PSm = lambda n, s, dt=F32: top.enter_context(nc.psum_tensor(n, s, dt))
        with top:
            self.persistent(SB, PSm)
            self.sc_sb = SB("sc_sb", [16, 512])
            self.startup()
            for p in range(NB):
                sfx = f"_p{p}"
                ps = ExitStack()
                S2 = lambda n, s, dt=F32: ps.enter_context(nc.sbuf_tensor(n + sfx, s, dt))
                with ps:
                    self.l3x_bf = S2("l3x_bf", [128, NDT * L], BF16)
                    a1 = ExitStack()
                    S3 = lambda n, s, dt=F32: a1.enter_context(nc.sbuf_tensor(n + sfx, s, dt))
                    with a1, nc.Block() as blk:
                        TA = {
                            "xsl": S3("xsl", [128, 2 * 4096]),
                            "ln1": S3("ln1", [128, 4096]),
                            "l1f": S3("l1f", [128, NDT * 512]),
                            "l1b": S3("l1b", [128, NDT * 512], BF16),
                            "wgT": S3("wgT", [128, NDT * D], BF16),
                            "gt": S3("gt", [128, 2 * 512]),
                            "xgf": S3("xgf", [128, 2 * 512]),
                        }
                        self.sq = S3("sq1", [128, 1024])
                        if p == 0:
                            TA["tap"] = (S3("A_f", [128, NDT * 256]),
                                         S3("Bp_f", [128, NDT * NST]),
                                         S3("Cp_f", [128, NDT * NST]),
                                         S3("u1", [128, 2304]),
                                         S3("dg_st", [128, 2 * J * 128], BF16))
                        self.phaseA1(p, TA)
                        self.fence()
                        self.flush(blk)
                    a2 = ExitStack()
                    S4 = lambda n, s, dt=F32: a2.enter_context(nc.sbuf_tensor(n + sfx, s, dt))
                    with a2, nc.Block() as blk:
                        TB = {
                            "dg": S4("dg", [128, 2 * J * 128], BF16),
                            "xb": S4("xb", [128, 2 * (J + 1024)], BF16),
                            "xf": S4("xf", [128, 1024]),
                            "yF": S4("yF", [128, 2 * 1024]),
                            "ytp": S4("ytp", [128, 2 * 8 * 1024]),
                            "xre": S4("xre", [128, 4096]),
                            "ln3": S4("ln3", [128, 4096]),
                            "l3f": S4("l3f", [128, 2 * 512]),
                        }
                        self.sq = S4("sq2", [128, 1024])
                        self.rt = {nm: S4("rt_" + nm, [128, sz]) for nm, sz in self.RT_SIZES}
                        self.gamma_b = S4("gamma_b", [128, D])
                        self.beta_b = S4("beta_b", [128, D])
                        self.phaseA2(p, TB)
                        self.flush(blk)
                    if STAGE >= 3:
                        ya = ExitStack()
                        self.y_acc = ya.enter_context(nc.sbuf_tensor("y_acc" + sfx, [128, NDT * L], F32))
                        ps.enter_context(ya.pop_all() if False else ya)
                        b1 = ExitStack()
                        S5 = lambda n, s, dt=F32: b1.enter_context(nc.sbuf_tensor(n + sfx, s, dt))
                        with b1, nc.Block() as blk:
                            TC = {
                                "we": S5("we", [128, 2 * 12288], BF16),
                                "hbf": S5("hbf", [128, 4 * L], BF16),
                                "h1s": S5("h1s", [128, L]),
                                "h3s": S5("h3s", [128, L]),
                                "cb": S5("cb", [128, L]),
                            }
                            self.phaseB(p, TC)
                            self.flush(blk)
                        z0 = ExitStack()
                        S60 = lambda n, s, dt=F32: z0.enter_context(nc.sbuf_tensor(n + sfx, s, dt))
                        with z0:
                            zw = S60("zw", [128, 2 * 8192], BF16)
                            za = ExitStack()
                            S6 = lambda n, s, dt=F32: za.enter_context(nc.sbuf_tensor(n + sfx, s, dt))
                            with za, nc.Block() as blk:
                                TD = {
                                    "zw": zw,
                                    "z1s": S6("z1s", [128, L]),
                                    "z3s": S6("z3s", [128, L]),
                                    "zzw": S6("zzw", [128, 2 * L], BF16),
                                }
                                self.phaseZ(p, TD)
                                self.fence()
                                self.flush(blk)
                            zb = ExitStack()
                            S6 = lambda n, s, dt=F32: zb.enter_context(nc.sbuf_tensor(n + sfx, s, dt))
                            with zb, nc.Block() as blk:
                                TE2 = {
                                    "zw": zw,
                                    "zzst": S6("zzst", [128, 4096], BF16),
                                    "y2f": S6("y2f", [128, 2 * 512]),
                                    "y2t": S6("y2t", [128, 4096]),
                                    "x1r": S6("x1r", [128, 4096]),
                                }
                                self.sq = S6("sq3", [128, 1024])
                                self.be2_sb = S6("be2_sb", [E, D])
                                self.phaseZ2(p, TE2)
                                self.flush(blk)
            with nc.Block() as blk:
                @self.on("sy")
                def _(sy):
                    sy.wait_ge(self.h["qS"], self.c["qS"])
                    sy.wait_ge(self.h["qG"], self.c["qG"])
                self.flush(blk)
        return nc


def make_consts():
    sel = np.zeros((E, E * 128), np.float32)
    for e in range(E):
        sel[e, e * 128:(e + 1) * 128] = 1.0
    return {
        "c_ident": np.eye(128, dtype=np.float32),
        "c_sel": sel,
        "c_ones": np.ones((1, 128), np.float32),
    }


_CACHE = {}


def get_nc(debug=False):
    key = (STAGE, debug)
    if key not in _CACHE:
        b = B(debug=debug)
        nc = b.build()
        _CACHE[key] = (nc, b.dbg)
    return _CACHE[key]


def kernel(**inputs):
    nc, _ = get_nc(False)
    consts = make_consts()
    x = np.ascontiguousarray(inputs["L0_attn_x"])
    ren = {
        "A": "L0_attn_A", "Bp": "L0_attn_Bp", "Cp": "L0_attn_Cp", "Dp": "L0_attn_Dp",
        "Wg": "L0_attn_Wg", "bg": "L0_attn_bg", "gamma": "L0_attn_gamma",
        "beta": "L0_attn_beta", "gate_w": "L0_ffn_gate_w", "gate_b": "L0_ffn_gate_b",
        "We1": "L0_ffn_We1", "be1": "L0_ffn_be1", "We2": "L0_ffn_We2",
        "be2": "L0_ffn_be2", "We3": "L0_ffn_We3", "be3": "L0_ffn_be3",
        "ws1": "L0_ffn_ws1", "bs1": "L0_ffn_bs1", "ws2": "L0_ffn_ws2",
        "bs2": "L0_ffn_bs2", "ws3": "L0_ffn_ws3", "bs3": "L0_ffn_bs3",
    }
    base = {k: np.ascontiguousarray(inputs[v]) for k, v in ren.items()}
    base.update(consts)
    in_maps = []
    for c in range(NCORE):
        m = dict(base)
        m["x"] = np.ascontiguousarray(x[c * NB:(c + 1) * NB])
        in_maps.append(m)
    res = run_bass_kernel_spmd(nc, in_maps, core_ids=list(range(NCORE)))
    return np.concatenate([r["out"] for r in res.results], axis=0)


if __name__ == "__main__":
    b = B(debug=False)
    b.build()
    print("build ok, instructions:", b.nc.n_instructions())


# version 12
# speedup vs baseline: 1.0849x; 1.0190x over previous
import numpy as np
from contextlib import ExitStack

import concourse.bass as bass
import concourse.mybir as mybir
from concourse.bass_utils import run_bass_kernel_spmd

F32 = mybir.dt.float32
BF16 = mybir.dt.bfloat16
AF = mybir.ActivationFunctionType
ALU = mybir.AluOpType
AX = mybir.AxisListType

D, L, NB, NST, E, INTER = 1024, 2048, 2, 16, 16, 512
NCORE = 8
J = 32
NEG = -1e30
EPS = 1e-5
NDT, NCH = 8, 4
STAGE = 3


class B:
    def __init__(self, debug=False):
        self.debug = debug
        nc = self.nc = bass.Bass("TRN2", target_bir_lowering=False, debug=False)
        self.ops = {k: [] for k in ("sy", "g", "t", "v", "s")}
        self.h = {}
        self.c = {}
        for s in ("qS", "qG", "tE", "vE", "sE", "pE", "F"):
            self.h[s] = nc.alloc_semaphore(s)
            self.c[s] = 0

    def on(self, eng):
        def deco(fn):
            self.ops[eng].append(fn)
        return deco

    def bump(self, s, k=1):
        self.c[s] += k
        return self.c[s]

    def fence(self):
        tgt = self.c["F"] + 5
        hF = self.h["F"]
        qs_now = self.c["qS"]
        qg_now = self.c["qG"]
        for k in ("sy", "g", "t", "v", "s"):
            def go(e, _t=tgt, _k=k, _qs=qs_now, _qg=qg_now):
                if _k in ("v", "s", "g"):
                    e.drain()
                if _k == "sy":
                    e.wait_ge(self.h["qS"], _qs)
                if _k == "g":
                    e.wait_ge(self.h["qG"], _qg)
                e.nop().then_inc(hF, 1)
                e.wait_ge(hF, _t)
            self.ops[k].append(go)
        self.c["F"] = tgt

    def flush(self, block):
        ops = self.ops
        self.ops = {k: [] for k in ops}
        block.sync(lambda e, o=tuple(ops["sy"]): [f(e) for f in o])
        block.gpsimd(lambda e, o=tuple(ops["g"]): [f(e) for f in o])
        block.tensor(lambda e, o=tuple(ops["t"]): [f(e) for f in o])
        block.vector(lambda e, o=tuple(ops["v"]): [f(e) for f in o])
        block.scalar(lambda e, o=tuple(ops["s"]): [f(e) for f in o])

    def declare(self):
        nc = self.nc
        self.x_in = nc.dram_tensor("x", [NB, L, D], F32, kind="ExternalInput").ap()
        self.w = {}
        for name, shape in [
            ("A", [D, NST, NST]), ("Bp", [D, NST]), ("Cp", [D, NST]), ("Dp", [D]),
            ("Wg", [D, D]), ("bg", [D]), ("gamma", [D]), ("beta", [D]),
            ("gate_w", [E, D]), ("gate_b", [E]),
            ("We1", [E, D, INTER]), ("be1", [E, INTER]), ("We2", [E, INTER, D]),
            ("be2", [E, D]), ("We3", [E, D, INTER]), ("be3", [E, INTER]),
            ("ws1", [D, D]), ("bs1", [D]), ("ws2", [D, D]), ("bs2", [D]),
            ("ws3", [D, D]), ("bs3", [D]),
        ]:
            self.w[name] = nc.dram_tensor(name, shape, F32, kind="ExternalInput").ap()
        self.ident_in = nc.dram_tensor("c_ident", [128, 128], F32, kind="ExternalInput").ap()
        self.sel_in = nc.dram_tensor("c_sel", [E, E * 128], F32, kind="ExternalInput").ap()
        self.ones_in = nc.dram_tensor("c_ones", [1, 128], F32, kind="ExternalInput").ap()
        self.out = nc.dram_tensor("out", [NB, L, D], F32, kind="ExternalOutput").ap()
        self.dbg = {}
        if self.debug:
            for nm, shape in [("x1", [NB, L, D]), ("comb", [NB, E, L]),
                              ("ln3", [NB, L, D]), ("xg", [NB, NDT, 128, L]),
                              ("y", [NB, L, D]), ("ln1", [NB, L, D]),
                              ("gate", [NB, NDT, 128, L]), ("wgT", [D, D]),
                              ("diag", [NDT, 128, J * 128]), ("vcol", [128, 5 * NDT]),
                              ("hbuf", [128, NDT * J]), ("gammab", [128, D]),
                              ("rt", [20, 128, 64]), ("yacc", [NDT, 128, L]), ("ffn", [128, 4096])]:
                self.dbg[nm] = nc.dram_tensor("dbg_" + nm, shape, F32, kind="ExternalOutput").ap()
        dr = lambda n, s, dt=BF16: nc.dram_tensor(n, s, dt, kind="Internal").ap()
        self.wb = {
            "We1": dr("wb_we1", [E, D, INTER]), "We2": dr("wb_we2", [E, INTER, D]),
            "We3": dr("wb_we3", [E, D, INTER]), "ws1": dr("wb_ws1", [D, D]),
            "ws2": dr("wb_ws2", [D, D]), "ws3": dr("wb_ws3", [D, D]),
            "Wg": dr("wb_wg", [D, D]),
        }
        self.wgT_dram = dr("wgT_dram", [D, D])
        self.gb_dram = dr("gb_dram", [2, 128, D], F32)
        self.zz_dram = dr("zz_dram", [NDT, 128, L])
        self.diag_dram = dr("diag_dram", [NDT, 128, J * 128])
        self.xg_f = dr("xg_f", [NDT, 128, L], F32)
        self.xg_b = dr("xg_b", [NDT, 128, J + L])
        self.x1_scr = dr("x1_scr", [L, D], F32)
        self.yacc = dr("yacc", [NDT, 128, L], F32)

    def persistent(self, SB, PSm):
        self.ident = SB("ident", [128, 128])
        self.sel_sb = SB("sel_sb", [E, E * 128])
        self.ones_sb = SB("ones_sb", [1, 128])
        self.gate_wT = SB("gate_wT", [128, NDT * E])
        self.gb_bcast = SB("gb_bcast", [128, E])
        self.vcol = SB("vcol", [128, 5 * NDT])
        self.be1c = SB("be1c", [128, 4 * E])
        self.be3c = SB("be3c", [128, 4 * E])
        self.epscol = SB("epscol", [128, 1])
        self.combine_T = SB("combine_T", [E, L])
        self.hbuf = SB("hbuf", [128, NDT * J])
        self.zpad = SB("zpad", [128, J], BF16)
        self.s1c = SB("s1c", [128, 4]); self.s2c = SB("s2c", [128, 4])
        self.muc = SB("muc", [128, 4]); self.varc = SB("varc", [128, 4])
        self.rstdc = SB("rstdc", [128, 4]); self.tnc = SB("tnc", [128, 4])
        self.RT_SIZES = [
            ("m", 4), ("rec", 4), ("sum", 4), ("sd", 64), ("ex", 64), ("orig", 64),
            ("sc2", 64), ("m1", 16), ("eqa", 64), ("sgm", 64), ("m2", 16),
            ("gsc", 16), ("gm1", 4), ("eqg", 16), ("gsc2", 16), ("gm2", 4),
            ("keep", 16), ("smt1", 64), ("smt2", 64), ("sm", 64), ("t1", 4),
            ("eq1", 64), ("sm2", 64), ("t2", 4), ("eq2", 64), ("comb", 64)]
        self.PA = PSm("psA", [128, 2048])
        self.PB = PSm("psB", [128, 2048])
        self.P = [self.PA[:, i * 512:(i + 1) * 512] for i in range(4)] + \
                 [self.PB[:, i * 512:(i + 1) * 512] for i in range(4)]

    def col(self, which, dt):
        k = {"bg": 0, "Dp": 1, "bs1": 2, "bs2": 3, "bs3": 4}[which]
        return self.vcol[:, k * NDT + dt: k * NDT + dt + 1]

    def emit_rstd(self, n=4):
        h, c = self.h, self.c
        vE_stats = self.bump("vE")

        @self.on("v")
        def _(v):
            v.drain()
            v.nop().then_inc(h["vE"], 1)
        sE_rstd = self.bump("sE")

        @self.on("s")
        def _(s, _w=vE_stats):
            s.wait_ge(h["vE"], _w)
            s.activation(self.rstdc[:, 0:n], self.varc[:, 0:n], AF.Ln,
                         bias=self.epscol[:, 0:1], scale=1.0)
            s.drain()
            s.activation(self.rstdc[:, 0:n], self.rstdc[:, 0:n], AF.Exp, scale=-0.5)
            s.drain()
            s.nop().then_inc(h["sE"], 1)

        @self.on("v")
        def _(v, _w=sE_rstd):
            v.wait_ge(h["sE"], _w)
            rc, tn, vc = self.rstdc[:, 0:n], self.tnc[:, 0:n], self.varc[:, 0:n]
            v.tensor_tensor(tn, rc, rc, ALU.mult); v.drain()
            v.tensor_tensor(tn, tn, vc, ALU.mult); v.drain()
            v.tensor_scalar(tn, tn, -0.5, 1.5, ALU.mult, ALU.add); v.drain()
            v.tensor_tensor(rc, rc, tn, ALU.mult); v.drain()

    def emit_stats(self, tiles):
        @self.on("v")
        def _(v):
            for tt, tile in enumerate(tiles):
                v.tensor_reduce(self.s1c[:, tt:tt + 1], tile, AX.X, ALU.add)
                v.scalar_tensor_tensor(self.sq[:], tile, 1.0, tile, ALU.mult, ALU.mult,
                                       accum_out=self.s2c[:, tt:tt + 1])
            v.drain()
            v.tensor_scalar(self.muc[:], self.s1c[:], 1.0 / D, None, ALU.mult)
            v.tensor_scalar(self.varc[:], self.s2c[:], 1.0 / D, None, ALU.mult)
            v.drain()
            v.tensor_tensor(self.tnc[:], self.muc[:], self.muc[:], ALU.mult)
            v.drain()
            v.tensor_tensor(self.varc[:], self.varc[:], self.tnc[:], ALU.subtract)
            v.drain()

    def startup(self):
        nc, h, w = self.nc, self.h, self.w
        st = ExitStack()
        S = lambda n, s, dt=F32: st.enter_context(nc.sbuf_tensor(n, s, dt))
        vrow = S("vrow", [8, 5 * 128])
        grow = S("grow", [1, D])
        brow = S("brow", [1, D])
        gbrow = S("gbrow", [1, E])
        gwt_st = S("gwt_st", [16, D])
        be_st = S("be_st", [16, 2 * INTER])
        wg_st = S("wg_st", [128, 2 * D])
        wgT_st = S("wgT_st", [128, 2 * D], BF16)
        gbst = S("gbst", [128, 2 * D])
        P = self.P

        with st, nc.Block() as blk:
            @self.on("sy")
            def _(sy):
                for dst, src in [
                    (self.ident[:], self.ident_in), (self.ones_sb[:], self.ones_in),
                    (self.sel_sb[:], self.sel_in),
                    (grow[0:1, :], w["gamma"].rearrange("(a d) -> a d", a=1)),
                    (brow[0:1, :], w["beta"].rearrange("(a d) -> a d", a=1)),
                    (gbrow[:], w["gate_b"].rearrange("(a e) -> a e", a=1)),
                    (gwt_st[:], w["gate_w"]),
                    (vrow[0:8, 0:128], w["bg"].rearrange("(dt p) -> dt p", p=128)),
                    (vrow[0:8, 128:256], w["Dp"].rearrange("(dt p) -> dt p", p=128)),
                    (vrow[0:8, 256:384], w["bs1"].rearrange("(dt p) -> dt p", p=128)),
                    (vrow[0:8, 384:512], w["bs2"].rearrange("(dt p) -> dt p", p=128)),
                    (vrow[0:8, 512:640], w["bs3"].rearrange("(dt p) -> dt p", p=128)),
                    (be_st[:, 0:INTER], w["be1"]),
                    (be_st[:, INTER:], w["be3"]),
                ]:
                    sy.dma_start(out=dst, in_=src).then_inc(h["qS"], 16)
            qS_loads = self.bump("qS", 16 * 14)


            @self.on("v")
            def _(v):
                v.memset(self.epscol[:], EPS)
                v.memset(self.zpad[:], 0.0)
                v.drain()

            tE_sm = self.bump("tE")

            @self.on("t")
            def _(t, _w=qS_loads):
                t.wait_ge(h["qS"], _w)
                for k in range(5):
                    t.transpose(P[0][0:128, k * 8:(k + 1) * 8],
                                vrow[0:8, k * 128:(k + 1) * 128], self.ident[0:8, 0:8])
                t.matmul(P[1][:, 0:512], self.ones_sb[:, 0:128], grow[0:1, 0:512], start=True, stop=True)
                t.matmul(P[2][:, 0:512], self.ones_sb[:, 0:128], grow[0:1, 512:1024], start=True, stop=True)
                t.matmul(P[3][:, 0:512], self.ones_sb[:, 0:128], brow[0:1, 0:512], start=True, stop=True)
                t.matmul(P[4][:, 0:512], self.ones_sb[:, 0:128], brow[0:1, 512:1024], start=True, stop=True)
                t.matmul(P[5][:, 0:E], self.ones_sb[:, 0:128], gbrow[0:1, :], start=True, stop=True)
                for dt in range(NDT):
                    t.transpose(P[6][0:128, dt * 16:(dt + 1) * 16],
                                gwt_st[0:16, dt * 128:(dt + 1) * 128], self.ident[0:16, 0:16])
                mmi = None
                for ib in range(4):
                    t.transpose(P[7][0:128, ib * 16:(ib + 1) * 16],
                                be_st[0:16, ib * 128:(ib + 1) * 128], self.ident[0:16, 0:16])
                    mmi = t.transpose(P[7][0:128, 64 + ib * 16: 64 + (ib + 1) * 16],
                                      be_st[0:16, INTER + ib * 128: INTER + (ib + 1) * 128],
                                      self.ident[0:16, 0:16])
                mmi.then_inc(h["tE"], 1)

            sE_sm = self.bump("sE")
            self._sE_sm = sE_sm

            @self.on("s")
            def _(s, _w=tE_sm):
                s.wait_ge(h["tE"], _w)
                s.copy(self.vcol[:, 0:40], P[0][:, 0:40])
                s.copy(gbst[:, 0:512], P[1][:, 0:512])
                s.copy(gbst[:, 512:1024], P[2][:, 0:512])
                s.copy(gbst[:, 1024:1536], P[3][:, 0:512])
                s.copy(gbst[:, 1536:2048], P[4][:, 0:512])
                s.copy(self.gb_bcast[:], P[5][:, 0:E])
                s.copy(self.gate_wT[:], P[6][:, 0:NDT * E])
                s.copy(self.be1c[:], P[7][:, 0:64])
                s.copy(self.be3c[:], P[7][:, 64:128])
                s.drain()
                s.nop().then_inc(h["sE"], 1)

            qG_gb = self.bump("qG", 16)

            @self.on("g")
            def _(g, _w=sE_sm):
                g.wait_ge(h["sE"], _w)
                g.dma_start(out=self.gb_dram.rearrange("a q d2 -> q a d2"),
                            in_=gbst[:].rearrange("q (a d2) -> q a d2", d2=D)).then_inc(h["qG"], 16)

            qS_wg0 = self.c["qS"] + 16
            self.bump("qS", 16 * NDT)
            tE_wg0 = self.c["tE"] + 1
            self.bump("tE", NDT)
            sE_wg0 = self.c["sE"] + 1
            self.bump("sE", NDT)
            qG_wgT0 = self.c["qG"] + 16
            self.bump("qG", 16 * NDT)

            @self.on("sy")
            def _(sy):
                for odt in range(NDT):
                    if odt >= 2:
                        sy.wait_ge(h["tE"], tE_wg0 + odt - 2)
                    sy.dma_start(out=wg_st[:, (odt % 2) * D:(odt % 2) * D + D],
                                 in_=w["Wg"].rearrange("(odt q) d2 -> odt q d2", q=128)[odt]
                                 ).then_inc(h["qS"], 16)

            @self.on("t")
            def _(t, _s=sE_sm):
                t.wait_ge(h["sE"], _s)
                for odt in range(NDT):
                    t.wait_ge(h["qS"], qS_wg0 + odt * 16)
                    if odt >= 1:
                        t.wait_ge(h["sE"], sE_wg0 + odt - 1)
                    b0, b1 = 2 * (odt % 2), 2 * (odt % 2) + 1
                    mmi = None
                    for kdt in range(NDT):
                        pb = P[b0] if kdt < 4 else P[b1]
                        mmi = t.transpose(pb[0:128, (kdt % 4) * 128:(kdt % 4 + 1) * 128],
                                          wg_st[:, (odt % 2) * D + kdt * 128:(odt % 2) * D + (kdt + 1) * 128],
                                          self.ident[:])
                    mmi.then_inc(h["tE"], 1)

            @self.on("s")
            def _(s):
                for odt in range(NDT):
                    s.wait_ge(h["tE"], tE_wg0 + odt)
                    if odt >= 2:
                        s.wait_ge(h["qG"], qG_wgT0 + 16 * (odt - 2))
                    b0, b1 = 2 * (odt % 2), 2 * (odt % 2) + 1
                    s.copy(wgT_st[:, (odt % 2) * D:(odt % 2) * D + 512], P[b0][:, 0:512])
                    s.copy(wgT_st[:, (odt % 2) * D + 512:(odt % 2) * D + D], P[b1][:, 0:512])
                    s.drain()
                    s.nop().then_inc(h["sE"], 1)

            @self.on("g")
            def _(g):
                for odt in range(NDT):
                    g.wait_ge(h["sE"], sE_wg0 + odt)
                    g.dma_start(out=self.wgT_dram.rearrange("(kdt p) e2 -> p kdt e2", p=128)[:, :, odt * 128:(odt + 1) * 128],
                                in_=wgT_st[:, (odt % 2) * D:(odt % 2) * D + D]
                                .rearrange("p (kdt q) -> p kdt q", q=128)).then_inc(h["qG"], 16)

            if self.debug and "wgT" in self.dbg:
                self.bump("qG", 80)

                @self.on("g")
                def _(g, _q=self.c["qG"] - 80):
                    g.wait_ge(h["qG"], _q)
                    g.dma_start(out=self.dbg["wgT"], in_=self.wgT_dram).then_inc(h["qG"], 16)
                    g.dma_start(out=self.dbg["diag"], in_=self.diag_dram).then_inc(h["qG"], 16)
                    g.dma_start(out=self.dbg["vcol"], in_=self.vcol[:]).then_inc(h["qG"], 16)
                    g.dma_start(out=self.dbg["hbuf"], in_=self.hbuf[:]).then_inc(h["qG"], 16)
                    g.dma_start(out=self.dbg["gammab"], in_=self.gb_dram[0]).then_inc(h["qG"], 16)

            self.fence()
            self.flush(blk)

    def emit_taps(self, A_f, Bp_f, Cp_f, u1, dg_st, qs_ld, se_sm):
        h = self.h
        vE_tap = self.bump("vE")

        @self.on("v")
        def _(v, _w=qs_ld, _ws=se_sm):
            v.wait_ge(h["qS"], _w)
            v.wait_ge(h["sE"], _ws)
            A3 = A_f[:].rearrange("p (dt n m) -> p (dt n) m", n=NST, m=NST)
            hb3 = lambda j: (self.hbuf[:].rearrange("p (dt jj) -> p dt jj", jj=J)
                             [:, :, j:j + 1].rearrange("p dt one -> p (dt one)"))
            v.tensor_copy(u1[:, 0:128], Bp_f[:])
            v.drain()
            for j in range(J):
                v.tensor_tensor(u1[:, 128:256], u1[:, 0:128], Cp_f[:], ALU.mult)
                v.drain()
                v.tensor_reduce(hb3(j), u1[:, 128:256].rearrange("p (dt n) -> p dt n", n=NST),
                                AX.X, ALU.add)
                v.drain()
                if j < J - 1:
                    v.tensor_tensor(u1[:, 256:2304].rearrange("p (dtn m) -> p dtn m", m=NST),
                                    A3, u1[:, 0:128].broadcast_to([128, 128, 16]), ALU.mult)
                    v.drain()
                    for half in range(2):
                        v.tensor_reduce(u1[:, half * 64: half * 64 + 64]
                                        .rearrange("p (dt m) -> p dt m", m=NST),
                                        u1[:, 256 + half * 1024: 256 + half * 1024 + 1024]
                                        .rearrange("p (dt n m) -> p dt m n", n=NST, m=NST),
                                        AX.X, ALU.add)
                    v.drain()
            v.tensor_tensor(hb3(0), hb3(0), self.vcol[:, NDT:2 * NDT], ALU.add)
            v.drain()
            v.nop().then_inc(h["vE"], 1)

        vE_dg0 = self.c["vE"] + 1
        self.bump("vE", NDT)
        qG_dg0 = self.c["qG"] + 16
        self.bump("qG", 16 * NDT + 16 * NDT)

        @self.on("v")
        def _(v, _w=vE_tap):
            v.wait_ge(h["vE"], _w)
            for dt in range(NDT):
                sl = (dt % 2) * J * 128
                if dt >= 2:
                    v.wait_ge(h["qG"], qG_dg0 + 16 * (dt - 2))
                for j in range(J):
                    v.tensor_scalar(dg_st[:, sl + j * 128: sl + (j + 1) * 128],
                                    self.ident[:],
                                    self.hbuf[:, dt * J + j: dt * J + j + 1],
                                    None, ALU.mult)
                v.drain()
                v.nop().then_inc(h["vE"], 1)

        @self.on("g")
        def _(g):
            for dt in range(NDT):
                g.wait_ge(h["vE"], vE_dg0 + dt)
                sl = (dt % 2) * J * 128
                g.dma_start(out=self.diag_dram[dt],
                            in_=dg_st[:, sl: sl + J * 128]).then_inc(h["qG"], 16)
            for dt in range(NDT):
                g.dma_start(out=self.xg_b[dt, :, 0:J], in_=self.zpad[:]).then_inc(h["qG"], 16)

    def phaseA1(self, p, T):
        h, P = self.h, self.P
        xsl, ln1 = T["xsl"], T["ln1"]
        l1f, l1b = T["l1f"], T["l1b"]
        wgT, gt, xgf = T["wgT"], T["gt"], T["xgf"]

        qs_wgT = self.bump("qS", 16)

        @self.on("sy")
        def _(sy):
            sy.dma_start(out=wgT[:].rearrange("pp (kdt e2) -> pp kdt e2", e2=D),
                         in_=self.wgT_dram.rearrange("(kdt pp) e2 -> pp kdt e2", pp=128)
                         ).then_inc(h["qS"], 16)

        ve_ln1 = [None] * NCH
        qsx = [None] * NCH
        for c in range(NCH):
            qsx[c] = self.bump("qS", 16)

            @self.on("sy")
            def _(sy, _c=c):
                if _c >= 2:
                    sy.wait_ge(h["vE"], ve_ln1[_c - 2])
                sy.dma_start(out=xsl[:, (_c % 2) * 4096:(_c % 2) * 4096 + 4096]
                             .rearrange("pp (tt d) -> pp tt d", d=D),
                             in_=self.x_in[p].rearrange("(cc tt q) d -> cc q tt d", q=128, tt=4)[_c]
                             ).then_inc(h["qS"], 16)

            xtile = lambda tt, _c=c: xsl[:, (_c % 2) * 4096 + tt * 1024:(_c % 2) * 4096 + (tt + 1) * 1024]

            @self.on("v")
            def _(v, _c=c, _q=qsx[c]):
                v.wait_ge(h["qS"], _q)
            self.emit_stats([xtile(tt) for tt in range(4)])
            self.emit_rstd()
            ve_ln1[c] = self.bump("vE")

            @self.on("v")
            def _(v, _xt=xtile):
                for tt in range(4):
                    v.tensor_scalar(ln1[:, tt * 1024:(tt + 1) * 1024], _xt(tt),
                                    self.muc[:, tt:tt + 1], self.rstdc[:, tt:tt + 1],
                                    ALU.subtract, ALU.mult)
                v.drain()
                v.nop().then_inc(h["vE"], 1)

            if self.debug and "ln1" in self.dbg:
                self.bump("qS", 16)

                @self.on("sy")
                def _(sy, _c=c, _v=ve_ln1[c] + 1, _p=p):
                    sy.wait_ge(h["vE"], _v)
                    sy.dma_start(out=self.dbg["ln1"][_p].rearrange("(cc tt q) d -> cc q tt d", q=128, tt=4)[_c],
                                 in_=ln1[:].rearrange("q (tt d) -> q tt d", d=D)).then_inc(h["qS"], 16)

            te_tp = [self.bump("tE") for _ in range(NDT)]
            se_tp = [self.bump("sE") for _ in range(NDT)]

            @self.on("t")
            def _(t, _c=c, _v=ve_ln1[c], _te=tuple(te_tp), _se=tuple(se_tp)):
                t.wait_ge(h["vE"], _v)
                for dt in range(NDT):
                    if dt >= 2:
                        t.wait_ge(h["sE"], _se[dt - 2])
                    mmi = None
                    for tt in range(4):
                        mmi = t.transpose(P[dt % 2][0:128, tt * 128:(tt + 1) * 128],
                                          ln1[:, tt * 1024 + dt * 128: tt * 1024 + (dt + 1) * 128],
                                          self.ident[:])
                    mmi.then_inc(h["tE"], 1)

            @self.on("s")
            def _(s, _te=tuple(te_tp)):
                for dt in range(NDT):
                    s.wait_ge(h["tE"], _te[dt])
                    s.copy(l1b[:, dt * 512:(dt + 1) * 512], P[dt % 2][:, 0:512])
                    s.copy(l1f[:, dt * 512:(dt + 1) * 512], P[dt % 2][:, 0:512])
                    s.drain()
                    s.nop().then_inc(h["sE"], 1)

            te_g = [self.bump("tE") for _ in range(NDT)]
            se_g = [self.bump("sE") for _ in range(NDT)]
            ve_xg = [self.bump("vE") for _ in range(NDT)]
            qs_xgf = [self.bump("qS", 16) for _ in range(NDT)]
            qg_xgb = [self.bump("qG", 16) for _ in range(NDT)]

            @self.on("t")
            def _(t, _c=c, _sall=se_tp[NDT - 1], _se=tuple(se_g), _q=qs_wgT):
                t.wait_ge(h["sE"], _sall)
                t.wait_ge(h["qS"], _q)
                for odt in range(NDT):
                    if odt >= 2:
                        t.wait_ge(h["sE"], _se[odt - 2])
                    mmi = None
                    for kdt in range(NDT):
                        mmi = t.matmul(P[2 + odt % 2][:, 0:512],
                                       wgT[:, kdt * D + odt * 128: kdt * D + (odt + 1) * 128],
                                       l1b[:, kdt * 512:(kdt + 1) * 512],
                                       start=(kdt == 0), stop=(kdt == NDT - 1))
                    mmi.then_inc(h["tE"], 1)

            @self.on("s")
            def _(s, _te=tuple(te_g), _ve=tuple(ve_xg)):
                for odt in range(NDT):
                    s.wait_ge(h["tE"], _te[odt])
                    if odt >= 2:
                        s.wait_ge(h["vE"], _ve[odt - 2])
                    s.activation(gt[:, (odt % 2) * 512:(odt % 2) * 512 + 512],
                                 P[2 + odt % 2][:, 0:512], AF.Sigmoid,
                                 bias=self.col("bg", odt), scale=1.0)
                    s.drain()
                    s.nop().then_inc(h["sE"], 1)

            @self.on("v")
            def _(v, _se=tuple(se_g), _qsx=tuple(qs_xgf), _qgx=tuple(qg_xgb)):
                for odt in range(NDT):
                    v.wait_ge(h["sE"], _se[odt])
                    if odt >= 2:
                        v.wait_ge(h["qS"], _qsx[odt - 2])
                        v.wait_ge(h["qG"], _qgx[odt - 2])
                    v.tensor_tensor(xgf[:, (odt % 2) * 512:(odt % 2) * 512 + 512],
                                    l1f[:, odt * 512:(odt + 1) * 512],
                                    gt[:, (odt % 2) * 512:(odt % 2) * 512 + 512], ALU.mult)
                    v.drain()
                    v.nop().then_inc(h["vE"], 1)

            @self.on("sy")
            def _(sy, _c=c, _ve=tuple(ve_xg)):
                for odt in range(NDT):
                    sy.wait_ge(h["vE"], _ve[odt])
                    sy.dma_start(out=self.xg_f[odt, :, _c * 512:(_c + 1) * 512],
                                 in_=xgf[:, (odt % 2) * 512:(odt % 2) * 512 + 512]
                                 ).then_inc(h["qS"], 16)

            @self.on("g")
            def _(g, _c=c, _ve=tuple(ve_xg)):
                for odt in range(NDT):
                    g.wait_ge(h["vE"], _ve[odt])
                    g.dma_start(out=self.xg_b[odt, :, J + _c * 512: J + (_c + 1) * 512],
                                in_=xgf[:, (odt % 2) * 512:(odt % 2) * 512 + 512]
                                ).then_inc(h["qG"], 16)
            if self.debug and "gate" in self.dbg:
                self.bump("qG", 16 * NDT)

                @self.on("g")
                def _(g, _c=c, _se=tuple(se_g), _p=p):
                    for odt in range(NDT):
                        g.wait_ge(h["sE"], _se[odt])
                        g.dma_start(out=self.dbg["gate"][_p, odt, :, _c * 512:(_c + 1) * 512],
                                    in_=gt[:, (odt % 2) * 512:(odt % 2) * 512 + 512]
                                    ).then_inc(h["qG"], 16)

        if p == 0:
            A_f, Bp_f, Cp_f, u1, dg_st = T["tap"]
            qs_abc = self.bump("qS", 48)

            @self.on("sy")
            def _(sy):
                sy.dma_start(out=A_f[:].rearrange("p (dt nm) -> p dt nm", nm=256),
                             in_=self.w["A"].rearrange("(dt p) n m -> p dt (n m)", p=128)
                             ).then_inc(h["qS"], 16)
                sy.dma_start(out=Bp_f[:].rearrange("p (dt n) -> p dt n", n=NST),
                             in_=self.w["Bp"].rearrange("(dt p) n -> p dt n", p=128)
                             ).then_inc(h["qS"], 16)
                sy.dma_start(out=Cp_f[:].rearrange("p (dt n) -> p dt n", n=NST),
                             in_=self.w["Cp"].rearrange("(dt p) n -> p dt n", p=128)
                             ).then_inc(h["qS"], 16)

            self.emit_taps(A_f, Bp_f, Cp_f, u1, dg_st, qs_abc, self._sE_sm)
            self.qG_casts = self.bump("qG", 16 * 6)

            @self.on("g")
            def _(g):
                for dst, src in [(self.wb["We1"], self.w["We1"]), (self.wb["We3"], self.w["We3"]),
                                 (self.wb["We2"], self.w["We2"]),
                                 (self.wb["ws1"], self.w["ws1"]), (self.wb["ws3"], self.w["ws3"]),
                                 (self.wb["ws2"], self.w["ws2"])]:
                    g.dma_start(out=dst, in_=src).then_inc(h["qG"], 16)

    def phaseA2(self, p, T):
        h, P = self.h, self.P
        dg, xb, xf = T["dg"], T["xb"], T["xf"]
        yF, ytp, xre, ln3 = T["yF"], T["ytp"], T["xre"], T["ln3"]
        l3f = T["l3f"]
        W = J + 1024

        qs_gb = self.bump("qS", 16)

        @self.on("sy")
        def _(sy):
            sy.dma_start(out=self.gamma_b[:], in_=self.gb_dram[0]).then_inc(h["qS"], 16)

        qs_gb2 = self.bump("qS", 16)

        @self.on("sy")
        def _(sy):
            sy.dma_start(out=self.beta_b[:], in_=self.gb_dram[1]).then_inc(h["qS"], 16)
        self._qs_gb = qs_gb2

        conv_hist = []
        ytr_hist = []
        join_hist = []

        seyt_hist = []
        self._a2_prev_ve_x1 = None

        def chain_gen(pr_, cc_):
            c = 2 * pr_ + cc_
            ytile = lambda tt, _cc=cc_, _pr=pr_: ytp[:, (_pr % 2) * 8192 + (_cc * 4 + tt) * 1024:(_pr % 2) * 8192 + (_cc * 4 + tt + 1) * 1024]
            qg_xre = self.bump("qG", 16)

            @self.on("g")
            def _(g, _c=c, _vx=self._a2_prev_ve_x1):
                if _vx is not None:
                    g.wait_ge(h["vE"], _vx)
                g.dma_start(out=xre[:].rearrange("q (tt d) -> q tt d", d=D),
                            in_=self.x_in[p].rearrange("(ch tt q) d -> ch q tt d", q=128, tt=4)[_c]
                            ).then_inc(h["qG"], 16)

            @self.on("v")
            def _(v, _s=seyt_hist[pr_], _q=self._qs_gb):
                v.wait_ge(h["sE"], _s)
                v.wait_ge(h["qS"], _q)
            self.emit_stats([ytile(tt) for tt in range(4)])
            yield
            self.emit_rstd()

            def attn_apply(tts):
                @self.on("v")
                def _(v, _yt=ytile, _tts=tuple(tts)):
                    for tt in _tts:
                        tile = _yt(tt)
                        v.tensor_scalar(self.sq[:], tile, self.muc[:, tt:tt + 1],
                                        self.rstdc[:, tt:tt + 1], ALU.subtract, ALU.mult)
                        v.drain()
                        v.tensor_tensor(self.sq[:], self.sq[:], self.gamma_b[:], ALU.mult)
                        v.drain()
                        v.scalar_tensor_tensor(tile, self.sq[:], 1.0, self.beta_b[:],
                                               ALU.mult, ALU.add,
                                               accum_out=self.s1c[:, tt:tt + 1])
                        v.drain()
                        v.scalar_tensor_tensor(self.sq[:], tile, 1.0, tile,
                                               ALU.mult, ALU.mult,
                                               accum_out=self.s2c[:, tt:tt + 1])
                        v.drain()
            attn_apply([0, 1])
            yield
            attn_apply([2, 3])

            @self.on("v")
            def _(v):
                v.tensor_scalar(self.muc[:], self.s1c[:], 1.0 / D, None, ALU.mult)
                v.tensor_scalar(self.varc[:], self.s2c[:], 1.0 / D, None, ALU.mult)
                v.drain()
                v.tensor_tensor(self.tnc[:], self.muc[:], self.muc[:], ALU.mult)
                v.drain()
                v.tensor_tensor(self.varc[:], self.varc[:], self.tnc[:], ALU.subtract)
                v.drain()
            self.emit_rstd()
            yield

            def ln2_apply(tts, first):
                @self.on("v")
                def _(v, _yt=ytile, _tts=tuple(tts), _first=first, _q=qg_xre):
                    if _first:
                        v.wait_ge(h["qG"], _q)
                    for tt in _tts:
                        tile = _yt(tt)
                        v.tensor_scalar(self.sq[:], tile, self.muc[:, tt:tt + 1],
                                        self.rstdc[:, tt:tt + 1], ALU.subtract, ALU.mult)
                        v.drain()
                        v.tensor_tensor(tile, self.sq[:],
                                        xre[:, tt * 1024:(tt + 1) * 1024],
                                        ALU.add)
                        v.drain()
                        v.tensor_reduce(self.s1c[:, tt:tt + 1], tile, AX.X, ALU.add)
                        v.scalar_tensor_tensor(self.sq[:], tile, 1.0, tile, ALU.mult, ALU.mult,
                                               accum_out=self.s2c[:, tt:tt + 1])
                    v.drain()
            ln2_apply([0, 1], True)
            yield
            ln2_apply([2, 3], False)
            ve_x1 = self.bump("vE")

            @self.on("v")
            def _(v):
                v.tensor_scalar(self.muc[:], self.s1c[:], 1.0 / D, None, ALU.mult)
                v.tensor_scalar(self.varc[:], self.s2c[:], 1.0 / D, None, ALU.mult)
                v.drain()
                v.tensor_tensor(self.tnc[:], self.muc[:], self.muc[:], ALU.mult)
                v.drain()
                v.tensor_tensor(self.varc[:], self.varc[:], self.tnc[:], ALU.subtract)
                v.drain()
                v.nop().then_inc(h["vE"], 1)
            self.emit_rstd()
            self._a2_prev_ve_x1 = ve_x1
            self.bump("qG", 16 * (2 if STAGE == 1 else 1))

            @self.on("g")
            def _(g, _c=c, _cc=cc_, _v=ve_x1, _pr2=pr_ % 2):
                g.wait_ge(h["vE"], _v)
                g.dma_start(out=self.x1_scr.rearrange("(ch tt q) d -> ch q tt d", q=128, tt=4)[_c],
                            in_=ytp[:, _pr2 * 8192 + _cc * 4096: _pr2 * 8192 + (_cc + 1) * 4096]
                            .rearrange("q (tt d) -> q tt d", d=D)).then_inc(h["qG"], 16)
                if STAGE == 1:
                    g.dma_start(out=self.out[p].rearrange("(ch tt q) d -> ch q tt d", q=128, tt=4)[_c],
                                in_=ytp[:, _pr2 * 8192 + _cc * 4096: _pr2 * 8192 + (_cc + 1) * 4096]
                                .rearrange("q (tt d) -> q tt d", d=D)).then_inc(h["qG"], 16)
            yield
            if cc_ == 0:
                ve_ln3 = self.bump("vE")

                @self.on("v")
                def _(v, _yt=ytile):
                    for tt in range(4):
                        v.tensor_scalar(ln3[:, tt * 1024:(tt + 1) * 1024], _yt(tt),
                                        self.muc[:, tt:tt + 1], self.rstdc[:, tt:tt + 1],
                                        ALU.subtract, ALU.mult)
                    v.drain()
                    v.nop().then_inc(h["vE"], 1)
                self._ve_ln3_c0 = ve_ln3

        def drive(it, n):
            k = 0
            while k < n:
                try:
                    next(it)
                except StopIteration:
                    return
                k += 1

        self._chain_it = None
        for pr in range(2):
            c0 = 2 * pr
            se_yt = []
            if pr == 1:
                def _chain2():
                    yield from chain_gen(0, 0)
                    yield from chain_gen(0, 1)
                self._chain_it = _chain2()
            for dt in range(NDT):
                gi = pr * NDT + dt
                qs_dg = self.bump("qS", 48)

                @self.on("sy")
                def _(sy, _dt=dt, _c0=c0, _gi=gi,
                      _g=(conv_hist[gi - 2] if gi >= 2 else None),
                      _gj=(join_hist[gi - 1] if gi >= 1 else None)):
                    if _g is not None:
                        sy.wait_ge(h["tE"], _g)
                    if _gj is not None:
                        sy.wait_ge(h["vE"], _gj)
                    sl = _dt % 2
                    sy.dma_start(out=dg[:, sl * J * 128:(sl + 1) * J * 128],
                                 in_=self.diag_dram[_dt]).then_inc(h["qS"], 16)
                    sy.dma_start(out=xb[:, sl * W: sl * W + W],
                                 in_=self.xg_b[_dt, :, _c0 * 512: _c0 * 512 + W]
                                 ).then_inc(h["qS"], 16)
                    sy.dma_start(out=xf[:],
                                 in_=self.xg_f[_dt, :, _c0 * 512: _c0 * 512 + 1024]
                                 ).then_inc(h["qS"], 16)

                te_conv = self.bump("tE")
                conv_hist.append(te_conv)
                ve_join = self.bump("vE")
                join_hist.append(ve_join)
                te_yt = self.bump("tE")
                ytr_hist.append(te_yt)
                se_yt.append(self.bump("sE"))

                @self.on("t")
                def _(t, _dt=dt, _q=qs_dg, _gi=gi,
                      _vjprev=(join_hist[gi - 1] if gi >= 1 else None),
                      _vj=ve_join,
                      _seprev=(se_yt[dt - 1] if dt >= 1 else None),
                      _se_prev_pair=(self._seyt_last if pr == 1 and dt == 0 else None)):
                    sl = _dt % 2
                    t.wait_ge(h["qS"], _q)
                    if _vjprev is not None:
                        t.wait_ge(h["vE"], _vjprev)
                    mmi = None
                    for j in range(J):
                        for cc in range(2):
                            mmi = t.matmul(P[cc][:, 0:512],
                                           dg[:, sl * J * 128 + j * 128: sl * J * 128 + (j + 1) * 128],
                                           xb[:, sl * W + J + cc * 512 - j: sl * W + J + cc * 512 - j + 512],
                                           start=(j == 0), stop=(j == J - 1),
                                           skip_group_check=True)
                    mmi.then_inc(h["tE"], 1)
                    t.wait_ge(h["vE"], _vj)
                    if _seprev is not None:
                        t.wait_ge(h["sE"], _seprev)
                    if _se_prev_pair is not None:
                        t.wait_ge(h["sE"], _se_prev_pair)
                    mmi = None
                    for wd in range(8):
                        pb = P[2] if wd < 4 else P[3]
                        mmi = t.transpose(pb[0:128, (wd % 4) * 128:(wd % 4 + 1) * 128],
                                          yF[:, sl * 1024 + wd * 128: sl * 1024 + (wd + 1) * 128],
                                          self.ident[:])
                    mmi.then_inc(h["tE"], 1)

                @self.on("v")
                def _(v, _dt=dt, _tc=te_conv, _gi=gi,
                      _typrev=(ytr_hist[gi - 2] if gi >= 2 else None)):
                    v.wait_ge(h["tE"], _tc)
                    if _typrev is not None:
                        v.wait_ge(h["tE"], _typrev)
                    sl = _dt % 2
                    v.tensor_tensor(yF[:, sl * 1024: sl * 1024 + 1024],
                                    self.PA[:, 0:1024], xf[:], ALU.add)
                    v.drain()
                    v.nop().then_inc(h["vE"], 1)

                @self.on("s")
                def _(s, _dt=dt, _ty=te_yt, _pr=pr):
                    s.wait_ge(h["tE"], _ty)
                    s.copy(ytp[:, (_pr % 2) * 8192:(_pr % 2) * 8192 + 8192].rearrange("q (wd d1) -> q wd d1", d1=1024)[:, 0:4, _dt * 128:(_dt + 1) * 128],
                           P[2][:, 0:512].rearrange("q (wd d1) -> q wd d1", d1=128))
                    s.copy(ytp[:, (_pr % 2) * 8192:(_pr % 2) * 8192 + 8192].rearrange("q (wd d1) -> q wd d1", d1=1024)[:, 4:8, _dt * 128:(_dt + 1) * 128],
                           P[3][:, 0:512].rearrange("q (wd d1) -> q wd d1", d1=128))
                    s.drain()
                    s.nop().then_inc(h["sE"], 1)

                if pr == 1:
                    drive(self._chain_it, 1)

            if pr == 1:
                drive(self._chain_it, 99)
            seyt_hist.append(se_yt[NDT - 1])
            self._seyt_last = se_yt[NDT - 1]

            if self.debug and "y" in self.dbg:
                self.bump("qS", 16)

                @self.on("sy")
                def _(sy, _pr=pr, _s=se_yt[NDT - 1], _p=p):
                    sy.wait_ge(h["sE"], _s)
                    sy.dma_start(out=self.dbg["y"][_p].rearrange("(q2 tt q) d -> q2 q tt d", q=128, tt=8)[_pr],
                                 in_=ytp[:, (_pr % 2) * 8192:(_pr % 2) * 8192 + 8192].rearrange("q (tt d) -> q tt d", d=D)).then_inc(h["qS"], 16)
            if self.debug and "xg" in self.dbg and pr == 0:
                self.bump("qG", 16 * NDT)

                @self.on("g")
                def _(g, _p=p, _q=self.c["qS"]):
                    g.wait_ge(h["qS"], _q)
                    for dt in range(NDT):
                        g.dma_start(out=self.dbg["xg"][_p, dt], in_=self.xg_f[dt]).then_inc(h["qG"], 16)

        for pr in range(2):
            c0 = 2 * pr
            for cc in range(2):
                c = c0 + cc
                ytile = lambda tt, _cc=cc, _pr=pr: ytp[:, (_pr % 2) * 8192 + (_cc * 4 + tt) * 1024:(_pr % 2) * 8192 + (_cc * 4 + tt + 1) * 1024]

                if pr == 0 and cc == 0:
                    ve_ln3 = self._ve_ln3_c0
                elif pr == 0 and cc == 1:
                    ve_ln3 = self.bump("vE")

                    @self.on("v")
                    def _(v, _yt=ytile, _t=self._te_l3_last):
                        v.wait_ge(h["tE"], _t)
                        for tt in range(4):
                            v.tensor_scalar(ln3[:, tt * 1024:(tt + 1) * 1024], _yt(tt),
                                            self.muc[:, tt:tt + 1], self.rstdc[:, tt:tt + 1],
                                            ALU.subtract, ALU.mult)
                        v.drain()
                        v.nop().then_inc(h["vE"], 1)
                else:
                    cg = chain_gen(pr, cc)
                    drive(cg, 99)
                    if cc == 0:
                        ve_ln3 = self._ve_ln3_c0
                    else:
                        ve_ln3 = self.bump("vE")

                        @self.on("v")
                        def _(v, _yt=ytile, _t=self._te_l3_last):
                            v.wait_ge(h["tE"], _t)
                            for tt in range(4):
                                v.tensor_scalar(ln3[:, tt * 1024:(tt + 1) * 1024], _yt(tt),
                                                self.muc[:, tt:tt + 1], self.rstdc[:, tt:tt + 1],
                                                ALU.subtract, ALU.mult)
                            v.drain()
                            v.nop().then_inc(h["vE"], 1)

                if self.debug and "ln3" in self.dbg:
                    self.bump("qS", 16)

                    @self.on("sy")
                    def _(sy, _c=c, _v=ve_ln3):
                        sy.wait_ge(h["vE"], _v)
                        sy.dma_start(out=self.dbg["ln3"][p].rearrange("(ch tt q) d -> ch q tt d", q=128, tt=4)[_c],
                                     in_=ln3[:].rearrange("q (tt d) -> q tt d", d=D)).then_inc(h["qS"], 16)

                se_l3 = []
                te_sc = []
                for dt in range(NDT):
                    te_l3 = self.bump("tE")
                    se_l3.append(self.bump("sE"))
                    te_sc.append(self.bump("tE"))

                    @self.on("t")
                    def _(t, _dt=dt, _v=ve_ln3, _se=se_l3[dt],
                          _seprev=(se_l3[dt - 2] if dt >= 2 else None),
                          _sg=(seyt_hist[1] if (pr == 0 and dt < 2) else None)):
                        if _dt == 0:
                            t.wait_ge(h["vE"], _v)
                        if _sg is not None:
                            t.wait_ge(h["sE"], _sg)
                        if _seprev is not None:
                            t.wait_ge(h["sE"], _seprev)
                        mmi = None
                        for tt in range(4):
                            mmi = t.transpose(P[2 + _dt % 2][0:128, tt * 128:(tt + 1) * 128],
                                              ln3[:, tt * 1024 + _dt * 128: tt * 1024 + (_dt + 1) * 128],
                                              self.ident[:])
                        mmi.then_inc(h["tE"], 1)
                        t.wait_ge(h["sE"], _se)
                        t.matmul(P[6][0:16, 0:512], self.gate_wT[:, _dt * 16:(_dt + 1) * 16],
                                 l3f[:, (_dt % 2) * 512:(_dt % 2) * 512 + 512],
                                 start=(_dt == 0), stop=(_dt == NDT - 1),
                                 skip_group_check=True).then_inc(h["tE"], 1)

                    @self.on("s")
                    def _(s, _c=c, _dt=dt, _tl=te_l3,
                          _tscprev=(te_sc[dt - 2] if dt >= 2 else None)):
                        s.wait_ge(h["tE"], _tl)
                        if _tscprev is not None:
                            s.wait_ge(h["tE"], _tscprev)
                        s.copy(self.l3x_bf[:, _dt * L + _c * 512: _dt * L + (_c + 1) * 512],
                               P[2 + _dt % 2][:, 0:512])
                        s.copy(l3f[:, (_dt % 2) * 512:(_dt % 2) * 512 + 512],
                               P[2 + _dt % 2][:, 0:512])
                        s.drain()
                        s.nop().then_inc(h["sE"], 1)

                self._te_l3_last = te_l3
                self.emit_routing(c, te_sc[NDT - 1], p)

            self.fence()

    def emit_routing(self, c, te_scores, p=0):
        h, P, rt = self.h, self.P, self.rt
        ve_sc = self.bump("vE")

        @self.on("v")
        def _(v, _t=te_scores):
            v.wait_ge(h["tE"], _t)
            v.tensor_copy(self.sc_sb[0:16, 0:512], P[6][0:16, 0:512])
            v.drain()
            v.nop().then_inc(h["vE"], 1)

        te_t1 = self.bump("tE")

        @self.on("t")
        def _(t, _v=ve_sc):
            t.wait_ge(h["vE"], _v)
            mmi = None
            for tt in range(4):
                mmi = t.transpose(P[7][0:128, tt * 16:(tt + 1) * 16],
                                  self.sc_sb[0:16, tt * 128:(tt + 1) * 128],
                                  self.ident[0:16, 0:16])
            mmi.then_inc(h["tE"], 1)

        ve_rt = self.bump("vE")

        @self.on("v")
        def _(v, _t=te_t1):
            v.wait_ge(h["tE"], _t)
            sc = rt["sd"]
            v.tensor_copy(sc[:], P[7][:, 0:64])
            v.drain()
            v3 = lambda a: a[:].rearrange("q (tt e) -> q tt e", e=16)
            g3 = lambda a: a[:].rearrange("q (tg j) -> q tg j", j=4)
            bc = lambda a, n=16: a[:].broadcast_to([128, a.shape[1] if hasattr(a, 'shape') else 4, n])
            v.tensor_reduce(rt["m"][:], v3(sc), AX.X, ALU.max)
            v.drain()
            v.tensor_tensor(v3(rt["ex"]), v3(sc), rt["m"][:].broadcast_to([128, 4, 16]), ALU.subtract)
            v.drain()
            v.nop().then_inc(h["vE"], 1)
        sE_exp = self.bump("sE")

        @self.on("s")
        def _(s, _v=ve_rt):
            s.wait_ge(h["vE"], _v)
            s.activation(rt["ex"][:], rt["ex"][:], AF.Exp)
            s.drain()
            s.nop().then_inc(h["sE"], 1)

        ve_comb = self.bump("vE")

        @self.on("v")
        def _(v, _s=sE_exp):
            v3 = lambda a: a[:].rearrange("q (tt e) -> q tt e", e=16)
            g3 = lambda a: a[:].rearrange("q (tg j) -> q tg j", j=4)
            v.wait_ge(h["sE"], _s)
            v.tensor_reduce(rt["sum"][:], v3(rt["ex"]), AX.X, ALU.add)
            v.drain()
            v.reciprocal(rt["rec"][:], rt["sum"][:])
            v.drain()
            v.tensor_tensor(v3(rt["orig"]), v3(rt["ex"]),
                            rt["rec"][:].broadcast_to([128, 4, 16]), ALU.mult)
            v.drain()
            gbv = self.gb_bcast[:, 0:16].broadcast_to([128, 16, 4]).rearrange("q e tt -> q tt e")
            v.tensor_tensor(v3(rt["sc2"]), v3(rt["orig"]), gbv, ALU.add)
            v.drain()
            v.tensor_reduce(rt["m1"][:], g3(rt["sc2"]), AX.X, ALU.max)
            v.drain()
            v.tensor_tensor(g3(rt["eqa"]), g3(rt["sc2"]),
                            rt["m1"][:].broadcast_to([128, 16, 4]), ALU.is_equal)
            v.drain()
            v.scalar_tensor_tensor(rt["sgm"][:], rt["eqa"][:], NEG, rt["sc2"][:],
                                   ALU.mult, ALU.add)
            v.drain()
            v.tensor_reduce(rt["m2"][:], g3(rt["sgm"]), AX.X, ALU.max)
            v.drain()
            v.tensor_tensor(rt["gsc"][:], rt["m1"][:], rt["m2"][:], ALU.add)
            v.drain()
            q3 = lambda a: a[:].rearrange("q (tt gg) -> q tt gg", gg=4)
            v.tensor_reduce(rt["gm1"][:], q3(rt["gsc"]), AX.X, ALU.max)
            v.drain()
            v.tensor_tensor(q3(rt["eqg"]), q3(rt["gsc"]),
                            rt["gm1"][:].broadcast_to([128, 4, 4]), ALU.is_equal)
            v.drain()
            v.scalar_tensor_tensor(rt["gsc2"][:], rt["eqg"][:], NEG, rt["gsc"][:],
                                   ALU.mult, ALU.add)
            v.drain()
            v.tensor_reduce(rt["gm2"][:], q3(rt["gsc2"]), AX.X, ALU.max)
            v.drain()
            v.tensor_tensor(q3(rt["keep"]), q3(rt["gsc"]),
                            rt["gm2"][:].broadcast_to([128, 4, 4]), ALU.is_ge)
            v.drain()
            v.tensor_tensor(g3(rt["smt1"]), g3(rt["sc2"]),
                            rt["keep"][:].broadcast_to([128, 16, 4]), ALU.mult)
            v.drain()
            v.tensor_scalar(g3(rt["smt2"]), rt["keep"][:].broadcast_to([128, 16, 4]),
                            1.0, -NEG, ALU.subtract, ALU.mult)
            v.drain()
            v.tensor_tensor(rt["sm"][:], rt["smt1"][:], rt["smt2"][:], ALU.add)
            v.drain()
            v.tensor_reduce(rt["t1"][:], v3(rt["sm"]), AX.X, ALU.max)
            v.drain()
            v.tensor_tensor(v3(rt["eq1"]), v3(rt["sm"]),
                            rt["t1"][:].broadcast_to([128, 4, 16]), ALU.is_equal)
            v.drain()
            v.scalar_tensor_tensor(rt["sm2"][:], rt["eq1"][:], NEG, rt["sm"][:],
                                   ALU.mult, ALU.add)
            v.drain()
            v.tensor_reduce(rt["t2"][:], v3(rt["sm2"]), AX.X, ALU.max)
            v.drain()
            v.tensor_tensor(v3(rt["eq2"]), v3(rt["sm2"]),
                            rt["t2"][:].broadcast_to([128, 4, 16]), ALU.is_equal)
            v.drain()
            v.tensor_tensor(rt["comb"][:], rt["eq1"][:], rt["eq2"][:], ALU.add)
            v.drain()
            v.tensor_tensor(rt["comb"][:], rt["comb"][:], rt["orig"][:], ALU.mult)
            v.drain()
            v.nop().then_inc(h["vE"], 1)

        if self.debug and "rt" in self.dbg and p == 0 and c == 0:
            keys = ["sd", "ex", "orig", "sc2", "m1", "sgm", "m2", "gsc", "gm1",
                    "gm2", "keep", "smt1", "smt2", "sm", "t1", "eq1",
                    "sm2", "t2", "eq2", "comb"]
            self.bump("qS", 16 * 20)

            @self.on("sy")
            def _(sy, _v=ve_comb, _keys=tuple(keys)):
                sy.wait_ge(h["vE"], _v)
                for i, kk in enumerate(_keys):
                    ap = rt[kk][:]
                    w = ap.shape[1] if hasattr(ap, "shape") else 64
                    sy.dma_start(out=self.dbg["rt"][i, :, 0:rt[kk].shape[1]],
                                 in_=ap).then_inc(h["qS"], 16)

        te_t2 = self.bump("tE")

        @self.on("t")
        def _(t, _v=ve_comb):
            t.wait_ge(h["vE"], _v)
            mmi = None
            for tt in range(4):
                mmi = t.transpose(P[7][0:16, tt * 128:(tt + 1) * 128],
                                  rt["comb"][:, tt * 16:(tt + 1) * 16], self.ident[:])
            mmi.then_inc(h["tE"], 1)

        ve_ct = self.bump("vE")

        @self.on("v")
        def _(v, _t=te_t2, _c=c):
            v.wait_ge(h["tE"], _t)
            v.tensor_copy(self.combine_T[0:16, _c * 512:(_c + 1) * 512], P[7][0:16, 0:512])
            v.drain()
            v.nop().then_inc(h["vE"], 1)

    def phaseB(self, p, T):
        h, P = self.h, self.P
        we, hbf, h1s, h3s, cb = T["we"], T["hbf"], T["h1s"], T["h3s"], T["cb"]
        WSLOT = 12288
        w1t = lambda sl, kdt, ib: we[:, sl * WSLOT + kdt * 512 + ib * 128: sl * WSLOT + kdt * 512 + (ib + 1) * 128]
        w3t = lambda sl, kdt, ib: we[:, sl * WSLOT + 4096 + kdt * 512 + ib * 128: sl * WSLOT + 4096 + kdt * 512 + (ib + 1) * 128]
        w2t = lambda sl, ib, odt: we[:, sl * WSLOT + 8192 + ib * 1024 + odt * 128: sl * WSLOT + 8192 + ib * 1024 + (odt + 1) * 128]

        qs_we = [None] * E
        prev_yo_te = {}
        prev_ve_yo_last = None

        for e in range(E):
            sl = e % 2
            qs_we[e] = self.bump("qS", 48)

            @self.on("sy")
            def _(sy, _e=e, _sl=sl):
                if _e >= 2:
                    sy.wait_ge(h["tE"], prev_yo_te[_e - 2])
                sy.dma_start(out=we[:, _sl * WSLOT: _sl * WSLOT + 4096]
                             .rearrange("q (kdt i) -> q kdt i", i=512),
                             in_=self.wb["We1"][_e].rearrange("(kdt q) i -> q kdt i", q=128)
                             ).then_inc(h["qS"], 16)
                sy.dma_start(out=we[:, _sl * WSLOT + 4096: _sl * WSLOT + 8192]
                             .rearrange("q (kdt i) -> q kdt i", i=512),
                             in_=self.wb["We3"][_e].rearrange("(kdt q) i -> q kdt i", q=128)
                             ).then_inc(h["qS"], 16)
                sy.dma_start(out=we[:, _sl * WSLOT + 8192: _sl * WSLOT + 12288]
                             .rearrange("q (ib d1) -> q ib d1", d1=1024),
                             in_=self.wb["We2"][_e].rearrange("(ib q) d1 -> q ib d1", q=128)
                             ).then_inc(h["qS"], 16)

            te_cb = [self.bump("tE") for _ in range(4)]
            se_cb = [self.bump("sE") for _ in range(4)]

            @self.on("t")
            def _(t, _e=e, _te=tuple(te_cb), _se=tuple(se_cb), _vy=prev_ve_yo_last):
                if _vy is not None:
                    t.wait_ge(h["vE"], _vy)
                for ch in range(4):
                    if ch >= 2:
                        t.wait_ge(h["sE"], _se[ch - 2])
                    t.matmul(P[6 + ch % 2][:, 0:512], self.sel_sb[:, _e * 128:(_e + 1) * 128],
                             self.combine_T[0:16, ch * 512:(ch + 1) * 512],
                             start=True, stop=True).then_inc(h["tE"], 1)

            @self.on("s")
            def _(s, _te=tuple(te_cb)):
                for ch in range(4):
                    s.wait_ge(h["tE"], _te[ch])
                    s.copy(cb[:, ch * 512:(ch + 1) * 512], P[6 + ch % 2][:, 0:512])
                    s.drain()
                    s.nop().then_inc(h["sE"], 1)

            if e == 0:
                prev_se_h1 = None
                prev_ve_h3s = None
                prev_ve_hbf = None
            ve_hbf = [None] * 4
            for ib in range(4):
                te_h1 = self.bump("tE")
                se_h1 = self.bump("sE")
                te_h3 = self.bump("tE")
                ve_h3s = self.bump("vE")
                ve_hbf[ib] = self.bump("vE")

                @self.on("t")
                def _(t, _e=e, _sl=sl, _ib=ib, _q=qs_we[e], _secb=se_cb[3],
                      _sh1=prev_se_h1, _vh3=prev_ve_h3s,
                      _vyo=(prev_ve_yo_last if (e >= 1 and ib == 0) else None)):
                    t.wait_ge(h["qS"], _q)
                    if _ib == 0:
                        t.wait_ge(h["sE"], _secb)
                    if _vyo is not None:
                        t.wait_ge(h["vE"], _vyo)
                    if _sh1 is not None:
                        t.wait_ge(h["sE"], _sh1)
                    mmi = None
                    for kdt in range(NDT):
                        for ch in range(4):
                            mmi = t.matmul(P[ch][:, 0:512], w1t(_sl, kdt, _ib),
                                           self.l3x_bf[:, kdt * L + ch * 512: kdt * L + (ch + 1) * 512],
                                           start=(kdt == 0), stop=(kdt == NDT - 1),
                                           skip_group_check=True)
                    mmi.then_inc(h["tE"], 1)
                    if _vh3 is not None:
                        t.wait_ge(h["vE"], _vh3)
                    mmi = None
                    for kdt in range(NDT):
                        for ch in range(4):
                            mmi = t.matmul(P[4 + ch][:, 0:512], w3t(_sl, kdt, _ib),
                                           self.l3x_bf[:, kdt * L + ch * 512: kdt * L + (ch + 1) * 512],
                                           start=(kdt == 0), stop=(kdt == NDT - 1),
                                           skip_group_check=True)
                    mmi.then_inc(h["tE"], 1)

                @self.on("s")
                def _(s, _e=e, _ib=ib, _t=te_h1, _vb=prev_ve_hbf):
                    s.wait_ge(h["tE"], _t)
                    if _vb is not None:
                        s.wait_ge(h["vE"], _vb)
                    s.activation(h1s[:], self.PA[:], AF.Silu,
                                 bias=self.be1c[:, _ib * 16 + _e: _ib * 16 + _e + 1],
                                 scale=1.0)
                    s.drain()
                    s.nop().then_inc(h["sE"], 1)

                @self.on("v")
                def _(v, _e=e, _ib=ib, _t=te_h3, _s1=se_h1, _secb=se_cb[3]):
                    v.wait_ge(h["tE"], _t)
                    if _ib == 0:
                        v.wait_ge(h["sE"], _secb)
                    v.scalar_tensor_tensor(h3s[:], self.PB[:],
                                           self.be3c[:, _ib * 16 + _e: _ib * 16 + _e + 1],
                                           cb[:], ALU.add, ALU.mult)
                    v.drain()
                    v.nop().then_inc(h["vE"], 1)
                    v.wait_ge(h["sE"], _s1)
                    v.tensor_tensor(hbf[:, _ib * L: (_ib + 1) * L], h3s[:], h1s[:], ALU.mult)
                    v.drain()
                    v.nop().then_inc(h["vE"], 1)
                prev_se_h1 = se_h1
                prev_ve_h3s = ve_h3s
                prev_ve_hbf = ve_hbf[ib]

            te_yo = [None] * NDT
            ve_yo = [None] * NDT
            for odt in range(NDT):
                te_yo[odt] = self.bump("tE")
                ve_yo[odt] = self.bump("vE")

            @self.on("t")
            def _(t, _e=e, _sl=sl, _ph=ve_hbf[3], _vy=tuple(ve_yo)):
                t.wait_ge(h["vE"], _ph)
                for odt in range(NDT):
                    if odt >= 2:
                        t.wait_ge(h["vE"], _vy[odt - 2])
                    pb = (odt % 2) * 4
                    mmi = None
                    for ib in range(4):
                        for ch in range(4):
                            mmi = t.matmul(P[pb + ch][:, 0:512], w2t(_sl, ib, odt),
                                           hbf[:, ib * L + ch * 512: ib * L + (ch + 1) * 512],
                                           start=(ib == 0), stop=(ib == 3),
                                           skip_group_check=True)
                    mmi.then_inc(h["tE"], 1)
            prev_yo_te[e] = te_yo[NDT - 1]

            @self.on("v")
            def _(v, _e=e, _ty=tuple(te_yo)):
                for odt in range(NDT):
                    v.wait_ge(h["tE"], _ty[odt])
                    pw = self.PA if odt % 2 == 0 else self.PB
                    dst = self.y_acc[:, odt * L: (odt + 1) * L]
                    if _e == 0:
                        v.tensor_copy(dst, pw[:])
                    else:
                        v.tensor_tensor(dst, dst, pw[:], ALU.add)
                    v.drain()
                    v.nop().then_inc(h["vE"], 1)
            prev_ve_yo_last = ve_yo[NDT - 1]
        self.fence()

    def phaseZ(self, p, T):
        h, P = self.h, self.P
        zw = T["zw"]
        z1s, z3s, zzw = T["z1s"], T["z3s"], T["zzw"]
        zw1 = lambda kdt, ob: zw[:, kdt * 1024 + ob * 128: kdt * 1024 + (ob + 1) * 128]
        zw3 = lambda kdt, ob: zw[:, 8192 + kdt * 1024 + ob * 128: 8192 + kdt * 1024 + (ob + 1) * 128]

        qs_zw = self.bump("qS", 32)

        @self.on("sy")
        def _(sy):
            for k, srcw in [(0, self.wb["ws1"]), (8192, self.wb["ws3"])]:
                sy.dma_start(out=zw[:, k: k + 8192].rearrange("q (kdt o) -> q kdt o", o=1024),
                             in_=srcw.rearrange("(kdt q) o -> q kdt o", q=128)
                             ).then_inc(h["qS"], 16)

        pe_zz = [None] * NDT
        te_z1_hist = []
        qg_zz = [None] * NDT
        for ob in range(NDT):
            te_z1 = self.bump("tE")
            te_z1_hist.append(te_z1)
            se_z1 = self.bump("sE")
            te_z3 = self.bump("tE")
            se_z3 = self.bump("sE")
            pe_zz[ob] = self.bump("pE")
            qg_zz[ob] = self.bump("qG", 16)

            @self.on("t")
            def _(t, _ob=ob, _q=qs_zw, _prev=pe_zz[ob - 1] if ob >= 1 else None):
                t.wait_ge(h["qS"], _q)
                if _prev is not None:
                    t.wait_ge(h["pE"], _prev)
                mmi = None
                for kdt in range(NDT):
                    for ch in range(4):
                        mmi = t.matmul(P[ch][:, 0:512], zw1(kdt, _ob),
                                       self.l3x_bf[:, kdt * L + ch * 512: kdt * L + (ch + 1) * 512],
                                       start=(kdt == 0), stop=(kdt == NDT - 1),
                                       skip_group_check=True)
                mmi.then_inc(h["tE"], 1)

            @self.on("s")
            def _(s, _ob=ob, _t=te_z1, _pz=(pe_zz[ob - 1] if ob >= 1 else None)):
                s.wait_ge(h["tE"], _t)
                if _pz is not None:
                    s.wait_ge(h["pE"], _pz)
                s.activation(z1s[:], self.PA[:], AF.Silu, bias=self.col("bs1", _ob), scale=1.0)
                s.drain()
                s.nop().then_inc(h["sE"], 1)

            @self.on("t")
            def _(t, _ob=ob):
                mmi = None
                for kdt in range(NDT):
                    for ch in range(4):
                        mmi = t.matmul(P[4 + ch][:, 0:512], zw3(kdt, _ob),
                                       self.l3x_bf[:, kdt * L + ch * 512: kdt * L + (ch + 1) * 512],
                                       start=(kdt == 0), stop=(kdt == NDT - 1),
                                       skip_group_check=True)
                mmi.then_inc(h["tE"], 1)

            @self.on("s")
            def _(s, _ob=ob, _t=te_z3, _pz=(pe_zz[ob - 1] if ob >= 1 else None)):
                s.wait_ge(h["tE"], _t)
                if _pz is not None:
                    s.wait_ge(h["pE"], _pz)
                s.activation(z3s[:], self.PB[:], AF.Identity, bias=self.col("bs3", _ob), scale=1.0)
                s.drain()
                s.nop().then_inc(h["sE"], 1)


            @self.on("g")
            def _(g, _ob=ob, _s3=se_z3, _qz=(qg_zz[ob - 2] if ob >= 2 else None)):
                g.wait_ge(h["sE"], _s3)
                if _qz is not None:
                    g.wait_ge(h["qG"], _qz)
                sl = (_ob % 2) * L
                g.tensor_tensor(zzw[:, sl: sl + L], z3s[:, 0:L], z1s[:, 0:L], ALU.mult)
                g.drain()
                g.nop().then_inc(h["pE"], 1)
                g.dma_start(out=self.zz_dram[_ob], in_=zzw[:, sl: sl + L]).then_inc(h["qG"], 16)
        self._z_state = (te_z1_hist, qg_zz)

    def phaseZ2(self, p, T):
        h, P = self.h, self.P
        zw, zzst = T["zw"], T["zzst"]
        y2f, y2t, x1r = T["y2f"], T["y2t"], T["x1r"]
        zw2 = lambda ob, odt: zw[:, ob * 1024 + odt * 128: ob * 1024 + (odt + 1) * 128]
        te_z1_hist, qg_zz = self._z_state

        qs_zw2 = self.bump("qS", 32)

        @self.on("sy")
        def _(sy, _t=te_z1_hist[NDT - 1]):
            sy.dma_start(out=self.be2_sb[:], in_=self.w["be2"]).then_inc(h["qS"], 16)
            sy.wait_ge(h["tE"], _t)
            sy.dma_start(out=zw[:, 0:8192].rearrange("q (kdt o) -> q kdt o", o=1024),
                         in_=self.wb["ws2"].rearrange("(kdt q) o -> q kdt o", q=128)
                         ).then_inc(h["qS"], 16)

        prev_ve_fin = None
        prev_qs_out = None
        te_z2_hist = []
        for ch in range(4):
            qs_zzst = self.bump("qS", 16)
            qs_x1 = self.bump("qS", 16)

            @self.on("sy")
            def _(sy, _ch=ch, _vf=prev_ve_fin, _qz=qg_zz[NDT - 1],
                  _tz=(te_z2_hist[ch - 1] if ch >= 1 else None)):
                sy.wait_ge(h["qG"], _qz)
                if _tz is not None:
                    sy.wait_ge(h["tE"], _tz)
                sy.dma_start(out=zzst[:, 0:4096]
                             .rearrange("q (ob o) -> q ob o", o=512),
                             in_=self.zz_dram.rearrange("ob q t -> q ob t")[:, :, _ch * 512:(_ch + 1) * 512]
                             ).then_inc(h["qS"], 16)
                if _vf is not None:
                    sy.wait_ge(h["vE"], _vf)
                sy.dma_start(out=x1r[:].rearrange("q (tt d) -> q tt d", d=D),
                             in_=self.x1_scr.rearrange("(cc tt q) d -> cc q tt d", q=128, tt=4)[_ch]
                             ).then_inc(h["qS"], 16)

            se_y2 = []
            ve_z2 = []
            for odt in range(NDT):
                te_z2 = self.bump("tE")
                ve_z2.append(self.bump("vE"))
                te_y2 = self.bump("tE")
                se_y2.append(self.bump("sE"))

                @self.on("t")
                def _(t, _ch=ch, _odt=odt, _vz=ve_z2[odt], _qw=qs_zw2, _qz=qs_zzst,
                      _vzprev=(ve_z2[odt - 2] if odt >= 2 else None),
                      _seprev=(se_y2[odt - 1] if odt >= 1 else None)):
                    if _odt == 0:
                        t.wait_ge(h["qS"], max(_qw, _qz))
                    if _vzprev is not None:
                        t.wait_ge(h["vE"], _vzprev)
                    mmi = None
                    for ob in range(NDT):
                        mmi = t.matmul(P[_odt % 2][:, 0:512], zw2(ob, _odt),
                                       zzst[:, ob * 512:(ob + 1) * 512],
                                       start=(ob == 0), stop=False, skip_group_check=True)
                    mmi = t.matmul(P[_odt % 2][:, 0:512],
                                   self.be2_sb[0:16, _odt * 128:(_odt + 1) * 128],
                                   self.combine_T[0:16, _ch * 512:(_ch + 1) * 512],
                                   start=False, stop=True, skip_group_check=True)
                    mmi.then_inc(h["tE"], 1)
                    t.wait_ge(h["vE"], _vz)
                    if _seprev is not None:
                        t.wait_ge(h["sE"], _seprev)
                    mmi = None
                    for tt in range(4):
                        mmi = t.transpose(P[2 + _odt % 2][0:128, tt * 128:(tt + 1) * 128],
                                          y2f[:, (_odt % 2) * 512 + tt * 128:(_odt % 2) * 512 + (tt + 1) * 128],
                                          self.ident[:])
                    mmi.then_inc(h["tE"], 1)

                @self.on("v")
                def _(v, _ch=ch, _odt=odt, _tz=te_z2):
                    v.wait_ge(h["tE"], _tz)
                    v.scalar_tensor_tensor(y2f[:, (_odt % 2) * 512:(_odt % 2) * 512 + 512],
                                           P[_odt % 2][:, 0:512], self.col("bs2", _odt),
                                           self.y_acc[:, _odt * L + _ch * 512: _odt * L + (_ch + 1) * 512],
                                           ALU.add, ALU.add)
                    v.drain()
                    v.nop().then_inc(h["vE"], 1)

                @self.on("s")
                def _(s, _odt=odt, _ty=te_y2, _qo=(prev_qs_out if odt == 0 else None)):
                    s.wait_ge(h["tE"], _ty)
                    if _qo is not None:
                        s.wait_ge(h["qS"], _qo)
                    s.copy(y2t[:].rearrange("q (tt d) -> q tt d", d=D)[:, :, _odt * 128:(_odt + 1) * 128],
                           P[2 + _odt % 2][:, 0:512].rearrange("q (tt d1) -> q tt d1", d1=128))
                    s.drain()
                    s.nop().then_inc(h["sE"], 1)
            te_z2_hist.append(self.c["tE"])

            @self.on("v")
            def _(v, _s=se_y2[NDT - 1]):
                v.wait_ge(h["sE"], _s)
            y2tile = lambda tt: y2t[:, tt * 1024:(tt + 1) * 1024]
            self.emit_stats([y2tile(tt) for tt in range(4)])
            self.emit_rstd()
            ve_fin = self.bump("vE")

            @self.on("v")
            def _(v, _q=qs_x1, _yt=y2tile):
                v.wait_ge(h["qS"], _q)
                for tt in range(4):
                    v.tensor_scalar(self.sq[:], _yt(tt), self.muc[:, tt:tt + 1],
                                    self.rstdc[:, tt:tt + 1], ALU.subtract, ALU.mult)
                    v.drain()
                    v.tensor_tensor(_yt(tt), self.sq[:],
                                    x1r[:, tt * 1024:(tt + 1) * 1024], ALU.add)
                    v.drain()
                v.nop().then_inc(h["vE"], 1)

            qs_out = self.bump("qS", 16)

            @self.on("sy")
            def _(sy, _ch=ch, _v=ve_fin, _p=p):
                sy.wait_ge(h["vE"], _v)
                sy.dma_start(out=self.out[_p].rearrange("(cc tt q) d -> cc q tt d", q=128, tt=4)[_ch],
                             in_=y2t[:].rearrange("q (tt d) -> q tt d", d=D)).then_inc(h["qS"], 16)
            prev_ve_fin = ve_fin
            prev_qs_out = qs_out

        if self.debug and "comb" in self.dbg:
            self.bump("qS", 16)

            @self.on("sy")
            def _(sy, _p=p):
                sy.dma_start(out=self.dbg["comb"][_p], in_=self.combine_T[:]).then_inc(h["qS"], 16)

        self.fence()

    def build(self):
        nc = self.nc
        self.declare()
        top = ExitStack()
        SB = lambda n, s, dt=F32: top.enter_context(nc.sbuf_tensor(n, s, dt))
        PSm = lambda n, s, dt=F32: top.enter_context(nc.psum_tensor(n, s, dt))
        with top:
            self.persistent(SB, PSm)
            self.sc_sb = SB("sc_sb", [16, 512])
            self.startup()
            for p in range(NB):
                sfx = f"_p{p}"
                ps = ExitStack()
                S2 = lambda n, s, dt=F32: ps.enter_context(nc.sbuf_tensor(n + sfx, s, dt))
                with ps:
                    self.l3x_bf = S2("l3x_bf", [128, NDT * L], BF16)
                    a1 = ExitStack()
                    S3 = lambda n, s, dt=F32: a1.enter_context(nc.sbuf_tensor(n + sfx, s, dt))
                    with a1, nc.Block() as blk:
                        TA = {
                            "xsl": S3("xsl", [128, 2 * 4096]),
                            "ln1": S3("ln1", [128, 4096]),
                            "l1f": S3("l1f", [128, NDT * 512]),
                            "l1b": S3("l1b", [128, NDT * 512], BF16),
                            "wgT": S3("wgT", [128, NDT * D], BF16),
                            "gt": S3("gt", [128, 2 * 512]),
                            "xgf": S3("xgf", [128, 2 * 512]),
                        }
                        self.sq = S3("sq1", [128, 1024])
                        if p == 0:
                            TA["tap"] = (S3("A_f", [128, NDT * 256]),
                                         S3("Bp_f", [128, NDT * NST]),
                                         S3("Cp_f", [128, NDT * NST]),
                                         S3("u1", [128, 2304]),
                                         S3("dg_st", [128, 2 * J * 128], BF16))
                        self.phaseA1(p, TA)
                        self.fence()
                        self.flush(blk)
                    a2 = ExitStack()
                    S4 = lambda n, s, dt=F32: a2.enter_context(nc.sbuf_tensor(n + sfx, s, dt))
                    with a2, nc.Block() as blk:
                        TB = {
                            "dg": S4("dg", [128, 2 * J * 128], BF16),
                            "xb": S4("xb", [128, 2 * (J + 1024)], BF16),
                            "xf": S4("xf", [128, 1024]),
                            "yF": S4("yF", [128, 2 * 1024]),
                            "ytp": S4("ytp", [128, 2 * 8 * 1024]),
                            "xre": S4("xre", [128, 4096]),
                            "ln3": S4("ln3", [128, 4096]),
                            "l3f": S4("l3f", [128, 2 * 512]),
                        }
                        self.sq = S4("sq2", [128, 1024])
                        self.rt = {nm: S4("rt_" + nm, [128, sz]) for nm, sz in self.RT_SIZES}
                        self.gamma_b = S4("gamma_b", [128, D])
                        self.beta_b = S4("beta_b", [128, D])
                        self.phaseA2(p, TB)
                        self.flush(blk)
                    if STAGE >= 3:
                        ya = ExitStack()
                        self.y_acc = ya.enter_context(nc.sbuf_tensor("y_acc" + sfx, [128, NDT * L], F32))
                        ps.enter_context(ya.pop_all() if False else ya)
                        b1 = ExitStack()
                        S5 = lambda n, s, dt=F32: b1.enter_context(nc.sbuf_tensor(n + sfx, s, dt))
                        with b1, nc.Block() as blk:
                            TC = {
                                "we": S5("we", [128, 2 * 12288], BF16),
                                "hbf": S5("hbf", [128, 4 * L], BF16),
                                "h1s": S5("h1s", [128, L]),
                                "h3s": S5("h3s", [128, L]),
                                "cb": S5("cb", [128, L]),
                            }
                            self.phaseB(p, TC)
                            self.flush(blk)
                        z0 = ExitStack()
                        S60 = lambda n, s, dt=F32: z0.enter_context(nc.sbuf_tensor(n + sfx, s, dt))
                        with z0:
                            zw = S60("zw", [128, 2 * 8192], BF16)
                            za = ExitStack()
                            S6 = lambda n, s, dt=F32: za.enter_context(nc.sbuf_tensor(n + sfx, s, dt))
                            with za, nc.Block() as blk:
                                TD = {
                                    "zw": zw,
                                    "z1s": S6("z1s", [128, L]),
                                    "z3s": S6("z3s", [128, L]),
                                    "zzw": S6("zzw", [128, 2 * L], BF16),
                                }
                                self.phaseZ(p, TD)
                                self.fence()
                                self.flush(blk)
                            zb = ExitStack()
                            S6 = lambda n, s, dt=F32: zb.enter_context(nc.sbuf_tensor(n + sfx, s, dt))
                            with zb, nc.Block() as blk:
                                TE2 = {
                                    "zw": zw,
                                    "zzst": S6("zzst", [128, 4096], BF16),
                                    "y2f": S6("y2f", [128, 2 * 512]),
                                    "y2t": S6("y2t", [128, 4096]),
                                    "x1r": S6("x1r", [128, 4096]),
                                }
                                self.sq = S6("sq3", [128, 1024])
                                self.be2_sb = S6("be2_sb", [E, D])
                                self.phaseZ2(p, TE2)
                                self.flush(blk)
            with nc.Block() as blk:
                @self.on("sy")
                def _(sy):
                    sy.wait_ge(self.h["qS"], self.c["qS"])
                    sy.wait_ge(self.h["qG"], self.c["qG"])
                self.flush(blk)
        return nc


def make_consts():
    sel = np.zeros((E, E * 128), np.float32)
    for e in range(E):
        sel[e, e * 128:(e + 1) * 128] = 1.0
    return {
        "c_ident": np.eye(128, dtype=np.float32),
        "c_sel": sel,
        "c_ones": np.ones((1, 128), np.float32),
    }


_CACHE = {}


def get_nc(debug=False):
    key = (STAGE, debug)
    if key not in _CACHE:
        b = B(debug=debug)
        nc = b.build()
        _CACHE[key] = (nc, b.dbg)
    return _CACHE[key]


def kernel(**inputs):
    nc, _ = get_nc(False)
    consts = make_consts()
    x = np.ascontiguousarray(inputs["L0_attn_x"])
    ren = {
        "A": "L0_attn_A", "Bp": "L0_attn_Bp", "Cp": "L0_attn_Cp", "Dp": "L0_attn_Dp",
        "Wg": "L0_attn_Wg", "bg": "L0_attn_bg", "gamma": "L0_attn_gamma",
        "beta": "L0_attn_beta", "gate_w": "L0_ffn_gate_w", "gate_b": "L0_ffn_gate_b",
        "We1": "L0_ffn_We1", "be1": "L0_ffn_be1", "We2": "L0_ffn_We2",
        "be2": "L0_ffn_be2", "We3": "L0_ffn_We3", "be3": "L0_ffn_be3",
        "ws1": "L0_ffn_ws1", "bs1": "L0_ffn_bs1", "ws2": "L0_ffn_ws2",
        "bs2": "L0_ffn_bs2", "ws3": "L0_ffn_ws3", "bs3": "L0_ffn_bs3",
    }
    base = {k: np.ascontiguousarray(inputs[v]) for k, v in ren.items()}
    base.update(consts)
    in_maps = []
    for c in range(NCORE):
        m = dict(base)
        m["x"] = np.ascontiguousarray(x[c * NB:(c + 1) * NB])
        in_maps.append(m)
    res = run_bass_kernel_spmd(nc, in_maps, core_ids=list(range(NCORE)))
    return np.concatenate([r["out"] for r in res.results], axis=0)


if __name__ == "__main__":
    b = B(debug=False)
    b.build()
    print("build ok, instructions:", b.nc.n_instructions())


# version 13
# speedup vs baseline: 1.0864x; 1.0014x over previous
import numpy as np
from contextlib import ExitStack

import concourse.bass as bass
import concourse.mybir as mybir
from concourse.bass_utils import run_bass_kernel_spmd

F32 = mybir.dt.float32
BF16 = mybir.dt.bfloat16
AF = mybir.ActivationFunctionType
ALU = mybir.AluOpType
AX = mybir.AxisListType

D, L, NB, NST, E, INTER = 1024, 2048, 2, 16, 16, 512
NCORE = 8
J = 28
NEG = -1e30
EPS = 1e-5
NDT, NCH = 8, 4
STAGE = 3


class B:
    def __init__(self, debug=False):
        self.debug = debug
        nc = self.nc = bass.Bass("TRN2", target_bir_lowering=False, debug=False)
        self.ops = {k: [] for k in ("sy", "g", "t", "v", "s")}
        self.h = {}
        self.c = {}
        for s in ("qS", "qG", "tE", "vE", "sE", "pE", "F"):
            self.h[s] = nc.alloc_semaphore(s)
            self.c[s] = 0

    def on(self, eng):
        def deco(fn):
            self.ops[eng].append(fn)
        return deco

    def bump(self, s, k=1):
        self.c[s] += k
        return self.c[s]

    def fence(self):
        tgt = self.c["F"] + 5
        hF = self.h["F"]
        qs_now = self.c["qS"]
        qg_now = self.c["qG"]
        for k in ("sy", "g", "t", "v", "s"):
            def go(e, _t=tgt, _k=k, _qs=qs_now, _qg=qg_now):
                if _k in ("v", "s", "g"):
                    e.drain()
                if _k == "sy":
                    e.wait_ge(self.h["qS"], _qs)
                if _k == "g":
                    e.wait_ge(self.h["qG"], _qg)
                e.nop().then_inc(hF, 1)
                e.wait_ge(hF, _t)
            self.ops[k].append(go)
        self.c["F"] = tgt

    def flush(self, block):
        ops = self.ops
        self.ops = {k: [] for k in ops}
        block.sync(lambda e, o=tuple(ops["sy"]): [f(e) for f in o])
        block.gpsimd(lambda e, o=tuple(ops["g"]): [f(e) for f in o])
        block.tensor(lambda e, o=tuple(ops["t"]): [f(e) for f in o])
        block.vector(lambda e, o=tuple(ops["v"]): [f(e) for f in o])
        block.scalar(lambda e, o=tuple(ops["s"]): [f(e) for f in o])

    def declare(self):
        nc = self.nc
        self.x_in = nc.dram_tensor("x", [NB, L, D], F32, kind="ExternalInput").ap()
        self.w = {}
        for name, shape in [
            ("A", [D, NST, NST]), ("Bp", [D, NST]), ("Cp", [D, NST]), ("Dp", [D]),
            ("Wg", [D, D]), ("bg", [D]), ("gamma", [D]), ("beta", [D]),
            ("gate_w", [E, D]), ("gate_b", [E]),
            ("We1", [E, D, INTER]), ("be1", [E, INTER]), ("We2", [E, INTER, D]),
            ("be2", [E, D]), ("We3", [E, D, INTER]), ("be3", [E, INTER]),
            ("ws1", [D, D]), ("bs1", [D]), ("ws2", [D, D]), ("bs2", [D]),
            ("ws3", [D, D]), ("bs3", [D]),
        ]:
            self.w[name] = nc.dram_tensor(name, shape, F32, kind="ExternalInput").ap()
        self.ident_in = nc.dram_tensor("c_ident", [128, 128], F32, kind="ExternalInput").ap()
        self.sel_in = nc.dram_tensor("c_sel", [E, E * 128], F32, kind="ExternalInput").ap()
        self.ones_in = nc.dram_tensor("c_ones", [1, 128], F32, kind="ExternalInput").ap()
        self.out = nc.dram_tensor("out", [NB, L, D], F32, kind="ExternalOutput").ap()
        self.dbg = {}
        if self.debug:
            for nm, shape in [("x1", [NB, L, D]), ("comb", [NB, E, L]),
                              ("ln3", [NB, L, D]), ("xg", [NB, NDT, 128, L]),
                              ("y", [NB, L, D]), ("ln1", [NB, L, D]),
                              ("gate", [NB, NDT, 128, L]), ("wgT", [D, D]),
                              ("diag", [NDT, 128, J * 128]), ("vcol", [128, 5 * NDT]),
                              ("hbuf", [128, NDT * J]), ("gammab", [128, D]),
                              ("rt", [20, 128, 64]), ("yacc", [NDT, 128, L]), ("ffn", [128, 4096])]:
                self.dbg[nm] = nc.dram_tensor("dbg_" + nm, shape, F32, kind="ExternalOutput").ap()
        dr = lambda n, s, dt=BF16: nc.dram_tensor(n, s, dt, kind="Internal").ap()
        self.wb = {
            "We1": dr("wb_we1", [E, D, INTER]), "We2": dr("wb_we2", [E, INTER, D]),
            "We3": dr("wb_we3", [E, D, INTER]), "ws1": dr("wb_ws1", [D, D]),
            "ws2": dr("wb_ws2", [D, D]), "ws3": dr("wb_ws3", [D, D]),
            "Wg": dr("wb_wg", [D, D]),
        }
        self.wgT_dram = dr("wgT_dram", [D, D])
        self.gb_dram = dr("gb_dram", [2, 128, D], F32)
        self.zz_dram = dr("zz_dram", [NDT, 128, L])
        self.diag_dram = dr("diag_dram", [NDT, 128, J * 128])
        self.xg_f = dr("xg_f", [NDT, 128, L], F32)
        self.xg_b = dr("xg_b", [NDT, 128, J + L])
        self.x1_scr = dr("x1_scr", [L, D], F32)
        self.yacc = dr("yacc", [NDT, 128, L], F32)

    def persistent(self, SB, PSm):
        self.ident = SB("ident", [128, 128])
        self.sel_sb = SB("sel_sb", [E, E * 128])
        self.ones_sb = SB("ones_sb", [1, 128])
        self.gate_wT = SB("gate_wT", [128, NDT * E])
        self.gb_bcast = SB("gb_bcast", [128, E])
        self.vcol = SB("vcol", [128, 5 * NDT])
        self.be1c = SB("be1c", [128, 4 * E])
        self.be3c = SB("be3c", [128, 4 * E])
        self.epscol = SB("epscol", [128, 1])
        self.combine_T = SB("combine_T", [E, L])
        self.hbuf = SB("hbuf", [128, NDT * J])
        self.zpad = SB("zpad", [128, J], BF16)
        self.s1c = SB("s1c", [128, 4]); self.s2c = SB("s2c", [128, 4])
        self.muc = SB("muc", [128, 4]); self.varc = SB("varc", [128, 4])
        self.rstdc = SB("rstdc", [128, 4]); self.tnc = SB("tnc", [128, 4])
        self.RT_SIZES = [
            ("m", 4), ("rec", 4), ("sum", 4), ("sd", 64), ("ex", 64), ("orig", 64),
            ("sc2", 64), ("m1", 16), ("eqa", 64), ("sgm", 64), ("m2", 16),
            ("gsc", 16), ("gm1", 4), ("eqg", 16), ("gsc2", 16), ("gm2", 4),
            ("keep", 16), ("smt1", 64), ("smt2", 64), ("sm", 64), ("t1", 4),
            ("eq1", 64), ("sm2", 64), ("t2", 4), ("eq2", 64), ("comb", 64)]
        self.PA = PSm("psA", [128, 2048])
        self.PB = PSm("psB", [128, 2048])
        self.P = [self.PA[:, i * 512:(i + 1) * 512] for i in range(4)] + \
                 [self.PB[:, i * 512:(i + 1) * 512] for i in range(4)]

    def col(self, which, dt):
        k = {"bg": 0, "Dp": 1, "bs1": 2, "bs2": 3, "bs3": 4}[which]
        return self.vcol[:, k * NDT + dt: k * NDT + dt + 1]

    def emit_rstd(self, n=4):
        h, c = self.h, self.c
        vE_stats = self.bump("vE")

        @self.on("v")
        def _(v):
            v.drain()
            v.nop().then_inc(h["vE"], 1)
        sE_rstd = self.bump("sE")

        @self.on("s")
        def _(s, _w=vE_stats):
            s.wait_ge(h["vE"], _w)
            s.activation(self.rstdc[:, 0:n], self.varc[:, 0:n], AF.Ln,
                         bias=self.epscol[:, 0:1], scale=1.0)
            s.drain()
            s.activation(self.rstdc[:, 0:n], self.rstdc[:, 0:n], AF.Exp, scale=-0.5)
            s.drain()
            s.nop().then_inc(h["sE"], 1)

        @self.on("v")
        def _(v, _w=sE_rstd):
            v.wait_ge(h["sE"], _w)
            rc, tn, vc = self.rstdc[:, 0:n], self.tnc[:, 0:n], self.varc[:, 0:n]
            v.tensor_tensor(tn, rc, rc, ALU.mult); v.drain()
            v.tensor_tensor(tn, tn, vc, ALU.mult); v.drain()
            v.tensor_scalar(tn, tn, -0.5, 1.5, ALU.mult, ALU.add); v.drain()
            v.tensor_tensor(rc, rc, tn, ALU.mult); v.drain()

    def emit_stats(self, tiles):
        @self.on("v")
        def _(v):
            for tt, tile in enumerate(tiles):
                v.tensor_reduce(self.s1c[:, tt:tt + 1], tile, AX.X, ALU.add)
                v.scalar_tensor_tensor(self.sq[:], tile, 1.0, tile, ALU.mult, ALU.mult,
                                       accum_out=self.s2c[:, tt:tt + 1])
            v.drain()
            v.tensor_scalar(self.muc[:], self.s1c[:], 1.0 / D, None, ALU.mult)
            v.tensor_scalar(self.varc[:], self.s2c[:], 1.0 / D, None, ALU.mult)
            v.drain()
            v.tensor_tensor(self.tnc[:], self.muc[:], self.muc[:], ALU.mult)
            v.drain()
            v.tensor_tensor(self.varc[:], self.varc[:], self.tnc[:], ALU.subtract)
            v.drain()

    def startup(self):
        nc, h, w = self.nc, self.h, self.w
        st = ExitStack()
        S = lambda n, s, dt=F32: st.enter_context(nc.sbuf_tensor(n, s, dt))
        vrow = S("vrow", [8, 5 * 128])
        grow = S("grow", [1, D])
        brow = S("brow", [1, D])
        gbrow = S("gbrow", [1, E])
        gwt_st = S("gwt_st", [16, D])
        be_st = S("be_st", [16, 2 * INTER])
        wg_st = S("wg_st", [128, 2 * D])
        wgT_st = S("wgT_st", [128, 2 * D], BF16)
        gbst = S("gbst", [128, 2 * D])
        P = self.P

        with st, nc.Block() as blk:
            @self.on("sy")
            def _(sy):
                for dst, src in [
                    (self.ident[:], self.ident_in), (self.ones_sb[:], self.ones_in),
                    (self.sel_sb[:], self.sel_in),
                    (grow[0:1, :], w["gamma"].rearrange("(a d) -> a d", a=1)),
                    (brow[0:1, :], w["beta"].rearrange("(a d) -> a d", a=1)),
                    (gbrow[:], w["gate_b"].rearrange("(a e) -> a e", a=1)),
                    (gwt_st[:], w["gate_w"]),
                    (vrow[0:8, 0:128], w["bg"].rearrange("(dt p) -> dt p", p=128)),
                    (vrow[0:8, 128:256], w["Dp"].rearrange("(dt p) -> dt p", p=128)),
                    (vrow[0:8, 256:384], w["bs1"].rearrange("(dt p) -> dt p", p=128)),
                    (vrow[0:8, 384:512], w["bs2"].rearrange("(dt p) -> dt p", p=128)),
                    (vrow[0:8, 512:640], w["bs3"].rearrange("(dt p) -> dt p", p=128)),
                    (be_st[:, 0:INTER], w["be1"]),
                    (be_st[:, INTER:], w["be3"]),
                ]:
                    sy.dma_start(out=dst, in_=src).then_inc(h["qS"], 16)
            qS_loads = self.bump("qS", 16 * 14)


            @self.on("v")
            def _(v):
                v.memset(self.epscol[:], EPS)
                v.memset(self.zpad[:], 0.0)
                v.drain()

            tE_sm = self.bump("tE")

            @self.on("t")
            def _(t, _w=qS_loads):
                t.wait_ge(h["qS"], _w)
                for k in range(5):
                    t.transpose(P[0][0:128, k * 8:(k + 1) * 8],
                                vrow[0:8, k * 128:(k + 1) * 128], self.ident[0:8, 0:8])
                t.matmul(P[1][:, 0:512], self.ones_sb[:, 0:128], grow[0:1, 0:512], start=True, stop=True)
                t.matmul(P[2][:, 0:512], self.ones_sb[:, 0:128], grow[0:1, 512:1024], start=True, stop=True)
                t.matmul(P[3][:, 0:512], self.ones_sb[:, 0:128], brow[0:1, 0:512], start=True, stop=True)
                t.matmul(P[4][:, 0:512], self.ones_sb[:, 0:128], brow[0:1, 512:1024], start=True, stop=True)
                t.matmul(P[5][:, 0:E], self.ones_sb[:, 0:128], gbrow[0:1, :], start=True, stop=True)
                for dt in range(NDT):
                    t.transpose(P[6][0:128, dt * 16:(dt + 1) * 16],
                                gwt_st[0:16, dt * 128:(dt + 1) * 128], self.ident[0:16, 0:16])
                mmi = None
                for ib in range(4):
                    t.transpose(P[7][0:128, ib * 16:(ib + 1) * 16],
                                be_st[0:16, ib * 128:(ib + 1) * 128], self.ident[0:16, 0:16])
                    mmi = t.transpose(P[7][0:128, 64 + ib * 16: 64 + (ib + 1) * 16],
                                      be_st[0:16, INTER + ib * 128: INTER + (ib + 1) * 128],
                                      self.ident[0:16, 0:16])
                mmi.then_inc(h["tE"], 1)

            sE_sm = self.bump("sE")
            self._sE_sm = sE_sm

            @self.on("s")
            def _(s, _w=tE_sm):
                s.wait_ge(h["tE"], _w)
                s.copy(self.vcol[:, 0:40], P[0][:, 0:40])
                s.copy(gbst[:, 0:512], P[1][:, 0:512])
                s.copy(gbst[:, 512:1024], P[2][:, 0:512])
                s.copy(gbst[:, 1024:1536], P[3][:, 0:512])
                s.copy(gbst[:, 1536:2048], P[4][:, 0:512])
                s.copy(self.gb_bcast[:], P[5][:, 0:E])
                s.copy(self.gate_wT[:], P[6][:, 0:NDT * E])
                s.copy(self.be1c[:], P[7][:, 0:64])
                s.copy(self.be3c[:], P[7][:, 64:128])
                s.drain()
                s.nop().then_inc(h["sE"], 1)

            qG_gb = self.bump("qG", 16)

            @self.on("g")
            def _(g, _w=sE_sm):
                g.wait_ge(h["sE"], _w)
                g.dma_start(out=self.gb_dram.rearrange("a q d2 -> q a d2"),
                            in_=gbst[:].rearrange("q (a d2) -> q a d2", d2=D)).then_inc(h["qG"], 16)

            qS_wg0 = self.c["qS"] + 16
            self.bump("qS", 16 * NDT)
            tE_wg0 = self.c["tE"] + 1
            self.bump("tE", NDT)
            sE_wg0 = self.c["sE"] + 1
            self.bump("sE", NDT)
            qG_wgT0 = self.c["qG"] + 16
            self.bump("qG", 16 * NDT)

            @self.on("sy")
            def _(sy):
                for odt in range(NDT):
                    if odt >= 2:
                        sy.wait_ge(h["tE"], tE_wg0 + odt - 2)
                    sy.dma_start(out=wg_st[:, (odt % 2) * D:(odt % 2) * D + D],
                                 in_=w["Wg"].rearrange("(odt q) d2 -> odt q d2", q=128)[odt]
                                 ).then_inc(h["qS"], 16)

            @self.on("t")
            def _(t, _s=sE_sm):
                t.wait_ge(h["sE"], _s)
                for odt in range(NDT):
                    t.wait_ge(h["qS"], qS_wg0 + odt * 16)
                    if odt >= 1:
                        t.wait_ge(h["sE"], sE_wg0 + odt - 1)
                    b0, b1 = 2 * (odt % 2), 2 * (odt % 2) + 1
                    mmi = None
                    for kdt in range(NDT):
                        pb = P[b0] if kdt < 4 else P[b1]
                        mmi = t.transpose(pb[0:128, (kdt % 4) * 128:(kdt % 4 + 1) * 128],
                                          wg_st[:, (odt % 2) * D + kdt * 128:(odt % 2) * D + (kdt + 1) * 128],
                                          self.ident[:])
                    mmi.then_inc(h["tE"], 1)

            @self.on("s")
            def _(s):
                for odt in range(NDT):
                    s.wait_ge(h["tE"], tE_wg0 + odt)
                    if odt >= 2:
                        s.wait_ge(h["qG"], qG_wgT0 + 16 * (odt - 2))
                    b0, b1 = 2 * (odt % 2), 2 * (odt % 2) + 1
                    s.copy(wgT_st[:, (odt % 2) * D:(odt % 2) * D + 512], P[b0][:, 0:512])
                    s.copy(wgT_st[:, (odt % 2) * D + 512:(odt % 2) * D + D], P[b1][:, 0:512])
                    s.drain()
                    s.nop().then_inc(h["sE"], 1)

            @self.on("g")
            def _(g):
                for odt in range(NDT):
                    g.wait_ge(h["sE"], sE_wg0 + odt)
                    g.dma_start(out=self.wgT_dram.rearrange("(kdt p) e2 -> p kdt e2", p=128)[:, :, odt * 128:(odt + 1) * 128],
                                in_=wgT_st[:, (odt % 2) * D:(odt % 2) * D + D]
                                .rearrange("p (kdt q) -> p kdt q", q=128)).then_inc(h["qG"], 16)

            if self.debug and "wgT" in self.dbg:
                self.bump("qG", 80)

                @self.on("g")
                def _(g, _q=self.c["qG"] - 80):
                    g.wait_ge(h["qG"], _q)
                    g.dma_start(out=self.dbg["wgT"], in_=self.wgT_dram).then_inc(h["qG"], 16)
                    g.dma_start(out=self.dbg["diag"], in_=self.diag_dram).then_inc(h["qG"], 16)
                    g.dma_start(out=self.dbg["vcol"], in_=self.vcol[:]).then_inc(h["qG"], 16)
                    g.dma_start(out=self.dbg["hbuf"], in_=self.hbuf[:]).then_inc(h["qG"], 16)
                    g.dma_start(out=self.dbg["gammab"], in_=self.gb_dram[0]).then_inc(h["qG"], 16)

            self.fence()
            self.flush(blk)

    def emit_taps(self, A_f, Bp_f, Cp_f, u1, dg_st, qs_ld, se_sm):
        h = self.h
        vE_tap = self.bump("vE")

        @self.on("v")
        def _(v, _w=qs_ld, _ws=se_sm):
            v.wait_ge(h["qS"], _w)
            v.wait_ge(h["sE"], _ws)
            A3 = A_f[:].rearrange("p (dt n m) -> p (dt n) m", n=NST, m=NST)
            hb3 = lambda j: (self.hbuf[:].rearrange("p (dt jj) -> p dt jj", jj=J)
                             [:, :, j:j + 1].rearrange("p dt one -> p (dt one)"))
            v.tensor_copy(u1[:, 0:128], Bp_f[:])
            v.drain()
            for j in range(J):
                v.tensor_tensor(u1[:, 128:256], u1[:, 0:128], Cp_f[:], ALU.mult)
                v.drain()
                v.tensor_reduce(hb3(j), u1[:, 128:256].rearrange("p (dt n) -> p dt n", n=NST),
                                AX.X, ALU.add)
                v.drain()
                if j < J - 1:
                    v.tensor_tensor(u1[:, 256:2304].rearrange("p (dtn m) -> p dtn m", m=NST),
                                    A3, u1[:, 0:128].broadcast_to([128, 128, 16]), ALU.mult)
                    v.drain()
                    for half in range(2):
                        v.tensor_reduce(u1[:, half * 64: half * 64 + 64]
                                        .rearrange("p (dt m) -> p dt m", m=NST),
                                        u1[:, 256 + half * 1024: 256 + half * 1024 + 1024]
                                        .rearrange("p (dt n m) -> p dt m n", n=NST, m=NST),
                                        AX.X, ALU.add)
                    v.drain()
            v.tensor_tensor(hb3(0), hb3(0), self.vcol[:, NDT:2 * NDT], ALU.add)
            v.drain()
            v.nop().then_inc(h["vE"], 1)

        vE_dg0 = self.c["vE"] + 1
        self.bump("vE", NDT)
        qG_dg0 = self.c["qG"] + 16
        self.bump("qG", 16 * NDT + 16 * NDT)

        @self.on("v")
        def _(v, _w=vE_tap):
            v.wait_ge(h["vE"], _w)
            for dt in range(NDT):
                sl = (dt % 2) * J * 128
                if dt >= 2:
                    v.wait_ge(h["qG"], qG_dg0 + 16 * (dt - 2))
                for j in range(J):
                    v.tensor_scalar(dg_st[:, sl + j * 128: sl + (j + 1) * 128],
                                    self.ident[:],
                                    self.hbuf[:, dt * J + j: dt * J + j + 1],
                                    None, ALU.mult)
                v.drain()
                v.nop().then_inc(h["vE"], 1)

        @self.on("g")
        def _(g):
            for dt in range(NDT):
                g.wait_ge(h["vE"], vE_dg0 + dt)
                sl = (dt % 2) * J * 128
                g.dma_start(out=self.diag_dram[dt],
                            in_=dg_st[:, sl: sl + J * 128]).then_inc(h["qG"], 16)
            for dt in range(NDT):
                g.dma_start(out=self.xg_b[dt, :, 0:J], in_=self.zpad[:]).then_inc(h["qG"], 16)

    def phaseA1(self, p, T):
        h, P = self.h, self.P
        xsl, ln1 = T["xsl"], T["ln1"]
        l1f, l1b = T["l1f"], T["l1b"]
        wgT, gt, xgf = T["wgT"], T["gt"], T["xgf"]

        qs_wgT = self.bump("qS", 16)

        @self.on("sy")
        def _(sy):
            sy.dma_start(out=wgT[:].rearrange("pp (kdt e2) -> pp kdt e2", e2=D),
                         in_=self.wgT_dram.rearrange("(kdt pp) e2 -> pp kdt e2", pp=128)
                         ).then_inc(h["qS"], 16)

        ve_ln1 = [None] * NCH
        qsx = [None] * NCH
        for c in range(NCH):
            qsx[c] = self.bump("qS", 16)

            @self.on("sy")
            def _(sy, _c=c):
                if _c >= 2:
                    sy.wait_ge(h["vE"], ve_ln1[_c - 2])
                sy.dma_start(out=xsl[:, (_c % 2) * 4096:(_c % 2) * 4096 + 4096]
                             .rearrange("pp (tt d) -> pp tt d", d=D),
                             in_=self.x_in[p].rearrange("(cc tt q) d -> cc q tt d", q=128, tt=4)[_c]
                             ).then_inc(h["qS"], 16)

            xtile = lambda tt, _c=c: xsl[:, (_c % 2) * 4096 + tt * 1024:(_c % 2) * 4096 + (tt + 1) * 1024]

            @self.on("v")
            def _(v, _c=c, _q=qsx[c]):
                v.wait_ge(h["qS"], _q)
            self.emit_stats([xtile(tt) for tt in range(4)])
            self.emit_rstd()
            ve_ln1[c] = self.bump("vE")

            @self.on("v")
            def _(v, _xt=xtile):
                for tt in range(4):
                    v.tensor_scalar(ln1[:, tt * 1024:(tt + 1) * 1024], _xt(tt),
                                    self.muc[:, tt:tt + 1], self.rstdc[:, tt:tt + 1],
                                    ALU.subtract, ALU.mult)
                v.drain()
                v.nop().then_inc(h["vE"], 1)

            if self.debug and "ln1" in self.dbg:
                self.bump("qS", 16)

                @self.on("sy")
                def _(sy, _c=c, _v=ve_ln1[c] + 1, _p=p):
                    sy.wait_ge(h["vE"], _v)
                    sy.dma_start(out=self.dbg["ln1"][_p].rearrange("(cc tt q) d -> cc q tt d", q=128, tt=4)[_c],
                                 in_=ln1[:].rearrange("q (tt d) -> q tt d", d=D)).then_inc(h["qS"], 16)

            te_tp = [self.bump("tE") for _ in range(NDT)]
            se_tp = [self.bump("sE") for _ in range(NDT)]

            @self.on("t")
            def _(t, _c=c, _v=ve_ln1[c], _te=tuple(te_tp), _se=tuple(se_tp)):
                t.wait_ge(h["vE"], _v)
                for dt in range(NDT):
                    if dt >= 2:
                        t.wait_ge(h["sE"], _se[dt - 2])
                    mmi = None
                    for tt in range(4):
                        mmi = t.transpose(P[dt % 2][0:128, tt * 128:(tt + 1) * 128],
                                          ln1[:, tt * 1024 + dt * 128: tt * 1024 + (dt + 1) * 128],
                                          self.ident[:])
                    mmi.then_inc(h["tE"], 1)

            @self.on("s")
            def _(s, _te=tuple(te_tp)):
                for dt in range(NDT):
                    s.wait_ge(h["tE"], _te[dt])
                    s.copy(l1b[:, dt * 512:(dt + 1) * 512], P[dt % 2][:, 0:512])
                    s.copy(l1f[:, dt * 512:(dt + 1) * 512], P[dt % 2][:, 0:512])
                    s.drain()
                    s.nop().then_inc(h["sE"], 1)

            te_g = [self.bump("tE") for _ in range(NDT)]
            se_g = [self.bump("sE") for _ in range(NDT)]
            ve_xg = [self.bump("vE") for _ in range(NDT)]
            qs_xgf = [self.bump("qS", 16) for _ in range(NDT)]
            qg_xgb = [self.bump("qG", 16) for _ in range(NDT)]

            @self.on("t")
            def _(t, _c=c, _sall=se_tp[NDT - 1], _se=tuple(se_g), _q=qs_wgT):
                t.wait_ge(h["sE"], _sall)
                t.wait_ge(h["qS"], _q)
                for odt in range(NDT):
                    if odt >= 2:
                        t.wait_ge(h["sE"], _se[odt - 2])
                    mmi = None
                    for kdt in range(NDT):
                        mmi = t.matmul(P[2 + odt % 2][:, 0:512],
                                       wgT[:, kdt * D + odt * 128: kdt * D + (odt + 1) * 128],
                                       l1b[:, kdt * 512:(kdt + 1) * 512],
                                       start=(kdt == 0), stop=(kdt == NDT - 1))
                    mmi.then_inc(h["tE"], 1)

            @self.on("s")
            def _(s, _te=tuple(te_g), _ve=tuple(ve_xg)):
                for odt in range(NDT):
                    s.wait_ge(h["tE"], _te[odt])
                    if odt >= 2:
                        s.wait_ge(h["vE"], _ve[odt - 2])
                    s.activation(gt[:, (odt % 2) * 512:(odt % 2) * 512 + 512],
                                 P[2 + odt % 2][:, 0:512], AF.Sigmoid,
                                 bias=self.col("bg", odt), scale=1.0)
                    s.drain()
                    s.nop().then_inc(h["sE"], 1)

            @self.on("v")
            def _(v, _se=tuple(se_g), _qsx=tuple(qs_xgf), _qgx=tuple(qg_xgb)):
                for odt in range(NDT):
                    v.wait_ge(h["sE"], _se[odt])
                    if odt >= 2:
                        v.wait_ge(h["qS"], _qsx[odt - 2])
                        v.wait_ge(h["qG"], _qgx[odt - 2])
                    v.tensor_tensor(xgf[:, (odt % 2) * 512:(odt % 2) * 512 + 512],
                                    l1f[:, odt * 512:(odt + 1) * 512],
                                    gt[:, (odt % 2) * 512:(odt % 2) * 512 + 512], ALU.mult)
                    v.drain()
                    v.nop().then_inc(h["vE"], 1)

            @self.on("sy")
            def _(sy, _c=c, _ve=tuple(ve_xg)):
                for odt in range(NDT):
                    sy.wait_ge(h["vE"], _ve[odt])
                    sy.dma_start(out=self.xg_f[odt, :, _c * 512:(_c + 1) * 512],
                                 in_=xgf[:, (odt % 2) * 512:(odt % 2) * 512 + 512]
                                 ).then_inc(h["qS"], 16)

            @self.on("g")
            def _(g, _c=c, _ve=tuple(ve_xg)):
                for odt in range(NDT):
                    g.wait_ge(h["vE"], _ve[odt])
                    g.dma_start(out=self.xg_b[odt, :, J + _c * 512: J + (_c + 1) * 512],
                                in_=xgf[:, (odt % 2) * 512:(odt % 2) * 512 + 512]
                                ).then_inc(h["qG"], 16)
            if self.debug and "gate" in self.dbg:
                self.bump("qG", 16 * NDT)

                @self.on("g")
                def _(g, _c=c, _se=tuple(se_g), _p=p):
                    for odt in range(NDT):
                        g.wait_ge(h["sE"], _se[odt])
                        g.dma_start(out=self.dbg["gate"][_p, odt, :, _c * 512:(_c + 1) * 512],
                                    in_=gt[:, (odt % 2) * 512:(odt % 2) * 512 + 512]
                                    ).then_inc(h["qG"], 16)

        if p == 0:
            A_f, Bp_f, Cp_f, u1, dg_st = T["tap"]
            qs_abc = self.bump("qS", 48)

            @self.on("sy")
            def _(sy):
                sy.dma_start(out=A_f[:].rearrange("p (dt nm) -> p dt nm", nm=256),
                             in_=self.w["A"].rearrange("(dt p) n m -> p dt (n m)", p=128)
                             ).then_inc(h["qS"], 16)
                sy.dma_start(out=Bp_f[:].rearrange("p (dt n) -> p dt n", n=NST),
                             in_=self.w["Bp"].rearrange("(dt p) n -> p dt n", p=128)
                             ).then_inc(h["qS"], 16)
                sy.dma_start(out=Cp_f[:].rearrange("p (dt n) -> p dt n", n=NST),
                             in_=self.w["Cp"].rearrange("(dt p) n -> p dt n", p=128)
                             ).then_inc(h["qS"], 16)

            self.emit_taps(A_f, Bp_f, Cp_f, u1, dg_st, qs_abc, self._sE_sm)
            self.qG_casts = self.bump("qG", 16 * 6)

            @self.on("g")
            def _(g):
                for dst, src in [(self.wb["We1"], self.w["We1"]), (self.wb["We3"], self.w["We3"]),
                                 (self.wb["We2"], self.w["We2"]),
                                 (self.wb["ws1"], self.w["ws1"]), (self.wb["ws3"], self.w["ws3"]),
                                 (self.wb["ws2"], self.w["ws2"])]:
                    g.dma_start(out=dst, in_=src).then_inc(h["qG"], 16)

    def phaseA2(self, p, T):
        h, P = self.h, self.P
        dg, xb, xf = T["dg"], T["xb"], T["xf"]
        yF, ytp, xre, ln3 = T["yF"], T["ytp"], T["xre"], T["ln3"]
        l3f = T["l3f"]
        W = J + 1024

        qs_gb = self.bump("qS", 16)

        @self.on("sy")
        def _(sy):
            sy.dma_start(out=self.gamma_b[:], in_=self.gb_dram[0]).then_inc(h["qS"], 16)

        qs_gb2 = self.bump("qS", 16)

        @self.on("sy")
        def _(sy):
            sy.dma_start(out=self.beta_b[:], in_=self.gb_dram[1]).then_inc(h["qS"], 16)
        self._qs_gb = qs_gb2

        conv_hist = []
        ytr_hist = []
        join_hist = []

        seyt_hist = []
        self._a2_prev_ve_x1 = None

        def chain_gen(pr_, cc_):
            c = 2 * pr_ + cc_
            ytile = lambda tt, _cc=cc_, _pr=pr_: ytp[:, (_pr % 2) * 8192 + (_cc * 4 + tt) * 1024:(_pr % 2) * 8192 + (_cc * 4 + tt + 1) * 1024]
            qg_xre = self.bump("qG", 16)

            @self.on("g")
            def _(g, _c=c, _vx=self._a2_prev_ve_x1):
                if _vx is not None:
                    g.wait_ge(h["vE"], _vx)
                g.dma_start(out=xre[:].rearrange("q (tt d) -> q tt d", d=D),
                            in_=self.x_in[p].rearrange("(ch tt q) d -> ch q tt d", q=128, tt=4)[_c]
                            ).then_inc(h["qG"], 16)

            @self.on("v")
            def _(v, _s=seyt_hist[pr_], _q=self._qs_gb):
                v.wait_ge(h["sE"], _s)
                v.wait_ge(h["qS"], _q)
            self.emit_stats([ytile(tt) for tt in range(4)])
            yield
            self.emit_rstd()

            def attn_apply(tts):
                @self.on("v")
                def _(v, _yt=ytile, _tts=tuple(tts)):
                    for tt in _tts:
                        tile = _yt(tt)
                        v.tensor_scalar(self.sq[:], tile, self.muc[:, tt:tt + 1],
                                        self.rstdc[:, tt:tt + 1], ALU.subtract, ALU.mult)
                        v.drain()
                        v.tensor_tensor(self.sq[:], self.sq[:], self.gamma_b[:], ALU.mult)
                        v.drain()
                        v.scalar_tensor_tensor(tile, self.sq[:], 1.0, self.beta_b[:],
                                               ALU.mult, ALU.add,
                                               accum_out=self.s1c[:, tt:tt + 1])
                        v.drain()
                        v.scalar_tensor_tensor(self.sq[:], tile, 1.0, tile,
                                               ALU.mult, ALU.mult,
                                               accum_out=self.s2c[:, tt:tt + 1])
                        v.drain()
            attn_apply([0, 1])
            yield
            attn_apply([2, 3])

            @self.on("v")
            def _(v):
                v.tensor_scalar(self.muc[:], self.s1c[:], 1.0 / D, None, ALU.mult)
                v.tensor_scalar(self.varc[:], self.s2c[:], 1.0 / D, None, ALU.mult)
                v.drain()
                v.tensor_tensor(self.tnc[:], self.muc[:], self.muc[:], ALU.mult)
                v.drain()
                v.tensor_tensor(self.varc[:], self.varc[:], self.tnc[:], ALU.subtract)
                v.drain()
            self.emit_rstd()
            yield

            def ln2_apply(tts, first):
                @self.on("v")
                def _(v, _yt=ytile, _tts=tuple(tts), _first=first, _q=qg_xre):
                    if _first:
                        v.wait_ge(h["qG"], _q)
                    for tt in _tts:
                        tile = _yt(tt)
                        v.tensor_scalar(self.sq[:], tile, self.muc[:, tt:tt + 1],
                                        self.rstdc[:, tt:tt + 1], ALU.subtract, ALU.mult)
                        v.drain()
                        v.tensor_tensor(tile, self.sq[:],
                                        xre[:, tt * 1024:(tt + 1) * 1024],
                                        ALU.add)
                        v.drain()
                        v.tensor_reduce(self.s1c[:, tt:tt + 1], tile, AX.X, ALU.add)
                        v.scalar_tensor_tensor(self.sq[:], tile, 1.0, tile, ALU.mult, ALU.mult,
                                               accum_out=self.s2c[:, tt:tt + 1])
                    v.drain()
            ln2_apply([0, 1], True)
            yield
            ln2_apply([2, 3], False)
            ve_x1 = self.bump("vE")

            @self.on("v")
            def _(v):
                v.tensor_scalar(self.muc[:], self.s1c[:], 1.0 / D, None, ALU.mult)
                v.tensor_scalar(self.varc[:], self.s2c[:], 1.0 / D, None, ALU.mult)
                v.drain()
                v.tensor_tensor(self.tnc[:], self.muc[:], self.muc[:], ALU.mult)
                v.drain()
                v.tensor_tensor(self.varc[:], self.varc[:], self.tnc[:], ALU.subtract)
                v.drain()
                v.nop().then_inc(h["vE"], 1)
            self.emit_rstd()
            self._a2_prev_ve_x1 = ve_x1
            self.bump("qG", 16 * (2 if STAGE == 1 else 1))

            @self.on("g")
            def _(g, _c=c, _cc=cc_, _v=ve_x1, _pr2=pr_ % 2):
                g.wait_ge(h["vE"], _v)
                g.dma_start(out=self.x1_scr.rearrange("(ch tt q) d -> ch q tt d", q=128, tt=4)[_c],
                            in_=ytp[:, _pr2 * 8192 + _cc * 4096: _pr2 * 8192 + (_cc + 1) * 4096]
                            .rearrange("q (tt d) -> q tt d", d=D)).then_inc(h["qG"], 16)
                if STAGE == 1:
                    g.dma_start(out=self.out[p].rearrange("(ch tt q) d -> ch q tt d", q=128, tt=4)[_c],
                                in_=ytp[:, _pr2 * 8192 + _cc * 4096: _pr2 * 8192 + (_cc + 1) * 4096]
                                .rearrange("q (tt d) -> q tt d", d=D)).then_inc(h["qG"], 16)
            yield
            if cc_ == 0:
                ve_ln3 = self.bump("vE")

                @self.on("v")
                def _(v, _yt=ytile):
                    for tt in range(4):
                        v.tensor_scalar(ln3[:, tt * 1024:(tt + 1) * 1024], _yt(tt),
                                        self.muc[:, tt:tt + 1], self.rstdc[:, tt:tt + 1],
                                        ALU.subtract, ALU.mult)
                    v.drain()
                    v.nop().then_inc(h["vE"], 1)
                self._ve_ln3_c0 = ve_ln3

        def drive(it, n):
            k = 0
            while k < n:
                try:
                    next(it)
                except StopIteration:
                    return
                k += 1

        self._chain_it = None
        for pr in range(2):
            c0 = 2 * pr
            se_yt = []
            if pr == 1:
                def _chain2():
                    yield from chain_gen(0, 0)
                    yield from chain_gen(0, 1)
                self._chain_it = _chain2()
            for dt in range(NDT):
                gi = pr * NDT + dt
                qs_dg = self.bump("qS", 48)

                @self.on("sy")
                def _(sy, _dt=dt, _c0=c0, _gi=gi,
                      _g=(conv_hist[gi - 2] if gi >= 2 else None),
                      _gj=(join_hist[gi - 1] if gi >= 1 else None)):
                    if _g is not None:
                        sy.wait_ge(h["tE"], _g)
                    if _gj is not None:
                        sy.wait_ge(h["vE"], _gj)
                    sl = _dt % 2
                    sy.dma_start(out=dg[:, sl * J * 128:(sl + 1) * J * 128],
                                 in_=self.diag_dram[_dt]).then_inc(h["qS"], 16)
                    sy.dma_start(out=xb[:, sl * W: sl * W + W],
                                 in_=self.xg_b[_dt, :, _c0 * 512: _c0 * 512 + W]
                                 ).then_inc(h["qS"], 16)
                    sy.dma_start(out=xf[:],
                                 in_=self.xg_f[_dt, :, _c0 * 512: _c0 * 512 + 1024]
                                 ).then_inc(h["qS"], 16)

                te_conv = self.bump("tE")
                conv_hist.append(te_conv)
                ve_join = self.bump("vE")
                join_hist.append(ve_join)
                te_yt = self.bump("tE")
                ytr_hist.append(te_yt)
                se_yt.append(self.bump("sE"))

                @self.on("t")
                def _(t, _dt=dt, _q=qs_dg, _gi=gi,
                      _vjprev=(join_hist[gi - 1] if gi >= 1 else None),
                      _vj=ve_join,
                      _seprev=(se_yt[dt - 1] if dt >= 1 else None),
                      _se_prev_pair=(self._seyt_last if pr == 1 and dt == 0 else None)):
                    sl = _dt % 2
                    t.wait_ge(h["qS"], _q)
                    if _vjprev is not None:
                        t.wait_ge(h["vE"], _vjprev)
                    mmi = None
                    for j in range(J):
                        for cc in range(2):
                            mmi = t.matmul(P[cc][:, 0:512],
                                           dg[:, sl * J * 128 + j * 128: sl * J * 128 + (j + 1) * 128],
                                           xb[:, sl * W + J + cc * 512 - j: sl * W + J + cc * 512 - j + 512],
                                           start=(j == 0), stop=(j == J - 1),
                                           skip_group_check=True)
                    mmi.then_inc(h["tE"], 1)
                    t.wait_ge(h["vE"], _vj)
                    if _seprev is not None:
                        t.wait_ge(h["sE"], _seprev)
                    if _se_prev_pair is not None:
                        t.wait_ge(h["sE"], _se_prev_pair)
                    mmi = None
                    for wd in range(8):
                        pb = P[2] if wd < 4 else P[3]
                        mmi = t.transpose(pb[0:128, (wd % 4) * 128:(wd % 4 + 1) * 128],
                                          yF[:, sl * 1024 + wd * 128: sl * 1024 + (wd + 1) * 128],
                                          self.ident[:])
                    mmi.then_inc(h["tE"], 1)

                @self.on("v")
                def _(v, _dt=dt, _tc=te_conv, _gi=gi,
                      _typrev=(ytr_hist[gi - 2] if gi >= 2 else None)):
                    v.wait_ge(h["tE"], _tc)
                    if _typrev is not None:
                        v.wait_ge(h["tE"], _typrev)
                    sl = _dt % 2
                    v.tensor_tensor(yF[:, sl * 1024: sl * 1024 + 1024],
                                    self.PA[:, 0:1024], xf[:], ALU.add)
                    v.drain()
                    v.nop().then_inc(h["vE"], 1)

                @self.on("s")
                def _(s, _dt=dt, _ty=te_yt, _pr=pr):
                    s.wait_ge(h["tE"], _ty)
                    s.copy(ytp[:, (_pr % 2) * 8192:(_pr % 2) * 8192 + 8192].rearrange("q (wd d1) -> q wd d1", d1=1024)[:, 0:4, _dt * 128:(_dt + 1) * 128],
                           P[2][:, 0:512].rearrange("q (wd d1) -> q wd d1", d1=128))
                    s.copy(ytp[:, (_pr % 2) * 8192:(_pr % 2) * 8192 + 8192].rearrange("q (wd d1) -> q wd d1", d1=1024)[:, 4:8, _dt * 128:(_dt + 1) * 128],
                           P[3][:, 0:512].rearrange("q (wd d1) -> q wd d1", d1=128))
                    s.drain()
                    s.nop().then_inc(h["sE"], 1)

                if pr == 1:
                    drive(self._chain_it, 1)

            if pr == 1:
                drive(self._chain_it, 99)
            seyt_hist.append(se_yt[NDT - 1])
            self._seyt_last = se_yt[NDT - 1]

            if self.debug and "y" in self.dbg:
                self.bump("qS", 16)

                @self.on("sy")
                def _(sy, _pr=pr, _s=se_yt[NDT - 1], _p=p):
                    sy.wait_ge(h["sE"], _s)
                    sy.dma_start(out=self.dbg["y"][_p].rearrange("(q2 tt q) d -> q2 q tt d", q=128, tt=8)[_pr],
                                 in_=ytp[:, (_pr % 2) * 8192:(_pr % 2) * 8192 + 8192].rearrange("q (tt d) -> q tt d", d=D)).then_inc(h["qS"], 16)
            if self.debug and "xg" in self.dbg and pr == 0:
                self.bump("qG", 16 * NDT)

                @self.on("g")
                def _(g, _p=p, _q=self.c["qS"]):
                    g.wait_ge(h["qS"], _q)
                    for dt in range(NDT):
                        g.dma_start(out=self.dbg["xg"][_p, dt], in_=self.xg_f[dt]).then_inc(h["qG"], 16)

        for pr in range(2):
            c0 = 2 * pr
            for cc in range(2):
                c = c0 + cc
                ytile = lambda tt, _cc=cc, _pr=pr: ytp[:, (_pr % 2) * 8192 + (_cc * 4 + tt) * 1024:(_pr % 2) * 8192 + (_cc * 4 + tt + 1) * 1024]

                if pr == 0 and cc == 0:
                    ve_ln3 = self._ve_ln3_c0
                elif pr == 0 and cc == 1:
                    ve_ln3 = self.bump("vE")

                    @self.on("v")
                    def _(v, _yt=ytile, _t=self._te_l3_last):
                        v.wait_ge(h["tE"], _t)
                        for tt in range(4):
                            v.tensor_scalar(ln3[:, tt * 1024:(tt + 1) * 1024], _yt(tt),
                                            self.muc[:, tt:tt + 1], self.rstdc[:, tt:tt + 1],
                                            ALU.subtract, ALU.mult)
                        v.drain()
                        v.nop().then_inc(h["vE"], 1)
                else:
                    cg = chain_gen(pr, cc)
                    drive(cg, 99)
                    if cc == 0:
                        ve_ln3 = self._ve_ln3_c0
                    else:
                        ve_ln3 = self.bump("vE")

                        @self.on("v")
                        def _(v, _yt=ytile, _t=self._te_l3_last):
                            v.wait_ge(h["tE"], _t)
                            for tt in range(4):
                                v.tensor_scalar(ln3[:, tt * 1024:(tt + 1) * 1024], _yt(tt),
                                                self.muc[:, tt:tt + 1], self.rstdc[:, tt:tt + 1],
                                                ALU.subtract, ALU.mult)
                            v.drain()
                            v.nop().then_inc(h["vE"], 1)

                if self.debug and "ln3" in self.dbg:
                    self.bump("qS", 16)

                    @self.on("sy")
                    def _(sy, _c=c, _v=ve_ln3):
                        sy.wait_ge(h["vE"], _v)
                        sy.dma_start(out=self.dbg["ln3"][p].rearrange("(ch tt q) d -> ch q tt d", q=128, tt=4)[_c],
                                     in_=ln3[:].rearrange("q (tt d) -> q tt d", d=D)).then_inc(h["qS"], 16)

                se_l3 = []
                te_sc = []
                for dt in range(NDT):
                    te_l3 = self.bump("tE")
                    se_l3.append(self.bump("sE"))
                    te_sc.append(self.bump("tE"))

                    @self.on("t")
                    def _(t, _dt=dt, _v=ve_ln3, _se=se_l3[dt],
                          _seprev=(se_l3[dt - 2] if dt >= 2 else None),
                          _sg=(seyt_hist[1] if (pr == 0 and dt < 2) else None)):
                        if _dt == 0:
                            t.wait_ge(h["vE"], _v)
                        if _sg is not None:
                            t.wait_ge(h["sE"], _sg)
                        if _seprev is not None:
                            t.wait_ge(h["sE"], _seprev)
                        mmi = None
                        for tt in range(4):
                            mmi = t.transpose(P[2 + _dt % 2][0:128, tt * 128:(tt + 1) * 128],
                                              ln3[:, tt * 1024 + _dt * 128: tt * 1024 + (_dt + 1) * 128],
                                              self.ident[:])
                        mmi.then_inc(h["tE"], 1)
                        t.wait_ge(h["sE"], _se)
                        t.matmul(P[6][0:16, 0:512], self.gate_wT[:, _dt * 16:(_dt + 1) * 16],
                                 l3f[:, (_dt % 2) * 512:(_dt % 2) * 512 + 512],
                                 start=(_dt == 0), stop=(_dt == NDT - 1),
                                 skip_group_check=True).then_inc(h["tE"], 1)

                    @self.on("s")
                    def _(s, _c=c, _dt=dt, _tl=te_l3,
                          _tscprev=(te_sc[dt - 2] if dt >= 2 else None)):
                        s.wait_ge(h["tE"], _tl)
                        if _tscprev is not None:
                            s.wait_ge(h["tE"], _tscprev)
                        s.copy(self.l3x_bf[:, _dt * L + _c * 512: _dt * L + (_c + 1) * 512],
                               P[2 + _dt % 2][:, 0:512])
                        s.copy(l3f[:, (_dt % 2) * 512:(_dt % 2) * 512 + 512],
                               P[2 + _dt % 2][:, 0:512])
                        s.drain()
                        s.nop().then_inc(h["sE"], 1)

                self._te_l3_last = te_l3
                self.emit_routing(c, te_sc[NDT - 1], p)

            self.fence()

    def emit_routing(self, c, te_scores, p=0):
        h, P, rt = self.h, self.P, self.rt
        ve_sc = self.bump("vE")

        @self.on("v")
        def _(v, _t=te_scores):
            v.wait_ge(h["tE"], _t)
            v.tensor_copy(self.sc_sb[0:16, 0:512], P[6][0:16, 0:512])
            v.drain()
            v.nop().then_inc(h["vE"], 1)

        te_t1 = self.bump("tE")

        @self.on("t")
        def _(t, _v=ve_sc):
            t.wait_ge(h["vE"], _v)
            mmi = None
            for tt in range(4):
                mmi = t.transpose(P[7][0:128, tt * 16:(tt + 1) * 16],
                                  self.sc_sb[0:16, tt * 128:(tt + 1) * 128],
                                  self.ident[0:16, 0:16])
            mmi.then_inc(h["tE"], 1)

        ve_rt = self.bump("vE")

        @self.on("v")
        def _(v, _t=te_t1):
            v.wait_ge(h["tE"], _t)
            sc = rt["sd"]
            v.tensor_copy(sc[:], P[7][:, 0:64])
            v.drain()
            v3 = lambda a: a[:].rearrange("q (tt e) -> q tt e", e=16)
            g3 = lambda a: a[:].rearrange("q (tg j) -> q tg j", j=4)
            bc = lambda a, n=16: a[:].broadcast_to([128, a.shape[1] if hasattr(a, 'shape') else 4, n])
            v.tensor_reduce(rt["m"][:], v3(sc), AX.X, ALU.max)
            v.drain()
            v.tensor_tensor(v3(rt["ex"]), v3(sc), rt["m"][:].broadcast_to([128, 4, 16]), ALU.subtract)
            v.drain()
            v.nop().then_inc(h["vE"], 1)
        sE_exp = self.bump("sE")

        @self.on("s")
        def _(s, _v=ve_rt):
            s.wait_ge(h["vE"], _v)
            s.activation(rt["ex"][:], rt["ex"][:], AF.Exp)
            s.drain()
            s.nop().then_inc(h["sE"], 1)

        ve_comb = self.bump("vE")

        @self.on("v")
        def _(v, _s=sE_exp):
            v3 = lambda a: a[:].rearrange("q (tt e) -> q tt e", e=16)
            g3 = lambda a: a[:].rearrange("q (tg j) -> q tg j", j=4)
            v.wait_ge(h["sE"], _s)
            v.tensor_reduce(rt["sum"][:], v3(rt["ex"]), AX.X, ALU.add)
            v.drain()
            v.reciprocal(rt["rec"][:], rt["sum"][:])
            v.drain()
            v.tensor_tensor(v3(rt["orig"]), v3(rt["ex"]),
                            rt["rec"][:].broadcast_to([128, 4, 16]), ALU.mult)
            v.drain()
            gbv = self.gb_bcast[:, 0:16].broadcast_to([128, 16, 4]).rearrange("q e tt -> q tt e")
            v.tensor_tensor(v3(rt["sc2"]), v3(rt["orig"]), gbv, ALU.add)
            v.drain()
            v.tensor_reduce(rt["m1"][:], g3(rt["sc2"]), AX.X, ALU.max)
            v.drain()
            v.tensor_tensor(g3(rt["eqa"]), g3(rt["sc2"]),
                            rt["m1"][:].broadcast_to([128, 16, 4]), ALU.is_equal)
            v.drain()
            v.scalar_tensor_tensor(rt["sgm"][:], rt["eqa"][:], NEG, rt["sc2"][:],
                                   ALU.mult, ALU.add)
            v.drain()
            v.tensor_reduce(rt["m2"][:], g3(rt["sgm"]), AX.X, ALU.max)
            v.drain()
            v.tensor_tensor(rt["gsc"][:], rt["m1"][:], rt["m2"][:], ALU.add)
            v.drain()
            q3 = lambda a: a[:].rearrange("q (tt gg) -> q tt gg", gg=4)
            v.tensor_reduce(rt["gm1"][:], q3(rt["gsc"]), AX.X, ALU.max)
            v.drain()
            v.tensor_tensor(q3(rt["eqg"]), q3(rt["gsc"]),
                            rt["gm1"][:].broadcast_to([128, 4, 4]), ALU.is_equal)
            v.drain()
            v.scalar_tensor_tensor(rt["gsc2"][:], rt["eqg"][:], NEG, rt["gsc"][:],
                                   ALU.mult, ALU.add)
            v.drain()
            v.tensor_reduce(rt["gm2"][:], q3(rt["gsc2"]), AX.X, ALU.max)
            v.drain()
            v.tensor_tensor(q3(rt["keep"]), q3(rt["gsc"]),
                            rt["gm2"][:].broadcast_to([128, 4, 4]), ALU.is_ge)
            v.drain()
            v.tensor_tensor(g3(rt["smt1"]), g3(rt["sc2"]),
                            rt["keep"][:].broadcast_to([128, 16, 4]), ALU.mult)
            v.drain()
            v.tensor_scalar(g3(rt["smt2"]), rt["keep"][:].broadcast_to([128, 16, 4]),
                            1.0, -NEG, ALU.subtract, ALU.mult)
            v.drain()
            v.tensor_tensor(rt["sm"][:], rt["smt1"][:], rt["smt2"][:], ALU.add)
            v.drain()
            v.tensor_reduce(rt["t1"][:], v3(rt["sm"]), AX.X, ALU.max)
            v.drain()
            v.tensor_tensor(v3(rt["eq1"]), v3(rt["sm"]),
                            rt["t1"][:].broadcast_to([128, 4, 16]), ALU.is_equal)
            v.drain()
            v.scalar_tensor_tensor(rt["sm2"][:], rt["eq1"][:], NEG, rt["sm"][:],
                                   ALU.mult, ALU.add)
            v.drain()
            v.tensor_reduce(rt["t2"][:], v3(rt["sm2"]), AX.X, ALU.max)
            v.drain()
            v.tensor_tensor(v3(rt["eq2"]), v3(rt["sm2"]),
                            rt["t2"][:].broadcast_to([128, 4, 16]), ALU.is_equal)
            v.drain()
            v.tensor_tensor(rt["comb"][:], rt["eq1"][:], rt["eq2"][:], ALU.add)
            v.drain()
            v.tensor_tensor(rt["comb"][:], rt["comb"][:], rt["orig"][:], ALU.mult)
            v.drain()
            v.nop().then_inc(h["vE"], 1)

        if self.debug and "rt" in self.dbg and p == 0 and c == 0:
            keys = ["sd", "ex", "orig", "sc2", "m1", "sgm", "m2", "gsc", "gm1",
                    "gm2", "keep", "smt1", "smt2", "sm", "t1", "eq1",
                    "sm2", "t2", "eq2", "comb"]
            self.bump("qS", 16 * 20)

            @self.on("sy")
            def _(sy, _v=ve_comb, _keys=tuple(keys)):
                sy.wait_ge(h["vE"], _v)
                for i, kk in enumerate(_keys):
                    ap = rt[kk][:]
                    w = ap.shape[1] if hasattr(ap, "shape") else 64
                    sy.dma_start(out=self.dbg["rt"][i, :, 0:rt[kk].shape[1]],
                                 in_=ap).then_inc(h["qS"], 16)

        te_t2 = self.bump("tE")

        @self.on("t")
        def _(t, _v=ve_comb):
            t.wait_ge(h["vE"], _v)
            mmi = None
            for tt in range(4):
                mmi = t.transpose(P[7][0:16, tt * 128:(tt + 1) * 128],
                                  rt["comb"][:, tt * 16:(tt + 1) * 16], self.ident[:])
            mmi.then_inc(h["tE"], 1)

        ve_ct = self.bump("vE")

        @self.on("v")
        def _(v, _t=te_t2, _c=c):
            v.wait_ge(h["tE"], _t)
            v.tensor_copy(self.combine_T[0:16, _c * 512:(_c + 1) * 512], P[7][0:16, 0:512])
            v.drain()
            v.nop().then_inc(h["vE"], 1)

    def phaseB(self, p, T):
        h, P = self.h, self.P
        we, hbf, h1s, h3s, cb = T["we"], T["hbf"], T["h1s"], T["h3s"], T["cb"]
        WSLOT = 12288
        w1t = lambda sl, kdt, ib: we[:, sl * WSLOT + kdt * 512 + ib * 128: sl * WSLOT + kdt * 512 + (ib + 1) * 128]
        w3t = lambda sl, kdt, ib: we[:, sl * WSLOT + 4096 + kdt * 512 + ib * 128: sl * WSLOT + 4096 + kdt * 512 + (ib + 1) * 128]
        w2t = lambda sl, ib, odt: we[:, sl * WSLOT + 8192 + ib * 1024 + odt * 128: sl * WSLOT + 8192 + ib * 1024 + (odt + 1) * 128]

        qs_we = [None] * E
        prev_yo_te = {}
        prev_ve_yo_last = None

        for e in range(E):
            sl = e % 2
            qs_we[e] = self.bump("qS", 48)

            @self.on("sy")
            def _(sy, _e=e, _sl=sl):
                if _e >= 2:
                    sy.wait_ge(h["tE"], prev_yo_te[_e - 2])
                sy.dma_start(out=we[:, _sl * WSLOT: _sl * WSLOT + 4096]
                             .rearrange("q (kdt i) -> q kdt i", i=512),
                             in_=self.wb["We1"][_e].rearrange("(kdt q) i -> q kdt i", q=128)
                             ).then_inc(h["qS"], 16)
                sy.dma_start(out=we[:, _sl * WSLOT + 4096: _sl * WSLOT + 8192]
                             .rearrange("q (kdt i) -> q kdt i", i=512),
                             in_=self.wb["We3"][_e].rearrange("(kdt q) i -> q kdt i", q=128)
                             ).then_inc(h["qS"], 16)
                sy.dma_start(out=we[:, _sl * WSLOT + 8192: _sl * WSLOT + 12288]
                             .rearrange("q (ib d1) -> q ib d1", d1=1024),
                             in_=self.wb["We2"][_e].rearrange("(ib q) d1 -> q ib d1", q=128)
                             ).then_inc(h["qS"], 16)

            te_cb = [self.bump("tE") for _ in range(4)]
            se_cb = [self.bump("sE") for _ in range(4)]

            @self.on("t")
            def _(t, _e=e, _te=tuple(te_cb), _se=tuple(se_cb), _vy=prev_ve_yo_last):
                if _vy is not None:
                    t.wait_ge(h["vE"], _vy)
                for ch in range(4):
                    if ch >= 2:
                        t.wait_ge(h["sE"], _se[ch - 2])
                    t.matmul(P[6 + ch % 2][:, 0:512], self.sel_sb[:, _e * 128:(_e + 1) * 128],
                             self.combine_T[0:16, ch * 512:(ch + 1) * 512],
                             start=True, stop=True).then_inc(h["tE"], 1)

            @self.on("s")
            def _(s, _te=tuple(te_cb)):
                for ch in range(4):
                    s.wait_ge(h["tE"], _te[ch])
                    s.copy(cb[:, ch * 512:(ch + 1) * 512], P[6 + ch % 2][:, 0:512])
                    s.drain()
                    s.nop().then_inc(h["sE"], 1)

            if e == 0:
                prev_se_h1 = None
                prev_ve_h3s = None
                prev_ve_hbf = None
            ve_hbf = [None] * 4
            for ib in range(4):
                te_h1 = self.bump("tE")
                se_h1 = self.bump("sE")
                te_h3 = self.bump("tE")
                ve_h3s = self.bump("vE")
                ve_hbf[ib] = self.bump("vE")

                @self.on("t")
                def _(t, _e=e, _sl=sl, _ib=ib, _q=qs_we[e], _secb=se_cb[3],
                      _sh1=prev_se_h1, _vh3=prev_ve_h3s,
                      _vyo=(prev_ve_yo_last if (e >= 1 and ib == 0) else None)):
                    t.wait_ge(h["qS"], _q)
                    if _ib == 0:
                        t.wait_ge(h["sE"], _secb)
                    if _vyo is not None:
                        t.wait_ge(h["vE"], _vyo)
                    if _sh1 is not None:
                        t.wait_ge(h["sE"], _sh1)
                    mmi = None
                    for kdt in range(NDT):
                        for ch in range(4):
                            mmi = t.matmul(P[ch][:, 0:512], w1t(_sl, kdt, _ib),
                                           self.l3x_bf[:, kdt * L + ch * 512: kdt * L + (ch + 1) * 512],
                                           start=(kdt == 0), stop=(kdt == NDT - 1),
                                           skip_group_check=True)
                    mmi.then_inc(h["tE"], 1)
                    if _vh3 is not None:
                        t.wait_ge(h["vE"], _vh3)
                    mmi = None
                    for kdt in range(NDT):
                        for ch in range(4):
                            mmi = t.matmul(P[4 + ch][:, 0:512], w3t(_sl, kdt, _ib),
                                           self.l3x_bf[:, kdt * L + ch * 512: kdt * L + (ch + 1) * 512],
                                           start=(kdt == 0), stop=(kdt == NDT - 1),
                                           skip_group_check=True)
                    mmi.then_inc(h["tE"], 1)

                @self.on("s")
                def _(s, _e=e, _ib=ib, _t=te_h1, _vb=prev_ve_hbf):
                    s.wait_ge(h["tE"], _t)
                    if _vb is not None:
                        s.wait_ge(h["vE"], _vb)
                    s.activation(h1s[:], self.PA[:], AF.Silu,
                                 bias=self.be1c[:, _ib * 16 + _e: _ib * 16 + _e + 1],
                                 scale=1.0)
                    s.drain()
                    s.nop().then_inc(h["sE"], 1)

                @self.on("v")
                def _(v, _e=e, _ib=ib, _t=te_h3, _s1=se_h1, _secb=se_cb[3]):
                    v.wait_ge(h["tE"], _t)
                    if _ib == 0:
                        v.wait_ge(h["sE"], _secb)
                    v.scalar_tensor_tensor(h3s[:], self.PB[:],
                                           self.be3c[:, _ib * 16 + _e: _ib * 16 + _e + 1],
                                           cb[:], ALU.add, ALU.mult)
                    v.drain()
                    v.nop().then_inc(h["vE"], 1)
                    v.wait_ge(h["sE"], _s1)
                    v.tensor_tensor(hbf[:, _ib * L: (_ib + 1) * L], h3s[:], h1s[:], ALU.mult)
                    v.drain()
                    v.nop().then_inc(h["vE"], 1)
                prev_se_h1 = se_h1
                prev_ve_h3s = ve_h3s
                prev_ve_hbf = ve_hbf[ib]

            te_yo = [None] * NDT
            ve_yo = [None] * NDT
            for odt in range(NDT):
                te_yo[odt] = self.bump("tE")
                ve_yo[odt] = self.bump("vE")

            @self.on("t")
            def _(t, _e=e, _sl=sl, _ph=ve_hbf[3], _vy=tuple(ve_yo)):
                t.wait_ge(h["vE"], _ph)
                for odt in range(NDT):
                    if odt >= 2:
                        t.wait_ge(h["vE"], _vy[odt - 2])
                    pb = (odt % 2) * 4
                    mmi = None
                    for ib in range(4):
                        for ch in range(4):
                            mmi = t.matmul(P[pb + ch][:, 0:512], w2t(_sl, ib, odt),
                                           hbf[:, ib * L + ch * 512: ib * L + (ch + 1) * 512],
                                           start=(ib == 0), stop=(ib == 3),
                                           skip_group_check=True)
                    mmi.then_inc(h["tE"], 1)
            prev_yo_te[e] = te_yo[NDT - 1]

            @self.on("v")
            def _(v, _e=e, _ty=tuple(te_yo)):
                for odt in range(NDT):
                    v.wait_ge(h["tE"], _ty[odt])
                    pw = self.PA if odt % 2 == 0 else self.PB
                    dst = self.y_acc[:, odt * L: (odt + 1) * L]
                    if _e == 0:
                        v.tensor_copy(dst, pw[:])
                    else:
                        v.tensor_tensor(dst, dst, pw[:], ALU.add)
                    v.drain()
                    v.nop().then_inc(h["vE"], 1)
            prev_ve_yo_last = ve_yo[NDT - 1]
        self.fence()

    def phaseZ(self, p, T):
        h, P = self.h, self.P
        zw = T["zw"]
        z1s, z3s, zzw = T["z1s"], T["z3s"], T["zzw"]
        zw1 = lambda kdt, ob: zw[:, kdt * 1024 + ob * 128: kdt * 1024 + (ob + 1) * 128]
        zw3 = lambda kdt, ob: zw[:, 8192 + kdt * 1024 + ob * 128: 8192 + kdt * 1024 + (ob + 1) * 128]

        qs_zw = self.bump("qS", 32)

        @self.on("sy")
        def _(sy):
            for k, srcw in [(0, self.wb["ws1"]), (8192, self.wb["ws3"])]:
                sy.dma_start(out=zw[:, k: k + 8192].rearrange("q (kdt o) -> q kdt o", o=1024),
                             in_=srcw.rearrange("(kdt q) o -> q kdt o", q=128)
                             ).then_inc(h["qS"], 16)

        pe_zz = [None] * NDT
        te_z1_hist = []
        qg_zz = [None] * NDT
        for ob in range(NDT):
            te_z1 = self.bump("tE")
            te_z1_hist.append(te_z1)
            se_z1 = self.bump("sE")
            te_z3 = self.bump("tE")
            se_z3 = self.bump("sE")
            pe_zz[ob] = self.bump("pE")
            qg_zz[ob] = self.bump("qG", 16)

            @self.on("t")
            def _(t, _ob=ob, _q=qs_zw, _prev=pe_zz[ob - 1] if ob >= 1 else None):
                t.wait_ge(h["qS"], _q)
                if _prev is not None:
                    t.wait_ge(h["pE"], _prev)
                mmi = None
                for kdt in range(NDT):
                    for ch in range(4):
                        mmi = t.matmul(P[ch][:, 0:512], zw1(kdt, _ob),
                                       self.l3x_bf[:, kdt * L + ch * 512: kdt * L + (ch + 1) * 512],
                                       start=(kdt == 0), stop=(kdt == NDT - 1),
                                       skip_group_check=True)
                mmi.then_inc(h["tE"], 1)

            @self.on("s")
            def _(s, _ob=ob, _t=te_z1, _pz=(pe_zz[ob - 1] if ob >= 1 else None)):
                s.wait_ge(h["tE"], _t)
                if _pz is not None:
                    s.wait_ge(h["pE"], _pz)
                s.activation(z1s[:], self.PA[:], AF.Silu, bias=self.col("bs1", _ob), scale=1.0)
                s.drain()
                s.nop().then_inc(h["sE"], 1)

            @self.on("t")
            def _(t, _ob=ob):
                mmi = None
                for kdt in range(NDT):
                    for ch in range(4):
                        mmi = t.matmul(P[4 + ch][:, 0:512], zw3(kdt, _ob),
                                       self.l3x_bf[:, kdt * L + ch * 512: kdt * L + (ch + 1) * 512],
                                       start=(kdt == 0), stop=(kdt == NDT - 1),
                                       skip_group_check=True)
                mmi.then_inc(h["tE"], 1)

            @self.on("s")
            def _(s, _ob=ob, _t=te_z3, _pz=(pe_zz[ob - 1] if ob >= 1 else None)):
                s.wait_ge(h["tE"], _t)
                if _pz is not None:
                    s.wait_ge(h["pE"], _pz)
                s.activation(z3s[:], self.PB[:], AF.Identity, bias=self.col("bs3", _ob), scale=1.0)
                s.drain()
                s.nop().then_inc(h["sE"], 1)


            @self.on("g")
            def _(g, _ob=ob, _s3=se_z3, _qz=(qg_zz[ob - 2] if ob >= 2 else None)):
                g.wait_ge(h["sE"], _s3)
                if _qz is not None:
                    g.wait_ge(h["qG"], _qz)
                sl = (_ob % 2) * L
                g.tensor_tensor(zzw[:, sl: sl + L], z3s[:, 0:L], z1s[:, 0:L], ALU.mult)
                g.drain()
                g.nop().then_inc(h["pE"], 1)
                g.dma_start(out=self.zz_dram[_ob], in_=zzw[:, sl: sl + L]).then_inc(h["qG"], 16)
        self._z_state = (te_z1_hist, qg_zz)

    def phaseZ2(self, p, T):
        h, P = self.h, self.P
        zw, zzst = T["zw"], T["zzst"]
        y2f, y2t, x1r = T["y2f"], T["y2t"], T["x1r"]
        zw2 = lambda ob, odt: zw[:, ob * 1024 + odt * 128: ob * 1024 + (odt + 1) * 128]
        te_z1_hist, qg_zz = self._z_state

        qs_zw2 = self.bump("qS", 32)

        @self.on("sy")
        def _(sy, _t=te_z1_hist[NDT - 1]):
            sy.dma_start(out=self.be2_sb[:], in_=self.w["be2"]).then_inc(h["qS"], 16)
            sy.wait_ge(h["tE"], _t)
            sy.dma_start(out=zw[:, 0:8192].rearrange("q (kdt o) -> q kdt o", o=1024),
                         in_=self.wb["ws2"].rearrange("(kdt q) o -> q kdt o", q=128)
                         ).then_inc(h["qS"], 16)

        prev_ve_fin = None
        prev_qs_out = None
        te_z2_hist = []
        for ch in range(4):
            qs_zzst = self.bump("qS", 16)
            qs_x1 = self.bump("qS", 16)

            @self.on("sy")
            def _(sy, _ch=ch, _vf=prev_ve_fin, _qz=qg_zz[NDT - 1],
                  _tz=(te_z2_hist[ch - 1] if ch >= 1 else None)):
                sy.wait_ge(h["qG"], _qz)
                if _tz is not None:
                    sy.wait_ge(h["tE"], _tz)
                sy.dma_start(out=zzst[:, 0:4096]
                             .rearrange("q (ob o) -> q ob o", o=512),
                             in_=self.zz_dram.rearrange("ob q t -> q ob t")[:, :, _ch * 512:(_ch + 1) * 512]
                             ).then_inc(h["qS"], 16)
                if _vf is not None:
                    sy.wait_ge(h["vE"], _vf)
                sy.dma_start(out=x1r[:].rearrange("q (tt d) -> q tt d", d=D),
                             in_=self.x1_scr.rearrange("(cc tt q) d -> cc q tt d", q=128, tt=4)[_ch]
                             ).then_inc(h["qS"], 16)

            se_y2 = []
            ve_z2 = []
            for odt in range(NDT):
                te_z2 = self.bump("tE")
                ve_z2.append(self.bump("vE"))
                te_y2 = self.bump("tE")
                se_y2.append(self.bump("sE"))

                @self.on("t")
                def _(t, _ch=ch, _odt=odt, _vz=ve_z2[odt], _qw=qs_zw2, _qz=qs_zzst,
                      _vzprev=(ve_z2[odt - 2] if odt >= 2 else None),
                      _seprev=(se_y2[odt - 1] if odt >= 1 else None)):
                    if _odt == 0:
                        t.wait_ge(h["qS"], max(_qw, _qz))
                    if _vzprev is not None:
                        t.wait_ge(h["vE"], _vzprev)
                    mmi = None
                    for ob in range(NDT):
                        mmi = t.matmul(P[_odt % 2][:, 0:512], zw2(ob, _odt),
                                       zzst[:, ob * 512:(ob + 1) * 512],
                                       start=(ob == 0), stop=False, skip_group_check=True)
                    mmi = t.matmul(P[_odt % 2][:, 0:512],
                                   self.be2_sb[0:16, _odt * 128:(_odt + 1) * 128],
                                   self.combine_T[0:16, _ch * 512:(_ch + 1) * 512],
                                   start=False, stop=True, skip_group_check=True)
                    mmi.then_inc(h["tE"], 1)
                    t.wait_ge(h["vE"], _vz)
                    if _seprev is not None:
                        t.wait_ge(h["sE"], _seprev)
                    mmi = None
                    for tt in range(4):
                        mmi = t.transpose(P[2 + _odt % 2][0:128, tt * 128:(tt + 1) * 128],
                                          y2f[:, (_odt % 2) * 512 + tt * 128:(_odt % 2) * 512 + (tt + 1) * 128],
                                          self.ident[:])
                    mmi.then_inc(h["tE"], 1)

                @self.on("v")
                def _(v, _ch=ch, _odt=odt, _tz=te_z2):
                    v.wait_ge(h["tE"], _tz)
                    v.scalar_tensor_tensor(y2f[:, (_odt % 2) * 512:(_odt % 2) * 512 + 512],
                                           P[_odt % 2][:, 0:512], self.col("bs2", _odt),
                                           self.y_acc[:, _odt * L + _ch * 512: _odt * L + (_ch + 1) * 512],
                                           ALU.add, ALU.add)
                    v.drain()
                    v.nop().then_inc(h["vE"], 1)

                @self.on("s")
                def _(s, _odt=odt, _ty=te_y2, _qo=(prev_qs_out if odt == 0 else None)):
                    s.wait_ge(h["tE"], _ty)
                    if _qo is not None:
                        s.wait_ge(h["qS"], _qo)
                    s.copy(y2t[:].rearrange("q (tt d) -> q tt d", d=D)[:, :, _odt * 128:(_odt + 1) * 128],
                           P[2 + _odt % 2][:, 0:512].rearrange("q (tt d1) -> q tt d1", d1=128))
                    s.drain()
                    s.nop().then_inc(h["sE"], 1)
            te_z2_hist.append(self.c["tE"])

            @self.on("v")
            def _(v, _s=se_y2[NDT - 1]):
                v.wait_ge(h["sE"], _s)
            y2tile = lambda tt: y2t[:, tt * 1024:(tt + 1) * 1024]
            self.emit_stats([y2tile(tt) for tt in range(4)])
            self.emit_rstd()
            ve_fin = self.bump("vE")

            @self.on("v")
            def _(v, _q=qs_x1, _yt=y2tile):
                v.wait_ge(h["qS"], _q)
                for tt in range(4):
                    v.tensor_scalar(self.sq[:], _yt(tt), self.muc[:, tt:tt + 1],
                                    self.rstdc[:, tt:tt + 1], ALU.subtract, ALU.mult)
                    v.drain()
                    v.tensor_tensor(_yt(tt), self.sq[:],
                                    x1r[:, tt * 1024:(tt + 1) * 1024], ALU.add)
                    v.drain()
                v.nop().then_inc(h["vE"], 1)

            qs_out = self.bump("qS", 16)

            @self.on("sy")
            def _(sy, _ch=ch, _v=ve_fin, _p=p):
                sy.wait_ge(h["vE"], _v)
                sy.dma_start(out=self.out[_p].rearrange("(cc tt q) d -> cc q tt d", q=128, tt=4)[_ch],
                             in_=y2t[:].rearrange("q (tt d) -> q tt d", d=D)).then_inc(h["qS"], 16)
            prev_ve_fin = ve_fin
            prev_qs_out = qs_out

        if self.debug and "comb" in self.dbg:
            self.bump("qS", 16)

            @self.on("sy")
            def _(sy, _p=p):
                sy.dma_start(out=self.dbg["comb"][_p], in_=self.combine_T[:]).then_inc(h["qS"], 16)

        self.fence()

    def build(self):
        nc = self.nc
        self.declare()
        top = ExitStack()
        SB = lambda n, s, dt=F32: top.enter_context(nc.sbuf_tensor(n, s, dt))
        PSm = lambda n, s, dt=F32: top.enter_context(nc.psum_tensor(n, s, dt))
        with top:
            self.persistent(SB, PSm)
            self.sc_sb = SB("sc_sb", [16, 512])
            self.startup()
            for p in range(NB):
                sfx = f"_p{p}"
                ps = ExitStack()
                S2 = lambda n, s, dt=F32: ps.enter_context(nc.sbuf_tensor(n + sfx, s, dt))
                with ps:
                    self.l3x_bf = S2("l3x_bf", [128, NDT * L], BF16)
                    a1 = ExitStack()
                    S3 = lambda n, s, dt=F32: a1.enter_context(nc.sbuf_tensor(n + sfx, s, dt))
                    with a1, nc.Block() as blk:
                        TA = {
                            "xsl": S3("xsl", [128, 2 * 4096]),
                            "ln1": S3("ln1", [128, 4096]),
                            "l1f": S3("l1f", [128, NDT * 512]),
                            "l1b": S3("l1b", [128, NDT * 512], BF16),
                            "wgT": S3("wgT", [128, NDT * D], BF16),
                            "gt": S3("gt", [128, 2 * 512]),
                            "xgf": S3("xgf", [128, 2 * 512]),
                        }
                        self.sq = S3("sq1", [128, 1024])
                        if p == 0:
                            TA["tap"] = (S3("A_f", [128, NDT * 256]),
                                         S3("Bp_f", [128, NDT * NST]),
                                         S3("Cp_f", [128, NDT * NST]),
                                         S3("u1", [128, 2304]),
                                         S3("dg_st", [128, 2 * J * 128], BF16))
                        self.phaseA1(p, TA)
                        self.fence()
                        self.flush(blk)
                    a2 = ExitStack()
                    S4 = lambda n, s, dt=F32: a2.enter_context(nc.sbuf_tensor(n + sfx, s, dt))
                    with a2, nc.Block() as blk:
                        TB = {
                            "dg": S4("dg", [128, 2 * J * 128], BF16),
                            "xb": S4("xb", [128, 2 * (J + 1024)], BF16),
                            "xf": S4("xf", [128, 1024]),
                            "yF": S4("yF", [128, 2 * 1024]),
                            "ytp": S4("ytp", [128, 2 * 8 * 1024]),
                            "xre": S4("xre", [128, 4096]),
                            "ln3": S4("ln3", [128, 4096]),
                            "l3f": S4("l3f", [128, 2 * 512]),
                        }
                        self.sq = S4("sq2", [128, 1024])
                        self.rt = {nm: S4("rt_" + nm, [128, sz]) for nm, sz in self.RT_SIZES}
                        self.gamma_b = S4("gamma_b", [128, D])
                        self.beta_b = S4("beta_b", [128, D])
                        self.phaseA2(p, TB)
                        self.flush(blk)
                    if STAGE >= 3:
                        ya = ExitStack()
                        self.y_acc = ya.enter_context(nc.sbuf_tensor("y_acc" + sfx, [128, NDT * L], F32))
                        ps.enter_context(ya.pop_all() if False else ya)
                        b1 = ExitStack()
                        S5 = lambda n, s, dt=F32: b1.enter_context(nc.sbuf_tensor(n + sfx, s, dt))
                        with b1, nc.Block() as blk:
                            TC = {
                                "we": S5("we", [128, 2 * 12288], BF16),
                                "hbf": S5("hbf", [128, 4 * L], BF16),
                                "h1s": S5("h1s", [128, L]),
                                "h3s": S5("h3s", [128, L]),
                                "cb": S5("cb", [128, L]),
                            }
                            self.phaseB(p, TC)
                            self.flush(blk)
                        z0 = ExitStack()
                        S60 = lambda n, s, dt=F32: z0.enter_context(nc.sbuf_tensor(n + sfx, s, dt))
                        with z0:
                            zw = S60("zw", [128, 2 * 8192], BF16)
                            za = ExitStack()
                            S6 = lambda n, s, dt=F32: za.enter_context(nc.sbuf_tensor(n + sfx, s, dt))
                            with za, nc.Block() as blk:
                                TD = {
                                    "zw": zw,
                                    "z1s": S6("z1s", [128, L]),
                                    "z3s": S6("z3s", [128, L]),
                                    "zzw": S6("zzw", [128, 2 * L], BF16),
                                }
                                self.phaseZ(p, TD)
                                self.fence()
                                self.flush(blk)
                            zb = ExitStack()
                            S6 = lambda n, s, dt=F32: zb.enter_context(nc.sbuf_tensor(n + sfx, s, dt))
                            with zb, nc.Block() as blk:
                                TE2 = {
                                    "zw": zw,
                                    "zzst": S6("zzst", [128, 4096], BF16),
                                    "y2f": S6("y2f", [128, 2 * 512]),
                                    "y2t": S6("y2t", [128, 4096]),
                                    "x1r": S6("x1r", [128, 4096]),
                                }
                                self.sq = S6("sq3", [128, 1024])
                                self.be2_sb = S6("be2_sb", [E, D])
                                self.phaseZ2(p, TE2)
                                self.flush(blk)
            with nc.Block() as blk:
                @self.on("sy")
                def _(sy):
                    sy.wait_ge(self.h["qS"], self.c["qS"])
                    sy.wait_ge(self.h["qG"], self.c["qG"])
                self.flush(blk)
        return nc


def make_consts():
    sel = np.zeros((E, E * 128), np.float32)
    for e in range(E):
        sel[e, e * 128:(e + 1) * 128] = 1.0
    return {
        "c_ident": np.eye(128, dtype=np.float32),
        "c_sel": sel,
        "c_ones": np.ones((1, 128), np.float32),
    }


_CACHE = {}


def get_nc(debug=False):
    key = (STAGE, debug)
    if key not in _CACHE:
        b = B(debug=debug)
        nc = b.build()
        _CACHE[key] = (nc, b.dbg)
    return _CACHE[key]


def kernel(**inputs):
    nc, _ = get_nc(False)
    consts = make_consts()
    x = np.ascontiguousarray(inputs["L0_attn_x"])
    ren = {
        "A": "L0_attn_A", "Bp": "L0_attn_Bp", "Cp": "L0_attn_Cp", "Dp": "L0_attn_Dp",
        "Wg": "L0_attn_Wg", "bg": "L0_attn_bg", "gamma": "L0_attn_gamma",
        "beta": "L0_attn_beta", "gate_w": "L0_ffn_gate_w", "gate_b": "L0_ffn_gate_b",
        "We1": "L0_ffn_We1", "be1": "L0_ffn_be1", "We2": "L0_ffn_We2",
        "be2": "L0_ffn_be2", "We3": "L0_ffn_We3", "be3": "L0_ffn_be3",
        "ws1": "L0_ffn_ws1", "bs1": "L0_ffn_bs1", "ws2": "L0_ffn_ws2",
        "bs2": "L0_ffn_bs2", "ws3": "L0_ffn_ws3", "bs3": "L0_ffn_bs3",
    }
    base = {k: np.ascontiguousarray(inputs[v]) for k, v in ren.items()}
    base.update(consts)
    in_maps = []
    for c in range(NCORE):
        m = dict(base)
        m["x"] = np.ascontiguousarray(x[c * NB:(c + 1) * NB])
        in_maps.append(m)
    res = run_bass_kernel_spmd(nc, in_maps, core_ids=list(range(NCORE)))
    return np.concatenate([r["out"] for r in res.results], axis=0)


if __name__ == "__main__":
    b = B(debug=False)
    b.build()
    print("build ok, instructions:", b.nc.n_instructions())


# version 14 (final)
# speedup vs baseline: 1.0932x; 1.0063x over previous
import numpy as np
from contextlib import ExitStack

import concourse.bass as bass
import concourse.mybir as mybir
from concourse.bass_utils import run_bass_kernel_spmd

F32 = mybir.dt.float32
BF16 = mybir.dt.bfloat16
AF = mybir.ActivationFunctionType
ALU = mybir.AluOpType
AX = mybir.AxisListType

D, L, NB, NST, E, INTER = 1024, 2048, 2, 16, 16, 512
NCORE = 8
J = 28
NEG = -1e30
EPS = 1e-5
NDT, NCH = 8, 4
STAGE = 3


class B:
    def __init__(self, debug=False):
        self.debug = debug
        nc = self.nc = bass.Bass("TRN2", target_bir_lowering=False, debug=False)
        self.ops = {k: [] for k in ("sy", "g", "t", "v", "s")}
        self.h = {}
        self.c = {}
        for s in ("qS", "qG", "tE", "vE", "sE", "pE", "F"):
            self.h[s] = nc.alloc_semaphore(s)
            self.c[s] = 0

    def on(self, eng):
        def deco(fn):
            self.ops[eng].append(fn)
        return deco

    def bump(self, s, k=1):
        self.c[s] += k
        return self.c[s]

    def fence(self):
        tgt = self.c["F"] + 5
        hF = self.h["F"]
        qs_now = self.c["qS"]
        qg_now = self.c["qG"]
        for k in ("sy", "g", "t", "v", "s"):
            def go(e, _t=tgt, _k=k, _qs=qs_now, _qg=qg_now):
                if _k in ("v", "s", "g"):
                    e.drain()
                if _k == "sy":
                    e.wait_ge(self.h["qS"], _qs)
                if _k == "g":
                    e.wait_ge(self.h["qG"], _qg)
                e.nop().then_inc(hF, 1)
                e.wait_ge(hF, _t)
            self.ops[k].append(go)
        self.c["F"] = tgt

    def flush(self, block):
        ops = self.ops
        self.ops = {k: [] for k in ops}
        block.sync(lambda e, o=tuple(ops["sy"]): [f(e) for f in o])
        block.gpsimd(lambda e, o=tuple(ops["g"]): [f(e) for f in o])
        block.tensor(lambda e, o=tuple(ops["t"]): [f(e) for f in o])
        block.vector(lambda e, o=tuple(ops["v"]): [f(e) for f in o])
        block.scalar(lambda e, o=tuple(ops["s"]): [f(e) for f in o])

    def declare(self):
        nc = self.nc
        self.x_in = nc.dram_tensor("x", [NB, L, D], F32, kind="ExternalInput").ap()
        self.w = {}
        for name, shape in [
            ("A", [D, NST, NST]), ("Bp", [D, NST]), ("Cp", [D, NST]), ("Dp", [D]),
            ("Wg", [D, D]), ("bg", [D]), ("gamma", [D]), ("beta", [D]),
            ("gate_w", [E, D]), ("gate_b", [E]),
            ("We1", [E, D, INTER]), ("be1", [E, INTER]), ("We2", [E, INTER, D]),
            ("be2", [E, D]), ("We3", [E, D, INTER]), ("be3", [E, INTER]),
            ("ws1", [D, D]), ("bs1", [D]), ("ws2", [D, D]), ("bs2", [D]),
            ("ws3", [D, D]), ("bs3", [D]),
        ]:
            self.w[name] = nc.dram_tensor(name, shape, F32, kind="ExternalInput").ap()
        self.ident_in = nc.dram_tensor("c_ident", [128, 128], F32, kind="ExternalInput").ap()
        self.sel_in = nc.dram_tensor("c_sel", [E, E * 128], F32, kind="ExternalInput").ap()
        self.ones_in = nc.dram_tensor("c_ones", [1, 128], F32, kind="ExternalInput").ap()
        self.out = nc.dram_tensor("out", [NB, L, D], F32, kind="ExternalOutput").ap()
        self.dbg = {}
        if self.debug:
            for nm, shape in [("x1", [NB, L, D]), ("comb", [NB, E, L]),
                              ("ln3", [NB, L, D]), ("xg", [NB, NDT, 128, L]),
                              ("y", [NB, L, D]), ("ln1", [NB, L, D]),
                              ("gate", [NB, NDT, 128, L]), ("wgT", [D, D]),
                              ("diag", [NDT, 128, J * 128]), ("vcol", [128, 5 * NDT]),
                              ("hbuf", [128, NDT * J]), ("gammab", [128, D]),
                              ("rt", [20, 128, 64]), ("yacc", [NDT, 128, L]), ("ffn", [128, 4096])]:
                self.dbg[nm] = nc.dram_tensor("dbg_" + nm, shape, F32, kind="ExternalOutput").ap()
        dr = lambda n, s, dt=BF16: nc.dram_tensor(n, s, dt, kind="Internal").ap()
        self.wb = {
            "We1": dr("wb_we1", [E, D, INTER]), "We2": dr("wb_we2", [E, INTER, D]),
            "We3": dr("wb_we3", [E, D, INTER]), "ws1": dr("wb_ws1", [D, D]),
            "ws2": dr("wb_ws2", [D, D]), "ws3": dr("wb_ws3", [D, D]),
            "Wg": dr("wb_wg", [D, D]),
        }
        self.wgT_dram = dr("wgT_dram", [D, D])
        self.gb_dram = dr("gb_dram", [2, 128, D], F32)
        self.zz_dram = dr("zz_dram", [NDT, 128, L])
        self.diag_dram = dr("diag_dram", [NDT, 128, J * 128])
        self.xg_f = dr("xg_f", [NDT, 128, L], F32)
        self.xg_b = dr("xg_b", [NDT, 128, J + L])
        self.x1_scr = dr("x1_scr", [L, D], F32)
        self.yacc = dr("yacc", [NDT, 128, L], F32)

    def persistent(self, SB, PSm):
        self.ident = SB("ident", [128, 128])
        self.sel_sb = SB("sel_sb", [E, E * 128])
        self.ones_sb = SB("ones_sb", [1, 128])
        self.gate_wT = SB("gate_wT", [128, NDT * E])
        self.gb_bcast = SB("gb_bcast", [128, E])
        self.vcol = SB("vcol", [128, 5 * NDT])
        self.be1c = SB("be1c", [128, 4 * E])
        self.be3c = SB("be3c", [128, 4 * E])
        self.epscol = SB("epscol", [128, 1])
        self.combine_T = SB("combine_T", [E, L])
        self.hbuf = SB("hbuf", [128, NDT * J])
        self.zpad = SB("zpad", [128, J], BF16)
        self.s1c = SB("s1c", [128, 4]); self.s2c = SB("s2c", [128, 4])
        self.muc = SB("muc", [128, 4]); self.varc = SB("varc", [128, 4])
        self.rstdc = SB("rstdc", [128, 4]); self.tnc = SB("tnc", [128, 4])
        self.RT_SIZES = [
            ("m", 4), ("rec", 4), ("sum", 4), ("sd", 64), ("ex", 64), ("orig", 64),
            ("sc2", 64), ("m1", 16), ("eqa", 64), ("sgm", 64), ("m2", 16),
            ("gsc", 16), ("gm1", 4), ("eqg", 16), ("gsc2", 16), ("gm2", 4),
            ("keep", 16), ("smt1", 64), ("smt2", 64), ("sm", 64), ("t1", 4),
            ("eq1", 64), ("sm2", 64), ("t2", 4), ("eq2", 64), ("comb", 64)]
        self.PA = PSm("psA", [128, 2048])
        self.PB = PSm("psB", [128, 2048])
        self.P = [self.PA[:, i * 512:(i + 1) * 512] for i in range(4)] + \
                 [self.PB[:, i * 512:(i + 1) * 512] for i in range(4)]

    def col(self, which, dt):
        k = {"bg": 0, "Dp": 1, "bs1": 2, "bs2": 3, "bs3": 4}[which]
        return self.vcol[:, k * NDT + dt: k * NDT + dt + 1]

    def emit_rstd(self, n=4):
        h, c = self.h, self.c
        vE_stats = self.bump("vE")

        @self.on("v")
        def _(v):
            v.drain()
            v.nop().then_inc(h["vE"], 1)
        sE_rstd = self.bump("sE")

        @self.on("s")
        def _(s, _w=vE_stats):
            s.wait_ge(h["vE"], _w)
            s.activation(self.rstdc[:, 0:n], self.varc[:, 0:n], AF.Ln,
                         bias=self.epscol[:, 0:1], scale=1.0)
            s.drain()
            s.activation(self.rstdc[:, 0:n], self.rstdc[:, 0:n], AF.Exp, scale=-0.5)
            s.drain()
            s.nop().then_inc(h["sE"], 1)

        @self.on("v")
        def _(v, _w=sE_rstd):
            v.wait_ge(h["sE"], _w)
            rc, tn, vc = self.rstdc[:, 0:n], self.tnc[:, 0:n], self.varc[:, 0:n]
            v.tensor_tensor(tn, rc, rc, ALU.mult); v.drain()
            v.tensor_tensor(tn, tn, vc, ALU.mult); v.drain()
            v.tensor_scalar(tn, tn, -0.5, 1.5, ALU.mult, ALU.add); v.drain()
            v.tensor_tensor(rc, rc, tn, ALU.mult); v.drain()

    def emit_stats(self, tiles):
        @self.on("v")
        def _(v):
            for tt, tile in enumerate(tiles):
                v.tensor_reduce(self.s1c[:, tt:tt + 1], tile, AX.X, ALU.add)
                v.scalar_tensor_tensor(self.sq[:], tile, 1.0, tile, ALU.mult, ALU.mult,
                                       accum_out=self.s2c[:, tt:tt + 1])
            v.drain()
            v.tensor_scalar(self.muc[:], self.s1c[:], 1.0 / D, None, ALU.mult)
            v.tensor_scalar(self.varc[:], self.s2c[:], 1.0 / D, None, ALU.mult)
            v.drain()
            v.tensor_tensor(self.tnc[:], self.muc[:], self.muc[:], ALU.mult)
            v.drain()
            v.tensor_tensor(self.varc[:], self.varc[:], self.tnc[:], ALU.subtract)
            v.drain()

    def startup(self):
        nc, h, w = self.nc, self.h, self.w
        st = ExitStack()
        S = lambda n, s, dt=F32: st.enter_context(nc.sbuf_tensor(n, s, dt))
        vrow = S("vrow", [8, 5 * 128])
        grow = S("grow", [1, D])
        brow = S("brow", [1, D])
        gbrow = S("gbrow", [1, E])
        gwt_st = S("gwt_st", [16, D])
        be_st = S("be_st", [16, 2 * INTER])
        wg_st = S("wg_st", [128, 2 * D])
        wgT_st = S("wgT_st", [128, 2 * D], BF16)
        gbst = S("gbst", [128, 2 * D])
        P = self.P

        with st, nc.Block() as blk:
            @self.on("sy")
            def _(sy):
                for dst, src in [
                    (self.ident[:], self.ident_in), (self.ones_sb[:], self.ones_in),
                    (self.sel_sb[:], self.sel_in),
                    (grow[0:1, :], w["gamma"].rearrange("(a d) -> a d", a=1)),
                    (brow[0:1, :], w["beta"].rearrange("(a d) -> a d", a=1)),
                    (gbrow[:], w["gate_b"].rearrange("(a e) -> a e", a=1)),
                    (gwt_st[:], w["gate_w"]),
                    (vrow[0:8, 0:128], w["bg"].rearrange("(dt p) -> dt p", p=128)),
                    (vrow[0:8, 128:256], w["Dp"].rearrange("(dt p) -> dt p", p=128)),
                    (vrow[0:8, 256:384], w["bs1"].rearrange("(dt p) -> dt p", p=128)),
                    (vrow[0:8, 384:512], w["bs2"].rearrange("(dt p) -> dt p", p=128)),
                    (vrow[0:8, 512:640], w["bs3"].rearrange("(dt p) -> dt p", p=128)),
                    (be_st[:, 0:INTER], w["be1"]),
                    (be_st[:, INTER:], w["be3"]),
                ]:
                    sy.dma_start(out=dst, in_=src).then_inc(h["qS"], 16)
            qS_loads = self.bump("qS", 16 * 14)


            @self.on("v")
            def _(v):
                v.memset(self.epscol[:], EPS)
                v.memset(self.zpad[:], 0.0)
                v.drain()

            tE_sm = self.bump("tE")

            @self.on("t")
            def _(t, _w=qS_loads):
                t.wait_ge(h["qS"], _w)
                for k in range(5):
                    t.transpose(P[0][0:128, k * 8:(k + 1) * 8],
                                vrow[0:8, k * 128:(k + 1) * 128], self.ident[0:8, 0:8])
                t.matmul(P[1][:, 0:512], self.ones_sb[:, 0:128], grow[0:1, 0:512], start=True, stop=True)
                t.matmul(P[2][:, 0:512], self.ones_sb[:, 0:128], grow[0:1, 512:1024], start=True, stop=True)
                t.matmul(P[3][:, 0:512], self.ones_sb[:, 0:128], brow[0:1, 0:512], start=True, stop=True)
                t.matmul(P[4][:, 0:512], self.ones_sb[:, 0:128], brow[0:1, 512:1024], start=True, stop=True)
                t.matmul(P[5][:, 0:E], self.ones_sb[:, 0:128], gbrow[0:1, :], start=True, stop=True)
                for dt in range(NDT):
                    t.transpose(P[6][0:128, dt * 16:(dt + 1) * 16],
                                gwt_st[0:16, dt * 128:(dt + 1) * 128], self.ident[0:16, 0:16])
                mmi = None
                for ib in range(4):
                    t.transpose(P[7][0:128, ib * 16:(ib + 1) * 16],
                                be_st[0:16, ib * 128:(ib + 1) * 128], self.ident[0:16, 0:16])
                    mmi = t.transpose(P[7][0:128, 64 + ib * 16: 64 + (ib + 1) * 16],
                                      be_st[0:16, INTER + ib * 128: INTER + (ib + 1) * 128],
                                      self.ident[0:16, 0:16])
                mmi.then_inc(h["tE"], 1)

            sE_sm = self.bump("sE")
            self._sE_sm = sE_sm

            @self.on("s")
            def _(s, _w=tE_sm):
                s.wait_ge(h["tE"], _w)
                s.copy(self.vcol[:, 0:40], P[0][:, 0:40])
                s.copy(gbst[:, 0:512], P[1][:, 0:512])
                s.copy(gbst[:, 512:1024], P[2][:, 0:512])
                s.copy(gbst[:, 1024:1536], P[3][:, 0:512])
                s.copy(gbst[:, 1536:2048], P[4][:, 0:512])
                s.copy(self.gb_bcast[:], P[5][:, 0:E])
                s.copy(self.gate_wT[:], P[6][:, 0:NDT * E])
                s.copy(self.be1c[:], P[7][:, 0:64])
                s.copy(self.be3c[:], P[7][:, 64:128])
                s.drain()
                s.nop().then_inc(h["sE"], 1)

            qG_gb = self.bump("qG", 16)

            @self.on("g")
            def _(g, _w=sE_sm):
                g.wait_ge(h["sE"], _w)
                g.dma_start(out=self.gb_dram.rearrange("a q d2 -> q a d2"),
                            in_=gbst[:].rearrange("q (a d2) -> q a d2", d2=D)).then_inc(h["qG"], 16)

            qS_wg0 = self.c["qS"] + 16
            self.bump("qS", 16 * NDT)
            tE_wg0 = self.c["tE"] + 1
            self.bump("tE", NDT)
            sE_wg0 = self.c["sE"] + 1
            self.bump("sE", NDT)
            qG_wgT0 = self.c["qG"] + 16
            self.bump("qG", 16 * NDT)

            @self.on("sy")
            def _(sy):
                for odt in range(NDT):
                    if odt >= 2:
                        sy.wait_ge(h["tE"], tE_wg0 + odt - 2)
                    sy.dma_start(out=wg_st[:, (odt % 2) * D:(odt % 2) * D + D],
                                 in_=w["Wg"].rearrange("(odt q) d2 -> odt q d2", q=128)[odt]
                                 ).then_inc(h["qS"], 16)

            @self.on("t")
            def _(t, _s=sE_sm):
                t.wait_ge(h["sE"], _s)
                for odt in range(NDT):
                    t.wait_ge(h["qS"], qS_wg0 + odt * 16)
                    if odt >= 1:
                        t.wait_ge(h["sE"], sE_wg0 + odt - 1)
                    b0, b1 = 2 * (odt % 2), 2 * (odt % 2) + 1
                    mmi = None
                    for kdt in range(NDT):
                        pb = P[b0] if kdt < 4 else P[b1]
                        mmi = t.transpose(pb[0:128, (kdt % 4) * 128:(kdt % 4 + 1) * 128],
                                          wg_st[:, (odt % 2) * D + kdt * 128:(odt % 2) * D + (kdt + 1) * 128],
                                          self.ident[:])
                    mmi.then_inc(h["tE"], 1)

            @self.on("s")
            def _(s):
                for odt in range(NDT):
                    s.wait_ge(h["tE"], tE_wg0 + odt)
                    if odt >= 2:
                        s.wait_ge(h["qG"], qG_wgT0 + 16 * (odt - 2))
                    b0, b1 = 2 * (odt % 2), 2 * (odt % 2) + 1
                    s.copy(wgT_st[:, (odt % 2) * D:(odt % 2) * D + 512], P[b0][:, 0:512])
                    s.copy(wgT_st[:, (odt % 2) * D + 512:(odt % 2) * D + D], P[b1][:, 0:512])
                    s.drain()
                    s.nop().then_inc(h["sE"], 1)

            @self.on("g")
            def _(g):
                for odt in range(NDT):
                    g.wait_ge(h["sE"], sE_wg0 + odt)
                    g.dma_start(out=self.wgT_dram.rearrange("(kdt p) e2 -> p kdt e2", p=128)[:, :, odt * 128:(odt + 1) * 128],
                                in_=wgT_st[:, (odt % 2) * D:(odt % 2) * D + D]
                                .rearrange("p (kdt q) -> p kdt q", q=128)).then_inc(h["qG"], 16)

            if self.debug and "wgT" in self.dbg:
                self.bump("qG", 80)

                @self.on("g")
                def _(g, _q=self.c["qG"] - 80):
                    g.wait_ge(h["qG"], _q)
                    g.dma_start(out=self.dbg["wgT"], in_=self.wgT_dram).then_inc(h["qG"], 16)
                    g.dma_start(out=self.dbg["diag"], in_=self.diag_dram).then_inc(h["qG"], 16)
                    g.dma_start(out=self.dbg["vcol"], in_=self.vcol[:]).then_inc(h["qG"], 16)
                    g.dma_start(out=self.dbg["hbuf"], in_=self.hbuf[:]).then_inc(h["qG"], 16)
                    g.dma_start(out=self.dbg["gammab"], in_=self.gb_dram[0]).then_inc(h["qG"], 16)

            self.fence()
            self.flush(blk)

    def emit_taps(self, A_f, Bp_f, Cp_f, u1, dg_st, qs_ld, se_sm):
        h = self.h
        vE_tap = self.bump("vE")

        @self.on("v")
        def _(v, _w=qs_ld, _ws=se_sm):
            v.wait_ge(h["qS"], _w)
            v.wait_ge(h["sE"], _ws)
            A3 = A_f[:].rearrange("p (dt n m) -> p (dt n) m", n=NST, m=NST)
            hb3 = lambda j: (self.hbuf[:].rearrange("p (dt jj) -> p dt jj", jj=J)
                             [:, :, j:j + 1].rearrange("p dt one -> p (dt one)"))
            v.tensor_copy(u1[:, 0:128], Bp_f[:])
            v.drain()
            for j in range(J):
                v.tensor_tensor(u1[:, 128:256], u1[:, 0:128], Cp_f[:], ALU.mult)
                v.drain()
                v.tensor_reduce(hb3(j), u1[:, 128:256].rearrange("p (dt n) -> p dt n", n=NST),
                                AX.X, ALU.add)
                v.drain()
                if j < J - 1:
                    v.tensor_tensor(u1[:, 256:2304].rearrange("p (dtn m) -> p dtn m", m=NST),
                                    A3, u1[:, 0:128].broadcast_to([128, 128, 16]), ALU.mult)
                    v.drain()
                    for half in range(2):
                        v.tensor_reduce(u1[:, half * 64: half * 64 + 64]
                                        .rearrange("p (dt m) -> p dt m", m=NST),
                                        u1[:, 256 + half * 1024: 256 + half * 1024 + 1024]
                                        .rearrange("p (dt n m) -> p dt m n", n=NST, m=NST),
                                        AX.X, ALU.add)
                    v.drain()
            v.tensor_tensor(hb3(0), hb3(0), self.vcol[:, NDT:2 * NDT], ALU.add)
            v.drain()
            v.nop().then_inc(h["vE"], 1)

        vE_dg0 = self.c["vE"] + 1
        self.bump("vE", NDT)
        qG_dg0 = self.c["qG"] + 16
        self.bump("qG", 16 * NDT + 16 * NDT)

        @self.on("v")
        def _(v, _w=vE_tap):
            v.wait_ge(h["vE"], _w)
            for dt in range(NDT):
                sl = (dt % 2) * J * 128
                if dt >= 2:
                    v.wait_ge(h["qG"], qG_dg0 + 16 * (dt - 2))
                for j in range(J):
                    v.tensor_scalar(dg_st[:, sl + j * 128: sl + (j + 1) * 128],
                                    self.ident[:],
                                    self.hbuf[:, dt * J + j: dt * J + j + 1],
                                    None, ALU.mult)
                v.drain()
                v.nop().then_inc(h["vE"], 1)

        @self.on("g")
        def _(g):
            for dt in range(NDT):
                g.wait_ge(h["vE"], vE_dg0 + dt)
                sl = (dt % 2) * J * 128
                g.dma_start(out=self.diag_dram[dt],
                            in_=dg_st[:, sl: sl + J * 128]).then_inc(h["qG"], 16)
            for dt in range(NDT):
                g.dma_start(out=self.xg_b[dt, :, 0:J], in_=self.zpad[:]).then_inc(h["qG"], 16)

    def phaseA1(self, p, T):
        h, P = self.h, self.P
        xsl, ln1 = T["xsl"], T["ln1"]
        l1f, l1b = T["l1f"], T["l1b"]
        wgT, gt, xgf = T["wgT"], T["gt"], T["xgf"]

        qs_wgT = self.bump("qS", 16)

        @self.on("sy")
        def _(sy):
            sy.dma_start(out=wgT[:].rearrange("pp (kdt e2) -> pp kdt e2", e2=D),
                         in_=self.wgT_dram.rearrange("(kdt pp) e2 -> pp kdt e2", pp=128)
                         ).then_inc(h["qS"], 16)

        ve_ln1 = [None] * NCH
        qsx = [None] * NCH
        for c in range(NCH):
            qsx[c] = self.bump("qS", 16)

            @self.on("sy")
            def _(sy, _c=c):
                if _c >= 2:
                    sy.wait_ge(h["vE"], ve_ln1[_c - 2])
                sy.dma_start(out=xsl[:, (_c % 2) * 4096:(_c % 2) * 4096 + 4096]
                             .rearrange("pp (tt d) -> pp tt d", d=D),
                             in_=self.x_in[p].rearrange("(cc tt q) d -> cc q tt d", q=128, tt=4)[_c]
                             ).then_inc(h["qS"], 16)

            xtile = lambda tt, _c=c: xsl[:, (_c % 2) * 4096 + tt * 1024:(_c % 2) * 4096 + (tt + 1) * 1024]

            @self.on("v")
            def _(v, _c=c, _q=qsx[c]):
                v.wait_ge(h["qS"], _q)
            self.emit_stats([xtile(tt) for tt in range(4)])
            self.emit_rstd()
            ve_ln1[c] = self.bump("vE")

            @self.on("v")
            def _(v, _xt=xtile):
                for tt in range(4):
                    v.tensor_scalar(ln1[:, tt * 1024:(tt + 1) * 1024], _xt(tt),
                                    self.muc[:, tt:tt + 1], self.rstdc[:, tt:tt + 1],
                                    ALU.subtract, ALU.mult)
                v.drain()
                v.nop().then_inc(h["vE"], 1)

            if self.debug and "ln1" in self.dbg:
                self.bump("qS", 16)

                @self.on("sy")
                def _(sy, _c=c, _v=ve_ln1[c] + 1, _p=p):
                    sy.wait_ge(h["vE"], _v)
                    sy.dma_start(out=self.dbg["ln1"][_p].rearrange("(cc tt q) d -> cc q tt d", q=128, tt=4)[_c],
                                 in_=ln1[:].rearrange("q (tt d) -> q tt d", d=D)).then_inc(h["qS"], 16)

            te_tp = [self.bump("tE") for _ in range(NDT)]
            se_tp = [self.bump("sE") for _ in range(NDT)]

            @self.on("t")
            def _(t, _c=c, _v=ve_ln1[c], _te=tuple(te_tp), _se=tuple(se_tp)):
                t.wait_ge(h["vE"], _v)
                for dt in range(NDT):
                    if dt >= 2:
                        t.wait_ge(h["sE"], _se[dt - 2])
                    mmi = None
                    for tt in range(4):
                        mmi = t.transpose(P[dt % 2][0:128, tt * 128:(tt + 1) * 128],
                                          ln1[:, tt * 1024 + dt * 128: tt * 1024 + (dt + 1) * 128],
                                          self.ident[:])
                    mmi.then_inc(h["tE"], 1)

            @self.on("s")
            def _(s, _te=tuple(te_tp)):
                for dt in range(NDT):
                    s.wait_ge(h["tE"], _te[dt])
                    s.copy(l1b[:, dt * 512:(dt + 1) * 512], P[dt % 2][:, 0:512])
                    s.copy(l1f[:, dt * 512:(dt + 1) * 512], P[dt % 2][:, 0:512])
                    s.drain()
                    s.nop().then_inc(h["sE"], 1)

            te_g = [self.bump("tE") for _ in range(NDT)]
            se_g = [self.bump("sE") for _ in range(NDT)]
            ve_xg = [self.bump("vE") for _ in range(NDT)]
            qs_xgf = [self.bump("qS", 16) for _ in range(NDT)]
            qg_xgb = [self.bump("qG", 16) for _ in range(NDT)]

            @self.on("t")
            def _(t, _c=c, _sall=se_tp[NDT - 1], _se=tuple(se_g), _q=qs_wgT):
                t.wait_ge(h["sE"], _sall)
                t.wait_ge(h["qS"], _q)
                for odt in range(NDT):
                    if odt >= 2:
                        t.wait_ge(h["sE"], _se[odt - 2])
                    mmi = None
                    for kdt in range(NDT):
                        mmi = t.matmul(P[2 + odt % 2][:, 0:512],
                                       wgT[:, kdt * D + odt * 128: kdt * D + (odt + 1) * 128],
                                       l1b[:, kdt * 512:(kdt + 1) * 512],
                                       start=(kdt == 0), stop=(kdt == NDT - 1))
                    mmi.then_inc(h["tE"], 1)

            @self.on("s")
            def _(s, _te=tuple(te_g), _ve=tuple(ve_xg)):
                for odt in range(NDT):
                    s.wait_ge(h["tE"], _te[odt])
                    if odt >= 2:
                        s.wait_ge(h["vE"], _ve[odt - 2])
                    s.activation(gt[:, (odt % 2) * 512:(odt % 2) * 512 + 512],
                                 P[2 + odt % 2][:, 0:512], AF.Sigmoid,
                                 bias=self.col("bg", odt), scale=1.0)
                    s.drain()
                    s.nop().then_inc(h["sE"], 1)

            @self.on("v")
            def _(v, _se=tuple(se_g), _qsx=tuple(qs_xgf), _qgx=tuple(qg_xgb)):
                for odt in range(NDT):
                    v.wait_ge(h["sE"], _se[odt])
                    if odt >= 2:
                        v.wait_ge(h["qS"], _qsx[odt - 2])
                        v.wait_ge(h["qG"], _qgx[odt - 2])
                    v.tensor_tensor(xgf[:, (odt % 2) * 512:(odt % 2) * 512 + 512],
                                    l1f[:, odt * 512:(odt + 1) * 512],
                                    gt[:, (odt % 2) * 512:(odt % 2) * 512 + 512], ALU.mult)
                    v.drain()
                    v.nop().then_inc(h["vE"], 1)

            @self.on("sy")
            def _(sy, _c=c, _ve=tuple(ve_xg)):
                for odt in range(NDT):
                    sy.wait_ge(h["vE"], _ve[odt])
                    sy.dma_start(out=self.xg_f[odt, :, _c * 512:(_c + 1) * 512],
                                 in_=xgf[:, (odt % 2) * 512:(odt % 2) * 512 + 512]
                                 ).then_inc(h["qS"], 16)

            @self.on("g")
            def _(g, _c=c, _ve=tuple(ve_xg)):
                for odt in range(NDT):
                    g.wait_ge(h["vE"], _ve[odt])
                    g.dma_start(out=self.xg_b[odt, :, J + _c * 512: J + (_c + 1) * 512],
                                in_=xgf[:, (odt % 2) * 512:(odt % 2) * 512 + 512]
                                ).then_inc(h["qG"], 16)
            if self.debug and "gate" in self.dbg:
                self.bump("qG", 16 * NDT)

                @self.on("g")
                def _(g, _c=c, _se=tuple(se_g), _p=p):
                    for odt in range(NDT):
                        g.wait_ge(h["sE"], _se[odt])
                        g.dma_start(out=self.dbg["gate"][_p, odt, :, _c * 512:(_c + 1) * 512],
                                    in_=gt[:, (odt % 2) * 512:(odt % 2) * 512 + 512]
                                    ).then_inc(h["qG"], 16)

        if p == 0:
            A_f, Bp_f, Cp_f, u1, dg_st = T["tap"]
            qs_abc = self.bump("qS", 48)

            @self.on("sy")
            def _(sy):
                sy.dma_start(out=A_f[:].rearrange("p (dt nm) -> p dt nm", nm=256),
                             in_=self.w["A"].rearrange("(dt p) n m -> p dt (n m)", p=128)
                             ).then_inc(h["qS"], 16)
                sy.dma_start(out=Bp_f[:].rearrange("p (dt n) -> p dt n", n=NST),
                             in_=self.w["Bp"].rearrange("(dt p) n -> p dt n", p=128)
                             ).then_inc(h["qS"], 16)
                sy.dma_start(out=Cp_f[:].rearrange("p (dt n) -> p dt n", n=NST),
                             in_=self.w["Cp"].rearrange("(dt p) n -> p dt n", p=128)
                             ).then_inc(h["qS"], 16)

            self.emit_taps(A_f, Bp_f, Cp_f, u1, dg_st, qs_abc, self._sE_sm)
            self.qG_casts = self.bump("qG", 16 * 6)

            @self.on("g")
            def _(g):
                for dst, src in [(self.wb["We1"], self.w["We1"]), (self.wb["We3"], self.w["We3"]),
                                 (self.wb["We2"], self.w["We2"]),
                                 (self.wb["ws1"], self.w["ws1"]), (self.wb["ws3"], self.w["ws3"]),
                                 (self.wb["ws2"], self.w["ws2"])]:
                    g.dma_start(out=dst, in_=src).then_inc(h["qG"], 16)

    def phaseA2(self, p, T):
        h, P = self.h, self.P
        dg, xb, xf = T["dg"], T["xb"], T["xf"]
        yF, ytp, xre, ln3 = T["yF"], T["ytp"], T["xre"], T["ln3"]
        l3f = T["l3f"]
        W = J + 1024

        qs_gb = self.bump("qS", 16)

        @self.on("sy")
        def _(sy):
            sy.dma_start(out=self.gamma_b[:], in_=self.gb_dram[0]).then_inc(h["qS"], 16)

        qs_gb2 = self.bump("qS", 16)

        @self.on("sy")
        def _(sy):
            sy.dma_start(out=self.beta_b[:], in_=self.gb_dram[1]).then_inc(h["qS"], 16)
        self._qs_gb = qs_gb2

        conv_hist = []
        ytr_hist = []
        join_hist = []

        seyt_hist = []
        self._a2_prev_ve_x1 = None

        def chain_gen(pr_, cc_):
            c = 2 * pr_ + cc_
            ytile = lambda tt, _cc=cc_, _pr=pr_: ytp[:, (_pr % 2) * 8192 + (_cc * 4 + tt) * 1024:(_pr % 2) * 8192 + (_cc * 4 + tt + 1) * 1024]
            qg_xre = self.bump("qG", 16)

            @self.on("g")
            def _(g, _c=c, _vx=self._a2_prev_ve_x1):
                if _vx is not None:
                    g.wait_ge(h["vE"], _vx)
                g.dma_start(out=xre[:].rearrange("q (tt d) -> q tt d", d=D),
                            in_=self.x_in[p].rearrange("(ch tt q) d -> ch q tt d", q=128, tt=4)[_c]
                            ).then_inc(h["qG"], 16)

            @self.on("v")
            def _(v, _s=seyt_hist[pr_], _q=self._qs_gb):
                v.wait_ge(h["sE"], _s)
                v.wait_ge(h["qS"], _q)
            self.emit_stats([ytile(tt) for tt in range(4)])
            yield
            self.emit_rstd()

            def attn_apply(tts):
                @self.on("v")
                def _(v, _yt=ytile, _tts=tuple(tts)):
                    for tt in _tts:
                        tile = _yt(tt)
                        v.tensor_scalar(self.sq[:], tile, self.muc[:, tt:tt + 1],
                                        self.rstdc[:, tt:tt + 1], ALU.subtract, ALU.mult)
                        v.drain()
                        v.tensor_tensor(self.sq[:], self.sq[:], self.gamma_b[:], ALU.mult)
                        v.drain()
                        v.scalar_tensor_tensor(tile, self.sq[:], 1.0, self.beta_b[:],
                                               ALU.mult, ALU.add,
                                               accum_out=self.s1c[:, tt:tt + 1])
                        v.drain()
                        v.scalar_tensor_tensor(self.sq[:], tile, 1.0, tile,
                                               ALU.mult, ALU.mult,
                                               accum_out=self.s2c[:, tt:tt + 1])
                        v.drain()
            attn_apply([0, 1])
            yield
            attn_apply([2, 3])

            @self.on("v")
            def _(v):
                v.tensor_scalar(self.muc[:], self.s1c[:], 1.0 / D, None, ALU.mult)
                v.tensor_scalar(self.varc[:], self.s2c[:], 1.0 / D, None, ALU.mult)
                v.drain()
                v.tensor_tensor(self.tnc[:], self.muc[:], self.muc[:], ALU.mult)
                v.drain()
                v.tensor_tensor(self.varc[:], self.varc[:], self.tnc[:], ALU.subtract)
                v.drain()
            self.emit_rstd()
            yield

            def ln2_apply(tts, first):
                @self.on("v")
                def _(v, _yt=ytile, _tts=tuple(tts), _first=first, _q=qg_xre):
                    if _first:
                        v.wait_ge(h["qG"], _q)
                    for tt in _tts:
                        tile = _yt(tt)
                        v.tensor_scalar(self.sq[:], tile, self.muc[:, tt:tt + 1],
                                        self.rstdc[:, tt:tt + 1], ALU.subtract, ALU.mult)
                        v.drain()
                        v.tensor_tensor(tile, self.sq[:],
                                        xre[:, tt * 1024:(tt + 1) * 1024],
                                        ALU.add)
                        v.drain()
                        v.tensor_reduce(self.s1c[:, tt:tt + 1], tile, AX.X, ALU.add)
                        v.scalar_tensor_tensor(self.sq[:], tile, 1.0, tile, ALU.mult, ALU.mult,
                                               accum_out=self.s2c[:, tt:tt + 1])
                    v.drain()
            ln2_apply([0, 1], True)
            yield
            ln2_apply([2, 3], False)
            ve_x1 = self.bump("vE")

            @self.on("v")
            def _(v):
                v.tensor_scalar(self.muc[:], self.s1c[:], 1.0 / D, None, ALU.mult)
                v.tensor_scalar(self.varc[:], self.s2c[:], 1.0 / D, None, ALU.mult)
                v.drain()
                v.tensor_tensor(self.tnc[:], self.muc[:], self.muc[:], ALU.mult)
                v.drain()
                v.tensor_tensor(self.varc[:], self.varc[:], self.tnc[:], ALU.subtract)
                v.drain()
                v.nop().then_inc(h["vE"], 1)
            self.emit_rstd()
            self._a2_prev_ve_x1 = ve_x1
            self.bump("qG", 16 * (2 if STAGE == 1 else 1))

            @self.on("g")
            def _(g, _c=c, _cc=cc_, _v=ve_x1, _pr2=pr_ % 2):
                g.wait_ge(h["vE"], _v)
                g.dma_start(out=self.x1_scr.rearrange("(ch tt q) d -> ch q tt d", q=128, tt=4)[_c],
                            in_=ytp[:, _pr2 * 8192 + _cc * 4096: _pr2 * 8192 + (_cc + 1) * 4096]
                            .rearrange("q (tt d) -> q tt d", d=D)).then_inc(h["qG"], 16)
                if STAGE == 1:
                    g.dma_start(out=self.out[p].rearrange("(ch tt q) d -> ch q tt d", q=128, tt=4)[_c],
                                in_=ytp[:, _pr2 * 8192 + _cc * 4096: _pr2 * 8192 + (_cc + 1) * 4096]
                                .rearrange("q (tt d) -> q tt d", d=D)).then_inc(h["qG"], 16)
            yield
            if cc_ == 0:
                ve_ln3 = self.bump("vE")

                @self.on("v")
                def _(v, _yt=ytile):
                    for tt in range(4):
                        v.tensor_scalar(ln3[:, tt * 1024:(tt + 1) * 1024], _yt(tt),
                                        self.muc[:, tt:tt + 1], self.rstdc[:, tt:tt + 1],
                                        ALU.subtract, ALU.mult)
                    v.drain()
                    v.nop().then_inc(h["vE"], 1)
                self._ve_ln3_c0 = ve_ln3

        def drive(it, n):
            k = 0
            while k < n:
                try:
                    next(it)
                except StopIteration:
                    return
                k += 1

        self._chain_it = None
        for pr in range(2):
            c0 = 2 * pr
            se_yt = []
            if pr == 1:
                def _chain2():
                    yield from chain_gen(0, 0)
                    yield from chain_gen(0, 1)
                self._chain_it = _chain2()
            for dt in range(NDT):
                gi = pr * NDT + dt
                qs_dg = self.bump("qS", 48)

                @self.on("sy")
                def _(sy, _dt=dt, _c0=c0, _gi=gi,
                      _g=(conv_hist[gi - 2] if gi >= 2 else None),
                      _gj=(join_hist[gi - 1] if gi >= 1 else None)):
                    if _g is not None:
                        sy.wait_ge(h["tE"], _g)
                    if _gj is not None:
                        sy.wait_ge(h["vE"], _gj)
                    sl = _dt % 2
                    sy.dma_start(out=dg[:, sl * J * 128:(sl + 1) * J * 128],
                                 in_=self.diag_dram[_dt]).then_inc(h["qS"], 16)
                    sy.dma_start(out=xb[:, sl * W: sl * W + W],
                                 in_=self.xg_b[_dt, :, _c0 * 512: _c0 * 512 + W]
                                 ).then_inc(h["qS"], 16)
                    sy.dma_start(out=xf[:],
                                 in_=self.xg_f[_dt, :, _c0 * 512: _c0 * 512 + 1024]
                                 ).then_inc(h["qS"], 16)

                te_conv = self.bump("tE")
                conv_hist.append(te_conv)
                ve_join = self.bump("vE")
                join_hist.append(ve_join)
                te_yt = self.bump("tE")
                ytr_hist.append(te_yt)
                se_yt.append(self.bump("sE"))

                @self.on("t")
                def _(t, _dt=dt, _q=qs_dg, _gi=gi,
                      _vjprev=(join_hist[gi - 1] if gi >= 1 else None),
                      _vj=ve_join,
                      _seprev=(se_yt[dt - 1] if dt >= 1 else None),
                      _se_prev_pair=(self._seyt_last if pr == 1 and dt == 0 else None)):
                    sl = _dt % 2
                    t.wait_ge(h["qS"], _q)
                    if _vjprev is not None:
                        t.wait_ge(h["vE"], _vjprev)
                    mmi = None
                    for j in range(J):
                        for cc in range(2):
                            mmi = t.matmul(P[cc][:, 0:512],
                                           dg[:, sl * J * 128 + j * 128: sl * J * 128 + (j + 1) * 128],
                                           xb[:, sl * W + J + cc * 512 - j: sl * W + J + cc * 512 - j + 512],
                                           start=(j == 0), stop=(j == J - 1),
                                           skip_group_check=True)
                    mmi.then_inc(h["tE"], 1)
                    t.wait_ge(h["vE"], _vj)
                    if _seprev is not None:
                        t.wait_ge(h["sE"], _seprev)
                    if _se_prev_pair is not None:
                        t.wait_ge(h["sE"], _se_prev_pair)
                    mmi = None
                    for wd in range(8):
                        pb = P[2] if wd < 4 else P[3]
                        mmi = t.transpose(pb[0:128, (wd % 4) * 128:(wd % 4 + 1) * 128],
                                          yF[:, sl * 1024 + wd * 128: sl * 1024 + (wd + 1) * 128],
                                          self.ident[:])
                    mmi.then_inc(h["tE"], 1)

                @self.on("v")
                def _(v, _dt=dt, _tc=te_conv, _gi=gi,
                      _typrev=(ytr_hist[gi - 2] if gi >= 2 else None)):
                    v.wait_ge(h["tE"], _tc)
                    if _typrev is not None:
                        v.wait_ge(h["tE"], _typrev)
                    sl = _dt % 2
                    v.tensor_tensor(yF[:, sl * 1024: sl * 1024 + 1024],
                                    self.PA[:, 0:1024], xf[:], ALU.add)
                    v.drain()
                    v.nop().then_inc(h["vE"], 1)

                @self.on("s")
                def _(s, _dt=dt, _ty=te_yt, _pr=pr):
                    s.wait_ge(h["tE"], _ty)
                    s.copy(ytp[:, (_pr % 2) * 8192:(_pr % 2) * 8192 + 8192].rearrange("q (wd d1) -> q wd d1", d1=1024)[:, 0:4, _dt * 128:(_dt + 1) * 128],
                           P[2][:, 0:512].rearrange("q (wd d1) -> q wd d1", d1=128))
                    s.copy(ytp[:, (_pr % 2) * 8192:(_pr % 2) * 8192 + 8192].rearrange("q (wd d1) -> q wd d1", d1=1024)[:, 4:8, _dt * 128:(_dt + 1) * 128],
                           P[3][:, 0:512].rearrange("q (wd d1) -> q wd d1", d1=128))
                    s.drain()
                    s.nop().then_inc(h["sE"], 1)

                if pr == 1:
                    drive(self._chain_it, 1)

            if pr == 1:
                drive(self._chain_it, 99)
            seyt_hist.append(se_yt[NDT - 1])
            self._seyt_last = se_yt[NDT - 1]

            if self.debug and "y" in self.dbg:
                self.bump("qS", 16)

                @self.on("sy")
                def _(sy, _pr=pr, _s=se_yt[NDT - 1], _p=p):
                    sy.wait_ge(h["sE"], _s)
                    sy.dma_start(out=self.dbg["y"][_p].rearrange("(q2 tt q) d -> q2 q tt d", q=128, tt=8)[_pr],
                                 in_=ytp[:, (_pr % 2) * 8192:(_pr % 2) * 8192 + 8192].rearrange("q (tt d) -> q tt d", d=D)).then_inc(h["qS"], 16)
            if self.debug and "xg" in self.dbg and pr == 0:
                self.bump("qG", 16 * NDT)

                @self.on("g")
                def _(g, _p=p, _q=self.c["qS"]):
                    g.wait_ge(h["qS"], _q)
                    for dt in range(NDT):
                        g.dma_start(out=self.dbg["xg"][_p, dt], in_=self.xg_f[dt]).then_inc(h["qG"], 16)

        for pr in range(2):
            c0 = 2 * pr
            for cc in range(2):
                c = c0 + cc
                ytile = lambda tt, _cc=cc, _pr=pr: ytp[:, (_pr % 2) * 8192 + (_cc * 4 + tt) * 1024:(_pr % 2) * 8192 + (_cc * 4 + tt + 1) * 1024]

                if pr == 0 and cc == 0:
                    ve_ln3 = self._ve_ln3_c0
                elif pr == 0 and cc == 1:
                    ve_ln3 = self.bump("vE")

                    @self.on("v")
                    def _(v, _yt=ytile, _t=self._te_l3_last):
                        v.wait_ge(h["tE"], _t)
                        for tt in range(4):
                            v.tensor_scalar(ln3[:, tt * 1024:(tt + 1) * 1024], _yt(tt),
                                            self.muc[:, tt:tt + 1], self.rstdc[:, tt:tt + 1],
                                            ALU.subtract, ALU.mult)
                        v.drain()
                        v.nop().then_inc(h["vE"], 1)
                else:
                    cg = chain_gen(pr, cc)
                    drive(cg, 99)
                    if cc == 0:
                        ve_ln3 = self._ve_ln3_c0
                    else:
                        ve_ln3 = self.bump("vE")

                        @self.on("v")
                        def _(v, _yt=ytile, _t=self._te_l3_last):
                            v.wait_ge(h["tE"], _t)
                            for tt in range(4):
                                v.tensor_scalar(ln3[:, tt * 1024:(tt + 1) * 1024], _yt(tt),
                                                self.muc[:, tt:tt + 1], self.rstdc[:, tt:tt + 1],
                                                ALU.subtract, ALU.mult)
                            v.drain()
                            v.nop().then_inc(h["vE"], 1)

                if self.debug and "ln3" in self.dbg:
                    self.bump("qS", 16)

                    @self.on("sy")
                    def _(sy, _c=c, _v=ve_ln3):
                        sy.wait_ge(h["vE"], _v)
                        sy.dma_start(out=self.dbg["ln3"][p].rearrange("(ch tt q) d -> ch q tt d", q=128, tt=4)[_c],
                                     in_=ln3[:].rearrange("q (tt d) -> q tt d", d=D)).then_inc(h["qS"], 16)

                se_l3 = []
                te_sc = []
                for dt in range(NDT):
                    te_l3 = self.bump("tE")
                    se_l3.append(self.bump("sE"))
                    te_sc.append(self.bump("tE"))

                    @self.on("t")
                    def _(t, _dt=dt, _v=ve_ln3, _se=se_l3[dt],
                          _seprev=(se_l3[dt - 2] if dt >= 2 else None),
                          _sg=(seyt_hist[1] if (pr == 0 and dt < 2) else None)):
                        if _dt == 0:
                            t.wait_ge(h["vE"], _v)
                        if _sg is not None:
                            t.wait_ge(h["sE"], _sg)
                        if _seprev is not None:
                            t.wait_ge(h["sE"], _seprev)
                        mmi = None
                        for tt in range(4):
                            mmi = t.transpose(P[2 + _dt % 2][0:128, tt * 128:(tt + 1) * 128],
                                              ln3[:, tt * 1024 + _dt * 128: tt * 1024 + (_dt + 1) * 128],
                                              self.ident[:])
                        mmi.then_inc(h["tE"], 1)
                        t.wait_ge(h["sE"], _se)
                        t.matmul(P[6][0:16, 0:512], self.gate_wT[:, _dt * 16:(_dt + 1) * 16],
                                 l3f[:, (_dt % 2) * 512:(_dt % 2) * 512 + 512],
                                 start=(_dt == 0), stop=(_dt == NDT - 1),
                                 skip_group_check=True).then_inc(h["tE"], 1)

                    @self.on("s")
                    def _(s, _c=c, _dt=dt, _tl=te_l3,
                          _tscprev=(te_sc[dt - 2] if dt >= 2 else None)):
                        s.wait_ge(h["tE"], _tl)
                        if _tscprev is not None:
                            s.wait_ge(h["tE"], _tscprev)
                        s.copy(self.l3x_bf[:, _dt * L + _c * 512: _dt * L + (_c + 1) * 512],
                               P[2 + _dt % 2][:, 0:512])
                        s.copy(l3f[:, (_dt % 2) * 512:(_dt % 2) * 512 + 512],
                               P[2 + _dt % 2][:, 0:512])
                        s.drain()
                        s.nop().then_inc(h["sE"], 1)

                self._te_l3_last = te_l3
                self.emit_routing(c, te_sc[NDT - 1], p)

            self.fence()

    def emit_routing(self, c, te_scores, p=0):
        h, P, rt = self.h, self.P, self.rt
        ve_sc = self.bump("vE")

        @self.on("v")
        def _(v, _t=te_scores):
            v.wait_ge(h["tE"], _t)
            v.tensor_copy(self.sc_sb[0:16, 0:512], P[6][0:16, 0:512])
            v.drain()
            v.nop().then_inc(h["vE"], 1)

        te_t1 = self.bump("tE")

        @self.on("t")
        def _(t, _v=ve_sc):
            t.wait_ge(h["vE"], _v)
            mmi = None
            for tt in range(4):
                mmi = t.transpose(P[7][0:128, tt * 16:(tt + 1) * 16],
                                  self.sc_sb[0:16, tt * 128:(tt + 1) * 128],
                                  self.ident[0:16, 0:16])
            mmi.then_inc(h["tE"], 1)

        ve_rt = self.bump("vE")

        @self.on("v")
        def _(v, _t=te_t1):
            v.wait_ge(h["tE"], _t)
            sc = rt["sd"]
            v.tensor_copy(sc[:], P[7][:, 0:64])
            v.drain()
            v3 = lambda a: a[:].rearrange("q (tt e) -> q tt e", e=16)
            g3 = lambda a: a[:].rearrange("q (tg j) -> q tg j", j=4)
            bc = lambda a, n=16: a[:].broadcast_to([128, a.shape[1] if hasattr(a, 'shape') else 4, n])
            v.tensor_reduce(rt["m"][:], v3(sc), AX.X, ALU.max)
            v.drain()
            v.tensor_tensor(v3(rt["ex"]), v3(sc), rt["m"][:].broadcast_to([128, 4, 16]), ALU.subtract)
            v.drain()
            v.nop().then_inc(h["vE"], 1)
        sE_exp = self.bump("sE")

        @self.on("s")
        def _(s, _v=ve_rt):
            s.wait_ge(h["vE"], _v)
            s.activation(rt["ex"][:], rt["ex"][:], AF.Exp)
            s.drain()
            s.nop().then_inc(h["sE"], 1)

        ve_comb = self.bump("vE")

        @self.on("v")
        def _(v, _s=sE_exp):
            v3 = lambda a: a[:].rearrange("q (tt e) -> q tt e", e=16)
            g3 = lambda a: a[:].rearrange("q (tg j) -> q tg j", j=4)
            v.wait_ge(h["sE"], _s)
            v.tensor_reduce(rt["sum"][:], v3(rt["ex"]), AX.X, ALU.add)
            v.drain()
            v.reciprocal(rt["rec"][:], rt["sum"][:])
            v.drain()
            v.tensor_tensor(v3(rt["orig"]), v3(rt["ex"]),
                            rt["rec"][:].broadcast_to([128, 4, 16]), ALU.mult)
            v.drain()
            gbv = self.gb_bcast[:, 0:16].broadcast_to([128, 16, 4]).rearrange("q e tt -> q tt e")
            v.tensor_tensor(v3(rt["sc2"]), v3(rt["orig"]), gbv, ALU.add)
            v.drain()
            v.tensor_reduce(rt["m1"][:], g3(rt["sc2"]), AX.X, ALU.max)
            v.drain()
            v.tensor_tensor(g3(rt["eqa"]), g3(rt["sc2"]),
                            rt["m1"][:].broadcast_to([128, 16, 4]), ALU.is_equal)
            v.drain()
            v.scalar_tensor_tensor(rt["sgm"][:], rt["eqa"][:], NEG, rt["sc2"][:],
                                   ALU.mult, ALU.add)
            v.drain()
            v.tensor_reduce(rt["m2"][:], g3(rt["sgm"]), AX.X, ALU.max)
            v.drain()
            v.tensor_tensor(rt["gsc"][:], rt["m1"][:], rt["m2"][:], ALU.add)
            v.drain()
            q3 = lambda a: a[:].rearrange("q (tt gg) -> q tt gg", gg=4)
            v.tensor_reduce(rt["gm1"][:], q3(rt["gsc"]), AX.X, ALU.max)
            v.drain()
            v.tensor_tensor(q3(rt["eqg"]), q3(rt["gsc"]),
                            rt["gm1"][:].broadcast_to([128, 4, 4]), ALU.is_equal)
            v.drain()
            v.scalar_tensor_tensor(rt["gsc2"][:], rt["eqg"][:], NEG, rt["gsc"][:],
                                   ALU.mult, ALU.add)
            v.drain()
            v.tensor_reduce(rt["gm2"][:], q3(rt["gsc2"]), AX.X, ALU.max)
            v.drain()
            v.tensor_tensor(q3(rt["keep"]), q3(rt["gsc"]),
                            rt["gm2"][:].broadcast_to([128, 4, 4]), ALU.is_ge)
            v.drain()
            v.tensor_tensor(g3(rt["smt1"]), g3(rt["sc2"]),
                            rt["keep"][:].broadcast_to([128, 16, 4]), ALU.mult)
            v.drain()
            v.tensor_scalar(g3(rt["smt2"]), rt["keep"][:].broadcast_to([128, 16, 4]),
                            1.0, -NEG, ALU.subtract, ALU.mult)
            v.drain()
            v.tensor_tensor(rt["sm"][:], rt["smt1"][:], rt["smt2"][:], ALU.add)
            v.drain()
            v.tensor_reduce(rt["t1"][:], v3(rt["sm"]), AX.X, ALU.max)
            v.drain()
            v.tensor_tensor(v3(rt["eq1"]), v3(rt["sm"]),
                            rt["t1"][:].broadcast_to([128, 4, 16]), ALU.is_equal)
            v.drain()
            v.scalar_tensor_tensor(rt["sm2"][:], rt["eq1"][:], NEG, rt["sm"][:],
                                   ALU.mult, ALU.add)
            v.drain()
            v.tensor_reduce(rt["t2"][:], v3(rt["sm2"]), AX.X, ALU.max)
            v.drain()
            v.tensor_tensor(v3(rt["eq2"]), v3(rt["sm2"]),
                            rt["t2"][:].broadcast_to([128, 4, 16]), ALU.is_equal)
            v.drain()
            v.tensor_tensor(rt["comb"][:], rt["eq1"][:], rt["eq2"][:], ALU.add)
            v.drain()
            v.tensor_tensor(rt["comb"][:], rt["comb"][:], rt["orig"][:], ALU.mult)
            v.drain()
            v.nop().then_inc(h["vE"], 1)

        if self.debug and "rt" in self.dbg and p == 0 and c == 0:
            keys = ["sd", "ex", "orig", "sc2", "m1", "sgm", "m2", "gsc", "gm1",
                    "gm2", "keep", "smt1", "smt2", "sm", "t1", "eq1",
                    "sm2", "t2", "eq2", "comb"]
            self.bump("qS", 16 * 20)

            @self.on("sy")
            def _(sy, _v=ve_comb, _keys=tuple(keys)):
                sy.wait_ge(h["vE"], _v)
                for i, kk in enumerate(_keys):
                    ap = rt[kk][:]
                    w = ap.shape[1] if hasattr(ap, "shape") else 64
                    sy.dma_start(out=self.dbg["rt"][i, :, 0:rt[kk].shape[1]],
                                 in_=ap).then_inc(h["qS"], 16)

        te_t2 = self.bump("tE")

        @self.on("t")
        def _(t, _v=ve_comb):
            t.wait_ge(h["vE"], _v)
            mmi = None
            for tt in range(4):
                mmi = t.transpose(P[7][0:16, tt * 128:(tt + 1) * 128],
                                  rt["comb"][:, tt * 16:(tt + 1) * 16], self.ident[:])
            mmi.then_inc(h["tE"], 1)

        ve_ct = self.bump("vE")

        @self.on("v")
        def _(v, _t=te_t2, _c=c):
            v.wait_ge(h["tE"], _t)
            v.tensor_copy(self.combine_T[0:16, _c * 512:(_c + 1) * 512], P[7][0:16, 0:512])
            v.drain()
            v.nop().then_inc(h["vE"], 1)

    def phaseB(self, p, T):
        h, P = self.h, self.P
        we, hbf, h1s, h3s, cb = T["we"], T["hbf"], T["h1s"], T["h3s"], T["cb"]
        WSLOT = 12288
        w1t = lambda sl, kdt, ib: we[:, sl * WSLOT + kdt * 512 + ib * 128: sl * WSLOT + kdt * 512 + (ib + 1) * 128]
        w3t = lambda sl, kdt, ib: we[:, sl * WSLOT + 4096 + kdt * 512 + ib * 128: sl * WSLOT + 4096 + kdt * 512 + (ib + 1) * 128]
        w2t = lambda sl, ib, odt: we[:, sl * WSLOT + 8192 + ib * 1024 + odt * 128: sl * WSLOT + 8192 + ib * 1024 + (odt + 1) * 128]

        qs_we = [None] * E
        prev_yo_te = {}
        prev_ve_yo_last = None
        prev_ve_yo_6 = None

        for e in range(E):
            sl = e % 2
            qs_we[e] = self.bump("qS", 48)

            @self.on("sy")
            def _(sy, _e=e, _sl=sl):
                if _e >= 2:
                    sy.wait_ge(h["tE"], prev_yo_te[_e - 2])
                sy.dma_start(out=we[:, _sl * WSLOT: _sl * WSLOT + 4096]
                             .rearrange("q (kdt i) -> q kdt i", i=512),
                             in_=self.wb["We1"][_e].rearrange("(kdt q) i -> q kdt i", q=128)
                             ).then_inc(h["qS"], 16)
                sy.dma_start(out=we[:, _sl * WSLOT + 4096: _sl * WSLOT + 8192]
                             .rearrange("q (kdt i) -> q kdt i", i=512),
                             in_=self.wb["We3"][_e].rearrange("(kdt q) i -> q kdt i", q=128)
                             ).then_inc(h["qS"], 16)
                sy.dma_start(out=we[:, _sl * WSLOT + 8192: _sl * WSLOT + 12288]
                             .rearrange("q (ib d1) -> q ib d1", d1=1024),
                             in_=self.wb["We2"][_e].rearrange("(ib q) d1 -> q ib d1", q=128)
                             ).then_inc(h["qS"], 16)

            te_cb = [self.bump("tE") for _ in range(4)]
            se_cb = [self.bump("sE") for _ in range(4)]

            @self.on("t")
            def _(t, _e=e, _te=tuple(te_cb), _se=tuple(se_cb), _vy=prev_ve_yo_6):
                if _vy is not None:
                    t.wait_ge(h["vE"], _vy)
                for ch in range(4):
                    if ch >= 2:
                        t.wait_ge(h["sE"], _se[ch - 2])
                    t.matmul(P[2 + ch % 2][:, 0:512], self.sel_sb[:, _e * 128:(_e + 1) * 128],
                             self.combine_T[0:16, ch * 512:(ch + 1) * 512],
                             start=True, stop=True).then_inc(h["tE"], 1)

            @self.on("s")
            def _(s, _te=tuple(te_cb)):
                for ch in range(4):
                    s.wait_ge(h["tE"], _te[ch])
                    s.copy(cb[:, ch * 512:(ch + 1) * 512], P[2 + ch % 2][:, 0:512])
                    s.drain()
                    s.nop().then_inc(h["sE"], 1)

            if e == 0:
                prev_se_h1 = None
                prev_ve_h3s = None
                prev_ve_hbf = None
            ve_hbf = [None] * 4
            for ib in range(4):
                te_h1 = self.bump("tE")
                se_h1 = self.bump("sE")
                te_h3 = self.bump("tE")
                ve_h3s = self.bump("vE")
                ve_hbf[ib] = self.bump("vE")

                @self.on("t")
                def _(t, _e=e, _sl=sl, _ib=ib, _q=qs_we[e], _secb=se_cb[3],
                      _sh1=prev_se_h1, _vh3=prev_ve_h3s,
                      _vyo=(prev_ve_yo_last if (e >= 1 and ib == 0) else None)):
                    t.wait_ge(h["qS"], _q)
                    if _ib == 0:
                        t.wait_ge(h["sE"], _secb)
                    if _sh1 is not None:
                        t.wait_ge(h["sE"], _sh1)
                    mmi = None
                    for kdt in range(NDT):
                        for ch in range(4):
                            mmi = t.matmul(P[ch][:, 0:512], w1t(_sl, kdt, _ib),
                                           self.l3x_bf[:, kdt * L + ch * 512: kdt * L + (ch + 1) * 512],
                                           start=(kdt == 0), stop=(kdt == NDT - 1),
                                           skip_group_check=True)
                    mmi.then_inc(h["tE"], 1)
                    if _vyo is not None:
                        t.wait_ge(h["vE"], _vyo)
                    if _vh3 is not None:
                        t.wait_ge(h["vE"], _vh3)
                    mmi = None
                    for kdt in range(NDT):
                        for ch in range(4):
                            mmi = t.matmul(P[4 + ch][:, 0:512], w3t(_sl, kdt, _ib),
                                           self.l3x_bf[:, kdt * L + ch * 512: kdt * L + (ch + 1) * 512],
                                           start=(kdt == 0), stop=(kdt == NDT - 1),
                                           skip_group_check=True)
                    mmi.then_inc(h["tE"], 1)

                @self.on("s")
                def _(s, _e=e, _ib=ib, _t=te_h1, _vb=prev_ve_hbf):
                    s.wait_ge(h["tE"], _t)
                    if _vb is not None:
                        s.wait_ge(h["vE"], _vb)
                    s.activation(h1s[:], self.PA[:], AF.Silu,
                                 bias=self.be1c[:, _ib * 16 + _e: _ib * 16 + _e + 1],
                                 scale=1.0)
                    s.drain()
                    s.nop().then_inc(h["sE"], 1)

                @self.on("v")
                def _(v, _e=e, _ib=ib, _t=te_h3, _s1=se_h1, _secb=se_cb[3]):
                    v.wait_ge(h["tE"], _t)
                    if _ib == 0:
                        v.wait_ge(h["sE"], _secb)
                    v.scalar_tensor_tensor(h3s[:], self.PB[:],
                                           self.be3c[:, _ib * 16 + _e: _ib * 16 + _e + 1],
                                           cb[:], ALU.add, ALU.mult)
                    v.drain()
                    v.nop().then_inc(h["vE"], 1)
                    v.wait_ge(h["sE"], _s1)
                    v.tensor_tensor(hbf[:, _ib * L: (_ib + 1) * L], h3s[:], h1s[:], ALU.mult)
                    v.drain()
                    v.nop().then_inc(h["vE"], 1)
                prev_se_h1 = se_h1
                prev_ve_h3s = ve_h3s
                prev_ve_hbf = ve_hbf[ib]

            te_yo = [None] * NDT
            ve_yo = [None] * NDT
            for odt in range(NDT):
                te_yo[odt] = self.bump("tE")
                ve_yo[odt] = self.bump("vE")

            @self.on("t")
            def _(t, _e=e, _sl=sl, _ph=ve_hbf[3], _vy=tuple(ve_yo)):
                t.wait_ge(h["vE"], _ph)
                for odt in range(NDT):
                    if odt >= 2:
                        t.wait_ge(h["vE"], _vy[odt - 2])
                    pb = (odt % 2) * 4
                    mmi = None
                    for ib in range(4):
                        for ch in range(4):
                            mmi = t.matmul(P[pb + ch][:, 0:512], w2t(_sl, ib, odt),
                                           hbf[:, ib * L + ch * 512: ib * L + (ch + 1) * 512],
                                           start=(ib == 0), stop=(ib == 3),
                                           skip_group_check=True)
                    mmi.then_inc(h["tE"], 1)
            prev_yo_te[e] = te_yo[NDT - 1]

            @self.on("v")
            def _(v, _e=e, _ty=tuple(te_yo)):
                for odt in range(NDT):
                    v.wait_ge(h["tE"], _ty[odt])
                    pw = self.PA if odt % 2 == 0 else self.PB
                    dst = self.y_acc[:, odt * L: (odt + 1) * L]
                    if _e == 0:
                        v.tensor_copy(dst, pw[:])
                    else:
                        v.tensor_tensor(dst, dst, pw[:], ALU.add)
                    v.drain()
                    v.nop().then_inc(h["vE"], 1)
            prev_ve_yo_last = ve_yo[NDT - 1]
            prev_ve_yo_6 = ve_yo[NDT - 2]
        self.fence()

    def phaseZ(self, p, T):
        h, P = self.h, self.P
        zw = T["zw"]
        z1s, z3s, zzw = T["z1s"], T["z3s"], T["zzw"]
        zw1 = lambda kdt, ob: zw[:, kdt * 1024 + ob * 128: kdt * 1024 + (ob + 1) * 128]
        zw3 = lambda kdt, ob: zw[:, 8192 + kdt * 1024 + ob * 128: 8192 + kdt * 1024 + (ob + 1) * 128]

        qs_zw = self.bump("qS", 32)

        @self.on("sy")
        def _(sy):
            for k, srcw in [(0, self.wb["ws1"]), (8192, self.wb["ws3"])]:
                sy.dma_start(out=zw[:, k: k + 8192].rearrange("q (kdt o) -> q kdt o", o=1024),
                             in_=srcw.rearrange("(kdt q) o -> q kdt o", q=128)
                             ).then_inc(h["qS"], 16)

        pe_zz = [None] * NDT
        te_z1_hist = []
        qg_zz = [None] * NDT
        for ob in range(NDT):
            te_z1 = self.bump("tE")
            te_z1_hist.append(te_z1)
            se_z1 = self.bump("sE")
            te_z3 = self.bump("tE")
            se_z3 = self.bump("sE")
            pe_zz[ob] = self.bump("pE")
            qg_zz[ob] = self.bump("qG", 16)

            @self.on("t")
            def _(t, _ob=ob, _q=qs_zw, _prev=pe_zz[ob - 1] if ob >= 1 else None):
                t.wait_ge(h["qS"], _q)
                if _prev is not None:
                    t.wait_ge(h["pE"], _prev)
                mmi = None
                for kdt in range(NDT):
                    for ch in range(4):
                        mmi = t.matmul(P[ch][:, 0:512], zw1(kdt, _ob),
                                       self.l3x_bf[:, kdt * L + ch * 512: kdt * L + (ch + 1) * 512],
                                       start=(kdt == 0), stop=(kdt == NDT - 1),
                                       skip_group_check=True)
                mmi.then_inc(h["tE"], 1)

            @self.on("s")
            def _(s, _ob=ob, _t=te_z1, _pz=(pe_zz[ob - 1] if ob >= 1 else None)):
                s.wait_ge(h["tE"], _t)
                if _pz is not None:
                    s.wait_ge(h["pE"], _pz)
                s.activation(z1s[:], self.PA[:], AF.Silu, bias=self.col("bs1", _ob), scale=1.0)
                s.drain()
                s.nop().then_inc(h["sE"], 1)

            @self.on("t")
            def _(t, _ob=ob):
                mmi = None
                for kdt in range(NDT):
                    for ch in range(4):
                        mmi = t.matmul(P[4 + ch][:, 0:512], zw3(kdt, _ob),
                                       self.l3x_bf[:, kdt * L + ch * 512: kdt * L + (ch + 1) * 512],
                                       start=(kdt == 0), stop=(kdt == NDT - 1),
                                       skip_group_check=True)
                mmi.then_inc(h["tE"], 1)

            @self.on("s")
            def _(s, _ob=ob, _t=te_z3, _pz=(pe_zz[ob - 1] if ob >= 1 else None)):
                s.wait_ge(h["tE"], _t)
                if _pz is not None:
                    s.wait_ge(h["pE"], _pz)
                s.activation(z3s[:], self.PB[:], AF.Identity, bias=self.col("bs3", _ob), scale=1.0)
                s.drain()
                s.nop().then_inc(h["sE"], 1)


            @self.on("g")
            def _(g, _ob=ob, _s3=se_z3, _qz=(qg_zz[ob - 2] if ob >= 2 else None)):
                g.wait_ge(h["sE"], _s3)
                if _qz is not None:
                    g.wait_ge(h["qG"], _qz)
                sl = (_ob % 2) * L
                g.tensor_tensor(zzw[:, sl: sl + L], z3s[:, 0:L], z1s[:, 0:L], ALU.mult)
                g.drain()
                g.nop().then_inc(h["pE"], 1)
                g.dma_start(out=self.zz_dram[_ob], in_=zzw[:, sl: sl + L]).then_inc(h["qG"], 16)
        self._z_state = (te_z1_hist, qg_zz)

    def phaseZ2(self, p, T):
        h, P = self.h, self.P
        zw, zzst = T["zw"], T["zzst"]
        y2f, y2t, x1r = T["y2f"], T["y2t"], T["x1r"]
        zw2 = lambda ob, odt: zw[:, ob * 1024 + odt * 128: ob * 1024 + (odt + 1) * 128]
        te_z1_hist, qg_zz = self._z_state

        qs_zw2 = self.bump("qS", 32)

        @self.on("sy")
        def _(sy, _t=te_z1_hist[NDT - 1]):
            sy.dma_start(out=self.be2_sb[:], in_=self.w["be2"]).then_inc(h["qS"], 16)
            sy.wait_ge(h["tE"], _t)
            sy.dma_start(out=zw[:, 0:8192].rearrange("q (kdt o) -> q kdt o", o=1024),
                         in_=self.wb["ws2"].rearrange("(kdt q) o -> q kdt o", q=128)
                         ).then_inc(h["qS"], 16)

        prev_ve_fin = None
        prev_qs_out = None
        te_z2_hist = []
        for ch in range(4):
            qs_zzst = self.bump("qS", 16)
            qs_x1 = self.bump("qS", 16)

            @self.on("sy")
            def _(sy, _ch=ch, _vf=prev_ve_fin, _qz=qg_zz[NDT - 1],
                  _tz=(te_z2_hist[ch - 1] if ch >= 1 else None)):
                sy.wait_ge(h["qG"], _qz)
                if _tz is not None:
                    sy.wait_ge(h["tE"], _tz)
                sy.dma_start(out=zzst[:, 0:4096]
                             .rearrange("q (ob o) -> q ob o", o=512),
                             in_=self.zz_dram.rearrange("ob q t -> q ob t")[:, :, _ch * 512:(_ch + 1) * 512]
                             ).then_inc(h["qS"], 16)
                if _vf is not None:
                    sy.wait_ge(h["vE"], _vf)
                sy.dma_start(out=x1r[:].rearrange("q (tt d) -> q tt d", d=D),
                             in_=self.x1_scr.rearrange("(cc tt q) d -> cc q tt d", q=128, tt=4)[_ch]
                             ).then_inc(h["qS"], 16)

            se_y2 = []
            ve_z2 = []
            for odt in range(NDT):
                te_z2 = self.bump("tE")
                ve_z2.append(self.bump("vE"))
                te_y2 = self.bump("tE")
                se_y2.append(self.bump("sE"))

                @self.on("t")
                def _(t, _ch=ch, _odt=odt, _vz=ve_z2[odt], _qw=qs_zw2, _qz=qs_zzst,
                      _vzprev=(ve_z2[odt - 2] if odt >= 2 else None),
                      _seprev=(se_y2[odt - 1] if odt >= 1 else None)):
                    if _odt == 0:
                        t.wait_ge(h["qS"], max(_qw, _qz))
                    if _vzprev is not None:
                        t.wait_ge(h["vE"], _vzprev)
                    mmi = None
                    for ob in range(NDT):
                        mmi = t.matmul(P[_odt % 2][:, 0:512], zw2(ob, _odt),
                                       zzst[:, ob * 512:(ob + 1) * 512],
                                       start=(ob == 0), stop=False, skip_group_check=True)
                    mmi = t.matmul(P[_odt % 2][:, 0:512],
                                   self.be2_sb[0:16, _odt * 128:(_odt + 1) * 128],
                                   self.combine_T[0:16, _ch * 512:(_ch + 1) * 512],
                                   start=False, stop=True, skip_group_check=True)
                    mmi.then_inc(h["tE"], 1)
                    t.wait_ge(h["vE"], _vz)
                    if _seprev is not None:
                        t.wait_ge(h["sE"], _seprev)
                    mmi = None
                    for tt in range(4):
                        mmi = t.transpose(P[2 + _odt % 2][0:128, tt * 128:(tt + 1) * 128],
                                          y2f[:, (_odt % 2) * 512 + tt * 128:(_odt % 2) * 512 + (tt + 1) * 128],
                                          self.ident[:])
                    mmi.then_inc(h["tE"], 1)

                @self.on("v")
                def _(v, _ch=ch, _odt=odt, _tz=te_z2):
                    v.wait_ge(h["tE"], _tz)
                    v.scalar_tensor_tensor(y2f[:, (_odt % 2) * 512:(_odt % 2) * 512 + 512],
                                           P[_odt % 2][:, 0:512], self.col("bs2", _odt),
                                           self.y_acc[:, _odt * L + _ch * 512: _odt * L + (_ch + 1) * 512],
                                           ALU.add, ALU.add)
                    v.drain()
                    v.nop().then_inc(h["vE"], 1)

                @self.on("s")
                def _(s, _odt=odt, _ty=te_y2, _qo=(prev_qs_out if odt == 0 else None)):
                    s.wait_ge(h["tE"], _ty)
                    if _qo is not None:
                        s.wait_ge(h["qS"], _qo)
                    s.copy(y2t[:].rearrange("q (tt d) -> q tt d", d=D)[:, :, _odt * 128:(_odt + 1) * 128],
                           P[2 + _odt % 2][:, 0:512].rearrange("q (tt d1) -> q tt d1", d1=128))
                    s.drain()
                    s.nop().then_inc(h["sE"], 1)
            te_z2_hist.append(self.c["tE"])

            @self.on("v")
            def _(v, _s=se_y2[NDT - 1]):
                v.wait_ge(h["sE"], _s)
            y2tile = lambda tt: y2t[:, tt * 1024:(tt + 1) * 1024]
            self.emit_stats([y2tile(tt) for tt in range(4)])
            self.emit_rstd()
            ve_fin = self.bump("vE")

            @self.on("v")
            def _(v, _q=qs_x1, _yt=y2tile):
                v.wait_ge(h["qS"], _q)
                for tt in range(4):
                    v.tensor_scalar(self.sq[:], _yt(tt), self.muc[:, tt:tt + 1],
                                    self.rstdc[:, tt:tt + 1], ALU.subtract, ALU.mult)
                    v.drain()
                    v.tensor_tensor(_yt(tt), self.sq[:],
                                    x1r[:, tt * 1024:(tt + 1) * 1024], ALU.add)
                    v.drain()
                v.nop().then_inc(h["vE"], 1)

            qs_out = self.bump("qS", 16)

            @self.on("sy")
            def _(sy, _ch=ch, _v=ve_fin, _p=p):
                sy.wait_ge(h["vE"], _v)
                sy.dma_start(out=self.out[_p].rearrange("(cc tt q) d -> cc q tt d", q=128, tt=4)[_ch],
                             in_=y2t[:].rearrange("q (tt d) -> q tt d", d=D)).then_inc(h["qS"], 16)
            prev_ve_fin = ve_fin
            prev_qs_out = qs_out

        if self.debug and "comb" in self.dbg:
            self.bump("qS", 16)

            @self.on("sy")
            def _(sy, _p=p):
                sy.dma_start(out=self.dbg["comb"][_p], in_=self.combine_T[:]).then_inc(h["qS"], 16)

        self.fence()

    def build(self):
        nc = self.nc
        self.declare()
        top = ExitStack()
        SB = lambda n, s, dt=F32: top.enter_context(nc.sbuf_tensor(n, s, dt))
        PSm = lambda n, s, dt=F32: top.enter_context(nc.psum_tensor(n, s, dt))
        with top:
            self.persistent(SB, PSm)
            self.sc_sb = SB("sc_sb", [16, 512])
            self.startup()
            for p in range(NB):
                sfx = f"_p{p}"
                ps = ExitStack()
                S2 = lambda n, s, dt=F32: ps.enter_context(nc.sbuf_tensor(n + sfx, s, dt))
                with ps:
                    self.l3x_bf = S2("l3x_bf", [128, NDT * L], BF16)
                    a1 = ExitStack()
                    S3 = lambda n, s, dt=F32: a1.enter_context(nc.sbuf_tensor(n + sfx, s, dt))
                    with a1, nc.Block() as blk:
                        TA = {
                            "xsl": S3("xsl", [128, 2 * 4096]),
                            "ln1": S3("ln1", [128, 4096]),
                            "l1f": S3("l1f", [128, NDT * 512]),
                            "l1b": S3("l1b", [128, NDT * 512], BF16),
                            "wgT": S3("wgT", [128, NDT * D], BF16),
                            "gt": S3("gt", [128, 2 * 512]),
                            "xgf": S3("xgf", [128, 2 * 512]),
                        }
                        self.sq = S3("sq1", [128, 1024])
                        if p == 0:
                            TA["tap"] = (S3("A_f", [128, NDT * 256]),
                                         S3("Bp_f", [128, NDT * NST]),
                                         S3("Cp_f", [128, NDT * NST]),
                                         S3("u1", [128, 2304]),
                                         S3("dg_st", [128, 2 * J * 128], BF16))
                        self.phaseA1(p, TA)
                        self.fence()
                        self.flush(blk)
                    a2 = ExitStack()
                    S4 = lambda n, s, dt=F32: a2.enter_context(nc.sbuf_tensor(n + sfx, s, dt))
                    with a2, nc.Block() as blk:
                        TB = {
                            "dg": S4("dg", [128, 2 * J * 128], BF16),
                            "xb": S4("xb", [128, 2 * (J + 1024)], BF16),
                            "xf": S4("xf", [128, 1024]),
                            "yF": S4("yF", [128, 2 * 1024]),
                            "ytp": S4("ytp", [128, 2 * 8 * 1024]),
                            "xre": S4("xre", [128, 4096]),
                            "ln3": S4("ln3", [128, 4096]),
                            "l3f": S4("l3f", [128, 2 * 512]),
                        }
                        self.sq = S4("sq2", [128, 1024])
                        self.rt = {nm: S4("rt_" + nm, [128, sz]) for nm, sz in self.RT_SIZES}
                        self.gamma_b = S4("gamma_b", [128, D])
                        self.beta_b = S4("beta_b", [128, D])
                        self.phaseA2(p, TB)
                        self.flush(blk)
                    if STAGE >= 3:
                        ya = ExitStack()
                        self.y_acc = ya.enter_context(nc.sbuf_tensor("y_acc" + sfx, [128, NDT * L], F32))
                        ps.enter_context(ya.pop_all() if False else ya)
                        b1 = ExitStack()
                        S5 = lambda n, s, dt=F32: b1.enter_context(nc.sbuf_tensor(n + sfx, s, dt))
                        with b1, nc.Block() as blk:
                            TC = {
                                "we": S5("we", [128, 2 * 12288], BF16),
                                "hbf": S5("hbf", [128, 4 * L], BF16),
                                "h1s": S5("h1s", [128, L]),
                                "h3s": S5("h3s", [128, L]),
                                "cb": S5("cb", [128, L]),
                            }
                            self.phaseB(p, TC)
                            self.flush(blk)
                        z0 = ExitStack()
                        S60 = lambda n, s, dt=F32: z0.enter_context(nc.sbuf_tensor(n + sfx, s, dt))
                        with z0:
                            zw = S60("zw", [128, 2 * 8192], BF16)
                            za = ExitStack()
                            S6 = lambda n, s, dt=F32: za.enter_context(nc.sbuf_tensor(n + sfx, s, dt))
                            with za, nc.Block() as blk:
                                TD = {
                                    "zw": zw,
                                    "z1s": S6("z1s", [128, L]),
                                    "z3s": S6("z3s", [128, L]),
                                    "zzw": S6("zzw", [128, 2 * L], BF16),
                                }
                                self.phaseZ(p, TD)
                                self.fence()
                                self.flush(blk)
                            zb = ExitStack()
                            S6 = lambda n, s, dt=F32: zb.enter_context(nc.sbuf_tensor(n + sfx, s, dt))
                            with zb, nc.Block() as blk:
                                TE2 = {
                                    "zw": zw,
                                    "zzst": S6("zzst", [128, 4096], BF16),
                                    "y2f": S6("y2f", [128, 2 * 512]),
                                    "y2t": S6("y2t", [128, 4096]),
                                    "x1r": S6("x1r", [128, 4096]),
                                }
                                self.sq = S6("sq3", [128, 1024])
                                self.be2_sb = S6("be2_sb", [E, D])
                                self.phaseZ2(p, TE2)
                                self.flush(blk)
            with nc.Block() as blk:
                @self.on("sy")
                def _(sy):
                    sy.wait_ge(self.h["qS"], self.c["qS"])
                    sy.wait_ge(self.h["qG"], self.c["qG"])
                self.flush(blk)
        return nc


def make_consts():
    sel = np.zeros((E, E * 128), np.float32)
    for e in range(E):
        sel[e, e * 128:(e + 1) * 128] = 1.0
    return {
        "c_ident": np.eye(128, dtype=np.float32),
        "c_sel": sel,
        "c_ones": np.ones((1, 128), np.float32),
    }


_CACHE = {}


def get_nc(debug=False):
    key = (STAGE, debug)
    if key not in _CACHE:
        b = B(debug=debug)
        nc = b.build()
        _CACHE[key] = (nc, b.dbg)
    return _CACHE[key]


def kernel(**inputs):
    nc, _ = get_nc(False)
    consts = make_consts()
    x = np.ascontiguousarray(inputs["L0_attn_x"])
    ren = {
        "A": "L0_attn_A", "Bp": "L0_attn_Bp", "Cp": "L0_attn_Cp", "Dp": "L0_attn_Dp",
        "Wg": "L0_attn_Wg", "bg": "L0_attn_bg", "gamma": "L0_attn_gamma",
        "beta": "L0_attn_beta", "gate_w": "L0_ffn_gate_w", "gate_b": "L0_ffn_gate_b",
        "We1": "L0_ffn_We1", "be1": "L0_ffn_be1", "We2": "L0_ffn_We2",
        "be2": "L0_ffn_be2", "We3": "L0_ffn_We3", "be3": "L0_ffn_be3",
        "ws1": "L0_ffn_ws1", "bs1": "L0_ffn_bs1", "ws2": "L0_ffn_ws2",
        "bs2": "L0_ffn_bs2", "ws3": "L0_ffn_ws3", "bs3": "L0_ffn_bs3",
    }
    base = {k: np.ascontiguousarray(inputs[v]) for k, v in ren.items()}
    base.update(consts)
    in_maps = []
    for c in range(NCORE):
        m = dict(base)
        m["x"] = np.ascontiguousarray(x[c * NB:(c + 1) * NB])
        in_maps.append(m)
    res = run_bass_kernel_spmd(nc, in_maps, core_ids=list(range(NCORE)))
    return np.concatenate([r["out"] for r in res.results], axis=0)


if __name__ == "__main__":
    b = B(debug=False)
    b.build()
    print("build ok, instructions:", b.nc.n_instructions())
